# version 06
# baseline (speedup 1.0000x reference)
import numpy as np
import ml_dtypes
import concourse.bass as bass
import concourse.mybir as mybir
from concourse.bass_utils import run_bass_kernel_spmd

F32 = mybir.dt.float32
BF16 = mybir.dt.bfloat16
I32 = mybir.dt.int32
U8 = mybir.dt.uint8
AF = mybir.ActivationFunctionType
ALU = mybir.AluOpType
AX = mybir.AxisListType
BF = ml_dtypes.bfloat16

NCORES = 8
T = 16384
D = 2048
H = 16
DK = 128
M = 128
KW = 4
C = 64
HB = 64
HID = 5632
NHH = HID // 128
SCALE = DK ** -0.5
EPS = 1e-5
L2EPS = 1e-6
LNEPS = 1e-5

ENGINES = ("pe", "act", "dve", "pool", "sp")
PSUM_COARSE = True
QUEUES = ("qs", "qa", "qp")


class KB:

    def __init__(self, nc):
        self.nc = nc
        self.q = {e: [] for e in ENGINES}
        self.cnt = {e: 0 for e in ENGINES}
        self.dcnt = {qn: 0 for qn in QUEUES}
        self.wm = {e: {} for e in ENGINES}
        self.lastw = {}
        self.readers = {}
        self.sems = {}
        for s in ("pe", "act", "dve", "pool", "qs", "qa", "qp"):
            self.sems[s] = nc.alloc_semaphore("sem_" + s)
        self.dma_engine = {"qs": "sp", "qa": "act", "qp": "pool"}
        self.fence = {qn: 0 for qn in QUEUES}

    def _sem_val(self, tok):
        s, v = tok
        if s in QUEUES:
            return (s, 16 * self._fence_for(s, v))
        return (s, v)

    def _fence_for(self, qn, k):
        if self.fence[qn] >= k:
            return self.fence[qn]
        n = self.dcnt[qn]
        eng = self.dma_engine[qn]
        sems = self.sems
        self.q[eng].append(lambda e, qn=qn, n=n: e.wait_ge(sems[qn], 16 * n))
        self.wm[eng][qn] = 16 * n
        self.fence[qn] = n
        return n

    def _edges(self, eng, reads, writes):
        need = {}
        for r in reads:
            t = self.lastw.get(r)
            if t is not None:
                s, v = self._sem_val(t)
                need[s] = max(need.get(s, 0), v)
        for w in writes:
            t = self.lastw.get(w)
            if t is not None:
                s, v = self._sem_val(t)
                need[s] = max(need.get(s, 0), v)
            for t in self.readers.get(w, ()):
                s, v = self._sem_val(t)
                need[s] = max(need.get(s, 0), v)
        wmap = self.wm[eng]
        out = []
        for s, v in need.items():
            if wmap.get(s, 0) < v:
                out.append((s, v))
                wmap[s] = v
        return out

    def op(self, eng, fn, reads=(), writes=(), sem_extra=None):
        reads = list(reads)
        writes = list(writes)
        for r in list(reads):
            if isinstance(r[0], str) and r[0].startswith("pb"):
                reads.remove(r)
                if r not in writes:
                    writes.append(r)
        if PSUM_COARSE:
            nw = []
            coarse = False
            for w in writes:
                if isinstance(w[0], str) and w[0].startswith("pb"):
                    coarse = True
                else:
                    nw.append(w)
            if coarse:
                nw.append(("PSUM",))
            writes = nw
        waits = self._edges(eng, reads, writes)
        self.cnt[eng] += 1
        tok = (eng, self.cnt[eng])
        sem = self.sems[eng]
        sems = self.sems

        def emit(e, waits=waits, fn=fn, sem=sem, sem_extra=sem_extra):
            for s, v in waits[1:]:
                e.wait_ge(sems[s], v)
            inst = fn(e)
            if waits:
                inst._wait_ge(sems[waits[0][0]], waits[0][1])
            inst.then_inc(sem, 1)
            if sem_extra is not None:
                inst.then_inc(sem_extra[0], sem_extra[1])
        self.q[eng].append(emit)
        for w in writes:
            self.lastw[w] = tok
            self.readers[w] = []
        for r in reads:
            self.readers.setdefault(r, []).append(tok)
        return tok

    def dma(self, qn, fn, reads=(), writes=()):
        eng = self.dma_engine[qn]
        waits = self._edges(eng, reads, writes)
        self.dcnt[qn] += 1
        tok = (qn, self.dcnt[qn])
        sem = self.sems[qn]
        sems = self.sems

        def emit(e, waits=waits, fn=fn, sem=sem):
            for s, v in waits[1:]:
                e.wait_ge(sems[s], v)
            inst = fn(e)
            if waits:
                inst._wait_ge(sems[waits[0][0]], waits[0][1])
            inst.then_inc(sem, 16)
        self.q[eng].append(emit)
        for w in writes:
            self.lastw[w] = tok
            self.readers[w] = []
        for r in reads:
            self.readers.setdefault(r, []).append(tok)
        return tok

    def barrier_wait(self, eng, toks):
        need = {}
        for t in toks:
            s, v = self._sem_val(t)
            need[s] = max(need.get(s, 0), v)
        wmap = self.wm[eng]
        sems = self.sems
        for s, v in need.items():
            if wmap.get(s, 0) < v:
                wmap[s] = v
                self.q[eng].append(lambda e, s=s, v=v: e.wait_ge(sems[s], v))

    def finish(self):
        toks = [(e, self.cnt[e]) for e in ENGINES if e != "sp" and self.cnt[e]]
        toks += [(qn, self.dcnt[qn]) for qn in QUEUES if self.dcnt[qn]]
        for e in ENGINES:
            self.barrier_wait(e, toks)

    def replay(self):
        nc = self.nc
        with nc.Block() as block:
            @block.tensor
            def _(e):
                for f in self.q["pe"]:
                    f(e)

            @block.scalar
            def _(e):
                for f in self.q["act"]:
                    f(e)

            @block.vector
            def _(e):
                for f in self.q["dve"]:
                    f(e)

            @block.gpsimd
            def _(e):
                for f in self.q["pool"]:
                    f(e)

            @block.sync
            def _(e):
                for f in self.q["sp"]:
                    f(e)


def ceil_div(a, b):
    return (a + b - 1) // b



class Cfg:

    def __init__(self, tloc=2048, phases=("aproj", "az", "amisc", "ascan", "aagg",
                                          "gather", "bscan", "f1", "f2"),
                 ncores=NCORES, debug_taps=(), host_gather=False):
        self.tloc = tloc
        self.nch = tloc // C
        self.ntb = tloc // 128
        self.phases = phases
        self.ncores = ncores
        self.debug_taps = debug_taps
        self.host_gather = host_gather


class Builder:
    def __init__(self, cfg):
        self.cfg = cfg
        self.nc = bass.Bass("TRN2", target_bir_lowering=False)
        self.kb = KB(self.nc)
        self.din = {}
        self.spills = {}
        self.sb = {}
        self.ps = {}
        self.ar_off = 0
        self._declare_dram()
        self._declare_sbuf()

    def inp(self, name, shape, dtype=F32):
        self.din[name] = self.nc.dram_tensor(name, list(shape), dtype,
                                             kind="ExternalInput").ap()
        return self.din[name]

    def _declare_dram(self):
        cfg, tl, nch = self.cfg, self.cfg.tloc, self.cfg.nch
        inp = self.inp
        inp("hT", [D, HB + tl])
        inp("h", [tl, D])
        for w in ("Wq", "Wk", "Wv", "Ww", "Wo"):
            inp(w, [D, D])
        inp("cqr", [128, 16 * KW]); inp("ckr", [128, 16 * KW]); inp("cvr", [128, 16 * KW])
        inp("Wf1", [D, 128]); inp("Wf2", [128, D]); inp("Wbr", [128, 16 * H])
        inp("Wg1", [D, 128]); inp("Wg2", [128, D])
        inp("bg2m", [128, 16]); inp("normw", [128, 1])
        inp("WgT8", [D, HID], U8); inp("WuT8", [D, HID], U8); inp("WdT8", [HID, D], U8)
        inp("sgm", [128, NHH]); inp("sum_", [128, NHH]); inp("sdr", [1, D])
        inp("ident", [128, 128], BF16)
        inp("identf", [128, 128])
        inp("tri", [C, C], BF16)
        inp("sumsels", [128, 16 * 32], BF16)
        inp("trige", [32, 32])
        inp("ones32", [32, 1])
        inp("ones1x128", [1, 128], BF16)
        inp("ones128", [128, 1], BF16)
        inp("mselM", [2, (NCORES - 1) * 128], BF16)
        inp("mselcol", [128, NCORES - 1])
        inp("m1col", [128, NCORES - 1])
        self.out_t = self.nc.dram_tensor("out", [tl, D], F32, kind="ExternalOutput").ap()
        sp = self.spills

        def spill(name, shape, dtype=BF16, kind="Internal"):
            sp[name] = self.nc.dram_tensor("sp_" + name, list(shape), dtype, kind=kind).ap()
        spill("q", [H, 128, tl])
        spill("kfm", [H, 128, tl])
        spill("ktm", [H, tl, 128])
        spill("vtm", [H, tl, 128])
        spill("wtm", [H, tl, 128])
        spill("g", [tl, D])
        spill("gs", [H, 128, tl])
        spill("u", [tl, D])
        spill("si", [tl, D])
        spill("x1", [tl, D])
        spill("G", [NHH, 128, tl])
        spill("ec", [nch, D])
        spill("wsuf", [nch, D])
        kci = "ExternalOutput" if cfg.host_gather else "Internal"
        kco = "ExternalInput" if cfg.host_gather else "Internal"
        spill("ccin", [257, D], BF16, kind=kci)
        spill("ccout", [cfg.ncores * 257, D], BF16, kind=kco)
        self.dbg = {}
        for name in cfg.debug_taps:
            s = sp[name]
            self.dbg[name] = self.nc.dram_tensor(
                "dbg_" + name, list(s.shape), s.dtype, kind="ExternalOutput").ap()

    def sbuf(self, name, shape, dtype=BF16):
        t = self.nc.alloc_sbuf_tensor("sb_" + name, list(shape), dtype)
        self.sb[name] = t
        return t

    def arena(self, name, shape, dtype):
        esz = {F32: 4, BF16: 2, U8: 1, I32: 4}[dtype]
        nbytes = shape[1] * esz
        off = self.ar_off
        assert off + nbytes <= self.AR_BYTES, f"arena overflow {name}: {off}+{nbytes}"
        ap = self.AR.ap().bitcast(dtype)[0:shape[0], off // esz: off // esz + shape[1]]
        self.ar_off = off + ((nbytes + 31) // 32) * 32
        self.sb[name] = ap
        return ap

    def arena_reset(self):
        self.ar_off = 0

    def _declare_sbuf(self):
        cfg, tl, nch = self.cfg, self.cfg.tloc, self.cfg.nch
        self.AR_BYTES = 156 * 1024
        self.AR = self.nc.alloc_sbuf_tensor("sb_AR", [128, self.AR_BYTES], U8)
        sbuf = self.sbuf
        sbuf("ident", [128, 128], BF16)
        sbuf("m1col", [128, NCORES - 1], F32)
        sbuf("tri", [C, C], BF16)
        sbuf("ones1x128", [1, 128], BF16)
        sbuf("ones128", [128, 1], BF16)
        sbuf("beta", [128, (tl // 128) * H], BF16)
        sbuf("beta2", [64, 2 * (tl // 128) * H], BF16)
        sbuf("cCrows", [32, D], F32)
        sbuf("ec16all", [16, nch * 128], BF16)
        sbuf("Sk", [128, H * M], F32)
        sbuf("Sv", [128, H * DK], F32)
        sbuf("Skbf", [128, H * M], BF16)
        sbuf("Svbf", [128, H * DK], BF16)
        sbuf("ecct", [128, 16], F32)
        sbuf("normw", [128, 1], F32)
        sbuf("bg2m", [128, 16], F32)
        sbuf("cw", [128, 3 * 16 * KW], F32)
        sbuf("sumsels", [128, 16 * 32], BF16)
        sbuf("trige", [32, 32], F32)
        sbuf("ones32", [32, 1], F32)
        sbuf("sgm", [128, NHH], F32)
        sbuf("sum_", [128, NHH], F32)
        sbuf("mselM", [2, (NCORES - 1) * 128], BF16)
        sbuf("mselcol", [128, NCORES - 1], F32)
        sbuf("row2", [2, D], BF16)
        self.CVALS = [L2EPS, LNEPS, EPS, SCALE, -0.5, -1.0, 0.5, 2.0, 1.0 / 128, -128.0,
                      1.0 / D, 1.0]
        sbuf("cvals", [128, len(self.CVALS)], F32)
        sbuf("c128", [128, 1], BF16)
        self.PSA = self.nc.alloc_psum_tensor("psA", [128, 2048], F32)
        self.PBH = {i: self.nc.alloc_psum_tensor("ps%d" % i, [128, 512], F32)
                    for i in range(4, 8)}

    def pb(self, i):
        if i < 4:
            return self.PSA.ap()[:, i * 512:(i + 1) * 512]
        return self.PBH[i].ap()

    def ld_consts(self):
        kb, din, sb = self.kb, self.din, self.sb
        pairs = [("ident", "ident"), ("tri", "tri"), ("ones1x128", "ones1x128"),
                 ("ones128", "ones128"), ("normw", "normw"), ("bg2m", "bg2m"),
                 ("sumsels", "sumsels"), ("trige", "trige"), ("ones32", "ones32"),
                 ("m1col", "m1col"),
                 ("sgm", "sgm"), ("sum_", "sum_"), ("mselcol", "mselcol")]
        for dn, sn in pairs:
            kb.dma("qs", lambda e, d=din[dn], s=sb[sn]: e.dma_start(out=s[:], in_=d[:]),
                   writes=[("sb", sn)])
        kb.dma("qs", lambda e: e.dma_start(out=sb["mselM"][:], in_=din["mselM"][:]),
               writes=[("sb", "mselM")])
        for i, dn in enumerate(("cqr", "ckr", "cvr")):
            kb.dma("qs", lambda e, d=din[dn], i=i: e.dma_start(
                out=sb["cw"][:, i * 64:(i + 1) * 64], in_=d[:]),
                writes=[("sb", "cw", i)])
        kb.op("dve", lambda e: e.memset(sb["row2"][:], 1.0), writes=[("sb", "row2one")])
        kb.op("dve", lambda e: e.memset(sb["c128"][:], 128.0), writes=[("sb", "c128")])
        for i, v in enumerate(self.CVALS):
            kb.op("dve", lambda e, i=i, v=v: e.memset(sb["cvals"][:, i:i + 1], v),
                  writes=[("cvals", i)])
            self.nc.const_aps.register(F32, v, sb["cvals"][:, i:i + 1])

    def alloc_a_arena(self):
        tl = self.cfg.tloc
        self.arena_reset()
        self.arena("hfm", [128, 16 * (HB + tl)], BF16)
        self.arena("wpan0", [128, 16 * 128], BF16)
        self.arena("wpan1", [128, 16 * 128], BF16)
        self.arena("pbf0", [128, HB + tl], BF16)
        self.arena("pbf1", [128, HB + tl], BF16)
        self.arena("convy0", [128, tl], BF16)
        self.arena("convy1", [128, tl], BF16)
        self.arena("tstg0", [128, tl], BF16)
        self.arena("tstg1", [128, tl], BF16)
        self.arena("z1", [128, tl], BF16)
        self.arena("zg1", [128, tl], BF16)
        self.arena("wsmall", [128, 2 * 16 * 128], BF16)
        self.arena("wf2", [128, D], BF16)
        self.arena("wg2", [128, D], BF16)
        self.arena("wbr", [128, 16 * H], BF16)
        self.arena("ztmp0", [128, 512], F32)
        self.arena("ztmp1", [128, 512], F32)
        self.arena("gstg", [128, D], BF16)
        self.arena("ecall", [32, D], BF16)
        self.arena("wsufb", [32, D], BF16)

    def jgroups(self):
        tl = self.cfg.tloc
        return [(0, HB)] + [(HB + i * 512, min(512, tl - i * 512))
                            for i in range(ceil_div(tl, 512))]

    def ld_hfm(self):
        kb, sb, din = self.kb, self.sb, self.din
        tl = self.cfg.tloc
        for dc in range(16):
            kb.dma("qp", lambda e, dc=dc: e.dma_start(
                out=sb["hfm"][:, dc * (HB + tl):(dc + 1) * (HB + tl)],
                in_=din["hT"][dc * 128:(dc + 1) * 128, :]),
                writes=[("hfm", dc)])

    def phase_aproj(self):
        kb, sb, din, sp = self.kb, self.sb, self.din, self.spills
        tl, ntb = self.cfg.tloc, self.cfg.ntb
        pb = self.pb
        HT = HB + tl
        tensors = [("q", "Wq", 0, ("q",), False),
                   ("k", "Wk", 1, ("kfm", "ktm"), True),
                   ("v", "Wv", 2, ("vtm",), True),
                   ("w", "Ww", 2, ("wtm",), True)]
        for xi, (xn, wn, cwi, spnames, want_tm) in enumerate(tensors):
            for fc in range(16):
                par = fc % 2
                wp = sb["wpan%d" % par]
                w4 = din[wn].rearrange("(dc p) (fc f) -> p dc fc f", p=128, f=128)
                kb.dma("qp", lambda e, w4=w4, fc=fc, wp=wp: e.dma_start(
                    out=wp[:], in_=w4[:, :, fc, :]),
                    writes=[("wpan", par)])
                pbf = sb["pbf%d" % par]
                for jg, (t0, tn) in enumerate(self.jgroups()):
                    bank = jg % 2
                    for dc in range(16):
                        kb.op("pe", lambda e, wp=wp, dc=dc, t0=t0, tn=tn, bank=bank: e.matmul(
                            pb(bank)[:, 0:tn],
                            wp[:, dc * 128:(dc + 1) * 128],
                            sb["hfm"][:, dc * HT + t0: dc * HT + t0 + tn],
                            start=(dc == 0), stop=(dc == 15)),
                            reads=[("wpan", par), ("hfm", dc)], writes=[("pb", bank)])
                    kb.op("act", lambda e, t0=t0, tn=tn, bank=bank, pbf=pbf: e.activation(
                        out=pbf[:, t0:t0 + tn], in_=pb(bank)[:, 0:tn], func=AF.Copy),
                        reads=[("pb", bank)], writes=[("pbf", par, jg)])
                cy = sb["convy%d" % par]
                pr = [("pbf", par, jg) for jg in range(len(self.jgroups()))]
                cwbase = cwi * 64 + fc * KW
                kb.op("dve", lambda e, cy=cy, pbf=pbf,
                      cc=sb["cw"][:, cwbase + 3:cwbase + 4]: e.tensor_scalar(
                    cy[:], pbf[:, HB:HB + tl], cc, None, op0=ALU.mult),
                    reads=pr + [("sb", "cw", cwi)], writes=[("convy", par)])
                for i, eng in ((2, "dve"), (1, "dve"), (0, "dve")):
                    kb.op(eng, lambda e, i=i, cy=cy, pbf=pbf,
                          cc=sb["cw"][:, cwbase + i:cwbase + i + 1]: e.scalar_tensor_tensor(
                        cy[:], pbf[:, HB - 3 + i:HB - 3 + i + tl], cc, cy[:],
                        op0=ALU.mult, op1=ALU.add),
                        reads=pr + [("convy", par)], writes=[("convy", par)])
                kb.op("act", lambda e, cy=cy: e.activation(out=cy[:], in_=cy[:], func=AF.Silu),
                      reads=[("convy", par)], writes=[("convy", par)])
                for nm in spnames:
                    if nm in ("q", "kfm"):
                        kb.dma("qs", lambda e, nm=nm, fc=fc, cy=cy: e.dma_start(
                            out=sp[nm][fc], in_=cy[:]),
                            reads=[("convy", par)], writes=[("sp" + nm, fc)])
                if want_tm:
                    nmtm = spnames[-1]
                    tstg = sb["tstg%d" % par]
                    for tb in range(ntb):
                        tbank = 2 + (tb // 8) % 2
                        kb.op("pe", lambda e, tb=tb, cy=cy, tbank=tbank: e.transpose(
                            pb(tbank).bitcast(BF16)[:, (tb % 8) * 128:(tb % 8) * 128 + 128],
                            cy[:, tb * 128:(tb + 1) * 128], sb["ident"][:]),
                            reads=[("convy", par), ("sb", "ident")],
                            writes=[("pbT", tbank, tb % 8)])
                        if tb % 8 == 7 or tb == ntb - 1:
                            lo = (tb // 8) * 8
                            kb.op("act", lambda e, lo=lo, tb=tb, tstg=tstg, tbank=tbank:
                                  e.activation(
                                      out=tstg[:, lo * 128:(tb + 1) * 128],
                                      in_=pb(tbank).bitcast(BF16)[:, 0:(tb + 1 - lo) * 128],
                                      func=AF.Copy),
                                  reads=[("pbT", tbank, i) for i in range(tb + 1 - lo)],
                                  writes=[("tstg", par, tb // 8)])
                    kb.dma("qs", lambda e, nmtm=nmtm, fc=fc, tstg=tstg: e.dma_start(
                        out=sp[nmtm][fc].rearrange("(tb p) f -> p tb f", p=128),
                        in_=tstg[:, 0:tl]),
                        reads=[("tstg", par, i) for i in range(ceil_div(ntb, 8))],
                        writes=[("sp" + nmtm, fc)])


def build_program(cfg):
    b = Builder(cfg)
    b.ld_consts()
    ph = cfg.phases
    if "aproj" in ph or "az" in ph or "amisc" in ph:
        b.alloc_a_arena()
        b.ld_hfm()
    for name in ("aproj", "az", "amisc", "ascan", "aagg", "gather", "bscan", "f1", "f2"):
        if name not in ph:
            continue
        if name == "ascan":
            b.alloc_ascan_arena()
        if name == "aagg":
            b.alloc_aagg_arena()
        if name == "bscan":
            b.alloc_b_arena()
        if name == "f1":
            b.alloc_f1_arena()
        if name == "f2":
            b.alloc_f2_arena()
        b.phase_fence()
        getattr(b, "phase_" + name)()
    b.kb.finish()
    for name, dst in b.dbg.items():
        srcap = b.spills[name]
        b.kb.dma("qa", lambda e, srcap=srcap, dst=dst: e.dma_start(
            out=dst.rearrange(_flat_str(dst.shape)), in_=srcap.rearrange(_flat_str(srcap.shape))))
    if b.dbg:
        b.kb.finish()
    b.kb.replay()
    return b


def _flat_str(shape):
    letters = "abcdef"[:len(shape)]
    return " ".join(letters) + " -> (" + " ".join(letters) + ")"


def host_prep(inputs, cfg):
    tl, n = cfg.tloc, cfg.ncores
    f32 = np.float32
    h_full = np.asarray(inputs["L0_attn_hidden_states"], f32)[0]
    hT_full = np.ascontiguousarray(h_full.T)
    zeros_halo = np.zeros((D, HB), f32)

    def u8_low_bytes(a_int32):
        a = np.ascontiguousarray(a_int32.astype(np.int32, copy=False))
        return np.ascontiguousarray(a.view(np.uint8).reshape(*a.shape, 4)[..., 0])

    g = {}
    g["Wq"] = np.asarray(inputs["L0_attn_Wq"], f32)
    g["Wk"] = np.asarray(inputs["L0_attn_Wk"], f32)
    g["Wv"] = np.asarray(inputs["L0_attn_Wv"], f32)
    g["Ww"] = np.asarray(inputs["L0_attn_Ww"], f32)
    g["Wo"] = np.asarray(inputs["L0_attn_Wo"], f32)
    for cn, dn in (("cqr", "L0_attn_cq"), ("ckr", "L0_attn_ck"), ("cvr", "L0_attn_cv")):
        c = np.asarray(inputs[dn], f32)
        g[cn] = np.ascontiguousarray(c.reshape(16, 128, KW).transpose(1, 0, 2).reshape(128, 16 * KW))
    g["Wf1"] = np.asarray(inputs["L0_attn_Wf1"], f32)
    g["Wf2"] = np.asarray(inputs["L0_attn_Wf2"], f32)
    wb = np.asarray(inputs["L0_attn_Wb"], f32)
    g["Wbr"] = np.ascontiguousarray(wb.reshape(16, 128, H).transpose(1, 0, 2).reshape(128, 16 * H))
    g["Wg1"] = np.asarray(inputs["L0_attn_Wg1"], f32)
    g["Wg2"] = np.asarray(inputs["L0_attn_Wg2"], f32)
    bg2 = np.asarray(inputs["L0_attn_bg2"], f32)
    g["bg2m"] = np.ascontiguousarray(bg2.reshape(16, 128).T)
    g["normw"] = np.ascontiguousarray(np.asarray(inputs["L0_attn_norm_w"], f32).reshape(128, 1))
    g["WgT8"] = u8_low_bytes(np.ascontiguousarray(np.asarray(inputs["L0_ffn_wq_gate"]).T))
    g["WuT8"] = u8_low_bytes(np.ascontiguousarray(np.asarray(inputs["L0_ffn_wq_up"]).T))
    g["WdT8"] = u8_low_bytes(np.ascontiguousarray(np.asarray(inputs["L0_ffn_wq_down"]).T))
    g["sgm"] = np.ascontiguousarray(np.asarray(inputs["L0_ffn_s_gate"], f32).reshape(NHH, 128).T)
    g["sum_"] = np.ascontiguousarray(np.asarray(inputs["L0_ffn_s_up"], f32).reshape(NHH, 128).T)
    g["sdr"] = np.ascontiguousarray(np.asarray(inputs["L0_ffn_s_down"], f32).reshape(1, D))
    g["ident"] = np.eye(128, dtype=BF)
    g["identf"] = np.eye(128, dtype=f32)
    sl = np.arange(C)
    g["tri"] = (sl[:, None] <= sl[None, :]).astype(BF)
    ss = np.zeros((128, 16 * 32), BF)
    for tb in range(16):
        for t in range(128):
            ss[t, tb * 32 + 2 * tb + (t // 64)] = 1
    g["sumsels"] = ss
    jj = np.arange(32)
    g["trige"] = (jj[:, None] >= jj[None, :]).astype(f32)
    g["ones32"] = np.ones((32, 1), f32)
    g["ones1x128"] = np.ones((1, 128), BF)
    g["ones128"] = np.ones((128, 1), BF)

    in_maps = []
    for ci in range(n):
        m = dict(g)
        s = ci * tl
        m["h"] = h_full[s:s + tl]
        if s >= HB:
            m["hT"] = np.ascontiguousarray(hT_full[:, s - HB:s + tl])
        else:
            m["hT"] = np.ascontiguousarray(
                np.concatenate([zeros_halo[:, :HB - s], hT_full[:, 0:s + tl]], axis=1))
        msel = np.zeros((2, (NCORES - 1) * 128), BF)
        mcol = np.zeros((128, NCORES - 1), f32)
        for j in range(NCORES - 1):
            sel = 1.0 if j < ci else 0.0
            msel[0, j * 128:(j + 1) * 128] = sel
            msel[1, j * 128:(j + 1) * 128] = 1.0 - sel
            mcol[:, j] = sel
        m["mselM"] = msel
        m["mselcol"] = mcol
        m["m1col"] = np.ascontiguousarray(1.0 - mcol)
        if cfg.host_gather:
            m["sp_ccout"] = np.zeros((cfg.ncores * 257, D), BF)
        in_maps.append(m)
    return in_maps


AZ_CUT = 0


def phase_az(self):
    kb, sb, din, sp = self.kb, self.sb, self.din, self.spills
    tl, ntb, nch = self.cfg.tloc, self.cfg.ntb, self.cfg.nch
    pb = self.pb
    HT = HB + tl
    wf1_4 = din["Wf1"].rearrange("(dc p) f -> p dc f", p=128)
    kb.dma("qp", lambda e: e.dma_start(out=sb["wsmall"][:, 0:16 * 128], in_=wf1_4),
           writes=[("wsmall", 0)])
    kb.dma("qp", lambda e: e.dma_start(out=sb["wf2"][:], in_=din["Wf2"][:]),
           writes=[("wf2",)])
    for jg in range(ceil_div(tl, 512)):
        t0, tn = jg * 512, min(512, tl - jg * 512)
        bank = jg % 2
        for dc in range(16):
            kb.op("pe", lambda e, dc=dc, t0=t0, tn=tn, bank=bank: e.matmul(
                pb(bank)[:, 0:tn],
                sb["wsmall"][:, dc * 128:(dc + 1) * 128],
                sb["hfm"][:, dc * HT + HB + t0: dc * HT + HB + t0 + tn],
                start=(dc == 0), stop=(dc == 15)),
                reads=[("wsmall", 0), ("hfm", dc)], writes=[("pb", bank)])
        kb.op("act", lambda e, t0=t0, tn=tn, bank=bank: e.activation(
            out=sb["z1"][:, t0:t0 + tn], in_=pb(bank)[:, 0:tn], func=AF.Copy),
            reads=[("pb", bank)], writes=[("z1", jg)])
    if AZ_CUT >= 4:
        return
    njt = ceil_div(D, 512)
    for tb in range(ntb):
        for jt in range(njt):
            bank = 2 + (jt % 2)
            kb.op("pe", lambda e, tb=tb, jt=jt, bank=bank: e.matmul(
                pb(bank)[:],
                sb["z1"][:, tb * 128:(tb + 1) * 128],
                sb["wf2"][:, jt * 512:(jt + 1) * 512],
                start=True, stop=True),
                reads=[("z1", tb // 4), ("wf2",)], writes=[("pb", bank)])
            kb.op("act", lambda e, bank=bank: e.activation(
                out=sb["ztmp0"][:], in_=pb(bank)[:], func=AF.Abs),
                reads=[("pb", bank)], writes=[("zt0",)])
            kb.op("act", lambda e: e.activation(
                out=sb["ztmp0"][:], in_=sb["ztmp0"][:], func=AF.Exp, scale=-1.0),
                reads=[("zt0",)], writes=[("zt0",)])
            kb.op("act", lambda e: e.activation(
                out=sb["ztmp0"][:], in_=sb["ztmp0"][:], func=AF.Ln, bias=1.0),
                reads=[("zt0",)], writes=[("zt0",)])
            kb.op("dve", lambda e, bank=bank: e.tensor_single_scalar(
                sb["ztmp1"][:], pb(bank)[:], 0.0, op=ALU.min),
                reads=[("pb", bank)], writes=[("zt1",)])
            kb.op("dve", lambda e, jt=jt: e.tensor_sub(
                sb["gstg"][:, jt * 512:(jt + 1) * 512], sb["ztmp1"][:], sb["ztmp0"][:]),
                reads=[("zt0",), ("zt1",)], writes=[("gstg", jt)])

        kb.dma("qs", lambda e, tb=tb: e.dma_start(
            out=sp["g"][tb * 128:(tb + 1) * 128, :], in_=sb["gstg"][:]),
            reads=[("gstg", jt) for jt in range(njt)], writes=[("spg", tb)])
    if AZ_CUT >= 3:
        return
    for jt in range(njt):
        bank = 4 + jt % 2
        for tb in range(ntb):
            par = tb % 2
            kb.dma("qs", lambda e, jt=jt, tb=tb, par=par: e.dma_start(
                out=sb["ztmp%d" % par][:].bitcast(BF16)[:, 0:512],
                in_=sp["g"][tb * 128:(tb + 1) * 128, jt * 512:(jt + 1) * 512]),
                reads=[("spg", tb)], writes=[("zt", par)])
            kb.op("pe", lambda e, tb=tb, jt=jt, par=par, bank=bank: e.matmul(
                pb(bank)[0:32, :],
                sb["sumsels"][:, tb * 32:(tb + 1) * 32],
                sb["ztmp%d" % par][:].bitcast(BF16)[:, 0:512],
                start=(tb == 0), stop=(tb == ntb - 1)),
                reads=[("zt", par), ("sb", "sumsels")], writes=[("pbcc", bank)])
        kb.op("dve", lambda e, jt=jt, bank=bank: e.tensor_copy(
            sb["cCrows"][:, jt * 512:(jt + 1) * 512], pb(bank)[0:32, :]),
            reads=[("pbcc", bank)], writes=[("cCrows", jt)])
        kb.op("act", lambda e, jt=jt, bank=bank: e.activation(
            out=sb["ecall"][:, jt * 512:(jt + 1) * 512], in_=pb(bank)[0:32, :],
            func=AF.Exp),
            reads=[("pbcc", bank)], writes=[("ecall", jt)])
    for jt in range(njt):
        kb.op("pe", lambda e, jt=jt: e.matmul(
            pb(jt % 2)[0:32, :], sb["trige"][:],
            sb["cCrows"][:, jt * 512:(jt + 1) * 512], start=True, stop=True),
            reads=[("cCrows", jt), ("sb", "trige")], writes=[("pb", jt % 2)])
        kb.op("act", lambda e, jt=jt: e.activation(
            out=sb["wsufb"][:, jt * 512:(jt + 1) * 512], in_=pb(jt % 2)[0:32, :],
            func=AF.Exp),
            reads=[("pb", jt % 2)], writes=[("wsufb", jt)])
    if AZ_CUT >= 2:
        return
    kb.dma("qs", lambda e: e.dma_start(out=sp["ec"][:], in_=sb["ecall"][0:nch, :]),
           reads=[("ecall", jt) for jt in range(njt)], writes=[("spec",)])
    kb.dma("qs", lambda e: e.dma_start(out=sp["wsuf"][:], in_=sb["wsufb"][0:nch, :]),
           reads=[("wsufb", jt) for jt in range(njt)], writes=[("spwsuf",)])
    if AZ_CUT >= 1:
        return
    kb.dma("qp", lambda e: e.dma_start(
        out=sb["ec16all"][:, 0:nch * 128],
        in_=sp["ec"].rearrange("j (h m) -> h j m", m=128)),
        reads=[("spec",)], writes=[("ec16all",)])

AM_CUT = 0


def phase_amisc(self):
    kb, sb, din, sp = self.kb, self.sb, self.din, self.spills
    tl, ntb = self.cfg.tloc, self.cfg.ntb
    pb = self.pb
    HT = HB + tl
    kb.dma("qp", lambda e: e.dma_start(out=sb["wbr"][:], in_=din["Wbr"][:]),
           writes=[("wbr",)])
    wg1_4 = din["Wg1"].rearrange("(dc p) f -> p dc f", p=128)
    kb.dma("qp", lambda e: e.dma_start(out=sb["wsmall"][:, 16 * 128:32 * 128], in_=wg1_4),
           writes=[("wsmall", 1)])
    kb.dma("qp", lambda e: e.dma_start(out=sb["wg2"][:], in_=din["Wg2"][:]),
           writes=[("wg2",)])
    for tb in range(ntb):
        for dc in range(16):
            kb.op("pe", lambda e, tb=tb, dc=dc: e.matmul(
                pb(0)[:, 0:H],
                sb["hfm"][:, dc * HT + HB + tb * 128: dc * HT + HB + (tb + 1) * 128],
                sb["wbr"][:, dc * H:(dc + 1) * H],
                start=(dc == 0), stop=(dc == 15)),
                reads=[("wbr",), ("hfm", dc)], writes=[("pb", 0)])
        kb.op("act", lambda e, tb=tb: e.activation(
            out=sb["beta"][:, tb * H:(tb + 1) * H], in_=pb(0)[:, 0:H], func=AF.Sigmoid),
            reads=[("pb", 0)], writes=[("beta", tb)])
    b2 = sb["beta2"].ap().rearrange("p (c j) -> p c j", j=H)
    kb.dma("qs", lambda e, b2=b2: e.dma_start(
        out=b2[:, 0::2, :], in_=sb["beta"][0:64, :].rearrange("p (c j) -> p c j", j=H)),
        reads=[("beta", tb) for tb in range(ntb)], writes=[("beta2", 0)])
    kb.dma("qs", lambda e, b2=b2: e.dma_start(
        out=b2[:, 1::2, :], in_=sb["beta"][64:128, :].rearrange("p (c j) -> p c j", j=H)),
        reads=[("beta", tb) for tb in range(ntb)], writes=[("beta2", 1)])
    if AM_CUT >= 2:
        return
    for jg in range(ceil_div(tl, 512)):
        t0, tn = jg * 512, min(512, tl - jg * 512)
        bank = 1
        for dc in range(16):
            kb.op("pe", lambda e, dc=dc, t0=t0, tn=tn: e.matmul(
                pb(bank)[:, 0:tn],
                sb["wsmall"][:, (16 + dc) * 128:(17 + dc) * 128],
                sb["hfm"][:, dc * HT + HB + t0: dc * HT + HB + t0 + tn],
                start=(dc == 0), stop=(dc == 15)),
                reads=[("wsmall", 1), ("hfm", dc)], writes=[("pb", bank)])
        kb.op("act", lambda e, t0=t0, tn=tn: e.activation(
            out=sb["zg1"][:, t0:t0 + tn], in_=pb(bank)[:, 0:tn], func=AF.Copy),
            reads=[("pb", bank)], writes=[("zg1", jg)])
    if AM_CUT >= 1:
        return
    for fc2 in range(16):
        for jg in range(ceil_div(tl, 512)):
            t0, tn = jg * 512, min(512, tl - jg * 512)
            bank = 2 + (jg % 2)
            kb.op("pe", lambda e, fc2=fc2, t0=t0, tn=tn, bank=bank: e.matmul(
                pb(bank)[:, 0:tn],
                sb["wg2"][:, fc2 * 128:(fc2 + 1) * 128],
                sb["zg1"][:, t0:t0 + tn], start=True, stop=True),
                reads=[("wg2",), ("zg1", jg)], writes=[("pb", bank)])
            kb.op("dve", lambda e, fc2=fc2, t0=t0, tn=tn, bank=bank: e.tensor_scalar(
                sb["z1"][:, t0:t0 + tn], pb(bank)[:, 0:tn],
                sb["bg2m"][:, fc2:fc2 + 1], None, op0=ALU.add),
                reads=[("pb", bank), ("sb", "bg2m")], writes=[("z1", jg)])
            kb.op("act", lambda e, t0=t0, tn=tn: e.activation(
                out=sb["z1"][:, t0:t0 + tn], in_=sb["z1"][:, t0:t0 + tn],
                func=AF.Sigmoid),
                reads=[("z1", jg)], writes=[("z1", jg)])
        kb.dma("qs", lambda e, fc2=fc2: e.dma_start(
            out=sp["gs"][fc2], in_=sb["z1"][:, 0:tl]),
            reads=[("z1", jg) for jg in range(ceil_div(tl, 512))],
            writes=[("spgs", fc2)])


Builder.phase_az = phase_az
Builder.phase_amisc = phase_amisc


def phase_fence(self):
    kb = self.kb
    toks = [(e, kb.cnt[e]) for e in ENGINES if kb.cnt[e]]
    toks += [(qn, kb.dcnt[qn]) for qn in QUEUES if kb.dcnt[qn]]
    for e in ENGINES:
        kb.barrier_wait(e, toks)


def alloc_ascan_arena(self):
    tl = self.cfg.tloc
    self.arena_reset()
    for par in (0, 1):
        self.arena("gch%d" % par, [64, D], BF16)
        self.arena("wtch%d" % par, [64, D], BF16)
        self.arena("qfch%d" % par, [128, H * C], BF16)
        self.arena("kfch%d" % par, [128, H * C], BF16)
        self.arena("einv%d" % par, [64, D], BF16)
        self.arena("bw%d" % par, [64, D], BF16)
        self.arena("ubuf%d" % par, [64, D], BF16)
        self.arena("sibuf%d" % par, [64, D], BF16)
        self.arena("pm%d" % par, [64, 1024], BF16)
        self.arena("ssq%d" % par, [64, 16], F32)
        self.arena("rb%d" % par, [64, 16], BF16)

    self.arena("sqtmp", [64, D], F32)


def alloc_aagg_arena(self):
    self.arena_reset()
    for par in (0, 1):
        self.arena("ktch%d" % par, [64, 4 * D], BF16)
        self.arena("vtch%d" % par, [64, 4 * D], BF16)
        self.arena("ucb%d" % par, [64, 4 * D], BF16)
        self.arena("wsb%d" % par, [64, D], BF16)
    self.arena("ccstg", [128, D], BF16)
    self.arena("aggK", [128, H * M], F32)
    self.arena("aggV", [128, H * DK], F32)
    self.arena("erow", [1, D], BF16)


def phase_ascan(self):
    kb, sb, sp = self.kb, self.sb, self.spills
    tl, nch = self.cfg.tloc, self.cfg.nch
    pb = self.pb
    PSA = self.PSA
    wtm3 = sp["wtm"].rearrange("h t f -> t h f")
    q3 = sp["q"].rearrange("h p t -> p h t")
    k3 = sp["kfm"].rearrange("h p t -> p h t")
    for ch in range(nch):
        par = ch % 2
        gch, wt = sb["gch%d" % par], sb["wtch%d" % par]
        qf, kf = sb["qfch%d" % par], sb["kfch%d" % par]
        einv, bw = sb["einv%d" % par], sb["bw%d" % par]
        ub, sib, pm = sb["ubuf%d" % par], sb["sibuf%d" % par], sb["pm%d" % par]
        ssq, rb = sb["ssq%d" % par], sb["rb%d" % par]
        kb.dma("qs", lambda e, ch=ch, gch=gch: e.dma_start(
            out=gch[:], in_=sp["g"][ch * C:(ch + 1) * C, :]),
            reads=[("spg", ch * C // 128)], writes=[("gch", par)])
        kb.dma("qs", lambda e, ch=ch, wt=wt: e.dma_start(
            out=wt[:], in_=wtm3[ch * C:(ch + 1) * C]),
            reads=[("spwtm", h) for h in range(H)], writes=[("wtch", par)])
        kb.dma("qs", lambda e, ch=ch, qf=qf: e.dma_start(
            out=qf[:], in_=q3[:, :, ch * C:(ch + 1) * C]),
            reads=[("spq", h) for h in range(H)], writes=[("qfch", par)])
        kb.dma("qs", lambda e, ch=ch, kf=kf: e.dma_start(
            out=kf[:], in_=k3[:, :, ch * C:(ch + 1) * C]),
            reads=[("spkfm", h) for h in range(H)], writes=[("kfch", par)])
        for jt in range(4):
            kb.op("pe", lambda e, jt=jt, gch=gch: e.matmul(
                PSA.ap()[0:C, jt * 512:(jt + 1) * 512], sb["tri"][:],
                gch[:, jt * 512:(jt + 1) * 512], start=True, stop=True),
                reads=[("gch", par), ("sb", "tri")], writes=[("pb", jt)])
        kb.op("act", lambda e, einv=einv: e.activation(
            out=einv[:], in_=PSA.ap()[0:C, :], func=AF.Exp, scale=-1.0),
            reads=[("pb", 0), ("pb", 1), ("pb", 2), ("pb", 3)], writes=[("einv", par)])
        kb.op("act", lambda e, wt=wt: e.activation(
            out=sb["sqtmp"][:], in_=wt[:], func=AF.Square),
            reads=[("wtch", par)], writes=[("sqtmp",)])
        kb.op("dve", lambda e, ssq=ssq: e.reduce_sum(
            ssq[:], sb["sqtmp"][:].rearrange("p (h m) -> p h m", m=M), axis=AX.X),
            reads=[("sqtmp",)], writes=[("ssq", par)])
        kb.op("act", lambda e, ssq=ssq: e.activation(
            out=ssq[:], in_=ssq[:], func=AF.Ln, bias=L2EPS),
            reads=[("ssq", par)], writes=[("ssq", par)])
        kb.op("act", lambda e, ssq=ssq: e.activation(
            out=ssq[:], in_=ssq[:], func=AF.Exp, scale=-0.5),
            reads=[("ssq", par)], writes=[("ssq", par)])
        bslice = sb["beta2"][:, ch * H:(ch + 1) * H]
        kb.op("dve", lambda e, rb=rb, ssq=ssq, bslice=bslice: e.tensor_mul(
            rb[:], ssq[:], bslice),
            reads=[("ssq", par), ("beta2", 0), ("beta2", 1)], writes=[("rbb", par)])
        kb.op("dve", lambda e, bw=bw, wt=wt, rb=rb: e.tensor_tensor(
            bw[:].rearrange("p (h m) -> p h m", m=M),
            wt[:].rearrange("p (h m) -> p h m", m=M),
            rb[:].unsqueeze(2).broadcast_to([C, H, M]), op=ALU.mult),
            reads=[("wtch", par), ("rbb", par)], writes=[("bwb", par)])
        kb.op("dve", lambda e, ub=ub, bw=bw, einv=einv: e.tensor_mul(
            ub[:], bw[:], einv[:]),
            reads=[("bwb", par), ("einv", par)], writes=[("ubuf", par)])
        kb.dma("qs", lambda e, ch=ch, ub=ub: e.dma_start(
            out=sp["u"][ch * C:(ch + 1) * C, :], in_=ub[:]),
            reads=[("ubuf", par)], writes=[("spu", ch)])
        for hh in range(H):
            bank = 4 + hh // 8
            kb.op("pe", lambda e, hh=hh, bank=bank, kf=kf, qf=qf: e.matmul(
                pb(bank)[0:C, (hh % 8) * C:(hh % 8 + 1) * C],
                kf[:, hh * C:(hh + 1) * C], qf[:, hh * C:(hh + 1) * C],
                start=True, stop=True),
                reads=[("qfch", par), ("kfch", par)], writes=[("pb", bank)])
        for b_ in (4, 5):
            kb.op("dve", lambda e, b_=b_, pm=pm: e.tensor_tensor(
                pm[:, (b_ - 4) * 512:(b_ - 3) * 512].rearrange("p (h s) -> p h s", s=C),
                pb(b_)[0:C, :].rearrange("p (h s) -> p h s", s=C),
                sb["tri"][:].unsqueeze(1).broadcast_to([C, 8, C]), op=ALU.mult),
                reads=[("pb", b_), ("sb", "tri")], writes=[("pmb", par, b_)])
        for hh in range(H):
            kb.op("pe", lambda e, hh=hh, pm=pm, ub=ub: e.matmul(
                PSA.ap()[0:C, hh * M:(hh + 1) * M],
                pm[:, hh * C:(hh + 1) * C], ub[:, hh * M:(hh + 1) * M],
                start=True, stop=True),
                reads=[("pmb", par, 4 + hh // 8), ("ubuf", par)],
                writes=[("pb", hh // 4)])
        kb.op("act", lambda e, sib=sib: e.activation(
            out=sib[:], in_=PSA.ap()[0:C, :], func=AF.Copy, scale=SCALE),
            reads=[("pb", 0), ("pb", 1), ("pb", 2), ("pb", 3)], writes=[("sibuf", par)])
        kb.dma("qs", lambda e, ch=ch, sib=sib: e.dma_start(
            out=sp["si"][ch * C:(ch + 1) * C, :], in_=sib[:]),
            reads=[("sibuf", par)], writes=[("spsi", ch)])


def phase_aagg(self):
    kb, sb, sp = self.kb, self.sb, self.spills
    tl, nch = self.cfg.tloc, self.cfg.nch
    pb = self.pb
    PSA = self.PSA
    BC = 4
    nblk = ceil_div(nch, BC)
    ktm3 = sp["ktm"].rearrange("h t f -> t h f")
    vtm3 = sp["vtm"].rearrange("h t f -> t h f")
    for blk in range(nblk):
        chs = [ch for ch in range(blk * BC, min((blk + 1) * BC, nch))]
        par = blk % 2
        kt, vt = sb["ktch%d" % par], sb["vtch%d" % par]
        uc, ws = sb["ucb%d" % par], sb["wsb%d" % par]
        nbc = len(chs)
        for i, ch in enumerate(chs):
            kb.dma("qs", lambda e, ch=ch, i=i, kt=kt: e.dma_start(
                out=kt[:, i * D:(i + 1) * D], in_=ktm3[ch * C:(ch + 1) * C]),
                reads=[("spktm", h) for h in range(H)], writes=[("ktch", par, i)])
            kb.dma("qs", lambda e, ch=ch, i=i, vt=vt: e.dma_start(
                out=vt[:, i * D:(i + 1) * D], in_=vtm3[ch * C:(ch + 1) * C]),
                reads=[("spvtm", h) for h in range(H)], writes=[("vtch", par, i)])
            kb.dma("qa", lambda e, ch=ch, i=i, ws=ws: e.dma_start(
                out=ws[:, (i % 2) * 0: D] if False else ws[:],
                in_=sp["wsuf"][ch:ch + 1, :].broadcast_to([C, D])),
                reads=[("spwsuf",)], writes=[("wsbb", par)])
            kb.dma("qa", lambda e, ch=ch, i=i, uc=uc: e.dma_start(
                out=uc[:, i * D:(i + 1) * D], in_=sp["u"][ch * C:(ch + 1) * C, :]),
                reads=[("spu", ch)], writes=[("ucb", par, i)])
            kb.op("dve", lambda e, i=i, uc=uc, ws=ws: e.tensor_mul(
                uc[:, i * D:(i + 1) * D], uc[:, i * D:(i + 1) * D], ws[:]),
                reads=[("ucb", par, i), ("wsbb", par)], writes=[("ucb", par, i)])
        for hh in range(H):
            bank = hh % 2
            for i in range(nbc):
                kb.op("pe", lambda e, hh=hh, i=i, kt=kt, uc=uc, bank=bank: e.matmul(
                    pb(bank)[:, 0:M],
                    kt[:, i * D + hh * DK: i * D + (hh + 1) * DK],
                    uc[:, i * D + hh * M: i * D + (hh + 1) * M],
                    start=(i == 0), stop=(i == nbc - 1)),
                    reads=[("ktch", par, i), ("ucb", par, i)], writes=[("pb", bank)])
            if blk == 0:
                kb.op("dve", lambda e, hh=hh, bank=bank: e.tensor_copy(
                    sb["aggK"][:, hh * M:(hh + 1) * M], pb(bank)[:, 0:M]),
                    reads=[("pb", bank)], writes=[("aggK", hh)])
            else:
                kb.op("dve", lambda e, hh=hh, bank=bank: e.tensor_add(
                    sb["aggK"][:, hh * M:(hh + 1) * M],
                    sb["aggK"][:, hh * M:(hh + 1) * M], pb(bank)[:, 0:M]),
                    reads=[("pb", bank), ("aggK", hh)], writes=[("aggK", hh)])
            bank2 = 2 + hh % 2
            for i in range(nbc):
                kb.op("pe", lambda e, hh=hh, i=i, uc=uc, vt=vt, bank2=bank2: e.matmul(
                    pb(bank2)[:, 0:DK],
                    uc[:, i * D + hh * M: i * D + (hh + 1) * M],
                    vt[:, i * D + hh * DK: i * D + (hh + 1) * DK],
                    start=(i == 0), stop=(i == nbc - 1)),
                    reads=[("ucb", par, i), ("vtch", par, i)], writes=[("pb", bank2)])
            if blk == 0:
                kb.op("dve", lambda e, hh=hh, bank2=bank2: e.tensor_copy(
                    sb["aggV"][:, hh * DK:(hh + 1) * DK], pb(bank2)[:, 0:DK]),
                    reads=[("pb", bank2)], writes=[("aggV", hh)])
            else:
                kb.op("dve", lambda e, hh=hh, bank2=bank2: e.tensor_add(
                    sb["aggV"][:, hh * DK:(hh + 1) * DK],
                    sb["aggV"][:, hh * DK:(hh + 1) * DK], pb(bank2)[:, 0:DK]),
                    reads=[("pb", bank2), ("aggV", hh)], writes=[("aggV", hh)])
    kb.op("dve", lambda e: e.tensor_copy(sb["ccstg"][:], sb["aggK"][:]),
          reads=[("aggK", hh) for hh in range(H)], writes=[("ccstg",)])
    kb.dma("qs", lambda e: e.dma_start(out=sp["ccin"][0:128, :], in_=sb["ccstg"][:]),
           reads=[("ccstg",)], writes=[("spccin", 0)])
    kb.op("dve", lambda e: e.tensor_copy(sb["ccstg"][:], sb["aggV"][:]),
          reads=[("aggV", hh) for hh in range(H)] + [("spccin", 0)], writes=[("ccstg",)])
    kb.dma("qs", lambda e: e.dma_start(out=sp["ccin"][128:256, :], in_=sb["ccstg"][:]),
           reads=[("ccstg",)], writes=[("spccin", 1)])
    for jt in range(4):
        kb.op("pe", lambda e, jt=jt: e.matmul(
            PSA.ap()[0:1, jt * 512:(jt + 1) * 512], sb["ones32"][:],
            sb["cCrows"][:, jt * 512:(jt + 1) * 512], start=True, stop=True),
            reads=[("cCrows", jt), ("sb", "ones32")], writes=[("pb", jt)])
        kb.op("act", lambda e, jt=jt: e.activation(
            out=sb["erow"][:, jt * 512:(jt + 1) * 512],
            in_=PSA.ap()[0:1, jt * 512:(jt + 1) * 512], func=AF.Exp),
            reads=[("pb", jt)], writes=[("erow", jt)])
    kb.dma("qs", lambda e: e.dma_start(out=sp["ccin"][256:257, :], in_=sb["erow"][:]),
           reads=[("erow", jt) for jt in range(4)], writes=[("spccin", 2)])


def phase_gather(self):
    kb, sp = self.kb, self.spills
    if self.cfg.host_gather:
        return
    n = self.cfg.ncores
    sem = self.nc.alloc_semaphore("sem_cc")
    kb.op("pool", lambda e: e.collective_compute(
        "AllGather", ALU.bypass, replica_groups=[list(range(n))],
        ins=[sp["ccin"][:].opt()], outs=[sp["ccout"][:].opt()]),
        reads=[("spccin", 0), ("spccin", 1), ("spccin", 2)],
        writes=[("spccout",)])


Builder.phase_fence = phase_fence
Builder.alloc_ascan_arena = alloc_ascan_arena
Builder.alloc_aagg_arena = alloc_aagg_arena
Builder.phase_ascan = phase_ascan
Builder.phase_aagg = phase_aagg
Builder.phase_gather = phase_gather


def alloc_b_arena(self):
    tl = self.cfg.tloc
    self.arena_reset()
    self.arena("wo", [128, 16 * 128 * 16], BF16)
    for par in (0, 1):
        self.arena("bgch%d" % par, [64, D], BF16)
        self.arena("bqf%d" % par, [128, H * C], BF16)
        self.arena("bkt%d" % par, [64, D], BF16)
        self.arena("bvt%d" % par, [64, D], BF16)
    self.arena("bub0", [64, D], BF16)
    self.arena("bsi0", [64, D], BF16)
    self.arena("bgs0", [128, H * C], BF16)
    self.arena("becb0", [128, D], BF16)
    self.arena("bh0", [128, D], F32)
    self.arena("bofm0", [128, D], BF16)
    self.arena("befull", [64, D], BF16)
    self.arena("bes", [64, D], BF16)
    self.arena("bs0", [64, D], BF16)
    self.arena("bamt", [128, H * C], BF16)
    self.arena("bufm", [128, H * C], BF16)

    self.arena("bosq", [128, H * C], BF16)
    self.arena("brstdb", [128, H * C], BF16)
    self.arena("bo1", [128, H * C], BF16)
    self.arena("br", [128, D], F32)
    self.arena("bx1", [128, D], BF16)
    self.arena("bmx", [64, H], F32)
    self.arena("bsm", [64, H], F32)
    self.arena("brs", [64, H], F32)
    self.arena("brr", [1, H * C], BF16)
    self.arena("bln", [128, 4], F32)
    self.arena("be16", [16, 128], BF16)
    self.sb["bfold"] = self.sb["bx1"]
    self.sb["bytm"] = self.sb["bsi0"]


def phase_bscan(self):
    kb, sb, sp, din = self.kb, self.sb, self.spills, self.din
    tl, nch, cfg = self.cfg.tloc, self.cfg.nch, self.cfg
    pb = self.pb
    PSA = self.PSA
    wo4 = din["Wo"].rearrange("(dc p) f -> p dc f", p=128)
    kb.dma("qp", lambda e: e.dma_start(out=sb["wo"][:], in_=wo4), writes=[("wo",)])
    kb.op("dve", lambda e: e.memset(sb["Sk"][:], 0.0), writes=[("Sk",)])
    kb.op("dve", lambda e: e.memset(sb["Sv"][:], 0.0), writes=[("Sv",)])
    nfold = cfg.ncores - 1
    for j in range(nfold):
        rb_ = 257 * j
        kb.dma("qp", lambda e, rb_=rb_: e.dma_start(
            out=sb["row2"][0:1, :], in_=sp["ccout"][rb_ + 256:rb_ + 257, :]),
            reads=[("spccout",)], writes=[("row2",)])
        kb.dma("qp", lambda e, rb_=rb_: e.dma_start(
            out=sb["be16"][:], in_=sp["ccout"][rb_ + 256:rb_ + 257, :].rearrange(
                "o (h m) -> (o h) m", m=128)),
            reads=[("spccout",)], writes=[("be16",)])
        for jt in range(4):
            kb.op("pe", lambda e, j=j, jt=jt: e.matmul(
                PSA.ap()[:, jt * 512:(jt + 1) * 512],
                sb["mselM"][:, j * 128:(j + 1) * 128],
                sb["row2"][:, jt * 512:(jt + 1) * 512], start=True, stop=True),
                reads=[("row2",), ("sb", "mselM")], writes=[("pb", jt)])
        kb.op("dve", lambda e: e.tensor_mul(sb["Sk"][:], sb["Sk"][:], PSA.ap()[:]),
              reads=[("pb", 0), ("pb", 1), ("pb", 2), ("pb", 3), ("Sk",)],
              writes=[("Sk",)])
        kb.dma("qp", lambda e, rb_=rb_: e.dma_start(
            out=sb["bfold"][:].bitcast(BF16)[:, 0:D], in_=sp["ccout"][rb_:rb_ + 128, :]),
            reads=[("spccout",)], writes=[("bx1",)])
        kb.op("dve", lambda e, j=j: e.scalar_tensor_tensor(
            sb["Sk"][:], sb["bfold"][:].bitcast(BF16)[:, 0:D], sb["mselcol"][:, j:j + 1],
            sb["Sk"][:], op0=ALU.mult, op1=ALU.add),
            reads=[("bx1",), ("Sk",), ("sb", "mselcol")], writes=[("Sk",)])
        kb.op("dve", lambda e, j=j: e.tensor_scalar(
            sb["be16"][:], sb["be16"][:], sb["mselcol"][0:16, j:j + 1],
            sb["m1col"][0:16, j:j + 1], op0=ALU.mult, op1=ALU.add),
            reads=[("be16",), ("sb", "mselcol"), ("sb", "m1col")], writes=[("be16",)])
        kb.op("pe", lambda e: e.transpose(
            pb(6).bitcast(BF16)[:, 0:16], sb["be16"][:], sb["ident"][0:16, 0:16]),
            reads=[("be16",), ("sb", "ident")], writes=[("pb", 6)])
        kb.op("dve", lambda e: e.tensor_copy(sb["ecct"][:], pb(6).bitcast(BF16)[:, 0:16]),
              reads=[("pb", 6)], writes=[("ecct",)])
        for hh in range(H):
            kb.op("act", lambda e, hh=hh: e.activation(
                out=sb["Sv"][:, hh * DK:(hh + 1) * DK],
                in_=sb["Sv"][:, hh * DK:(hh + 1) * DK], func=AF.Copy,
                scale=sb["ecct"][:, hh:hh + 1]),
                reads=[("ecct",), ("Sv",)], writes=[("Sv",)])
        kb.dma("qp", lambda e, rb_=rb_: e.dma_start(
            out=sb["bfold"][:], in_=sp["ccout"][rb_ + 128:rb_ + 256, :]),
            reads=[("spccout",), ("Sk",)], writes=[("bx1",)])
        kb.op("dve", lambda e, j=j: e.scalar_tensor_tensor(
            sb["Sv"][:], sb["bfold"][:].bitcast(BF16)[:, 0:D], sb["mselcol"][:, j:j + 1],
            sb["Sv"][:], op0=ALU.mult, op1=ALU.add),
            reads=[("bx1",), ("Sv",), ("sb", "mselcol")], writes=[("Sv",)])
    q3 = sp["q"].rearrange("h p t -> p h t")
    gs3 = sp["gs"].rearrange("h p t -> p h t")
    ktm3 = sp["ktm"].rearrange("h t f -> t h f")
    vtm3 = sp["vtm"].rearrange("h t f -> t h f")
    for ch in range(nch):
        par = ch % 2
        gch, qf = sb["bgch%d" % par], sb["bqf%d" % par]
        kt, vt = sb["bkt%d" % par], sb["bvt%d" % par]
        ub, sib = sb["bub0"], sb["bsi0"]
        gsf, ecb = sb["bgs0"], sb["becb0"]
        ppar = 0
        hb, ofm = sb["bh0"], sb["bofm0"]
        kb.dma("qs", lambda e, ch=ch, gch=gch: e.dma_start(
            out=gch[:], in_=sp["g"][ch * C:(ch + 1) * C, :]),
            reads=[("spg", ch * C // 128)], writes=[("bgch", par)])
        kb.dma("qs", lambda e, ch=ch, qf=qf: e.dma_start(
            out=qf[:], in_=q3[:, :, ch * C:(ch + 1) * C]),
            reads=[("spq", h) for h in range(H)], writes=[("bqf", par)])
        kb.dma("qs", lambda e, ch=ch, kt=kt: e.dma_start(
            out=kt[:], in_=ktm3[ch * C:(ch + 1) * C]),
            reads=[("spktm", h) for h in range(H)], writes=[("bkt", par)])
        kb.dma("qs", lambda e, ch=ch, vt=vt: e.dma_start(
            out=vt[:], in_=vtm3[ch * C:(ch + 1) * C]),
            reads=[("spvtm", h) for h in range(H)], writes=[("bvt", par)])
        kb.dma("qs", lambda e, ch=ch, ub=ub: e.dma_start(
            out=ub[:], in_=sp["u"][ch * C:(ch + 1) * C, :]),
            reads=[("spu", ch)], writes=[("bub",)])
        kb.dma("qs", lambda e, ch=ch, sib=sib: e.dma_start(
            out=sib[:], in_=sp["si"][ch * C:(ch + 1) * C, :]),
            reads=[("spsi", ch)], writes=[("bsiy",)])
        kb.dma("qa", lambda e, ch=ch, gsf=gsf: e.dma_start(
            out=gsf[:], in_=gs3[:, :, ch * C:(ch + 1) * C]),
            reads=[("spgs", h) for h in range(H)], writes=[("bgs",)])
        kb.dma("qa", lambda e, ch=ch, ecb=ecb: e.dma_start(
            out=ecb[:], in_=sp["ec"][ch:ch + 1, :].broadcast_to([128, D])),
            reads=[("spec",)], writes=[("becb",)])
        kb.dma("qa", lambda e, ch=ch, hb=hb: e.dma_start(
            out=hb[(ch % 2) * 64:(ch % 2) * 64 + 64, :],
            in_=din["h"][ch * C:(ch + 1) * C, :]),
            writes=[("bh", ch % 2)])
        for jt in range(4):
            kb.op("pe", lambda e, jt=jt, gch=gch: e.matmul(
                PSA.ap()[0:C, jt * 512:(jt + 1) * 512], sb["tri"][:],
                gch[:, jt * 512:(jt + 1) * 512], start=True, stop=True),
                reads=[("bgch", par), ("sb", "tri")], writes=[("pb", jt)])
        kb.op("act", lambda e: e.activation(
            out=sb["befull"][:], in_=PSA.ap()[0:C, :], func=AF.Exp),
            reads=[("pb", 0), ("pb", 1), ("pb", 2), ("pb", 3)], writes=[("befull",)])
        e16sl = sb["ec16all"][:, ch * 128:(ch + 1) * 128]
        kb.op("pe", lambda e, e16sl=e16sl: e.transpose(
            pb(6).bitcast(BF16)[:, 0:16], e16sl, sb["ident"][0:16, 0:16]),
            reads=[("ec16all",), ("sb", "ident")], writes=[("pb", 6)])
        kb.op("dve", lambda e: e.tensor_copy(sb["ecct"][:], pb(6).bitcast(BF16)[:, 0:16]),
              reads=[("pb", 6)], writes=[("ecct",)])
        kb.op("dve", lambda e: e.tensor_scalar_mul(sb["Skbf"][:], sb["Sk"][:], SCALE),
              reads=[("Sk",)], writes=[("Skbf",)])
        kb.op("pool", lambda e: e.tensor_copy(sb["Svbf"][:], sb["Sv"][:]),
              reads=[("Sv",)], writes=[("Svbf",)])
        for hh in range(H):
            kb.op("pe", lambda e, hh=hh, qf=qf: e.matmul(
                PSA.ap()[0:C, hh * M:(hh + 1) * M],
                qf[:, hh * C:(hh + 1) * C], sb["Skbf"][:, hh * M:(hh + 1) * M],
                start=True, stop=True),
                reads=[("bqf", par), ("Skbf",)], writes=[("pb", hh // 4)])
        kb.op("dve", lambda e, sib=sib: e.tensor_add(
            sb["bs0"][:], PSA.ap()[0:C, :], sib[:]),
            reads=[("pb", 0), ("pb", 1), ("pb", 2), ("pb", 3), ("bsiy",)],
            writes=[("bs0",)])
        kb.op("dve", lambda e: e.tensor_mul(sb["bs0"][:], sb["bs0"][:], sb["befull"][:]),
              reads=[("bs0",), ("befull",)], writes=[("bs0",)])
        s3 = sb["bs0"][:].rearrange("p (h m) -> p h m", m=M)
        kb.op("dve", lambda e, s3=s3: e.reduce_max(sb["bmx"][:], s3, axis=AX.X),
              reads=[("bs0",)], writes=[("bmx",)])
        kb.op("dve", lambda e, s3=s3: e.tensor_tensor(
            s3, s3, sb["bmx"][:].unsqueeze(2).broadcast_to([C, H, M]), op=ALU.subtract),
            reads=[("bs0",), ("bmx",)], writes=[("bs0",)])
        kb.op("act", lambda e: e.activation(
            out=sb["bes"][:], in_=sb["bs0"][:], func=AF.Exp),
            reads=[("bs0",)], writes=[("bes",)])
        es3 = sb["bes"][:].rearrange("p (h m) -> p h m", m=M)
        kb.op("dve", lambda e, es3=es3: e.reduce_sum(sb["bsm"][:], es3, axis=AX.X),
              reads=[("bes",)], writes=[("bsm",)])
        kb.op("dve", lambda e: e.reciprocal(sb["brs"][:], sb["bsm"][:]),
              reads=[("bsm",)], writes=[("brs",)])
        kb.op("dve", lambda e, es3=es3: e.tensor_tensor(
            es3, es3, sb["brs"][:].unsqueeze(2).broadcast_to([C, H, M]), op=ALU.mult),
            reads=[("bes",), ("brs",)], writes=[("bes",)])
        kb.op("dve", lambda e: e.tensor_mul(sb["bes"][:], sb["bes"][:], sb["befull"][:]),
              reads=[("bes",), ("befull",)], writes=[("bes",)])
        for nm, srcb, dstb in (("amt", "bes", "bamt"), ("ufm", None, "bufm")):
            srct = sb["bes"] if nm == "amt" else ub
            srcreg = ("bes",) if nm == "amt" else ("bub",)
            for hh in range(H):
                kb.op("pe", lambda e, hh=hh, srct=srct: e.transpose(
                    pb(6).bitcast(BF16)[:, hh * C:(hh + 1) * C],
                    srct[:, hh * M:(hh + 1) * M], sb["ident"][0:C, 0:C]),
                    reads=[srcreg, ("sb", "ident")], writes=[("pb", 6)])
            kb.op("act", lambda e, dstb=dstb: e.activation(
                out=sb[dstb][:], in_=pb(6).bitcast(BF16)[:, 0:H * C], func=AF.Copy),
                reads=[("pb", 6) for hh in range(H)], writes=[(dstb,)])
        for hh in range(H):
            bank = 4 + hh // 8
            kb.op("pe", lambda e, hh=hh, bank=bank: e.matmul(
                pb(bank)[0:C, (hh % 8) * C:(hh % 8 + 1) * C],
                sb["bufm"][:, hh * C:(hh + 1) * C], sb["bamt"][:, hh * C:(hh + 1) * C],
                start=True, stop=True),
                reads=[("bufm",), ("bamt",)], writes=[("pb", bank)])
        for b_ in (4, 5):
            kb.op("dve", lambda e, b_=b_: e.tensor_tensor(
                sb["bytm"][:, (b_ - 4) * 512:(b_ - 3) * 512].rearrange(
                    "p (h s) -> p h s", s=C),
                pb(b_)[0:C, :].rearrange("p (h s) -> p h s", s=C),
                sb["tri"][:].unsqueeze(1).broadcast_to([C, 8, C]), op=ALU.mult),
                reads=[("pb", b_), ("sb", "tri")], writes=[("bsiy",)])
        for hh in range(H):
            kb.op("pe", lambda e, hh=hh, vt=vt: e.matmul(
                PSA.ap()[:, hh * C:(hh + 1) * C],
                vt[:, hh * DK:(hh + 1) * DK], sb["bytm"][:, hh * C:(hh + 1) * C],
                start=True, stop=False),
                reads=[("bvt", par), ("bsiy",)], writes=[("pb", hh // 8)])
            kb.op("pe", lambda e, hh=hh: e.matmul(
                PSA.ap()[:, hh * C:(hh + 1) * C],
                sb["Svbf"][:, hh * DK:(hh + 1) * DK], sb["bamt"][:, hh * C:(hh + 1) * C],
                start=False, stop=True),
                reads=[("Svbf",), ("bamt",)], writes=[("pb", hh // 8)])
        kb.op("act", lambda e: e.activation(
            out=sb["bosq"][:], in_=PSA.ap()[:, 0:H * C], func=AF.Square),
            reads=[("pb", 0), ("pb", 1)], writes=[("bosq",)])
        for i2 in range(2):
            kb.op("pe", lambda e, i2=i2: e.matmul(
                PSA.ap()[0:1, 1024 + i2 * 512:1024 + (i2 + 1) * 512],
                sb["ones128"][:], sb["bosq"][:, i2 * 512:(i2 + 1) * 512],
                start=True, stop=True),
                reads=[("bosq",), ("sb", "ones128")], writes=[("pb", 2 + i2)])
        kb.op("act", lambda e: e.activation(
            out=sb["brr"][:], in_=PSA.ap()[0:1, 1024:2048], func=AF.Ln,
            scale=1.0 / 128, bias=EPS),
            reads=[("pb", 2), ("pb", 3)], writes=[("brr",)])
        kb.op("act", lambda e: e.activation(
            out=sb["brr"][:], in_=sb["brr"][:], func=AF.Exp, scale=-0.5),
            reads=[("brr",)], writes=[("brr",)])
        for i2 in range(2):
            kb.op("pe", lambda e, i2=i2: e.matmul(
                PSA.ap()[:, 1024 + i2 * 512:1024 + (i2 + 1) * 512],
                sb["ones1x128"][:], sb["brr"][:, i2 * 512:(i2 + 1) * 512],
                start=True, stop=True),
                reads=[("brr",), ("sb", "ones1x128")], writes=[("pb", 2 + i2)])
        kb.op("act", lambda e: e.activation(
            out=sb["brstdb"][:], in_=PSA.ap()[:, 1024:2048], func=AF.Copy),
            reads=[("pb", 2), ("pb", 3)], writes=[("brstdb",)])
        kb.op("dve", lambda e: e.scalar_tensor_tensor(
            sb["bo1"][:], PSA.ap()[:, 0:H * C], sb["normw"][:], sb["brstdb"][:],
            op0=ALU.mult, op1=ALU.mult),
            reads=[("pb", 0), ("pb", 1), ("brstdb",), ("sb", "normw")],
            writes=[("bo1",)])
        kb.op("pool", lambda e, gsf=gsf, ofm=ofm, ch=ch: e.tensor_tensor(
            ofm[:].rearrange("p (h c t) -> p h c t", c=2, t=C)[:, :, ch % 2, :],
            sb["bo1"][:].rearrange("p (h t) -> p h t", t=C), gsf[:].rearrange(
                "p (h t) -> p h t", t=C),
            op=ALU.mult),
            reads=[("bo1",), ("bgs",)], writes=[("bofm", ch % 2)])
        for hh in range(H):
            bank = 4 + hh // 4
            kb.op("pe", lambda e, hh=hh, kt=kt, ub=ub, bank=bank: e.matmul(
                pb(bank)[:, (hh % 4) * M:(hh % 4 + 1) * M],
                kt[:, hh * DK:(hh + 1) * DK], ub[:, hh * M:(hh + 1) * M],
                start=True, stop=True),
                reads=[("bkt", par), ("bub",)], writes=[("pb", bank)])
        for b_ in range(4, 8):
            kb.op("dve", lambda e, b_=b_: e.tensor_add(
                sb["Sk"][:, (b_ - 4) * 512:(b_ - 3) * 512],
                sb["Sk"][:, (b_ - 4) * 512:(b_ - 3) * 512], pb(b_)[:]),
                reads=[("pb", b_), ("Sk",), ("Skbf",)], writes=[("Sk",)])
        kb.op("dve", lambda e, ecb=ecb: e.tensor_mul(sb["Sk"][:], sb["Sk"][:], ecb[:]),
              reads=[("Sk",), ("becb",)], writes=[("Sk",)])
        for hh in range(H):
            kb.op("pe", lambda e, hh=hh, ub=ub, vt=vt: e.matmul(
                PSA.ap()[:, hh * DK:(hh + 1) * DK],
                ub[:, hh * M:(hh + 1) * M], vt[:, hh * DK:(hh + 1) * DK],
                start=True, stop=True),
                reads=[("bub",), ("bvt", par), ("bo1",)], writes=[("pb", hh // 4)])
        for b_ in range(4):
            kb.op("dve", lambda e, b_=b_: e.tensor_add(
                sb["Sv"][:, b_ * 512:(b_ + 1) * 512],
                sb["Sv"][:, b_ * 512:(b_ + 1) * 512], PSA.ap()[:, b_ * 512:(b_ + 1) * 512]),
                reads=[("pb", b_), ("Sv",), ("Svbf",)], writes=[("Sv",)])
        for hh in range(H):
            kb.op("act", lambda e, hh=hh: e.activation(
                out=sb["Sv"][:, hh * DK:(hh + 1) * DK],
                in_=sb["Sv"][:, hh * DK:(hh + 1) * DK], func=AF.Copy,
                scale=sb["ecct"][:, hh:hh + 1]),
                reads=[("ecct",), ("Sv",)], writes=[("Sv",)])
        if ch % 2 == 1:
            for jt in range(4):
                for dc in range(16):
                    lhs = sb["bofm%d" % ppar][:, dc * 128:(dc + 1) * 128]
                    kb.op("pe", lambda e, jt=jt, dc=dc, lhs=lhs: e.matmul(
                        PSA.ap()[:, jt * 512:(jt + 1) * 512],
                        lhs, sb["wo"][:, dc * D + jt * 512: dc * D + (jt + 1) * 512],
                        start=(dc == 0), stop=(dc == 15)),
                        reads=[("bofm", 0), ("bofm", 1), ("wo",)],
                        writes=[("pb", jt)])
            kb.op("dve", lambda e: e.tensor_add(
                sb["br"][:], PSA.ap()[:], sb["bh%d" % ppar][:]),
                reads=[("pb", 0), ("pb", 1), ("pb", 2), ("pb", 3),
                       ("bh", 0), ("bh", 1)],
                writes=[("br",)])
            kb.op("dve", lambda e: e.reduce_sum(
                sb["bln"][:, 0:1], sb["br"][:].unsqueeze(1), axis=AX.X),
                reads=[("br",)], writes=[("bln", 0)])
            kb.op("act", lambda e: e.activation(
                out=PSA.ap()[:], in_=sb["br"][:], func=AF.Square),
                reads=[("br",), ("pb", 0), ("pb", 1), ("pb", 2), ("pb", 3)],
                writes=[("pb", 0), ("pb", 1), ("pb", 2), ("pb", 3)])
            kb.op("dve", lambda e: e.reduce_sum(
                sb["bln"][:, 1:2], PSA.ap()[:].unsqueeze(1), axis=AX.X),
                reads=[("pb", 0), ("pb", 1), ("pb", 2), ("pb", 3)], writes=[("bln", 1)])
            kb.op("dve", lambda e: e.tensor_scalar(
                sb["bln"][:, 0:1], sb["bln"][:, 0:1], 1.0 / D, None, op0=ALU.mult),
                reads=[("bln", 0)], writes=[("bln", 0)])
            kb.op("dve", lambda e: e.tensor_scalar(
                sb["bln"][:, 1:2], sb["bln"][:, 1:2], 1.0 / D, None, op0=ALU.mult),
                reads=[("bln", 1)], writes=[("bln", 1)])
            kb.op("dve", lambda e: e.tensor_mul(
                sb["bln"][:, 2:3], sb["bln"][:, 0:1], sb["bln"][:, 0:1]),
                reads=[("bln", 0)], writes=[("bln", 2)])
            kb.op("dve", lambda e: e.tensor_sub(
                sb["bln"][:, 1:2], sb["bln"][:, 1:2], sb["bln"][:, 2:3]),
                reads=[("bln", 1), ("bln", 2)], writes=[("bln", 1)])
            kb.op("act", lambda e: e.activation(
                out=sb["bln"][:, 1:2], in_=sb["bln"][:, 1:2], func=AF.Ln, bias=LNEPS),
                reads=[("bln", 1)], writes=[("bln", 1)])
            kb.op("act", lambda e: e.activation(
                out=sb["bln"][:, 1:2], in_=sb["bln"][:, 1:2], func=AF.Exp, scale=-0.5),
                reads=[("bln", 1)], writes=[("bln", 1)])
            kb.op("dve", lambda e: e.tensor_scalar(
                sb["bln"][:, 2:3], sb["bln"][:, 0:1], -1.0, None, op0=ALU.mult),
                reads=[("bln", 0)], writes=[("bln", 2)])
            kb.op("dve", lambda e: e.tensor_scalar(
                sb["bx1"][:], sb["br"][:], sb["bln"][:, 2:3], sb["bln"][:, 1:2],
                op0=ALU.add, op1=ALU.mult),
                reads=[("br",), ("bln", 1), ("bln", 2)], writes=[("bx1",)])
            kb.dma("qa", lambda e, ch=ch: e.dma_start(
                out=sp["x1"][(ch - 1) * C:(ch + 1) * C, :], in_=sb["bx1"][:]),
                reads=[("bx1",)], writes=[("spx1", ch // 2)])


Builder.alloc_b_arena = alloc_b_arena
Builder.phase_bscan = phase_bscan


def alloc_f1_arena(self):
    tl = self.cfg.tloc
    self.arena_reset()
    self.arena("x1fm", [128, 16 * tl], BF16)
    for par in (0, 1):
        self.arena("wgb%d" % par, [128, 16 * 128], BF16)
        self.arena("wub%d" % par, [128, 16 * 128], BF16)
        self.arena("xt%d" % par, [128, D], BF16)
    self.arena("gact", [128, tl], BF16)
    self.arena("uact", [128, tl], BF16)
    self.arena("gm0", [128, tl], BF16)
    self.arena("gm1", [128, tl], BF16)


def phase_f1(self):
    kb, sb, sp, din = self.kb, self.sb, self.spills, self.din
    tl, ntb = self.cfg.tloc, self.cfg.ntb
    pb = self.pb
    PSA = self.PSA
    njt = ceil_div(tl, 512)
    for tb in range(ntb):
        par = tb % 2
        xt = sb["xt%d" % par]
        kb.dma("qs", lambda e, tb=tb, xt=xt: e.dma_start(
            out=xt[:], in_=sp["x1"][tb * 128:(tb + 1) * 128, :]),
            reads=[("spx1", tb // 1)], writes=[("xt", par)])
        for half in range(2):
            for dc8 in range(8):
                dc = half * 8 + dc8
                kb.op("pe", lambda e, dc=dc, dc8=dc8, xt=xt: e.transpose(
                    pb(6).bitcast(BF16)[:, dc8 * 128:(dc8 + 1) * 128],
                    xt[:, dc * 128:(dc + 1) * 128], sb["ident"][:]),
                    reads=[("xt", par), ("sb", "ident")], writes=[("pb", 6)])
            kb.op("act", lambda e, tb=tb, half=half: e.activation(
                out=sb["x1fm"][:].rearrange("p (dc t) -> p dc t", t=tl)[
                    :, half * 8:(half + 1) * 8, tb * 128:(tb + 1) * 128],
                in_=pb(6).bitcast(BF16)[:, 0:1024].rearrange(
                    "p (dc t) -> p dc t", t=128),
                func=AF.Copy),
                reads=[("pb", 6) for i in range(8)], writes=[("x1fm", tb, half)])
    x1fm_rd = [("x1fm", tb, half) for tb in range(ntb) for half in range(2)]
    wg4 = din["WgT8"].rearrange("(dc p) f -> p dc f", p=128)
    wu4 = din["WuT8"].rearrange("(dc p) f -> p dc f", p=128)
    for hc in range(NHH):
        par = hc % 2
        wgb, wub = sb["wgb%d" % par], sb["wub%d" % par]
        kb.dma("qp", lambda e, hc=hc, wgb=wgb: e.dma_start(
            out=wgb[:], in_=wg4[:, :, hc * 128:(hc + 1) * 128]),
            writes=[("wgb", par)])
        kb.op("pool", lambda e, wgb=wgb: e.tensor_tensor(
            wgb[:], wgb[:], sb["c128"][:].broadcast_to([128, 16 * 128]), op=ALU.subtract),
            reads=[("wgb", par), ("sb", "c128")], writes=[("wgb", par)])
        kb.dma("qp", lambda e, hc=hc, wub=wub: e.dma_start(
            out=wub[:], in_=wu4[:, :, hc * 128:(hc + 1) * 128]),
            writes=[("wub", par)])
        kb.op("pool", lambda e, wub=wub: e.tensor_tensor(
            wub[:], wub[:], sb["c128"][:].broadcast_to([128, 16 * 128]), op=ALU.subtract),
            reads=[("wub", par), ("sb", "c128")], writes=[("wub", par)])
        for jt in range(njt):
            t0, tn = jt * 512, min(512, tl - jt * 512)
            for dc in range(16):
                kb.op("pe", lambda e, dc=dc, jt=jt, t0=t0, tn=tn, wgb=wgb: e.matmul(
                    PSA.ap()[:, jt * 512:jt * 512 + tn],
                    wgb[:, dc * 128:(dc + 1) * 128],
                    sb["x1fm"][:, dc * tl + t0: dc * tl + t0 + tn],
                    start=(dc == 0), stop=(dc == 15)),
                    reads=[("wgb", par)] + x1fm_rd, writes=[("pb", jt)])
            for dc in range(16):
                kb.op("pe", lambda e, dc=dc, jt=jt, t0=t0, tn=tn, wub=wub: e.matmul(
                    pb(4 + jt)[:, 0:tn],
                    wub[:, dc * 128:(dc + 1) * 128],
                    sb["x1fm"][:, dc * tl + t0: dc * tl + t0 + tn],
                    start=(dc == 0), stop=(dc == 15)),
                    reads=[("wub", par)] + x1fm_rd, writes=[("pb", 4 + jt)])
            kb.op("act", lambda e, hc=hc, jt=jt, t0=t0, tn=tn: e.activation(
                out=sb["gact"][:, t0:t0 + tn], in_=PSA.ap()[:, jt * 512:jt * 512 + tn],
                func=AF.Silu, scale=sb["sgm"][:, hc:hc + 1]),
                reads=[("pb", jt), ("sb", "sgm")], writes=[("gact", jt)])
            kb.op("act", lambda e, hc=hc, jt=jt, t0=t0, tn=tn: e.activation(
                out=sb["uact"][:, t0:t0 + tn], in_=pb(4 + jt)[:, 0:tn],
                func=AF.Copy, scale=sb["sum_"][:, hc:hc + 1]),
                reads=[("pb", 4 + jt), ("sb", "sum_")], writes=[("uact", jt)])
        gm = sb["gm%d" % par]
        kb.op("dve", lambda e, gm=gm: e.tensor_mul(gm[:], sb["gact"][:], sb["uact"][:]),
              reads=[("gact", jt) for jt in range(njt)] +
                    [("uact", jt) for jt in range(njt)],
              writes=[("gm", par)])
        kb.dma("qs", lambda e, hc=hc, gm=gm: e.dma_start(out=sp["G"][hc], in_=gm[:]),
               reads=[("gm", par)], writes=[("spG", hc)])


def alloc_f2_arena(self):
    tl, ntb = self.cfg.tloc, self.cfg.ntb
    self.arena_reset()
    self.arena("wdh", [128, NHH * 512], BF16)
    self.arena("gtb", [128, NHH * 128], BF16)
    self.arena("x1tb0", [128, 512], BF16)
    self.arena("x1tb1", [128, 512], BF16)
    self.arena("r2q0", [128, ntb * 512], BF16)
    self.arena("r2q1", [128, ntb * 512], BF16)
    self.arena("r2q2", [128, ntb * 512], BF16)
    self.arena("r2c", [128, 512], BF16)
    self.arena("outstg0", [128, D], F32)
    self.arena("fln", [128, 12], F32)
    self.arena("sdb", [128, 512], BF16)
    self.arena("sdrow", [1, D], BF16)


def phase_f2(self):
    kb, sb, sp, din = self.kb, self.sb, self.spills, self.din
    tl, ntb = self.cfg.tloc, self.cfg.ntb
    pb = self.pb
    PSA = self.PSA
    wd4 = din["WdT8"].rearrange("(hc p) f -> p hc f", p=128)
    g3 = sp["G"]
    kb.dma("qp", lambda e: e.dma_start(out=sb["sdrow"][:], in_=din["sdr"][:]),
           writes=[("sdrow",)])
    for q in range(4):
        kb.op("pe", lambda e, q=q: e.matmul(
            PSA.ap()[:, 1536:2048], sb["ones1x128"][:],
            sb["sdrow"][:, q * 512:(q + 1) * 512], start=True, stop=True),
            reads=[("sdrow",), ("sb", "ones1x128")], writes=[("pb", 3)])
        kb.op("act", lambda e: e.activation(
            out=sb["sdb"][:], in_=PSA.ap()[:, 1536:2048], func=AF.Copy),
            reads=[("pb", 3)], writes=[("sdb",)])
        kb.dma("qp", lambda e, q=q: e.dma_start(
            out=sb["wdh"][:], in_=wd4[:, :, q * 512:(q + 1) * 512]),
            writes=[("wdh",)])
        for hc in range(NHH):
            kb.op("pool", lambda e, hc=hc: e.tensor_tensor(
                sb["wdh"][:, hc * 512:(hc + 1) * 512],
                sb["wdh"][:, hc * 512:(hc + 1) * 512],
                sb["c128"][:].broadcast_to([128, 512]), op=ALU.subtract),
                reads=[("wdh",), ("sb", "c128")], writes=[("wdh",)])
            kb.op("pool", lambda e, hc=hc: e.tensor_tensor(
                sb["wdh"][:, hc * 512:(hc + 1) * 512],
                sb["wdh"][:, hc * 512:(hc + 1) * 512], sb["sdb"][:], op=ALU.mult),
                reads=[("wdh",), ("sdb",)], writes=[("wdh",)])
        for tb in range(ntb):
            par = tb % 2
            xt = sb["x1tb%d" % par]
            kb.dma("qs", lambda e, tb=tb: e.dma_start(
                out=sb["gtb"][:], in_=g3[:, :, tb * 128:(tb + 1) * 128].rearrange(
                    "hc p t -> p hc t")),
                reads=[("spG", hc) for hc in range(NHH)], writes=[("gtb",)])
            kb.dma("qa", lambda e, tb=tb, xt=xt, q=q: e.dma_start(
                out=xt[:], in_=sp["x1"][tb * 128:(tb + 1) * 128, q * 512:(q + 1) * 512]),
                reads=[("spx1", tb)], writes=[("x1tb", par)])
            bk = tb % 2
            for hc in range(NHH):
                kb.op("pe", lambda e, hc=hc, bk=bk: e.matmul(
                    PSA.ap()[:, bk * 512:(bk + 1) * 512],
                    sb["gtb"][:, hc * 128:(hc + 1) * 128],
                    sb["wdh"][:, hc * 512:(hc + 1) * 512],
                    start=(hc == 0), stop=(hc == NHH - 1)),
                    reads=[("gtb",), ("wdh",)], writes=[("pb", bk)])
            if q < 3:
                kb.op("dve", lambda e, tb=tb, bk=bk, xt=xt, q=q: e.tensor_add(
                    sb["r2q%d" % q][:, tb * 512:(tb + 1) * 512],
                    PSA.ap()[:, bk * 512:(bk + 1) * 512], xt[:]),
                    reads=[("pb", bk), ("x1tb", par)], writes=[("r2q", q, tb)])
            else:
                kb.op("dve", lambda e, tb=tb, bk=bk, xt=xt: e.tensor_add(
                    sb["r2c"][:], PSA.ap()[:, bk * 512:(bk + 1) * 512], xt[:]),
                    reads=[("pb", bk), ("x1tb", par)], writes=[("r2c",)])
                parts = [sb["r2q0"][:, tb * 512:(tb + 1) * 512],
                         sb["r2q1"][:, tb * 512:(tb + 1) * 512],
                         sb["r2q2"][:, tb * 512:(tb + 1) * 512],
                         sb["r2c"][:]]
                rregs = [("r2q", 0, tb), ("r2q", 1, tb), ("r2q", 2, tb), ("r2c",)]
                for i4, (p4, rr) in enumerate(zip(parts, rregs)):
                    kb.op("dve", lambda e, p4=p4, i4=i4: e.reduce_sum(
                        sb["fln"][:, i4:i4 + 1], p4.unsqueeze(1), axis=AX.X),
                        reads=[rr], writes=[("fln", i4)])
                    kb.op("act", lambda e, p4=p4, i4=i4: e.activation(
                        out=pb(4 + i4)[:], in_=p4, func=AF.Square),
                        reads=[rr, ("pb", 4 + i4)], writes=[("pb", 4 + i4)])
                    kb.op("dve", lambda e, i4=i4: e.reduce_sum(
                        sb["fln"][:, 4 + i4:5 + i4], pb(4 + i4)[:].unsqueeze(1), axis=AX.X),
                        reads=[("pb", 4 + i4)], writes=[("fln", 4 + i4)])
                for i4 in (1, 2, 3):
                    kb.op("dve", lambda e, i4=i4: e.tensor_add(
                        sb["fln"][:, 0:1], sb["fln"][:, 0:1], sb["fln"][:, i4:i4 + 1]),
                        reads=[("fln", 0), ("fln", i4)], writes=[("fln", 0)])
                    kb.op("dve", lambda e, i4=i4: e.tensor_add(
                        sb["fln"][:, 4:5], sb["fln"][:, 4:5], sb["fln"][:, 4 + i4:5 + i4]),
                        reads=[("fln", 4), ("fln", 4 + i4)], writes=[("fln", 4)])
                kb.op("dve", lambda e: e.tensor_scalar(
                    sb["fln"][:, 0:1], sb["fln"][:, 0:1], 1.0 / D, None, op0=ALU.mult),
                    reads=[("fln", 0)], writes=[("fln", 0)])
                kb.op("dve", lambda e: e.tensor_scalar(
                    sb["fln"][:, 4:5], sb["fln"][:, 4:5], 1.0 / D, None, op0=ALU.mult),
                    reads=[("fln", 4)], writes=[("fln", 4)])
                kb.op("dve", lambda e: e.tensor_mul(
                    sb["fln"][:, 8:9], sb["fln"][:, 0:1], sb["fln"][:, 0:1]),
                    reads=[("fln", 0)], writes=[("fln", 8)])
                kb.op("dve", lambda e: e.tensor_sub(
                    sb["fln"][:, 4:5], sb["fln"][:, 4:5], sb["fln"][:, 8:9]),
                    reads=[("fln", 4), ("fln", 8)], writes=[("fln", 4)])
                kb.op("act", lambda e: e.activation(
                    out=sb["fln"][:, 4:5], in_=sb["fln"][:, 4:5], func=AF.Ln, bias=LNEPS),
                    reads=[("fln", 4)], writes=[("fln", 4)])
                kb.op("act", lambda e: e.activation(
                    out=sb["fln"][:, 4:5], in_=sb["fln"][:, 4:5], func=AF.Exp, scale=-0.5),
                    reads=[("fln", 4)], writes=[("fln", 4)])
                kb.op("dve", lambda e: e.tensor_scalar(
                    sb["fln"][:, 8:9], sb["fln"][:, 0:1], -1.0, None, op0=ALU.mult),
                    reads=[("fln", 0)], writes=[("fln", 8)])
                for i4, (p4, rr) in enumerate(zip(parts, rregs)):
                    kb.op("dve", lambda e, p4=p4, i4=i4: e.tensor_scalar(
                        sb["outstg0"][:, i4 * 512:(i4 + 1) * 512], p4,
                        sb["fln"][:, 8:9], sb["fln"][:, 4:5],
                        op0=ALU.add, op1=ALU.mult),
                        reads=[rr, ("fln", 4), ("fln", 8)], writes=[("ost", i4)])
                kb.dma("qa", lambda e, tb=tb: e.dma_start(
                    out=self.out_t[tb * 128:(tb + 1) * 128, :], in_=sb["outstg0"][:]),
                    reads=[("ost", 0), ("ost", 1), ("ost", 2), ("ost", 3)],
                    writes=[("outd", tb)])


Builder.alloc_f1_arena = alloc_f1_arena
Builder.phase_f1 = phase_f1
Builder.alloc_f2_arena = alloc_f2_arena
Builder.phase_f2 = phase_f2


def kernel(**inputs):
    cfg = Cfg(tloc=T // NCORES, ncores=NCORES)
    in_maps = host_prep(inputs, cfg)
    b = build_program(cfg)
    res = run_bass_kernel_spmd(b.nc, in_maps, core_ids=list(range(NCORES)))
    out = np.concatenate([np.asarray(res.results[ci]["out"]) for ci in range(NCORES)],
                         axis=0)
    return out.reshape(1, T, D).astype(np.float32)


# version 08
# speedup vs baseline: 1.1069x; 1.1069x over previous
import numpy as np
import ml_dtypes
import concourse.bass as bass
import concourse.mybir as mybir
from concourse.bass_utils import run_bass_kernel_spmd

F32 = mybir.dt.float32
BF16 = mybir.dt.bfloat16
I32 = mybir.dt.int32
U8 = mybir.dt.uint8
AF = mybir.ActivationFunctionType
ALU = mybir.AluOpType
AX = mybir.AxisListType
BF = ml_dtypes.bfloat16

NCORES = 8
T = 16384
D = 2048
H = 16
DK = 128
M = 128
KW = 4
C = 64
HB = 64
HID = 5632
NHH = HID // 128
SCALE = DK ** -0.5
EPS = 1e-5
L2EPS = 1e-6
LNEPS = 1e-5

ENGINES = ("pe", "act", "dve", "pool", "sp")
PSUM_COARSE = False
QUEUES = ("qs", "qa", "qp")


class KB:

    def __init__(self, nc):
        self.nc = nc
        self.q = {e: [] for e in ENGINES}
        self.cnt = {e: 0 for e in ENGINES}
        self.dcnt = {qn: 0 for qn in QUEUES}
        self.wm = {e: {} for e in ENGINES}
        self.lastw = {}
        self.readers = {}
        self.sems = {}
        for s in ("pe", "act", "dve", "pool", "qs", "qa", "qp"):
            self.sems[s] = nc.alloc_semaphore("sem_" + s)
        self.dma_engine = {"qs": "sp", "qa": "act", "qp": "pool"}
        self.fence = {qn: 0 for qn in QUEUES}

    def _sem_val(self, tok):
        s, v = tok
        if s in QUEUES:
            return (s, 16 * self._fence_for(s, v))
        return (s, v)

    def _fence_for(self, qn, k):
        if self.fence[qn] >= k:
            return self.fence[qn]
        n = self.dcnt[qn]
        eng = self.dma_engine[qn]
        sems = self.sems
        self.q[eng].append(lambda e, qn=qn, n=n: e.wait_ge(sems[qn], 16 * n))
        self.wm[eng][qn] = 16 * n
        self.fence[qn] = n
        return n

    def _edges(self, eng, reads, writes):
        need = {}
        for r in reads:
            t = self.lastw.get(r)
            if t is not None:
                s, v = self._sem_val(t)
                need[s] = max(need.get(s, 0), v)
        for w in writes:
            t = self.lastw.get(w)
            if t is not None:
                s, v = self._sem_val(t)
                need[s] = max(need.get(s, 0), v)
            for t in self.readers.get(w, ()):
                s, v = self._sem_val(t)
                need[s] = max(need.get(s, 0), v)
        wmap = self.wm[eng]
        out = []
        for s, v in need.items():
            if wmap.get(s, 0) < v:
                out.append((s, v))
                wmap[s] = v
        return out

    def op(self, eng, fn, reads=(), writes=(), sem_extra=None):
        reads = list(reads)
        writes = list(writes)
        for r in list(reads):
            if isinstance(r[0], str) and r[0].startswith("pb"):
                reads.remove(r)
                if r not in writes:
                    writes.append(r)
        if PSUM_COARSE:
            nw = []
            coarse = False
            for w in writes:
                if isinstance(w[0], str) and w[0].startswith("pb"):
                    coarse = True
                else:
                    nw.append(w)
            if coarse:
                nw.append(("PSUM",))
            writes = nw
        waits = self._edges(eng, reads, writes)
        self.cnt[eng] += 1
        tok = (eng, self.cnt[eng])
        sem = self.sems[eng]
        sems = self.sems

        def emit(e, waits=waits, fn=fn, sem=sem, sem_extra=sem_extra):
            for s, v in waits[1:]:
                e.wait_ge(sems[s], v)
            inst = fn(e)
            if waits:
                inst._wait_ge(sems[waits[0][0]], waits[0][1])
            inst.then_inc(sem, 1)
            if sem_extra is not None:
                inst.then_inc(sem_extra[0], sem_extra[1])
        self.q[eng].append(emit)
        for w in writes:
            self.lastw[w] = tok
            self.readers[w] = []
        for r in reads:
            self.readers.setdefault(r, []).append(tok)
        return tok

    def dma(self, qn, fn, reads=(), writes=()):
        eng = self.dma_engine[qn]
        waits = self._edges(eng, reads, writes)
        self.dcnt[qn] += 1
        tok = (qn, self.dcnt[qn])
        sem = self.sems[qn]
        sems = self.sems

        def emit(e, waits=waits, fn=fn, sem=sem):
            for s, v in waits[1:]:
                e.wait_ge(sems[s], v)
            inst = fn(e)
            if waits:
                inst._wait_ge(sems[waits[0][0]], waits[0][1])
            inst.then_inc(sem, 16)
        self.q[eng].append(emit)
        for w in writes:
            self.lastw[w] = tok
            self.readers[w] = []
        for r in reads:
            self.readers.setdefault(r, []).append(tok)
        return tok

    def barrier_wait(self, eng, toks):
        need = {}
        for t in toks:
            s, v = self._sem_val(t)
            need[s] = max(need.get(s, 0), v)
        wmap = self.wm[eng]
        sems = self.sems
        for s, v in need.items():
            if wmap.get(s, 0) < v:
                wmap[s] = v
                self.q[eng].append(lambda e, s=s, v=v: e.wait_ge(sems[s], v))

    def finish(self):
        toks = [(e, self.cnt[e]) for e in ENGINES if e != "sp" and self.cnt[e]]
        toks += [(qn, self.dcnt[qn]) for qn in QUEUES if self.dcnt[qn]]
        for e in ENGINES:
            self.barrier_wait(e, toks)

    def replay(self):
        nc = self.nc
        with nc.Block() as block:
            @block.tensor
            def _(e):
                for f in self.q["pe"]:
                    f(e)

            @block.scalar
            def _(e):
                for f in self.q["act"]:
                    f(e)

            @block.vector
            def _(e):
                for f in self.q["dve"]:
                    f(e)

            @block.gpsimd
            def _(e):
                for f in self.q["pool"]:
                    f(e)

            @block.sync
            def _(e):
                for f in self.q["sp"]:
                    f(e)


def ceil_div(a, b):
    return (a + b - 1) // b



class Cfg:

    def __init__(self, tloc=2048, phases=("aproj", "az", "amisc", "ascan", "aagg",
                                          "gather", "bscan", "f1", "f2"),
                 ncores=NCORES, debug_taps=(), host_gather=False):
        self.tloc = tloc
        self.nch = tloc // C
        self.ntb = tloc // 128
        self.phases = phases
        self.ncores = ncores
        self.debug_taps = debug_taps
        self.host_gather = host_gather


class Builder:
    def __init__(self, cfg):
        self.cfg = cfg
        self.nc = bass.Bass("TRN2", target_bir_lowering=False)
        self.kb = KB(self.nc)
        self.din = {}
        self.spills = {}
        self.sb = {}
        self.ps = {}
        self.ar_off = 0
        self._declare_dram()
        self._declare_sbuf()

    def inp(self, name, shape, dtype=F32):
        self.din[name] = self.nc.dram_tensor(name, list(shape), dtype,
                                             kind="ExternalInput").ap()
        return self.din[name]

    def _declare_dram(self):
        cfg, tl, nch = self.cfg, self.cfg.tloc, self.cfg.nch
        inp = self.inp
        inp("hT", [D, HB + tl])
        inp("h", [tl, D])
        for w in ("Wq", "Wk", "Wv", "Ww", "Wo"):
            inp(w, [D, D])
        inp("cqr", [128, 16 * KW]); inp("ckr", [128, 16 * KW]); inp("cvr", [128, 16 * KW])
        inp("Wf1", [D, 128]); inp("Wf2", [128, D]); inp("Wbr", [128, 16 * H])
        inp("Wg1", [D, 128]); inp("Wg2", [128, D])
        inp("bg2m", [128, 16]); inp("normw", [128, 1])
        inp("WgT8", [D, HID], U8); inp("WuT8", [D, HID], U8); inp("WdT8", [HID, D], U8)
        inp("sgm", [128, NHH]); inp("sum_", [128, NHH]); inp("sdr", [1, D])
        inp("ident", [128, 128], BF16)
        inp("identf", [128, 128])
        inp("tri", [C, C], BF16)
        inp("sumsels", [128, 16 * 32], BF16)
        inp("trige", [32, 32])
        inp("ones32", [32, 1])
        inp("ones1x128", [1, 128], BF16)
        inp("ones128", [128, 1], BF16)
        inp("mselM", [2, (NCORES - 1) * 128], BF16)
        inp("mselcol", [128, NCORES - 1])
        inp("m1col", [128, NCORES - 1])
        self.out_t = self.nc.dram_tensor("out", [tl, D], F32, kind="ExternalOutput").ap()
        sp = self.spills

        def spill(name, shape, dtype=BF16, kind="Internal"):
            sp[name] = self.nc.dram_tensor("sp_" + name, list(shape), dtype, kind=kind).ap()
        spill("q", [H, 128, tl])
        spill("kfm", [H, 128, tl])
        spill("ktm", [H, tl, 128])
        spill("vtm", [H, tl, 128])
        spill("wtm", [H, tl, 128])
        spill("g", [tl, D])
        spill("gs", [H, 128, tl])
        spill("u", [tl, D])
        spill("si", [tl, D])
        spill("x1", [tl, D])
        spill("G", [NHH, 128, tl])
        spill("ec", [nch, D])
        spill("wsuf", [nch, D])
        kci = "ExternalOutput" if cfg.host_gather else "Internal"
        kco = "ExternalInput" if cfg.host_gather else "Internal"
        spill("ccin", [257, D], BF16, kind=kci)
        spill("ccout", [cfg.ncores * 257, D], BF16, kind=kco)
        self.dbg = {}
        for name in cfg.debug_taps:
            s = sp[name]
            self.dbg[name] = self.nc.dram_tensor(
                "dbg_" + name, list(s.shape), s.dtype, kind="ExternalOutput").ap()

    def sbuf(self, name, shape, dtype=BF16):
        t = self.nc.alloc_sbuf_tensor("sb_" + name, list(shape), dtype)
        self.sb[name] = t
        return t

    def arena(self, name, shape, dtype):
        esz = {F32: 4, BF16: 2, U8: 1, I32: 4}[dtype]
        nbytes = shape[1] * esz
        off = self.ar_off
        assert off + nbytes <= self.AR_BYTES, f"arena overflow {name}: {off}+{nbytes}"
        ap = self.AR.ap().bitcast(dtype)[0:shape[0], off // esz: off // esz + shape[1]]
        self.ar_off = off + ((nbytes + 31) // 32) * 32
        self.sb[name] = ap
        return ap

    def arena_reset(self):
        self.ar_off = 0

    def _declare_sbuf(self):
        cfg, tl, nch = self.cfg, self.cfg.tloc, self.cfg.nch
        self.AR_BYTES = 156 * 1024
        self.AR = self.nc.alloc_sbuf_tensor("sb_AR", [128, self.AR_BYTES], U8)
        sbuf = self.sbuf
        sbuf("ident", [128, 128], BF16)
        sbuf("m1col", [128, NCORES - 1], F32)
        sbuf("tri", [C, C], BF16)
        sbuf("ones1x128", [1, 128], BF16)
        sbuf("ones128", [128, 1], BF16)
        sbuf("beta", [128, (tl // 128) * H], BF16)
        sbuf("beta2", [64, 2 * (tl // 128) * H], BF16)
        sbuf("cCrows", [32, D], F32)
        sbuf("ec16all", [16, nch * 128], BF16)
        sbuf("Sk", [128, H * M], F32)
        sbuf("Sv", [128, H * DK], F32)
        sbuf("Skbf", [128, H * M], BF16)
        sbuf("Svbf", [128, H * DK], BF16)
        sbuf("ecct", [128, 16], F32)
        sbuf("normw", [128, 1], F32)
        sbuf("bg2m", [128, 16], F32)
        sbuf("cw", [128, 3 * 16 * KW], F32)
        sbuf("sumsels", [128, 16 * 32], BF16)
        sbuf("trige", [32, 32], F32)
        sbuf("ones32", [32, 1], F32)
        sbuf("sgm", [128, NHH], F32)
        sbuf("sum_", [128, NHH], F32)
        sbuf("mselM", [2, (NCORES - 1) * 128], BF16)
        sbuf("mselcol", [128, NCORES - 1], F32)
        sbuf("row2", [2, D], BF16)
        self.CVALS = [L2EPS, LNEPS, EPS, SCALE, -0.5, -1.0, 0.5, 2.0, 1.0 / 128, -128.0,
                      1.0 / D, 1.0]
        sbuf("cvals", [128, len(self.CVALS)], F32)
        sbuf("c128", [128, 1], BF16)
        self.PSA = self.nc.alloc_psum_tensor("psA", [128, 2048], F32)
        self.PBH = {i: self.nc.alloc_psum_tensor("ps%d" % i, [128, 512], F32)
                    for i in range(4, 8)}

    def pb(self, i):
        if i < 4:
            return self.PSA.ap()[:, i * 512:(i + 1) * 512]
        return self.PBH[i].ap()

    def ld_consts(self):
        kb, din, sb = self.kb, self.din, self.sb
        pairs = [("ident", "ident"), ("tri", "tri"), ("ones1x128", "ones1x128"),
                 ("ones128", "ones128"), ("normw", "normw"), ("bg2m", "bg2m"),
                 ("sumsels", "sumsels"), ("trige", "trige"), ("ones32", "ones32"),
                 ("m1col", "m1col"),
                 ("sgm", "sgm"), ("sum_", "sum_"), ("mselcol", "mselcol")]
        for dn, sn in pairs:
            kb.dma("qs", lambda e, d=din[dn], s=sb[sn]: e.dma_start(out=s[:], in_=d[:]),
                   writes=[("sb", sn)])
        kb.dma("qs", lambda e: e.dma_start(out=sb["mselM"][:], in_=din["mselM"][:]),
               writes=[("sb", "mselM")])
        for i, dn in enumerate(("cqr", "ckr", "cvr")):
            kb.dma("qs", lambda e, d=din[dn], i=i: e.dma_start(
                out=sb["cw"][:, i * 64:(i + 1) * 64], in_=d[:]),
                writes=[("sb", "cw", i)])
        kb.op("dve", lambda e: e.memset(sb["row2"][:], 1.0), writes=[("sb", "row2one")])
        kb.op("dve", lambda e: e.memset(sb["c128"][:], 128.0), writes=[("sb", "c128")])
        for i, v in enumerate(self.CVALS):
            kb.op("dve", lambda e, i=i, v=v: e.memset(sb["cvals"][:, i:i + 1], v),
                  writes=[("cvals", i)])
            self.nc.const_aps.register(F32, v, sb["cvals"][:, i:i + 1])

    def alloc_a_arena(self):
        tl = self.cfg.tloc
        self.arena_reset()
        self.arena("hfm", [128, 16 * (HB + tl)], BF16)
        self.arena("wpan0", [128, 16 * 128], BF16)
        self.arena("wpan1", [128, 16 * 128], BF16)
        self.arena("pbf0", [128, HB + tl], BF16)
        self.arena("pbf1", [128, HB + tl], BF16)
        self.arena("convy0", [128, tl], BF16)
        self.arena("convy1", [128, tl], BF16)
        self.arena("tstg0", [128, tl], BF16)
        self.arena("tstg1", [128, tl], BF16)
        self.arena("z1", [128, tl], BF16)
        self.arena("zg1", [128, tl], BF16)
        self.arena("wsmall", [128, 2 * 16 * 128], BF16)
        self.arena("wf2", [128, D], BF16)
        self.arena("wg2", [128, D], BF16)
        self.arena("wbr", [128, 16 * H], BF16)
        self.arena("ztmp0", [128, 512], F32)
        self.arena("ztmp1", [128, 512], F32)
        self.arena("gstg", [128, D], BF16)
        self.arena("ecall", [32, D], BF16)
        self.arena("wsufb", [32, D], BF16)

    def jgroups(self):
        tl = self.cfg.tloc
        return [(0, HB)] + [(HB + i * 512, min(512, tl - i * 512))
                            for i in range(ceil_div(tl, 512))]

    def ld_hfm(self):
        kb, sb, din = self.kb, self.sb, self.din
        tl = self.cfg.tloc
        for dc in range(16):
            kb.dma("qp", lambda e, dc=dc: e.dma_start(
                out=sb["hfm"][:, dc * (HB + tl):(dc + 1) * (HB + tl)],
                in_=din["hT"][dc * 128:(dc + 1) * 128, :]),
                writes=[("hfm", dc)])

    def phase_aproj(self):
        kb, sb, din, sp = self.kb, self.sb, self.din, self.spills
        tl, ntb = self.cfg.tloc, self.cfg.ntb
        pb = self.pb
        HT = HB + tl
        tensors = [("q", "Wq", 0, ("q",), False),
                   ("k", "Wk", 1, ("kfm", "ktm"), True),
                   ("v", "Wv", 2, ("vtm",), True),
                   ("w", "Ww", 2, ("wtm",), True)]
        for xi, (xn, wn, cwi, spnames, want_tm) in enumerate(tensors):
            for fc in range(16):
                par = fc % 2
                wp = sb["wpan%d" % par]
                w4 = din[wn].rearrange("(dc p) (fc f) -> p dc fc f", p=128, f=128)
                kb.dma("qp", lambda e, w4=w4, fc=fc, wp=wp: e.dma_start(
                    out=wp[:], in_=w4[:, :, fc, :]),
                    writes=[("wpan", par)])
                pbf = sb["pbf%d" % par]
                for jg, (t0, tn) in enumerate(self.jgroups()):
                    bank = jg % 2
                    for dc in range(16):
                        kb.op("pe", lambda e, wp=wp, dc=dc, t0=t0, tn=tn, bank=bank: e.matmul(
                            pb(bank)[:, 0:tn],
                            wp[:, dc * 128:(dc + 1) * 128],
                            sb["hfm"][:, dc * HT + t0: dc * HT + t0 + tn],
                            start=(dc == 0), stop=(dc == 15)),
                            reads=[("wpan", par), ("hfm", dc)], writes=[("pb", bank)])
                    kb.op("act", lambda e, t0=t0, tn=tn, bank=bank, pbf=pbf: e.activation(
                        out=pbf[:, t0:t0 + tn], in_=pb(bank)[:, 0:tn], func=AF.Copy),
                        reads=[("pb", bank)], writes=[("pbf", par, jg)])
                cy = sb["convy%d" % par]
                pr = [("pbf", par, jg) for jg in range(len(self.jgroups()))]
                cwbase = cwi * 64 + fc * KW
                kb.op("dve", lambda e, cy=cy, pbf=pbf,
                      cc=sb["cw"][:, cwbase + 3:cwbase + 4]: e.tensor_scalar(
                    cy[:], pbf[:, HB:HB + tl], cc, None, op0=ALU.mult),
                    reads=pr + [("sb", "cw", cwi)], writes=[("convy", par)])
                for i, eng in ((2, "dve"), (1, "dve"), (0, "dve")):
                    kb.op(eng, lambda e, i=i, cy=cy, pbf=pbf,
                          cc=sb["cw"][:, cwbase + i:cwbase + i + 1]: e.scalar_tensor_tensor(
                        cy[:], pbf[:, HB - 3 + i:HB - 3 + i + tl], cc, cy[:],
                        op0=ALU.mult, op1=ALU.add),
                        reads=pr + [("convy", par)], writes=[("convy", par)])
                kb.op("act", lambda e, cy=cy: e.activation(out=cy[:], in_=cy[:], func=AF.Silu),
                      reads=[("convy", par)], writes=[("convy", par)])
                for nm in spnames:
                    if nm in ("q", "kfm"):
                        kb.dma("qs", lambda e, nm=nm, fc=fc, cy=cy: e.dma_start(
                            out=sp[nm][fc], in_=cy[:]),
                            reads=[("convy", par)], writes=[("sp" + nm, fc)])
                if want_tm:
                    nmtm = spnames[-1]
                    tstg = sb["tstg%d" % par]
                    for tb in range(ntb):
                        tbank = 2 + (tb // 8) % 2
                        kb.op("pe", lambda e, tb=tb, cy=cy, tbank=tbank: e.transpose(
                            pb(tbank).bitcast(BF16)[:, (tb % 8) * 128:(tb % 8) * 128 + 128],
                            cy[:, tb * 128:(tb + 1) * 128], sb["ident"][:]),
                            reads=[("convy", par), ("sb", "ident")],
                            writes=[("pbT", tbank, tb % 8)])
                        if tb % 8 == 7 or tb == ntb - 1:
                            lo = (tb // 8) * 8
                            kb.op("act", lambda e, lo=lo, tb=tb, tstg=tstg, tbank=tbank:
                                  e.activation(
                                      out=tstg[:, lo * 128:(tb + 1) * 128],
                                      in_=pb(tbank).bitcast(BF16)[:, 0:(tb + 1 - lo) * 128],
                                      func=AF.Copy),
                                  reads=[("pbT", tbank, i) for i in range(tb + 1 - lo)],
                                  writes=[("tstg", par, tb // 8)])
                    kb.dma("qs", lambda e, nmtm=nmtm, fc=fc, tstg=tstg: e.dma_start(
                        out=sp[nmtm][fc].rearrange("(tb p) f -> p tb f", p=128),
                        in_=tstg[:, 0:tl]),
                        reads=[("tstg", par, i) for i in range(ceil_div(ntb, 8))],
                        writes=[("sp" + nmtm, fc)])


def build_program(cfg):
    b = Builder(cfg)
    b.ld_consts()
    ph = cfg.phases
    if "aproj" in ph or "az" in ph or "amisc" in ph:
        b.alloc_a_arena()
        b.ld_hfm()
    for name in ("aproj", "az", "amisc", "ascan", "aagg", "gather", "bscan", "f1", "f2"):
        if name not in ph:
            continue
        if name == "ascan":
            b.alloc_ascan_arena()
        if name == "aagg":
            b.alloc_aagg_arena()
        if name == "bscan":
            b.alloc_b_arena()
        if name == "f1":
            b.alloc_f1_arena()
        if name == "f2":
            b.alloc_f2_arena()
        b.phase_fence()
        getattr(b, "phase_" + name)()
    b.kb.finish()
    for name, dst in b.dbg.items():
        srcap = b.spills[name]
        b.kb.dma("qa", lambda e, srcap=srcap, dst=dst: e.dma_start(
            out=dst.rearrange(_flat_str(dst.shape)), in_=srcap.rearrange(_flat_str(srcap.shape))))
    if b.dbg:
        b.kb.finish()
    b.kb.replay()
    return b


def _flat_str(shape):
    letters = "abcdef"[:len(shape)]
    return " ".join(letters) + " -> (" + " ".join(letters) + ")"


def host_prep(inputs, cfg):
    tl, n = cfg.tloc, cfg.ncores
    f32 = np.float32
    h_full = np.asarray(inputs["L0_attn_hidden_states"], f32)[0]
    hT_full = np.ascontiguousarray(h_full.T)
    zeros_halo = np.zeros((D, HB), f32)

    def u8_low_bytes(a_int32):
        a = np.ascontiguousarray(a_int32.astype(np.int32, copy=False))
        return np.ascontiguousarray(a.view(np.uint8).reshape(*a.shape, 4)[..., 0])

    g = {}
    g["Wq"] = np.asarray(inputs["L0_attn_Wq"], f32)
    g["Wk"] = np.asarray(inputs["L0_attn_Wk"], f32)
    g["Wv"] = np.asarray(inputs["L0_attn_Wv"], f32)
    g["Ww"] = np.asarray(inputs["L0_attn_Ww"], f32)
    g["Wo"] = np.asarray(inputs["L0_attn_Wo"], f32)
    for cn, dn in (("cqr", "L0_attn_cq"), ("ckr", "L0_attn_ck"), ("cvr", "L0_attn_cv")):
        c = np.asarray(inputs[dn], f32)
        g[cn] = np.ascontiguousarray(c.reshape(16, 128, KW).transpose(1, 0, 2).reshape(128, 16 * KW))
    g["Wf1"] = np.asarray(inputs["L0_attn_Wf1"], f32)
    g["Wf2"] = np.asarray(inputs["L0_attn_Wf2"], f32)
    wb = np.asarray(inputs["L0_attn_Wb"], f32)
    g["Wbr"] = np.ascontiguousarray(wb.reshape(16, 128, H).transpose(1, 0, 2).reshape(128, 16 * H))
    g["Wg1"] = np.asarray(inputs["L0_attn_Wg1"], f32)
    g["Wg2"] = np.asarray(inputs["L0_attn_Wg2"], f32)
    bg2 = np.asarray(inputs["L0_attn_bg2"], f32)
    g["bg2m"] = np.ascontiguousarray(bg2.reshape(16, 128).T)
    g["normw"] = np.ascontiguousarray(np.asarray(inputs["L0_attn_norm_w"], f32).reshape(128, 1))
    g["WgT8"] = u8_low_bytes(np.ascontiguousarray(np.asarray(inputs["L0_ffn_wq_gate"]).T))
    g["WuT8"] = u8_low_bytes(np.ascontiguousarray(np.asarray(inputs["L0_ffn_wq_up"]).T))
    g["WdT8"] = u8_low_bytes(np.ascontiguousarray(np.asarray(inputs["L0_ffn_wq_down"]).T))
    g["sgm"] = np.ascontiguousarray(np.asarray(inputs["L0_ffn_s_gate"], f32).reshape(NHH, 128).T)
    g["sum_"] = np.ascontiguousarray(np.asarray(inputs["L0_ffn_s_up"], f32).reshape(NHH, 128).T)
    g["sdr"] = np.ascontiguousarray(np.asarray(inputs["L0_ffn_s_down"], f32).reshape(1, D))
    g["ident"] = np.eye(128, dtype=BF)
    g["identf"] = np.eye(128, dtype=f32)
    sl = np.arange(C)
    g["tri"] = (sl[:, None] <= sl[None, :]).astype(BF)
    ss = np.zeros((128, 16 * 32), BF)
    for tb in range(16):
        for t in range(128):
            ss[t, tb * 32 + 2 * tb + (t // 64)] = 1
    g["sumsels"] = ss
    jj = np.arange(32)
    g["trige"] = (jj[:, None] >= jj[None, :]).astype(f32)
    g["ones32"] = np.ones((32, 1), f32)
    g["ones1x128"] = np.ones((1, 128), BF)
    g["ones128"] = np.ones((128, 1), BF)

    in_maps = []
    for ci in range(n):
        m = dict(g)
        s = ci * tl
        m["h"] = h_full[s:s + tl]
        if s >= HB:
            m["hT"] = np.ascontiguousarray(hT_full[:, s - HB:s + tl])
        else:
            m["hT"] = np.ascontiguousarray(
                np.concatenate([zeros_halo[:, :HB - s], hT_full[:, 0:s + tl]], axis=1))
        msel = np.zeros((2, (NCORES - 1) * 128), BF)
        mcol = np.zeros((128, NCORES - 1), f32)
        for j in range(NCORES - 1):
            sel = 1.0 if j < ci else 0.0
            msel[0, j * 128:(j + 1) * 128] = sel
            msel[1, j * 128:(j + 1) * 128] = 1.0 - sel
            mcol[:, j] = sel
        m["mselM"] = msel
        m["mselcol"] = mcol
        m["m1col"] = np.ascontiguousarray(1.0 - mcol)
        if cfg.host_gather:
            m["sp_ccout"] = np.zeros((cfg.ncores * 257, D), BF)
        in_maps.append(m)
    return in_maps


AZ_CUT = 0


def phase_az(self):
    kb, sb, din, sp = self.kb, self.sb, self.din, self.spills
    tl, ntb, nch = self.cfg.tloc, self.cfg.ntb, self.cfg.nch
    pb = self.pb
    HT = HB + tl
    wf1_4 = din["Wf1"].rearrange("(dc p) f -> p dc f", p=128)
    kb.dma("qp", lambda e: e.dma_start(out=sb["wsmall"][:, 0:16 * 128], in_=wf1_4),
           writes=[("wsmall", 0)])
    kb.dma("qp", lambda e: e.dma_start(out=sb["wf2"][:], in_=din["Wf2"][:]),
           writes=[("wf2",)])
    for jg in range(ceil_div(tl, 512)):
        t0, tn = jg * 512, min(512, tl - jg * 512)
        bank = jg % 2
        for dc in range(16):
            kb.op("pe", lambda e, dc=dc, t0=t0, tn=tn, bank=bank: e.matmul(
                pb(bank)[:, 0:tn],
                sb["wsmall"][:, dc * 128:(dc + 1) * 128],
                sb["hfm"][:, dc * HT + HB + t0: dc * HT + HB + t0 + tn],
                start=(dc == 0), stop=(dc == 15)),
                reads=[("wsmall", 0), ("hfm", dc)], writes=[("pb", bank)])
        kb.op("act", lambda e, t0=t0, tn=tn, bank=bank: e.activation(
            out=sb["z1"][:, t0:t0 + tn], in_=pb(bank)[:, 0:tn], func=AF.Copy),
            reads=[("pb", bank)], writes=[("z1", jg)])
    if AZ_CUT >= 4:
        return
    njt = ceil_div(D, 512)
    for tb in range(ntb):
        for jt in range(njt):
            bank = 2 + (jt % 2)
            kb.op("pe", lambda e, tb=tb, jt=jt, bank=bank: e.matmul(
                pb(bank)[:],
                sb["z1"][:, tb * 128:(tb + 1) * 128],
                sb["wf2"][:, jt * 512:(jt + 1) * 512],
                start=True, stop=True),
                reads=[("z1", tb // 4), ("wf2",)], writes=[("pb", bank)])
            kb.op("act", lambda e, bank=bank: e.activation(
                out=sb["ztmp0"][:], in_=pb(bank)[:], func=AF.Abs),
                reads=[("pb", bank)], writes=[("zt0",)])
            kb.op("act", lambda e: e.activation(
                out=sb["ztmp0"][:], in_=sb["ztmp0"][:], func=AF.Exp, scale=-1.0),
                reads=[("zt0",)], writes=[("zt0",)])
            kb.op("act", lambda e: e.activation(
                out=sb["ztmp0"][:], in_=sb["ztmp0"][:], func=AF.Ln, bias=1.0),
                reads=[("zt0",)], writes=[("zt0",)])
            kb.op("dve", lambda e, bank=bank: e.tensor_single_scalar(
                sb["ztmp1"][:], pb(bank)[:], 0.0, op=ALU.min),
                reads=[("pb", bank)], writes=[("zt1",)])
            kb.op("dve", lambda e, jt=jt: e.tensor_sub(
                sb["gstg"][:, jt * 512:(jt + 1) * 512], sb["ztmp1"][:], sb["ztmp0"][:]),
                reads=[("zt0",), ("zt1",)], writes=[("gstg", jt)])

        kb.dma("qs", lambda e, tb=tb: e.dma_start(
            out=sp["g"][tb * 128:(tb + 1) * 128, :], in_=sb["gstg"][:]),
            reads=[("gstg", jt) for jt in range(njt)], writes=[("spg", tb)])
    if AZ_CUT >= 3:
        return
    for jt in range(njt):
        bank = 4 + jt % 2
        for tb in range(ntb):
            par = tb % 2
            kb.dma("qs", lambda e, jt=jt, tb=tb, par=par: e.dma_start(
                out=sb["ztmp%d" % par][:].bitcast(BF16)[:, 0:512],
                in_=sp["g"][tb * 128:(tb + 1) * 128, jt * 512:(jt + 1) * 512]),
                reads=[("spg", tb)], writes=[("zt", par)])
            kb.op("pe", lambda e, tb=tb, jt=jt, par=par, bank=bank: e.matmul(
                pb(bank)[0:32, :],
                sb["sumsels"][:, tb * 32:(tb + 1) * 32],
                sb["ztmp%d" % par][:].bitcast(BF16)[:, 0:512],
                start=(tb == 0), stop=(tb == ntb - 1)),
                reads=[("zt", par), ("sb", "sumsels")], writes=[("pbcc", bank)])
        kb.op("dve", lambda e, jt=jt, bank=bank: e.tensor_copy(
            sb["cCrows"][:, jt * 512:(jt + 1) * 512], pb(bank)[0:32, :]),
            reads=[("pbcc", bank)], writes=[("cCrows", jt)])
        kb.op("act", lambda e, jt=jt, bank=bank: e.activation(
            out=sb["ecall"][:, jt * 512:(jt + 1) * 512], in_=pb(bank)[0:32, :],
            func=AF.Exp),
            reads=[("pbcc", bank)], writes=[("ecall", jt)])
    for jt in range(njt):
        kb.op("pe", lambda e, jt=jt: e.matmul(
            pb(jt % 2)[0:32, :], sb["trige"][:],
            sb["cCrows"][:, jt * 512:(jt + 1) * 512], start=True, stop=True),
            reads=[("cCrows", jt), ("sb", "trige")], writes=[("pb", jt % 2)])
        kb.op("act", lambda e, jt=jt: e.activation(
            out=sb["wsufb"][:, jt * 512:(jt + 1) * 512], in_=pb(jt % 2)[0:32, :],
            func=AF.Exp),
            reads=[("pb", jt % 2)], writes=[("wsufb", jt)])
    if AZ_CUT >= 2:
        return
    kb.dma("qs", lambda e: e.dma_start(out=sp["ec"][:], in_=sb["ecall"][0:nch, :]),
           reads=[("ecall", jt) for jt in range(njt)], writes=[("spec",)])
    kb.dma("qs", lambda e: e.dma_start(out=sp["wsuf"][:], in_=sb["wsufb"][0:nch, :]),
           reads=[("wsufb", jt) for jt in range(njt)], writes=[("spwsuf",)])
    if AZ_CUT >= 1:
        return
    kb.dma("qp", lambda e: e.dma_start(
        out=sb["ec16all"][:, 0:nch * 128],
        in_=sp["ec"].rearrange("j (h m) -> h j m", m=128)),
        reads=[("spec",)], writes=[("ec16all",)])

AM_CUT = 0


def phase_amisc(self):
    kb, sb, din, sp = self.kb, self.sb, self.din, self.spills
    tl, ntb = self.cfg.tloc, self.cfg.ntb
    pb = self.pb
    HT = HB + tl
    kb.dma("qp", lambda e: e.dma_start(out=sb["wbr"][:], in_=din["Wbr"][:]),
           writes=[("wbr",)])
    wg1_4 = din["Wg1"].rearrange("(dc p) f -> p dc f", p=128)
    kb.dma("qp", lambda e: e.dma_start(out=sb["wsmall"][:, 16 * 128:32 * 128], in_=wg1_4),
           writes=[("wsmall", 1)])
    kb.dma("qp", lambda e: e.dma_start(out=sb["wg2"][:], in_=din["Wg2"][:]),
           writes=[("wg2",)])
    for tb in range(ntb):
        for dc in range(16):
            kb.op("pe", lambda e, tb=tb, dc=dc: e.matmul(
                pb(0)[:, 0:H],
                sb["hfm"][:, dc * HT + HB + tb * 128: dc * HT + HB + (tb + 1) * 128],
                sb["wbr"][:, dc * H:(dc + 1) * H],
                start=(dc == 0), stop=(dc == 15)),
                reads=[("wbr",), ("hfm", dc)], writes=[("pb", 0)])
        kb.op("act", lambda e, tb=tb: e.activation(
            out=sb["beta"][:, tb * H:(tb + 1) * H], in_=pb(0)[:, 0:H], func=AF.Sigmoid),
            reads=[("pb", 0)], writes=[("beta", tb)])
    b2 = sb["beta2"].ap().rearrange("p (c j) -> p c j", j=H)
    kb.dma("qs", lambda e, b2=b2: e.dma_start(
        out=b2[:, 0::2, :], in_=sb["beta"][0:64, :].rearrange("p (c j) -> p c j", j=H)),
        reads=[("beta", tb) for tb in range(ntb)], writes=[("beta2", 0)])
    kb.dma("qs", lambda e, b2=b2: e.dma_start(
        out=b2[:, 1::2, :], in_=sb["beta"][64:128, :].rearrange("p (c j) -> p c j", j=H)),
        reads=[("beta", tb) for tb in range(ntb)], writes=[("beta2", 1)])
    if AM_CUT >= 2:
        return
    for jg in range(ceil_div(tl, 512)):
        t0, tn = jg * 512, min(512, tl - jg * 512)
        bank = 1
        for dc in range(16):
            kb.op("pe", lambda e, dc=dc, t0=t0, tn=tn, bank=bank: e.matmul(
                pb(bank)[:, 0:tn],
                sb["wsmall"][:, (16 + dc) * 128:(17 + dc) * 128],
                sb["hfm"][:, dc * HT + HB + t0: dc * HT + HB + t0 + tn],
                start=(dc == 0), stop=(dc == 15)),
                reads=[("wsmall", 1), ("hfm", dc)], writes=[("pb", bank)])
        kb.op("act", lambda e, t0=t0, tn=tn, bank=bank: e.activation(
            out=sb["zg1"][:, t0:t0 + tn], in_=pb(bank)[:, 0:tn], func=AF.Copy),
            reads=[("pb", bank)], writes=[("zg1", jg)])
    if AM_CUT >= 1:
        return
    for fc2 in range(16):
        for jg in range(ceil_div(tl, 512)):
            t0, tn = jg * 512, min(512, tl - jg * 512)
            bank = 2 + (jg % 2)
            kb.op("pe", lambda e, fc2=fc2, t0=t0, tn=tn, bank=bank: e.matmul(
                pb(bank)[:, 0:tn],
                sb["wg2"][:, fc2 * 128:(fc2 + 1) * 128],
                sb["zg1"][:, t0:t0 + tn], start=True, stop=True),
                reads=[("wg2",), ("zg1", jg)], writes=[("pb", bank)])
            kb.op("dve", lambda e, fc2=fc2, t0=t0, tn=tn, bank=bank: e.tensor_scalar(
                sb["z1"][:, t0:t0 + tn], pb(bank)[:, 0:tn],
                sb["bg2m"][:, fc2:fc2 + 1], None, op0=ALU.add),
                reads=[("pb", bank), ("sb", "bg2m")], writes=[("z1", jg)])
            kb.op("act", lambda e, t0=t0, tn=tn: e.activation(
                out=sb["z1"][:, t0:t0 + tn], in_=sb["z1"][:, t0:t0 + tn],
                func=AF.Sigmoid),
                reads=[("z1", jg)], writes=[("z1", jg)])
        kb.dma("qs", lambda e, fc2=fc2: e.dma_start(
            out=sp["gs"][fc2], in_=sb["z1"][:, 0:tl]),
            reads=[("z1", jg) for jg in range(ceil_div(tl, 512))],
            writes=[("spgs", fc2)])


Builder.phase_az = phase_az
Builder.phase_amisc = phase_amisc


def phase_fence(self):
    kb = self.kb
    toks = [(e, kb.cnt[e]) for e in ENGINES if kb.cnt[e]]
    toks += [(qn, kb.dcnt[qn]) for qn in QUEUES if kb.dcnt[qn]]
    for e in ENGINES:
        kb.barrier_wait(e, toks)


def alloc_ascan_arena(self):
    tl = self.cfg.tloc
    self.arena_reset()
    for par in (0, 1):
        self.arena("gch%d" % par, [64, D], BF16)
        self.arena("wtch%d" % par, [64, D], BF16)
        self.arena("qfch%d" % par, [128, H * C], BF16)
        self.arena("kfch%d" % par, [128, H * C], BF16)
        self.arena("einv%d" % par, [64, D], BF16)
        self.arena("bw%d" % par, [64, D], BF16)
        self.arena("ubuf%d" % par, [64, D], BF16)
        self.arena("sibuf%d" % par, [64, D], BF16)
        self.arena("pm%d" % par, [64, 1024], BF16)
        self.arena("ssq%d" % par, [64, 16], F32)
        self.arena("rb%d" % par, [64, 16], BF16)

    self.arena("sqtmp", [64, D], F32)


def alloc_aagg_arena(self):
    self.arena_reset()
    for par in (0, 1):
        self.arena("ktch%d" % par, [64, 4 * D], BF16)
        self.arena("vtch%d" % par, [64, 4 * D], BF16)
        self.arena("ucb%d" % par, [64, 4 * D], BF16)
        self.arena("wsb%d" % par, [64, D], BF16)
    self.arena("ccstg", [128, D], BF16)
    self.arena("aggK", [128, H * M], F32)
    self.arena("aggV", [128, H * DK], F32)
    self.arena("erow", [1, D], BF16)


def phase_ascan(self):
    kb, sb, sp = self.kb, self.sb, self.spills
    tl, nch = self.cfg.tloc, self.cfg.nch
    pb = self.pb
    PSA = self.PSA
    wtm3 = sp["wtm"].rearrange("h t f -> t h f")
    q3 = sp["q"].rearrange("h p t -> p h t")
    k3 = sp["kfm"].rearrange("h p t -> p h t")
    for ch in range(nch):
        par = ch % 2
        gch, wt = sb["gch%d" % par], sb["wtch%d" % par]
        qf, kf = sb["qfch%d" % par], sb["kfch%d" % par]
        einv, bw = sb["einv%d" % par], sb["bw%d" % par]
        ub, sib, pm = sb["ubuf%d" % par], sb["sibuf%d" % par], sb["pm%d" % par]
        ssq, rb = sb["ssq%d" % par], sb["rb%d" % par]
        kb.dma("qs", lambda e, ch=ch, gch=gch: e.dma_start(
            out=gch[:], in_=sp["g"][ch * C:(ch + 1) * C, :]),
            reads=[("spg", ch * C // 128)], writes=[("gch", par)])
        kb.dma("qs", lambda e, ch=ch, wt=wt: e.dma_start(
            out=wt[:], in_=wtm3[ch * C:(ch + 1) * C]),
            reads=[("spwtm", h) for h in range(H)], writes=[("wtch", par)])
        kb.dma("qs", lambda e, ch=ch, qf=qf: e.dma_start(
            out=qf[:], in_=q3[:, :, ch * C:(ch + 1) * C]),
            reads=[("spq", h) for h in range(H)], writes=[("qfch", par)])
        kb.dma("qs", lambda e, ch=ch, kf=kf: e.dma_start(
            out=kf[:], in_=k3[:, :, ch * C:(ch + 1) * C]),
            reads=[("spkfm", h) for h in range(H)], writes=[("kfch", par)])
        for jt in range(4):
            kb.op("pe", lambda e, jt=jt, gch=gch: e.matmul(
                PSA.ap()[0:C, jt * 512:(jt + 1) * 512], sb["tri"][:],
                gch[:, jt * 512:(jt + 1) * 512], start=True, stop=True),
                reads=[("gch", par), ("sb", "tri")], writes=[("pb", jt)])
        kb.op("act", lambda e, einv=einv: e.activation(
            out=einv[:], in_=PSA.ap()[0:C, :], func=AF.Exp, scale=-1.0),
            reads=[("pb", 0), ("pb", 1), ("pb", 2), ("pb", 3)], writes=[("einv", par)])
        kb.op("act", lambda e, wt=wt: e.activation(
            out=sb["sqtmp"][:], in_=wt[:], func=AF.Square),
            reads=[("wtch", par)], writes=[("sqtmp",)])
        kb.op("dve", lambda e, ssq=ssq: e.reduce_sum(
            ssq[:], sb["sqtmp"][:].rearrange("p (h m) -> p h m", m=M), axis=AX.X),
            reads=[("sqtmp",)], writes=[("ssq", par)])
        kb.op("act", lambda e, ssq=ssq: e.activation(
            out=ssq[:], in_=ssq[:], func=AF.Ln, bias=L2EPS),
            reads=[("ssq", par)], writes=[("ssq", par)])
        kb.op("act", lambda e, ssq=ssq: e.activation(
            out=ssq[:], in_=ssq[:], func=AF.Exp, scale=-0.5),
            reads=[("ssq", par)], writes=[("ssq", par)])
        bslice = sb["beta2"][:, ch * H:(ch + 1) * H]
        kb.op("dve", lambda e, rb=rb, ssq=ssq, bslice=bslice: e.tensor_mul(
            rb[:], ssq[:], bslice),
            reads=[("ssq", par), ("beta2", 0), ("beta2", 1)], writes=[("rbb", par)])
        kb.op("dve", lambda e, bw=bw, wt=wt, rb=rb: e.tensor_tensor(
            bw[:].rearrange("p (h m) -> p h m", m=M),
            wt[:].rearrange("p (h m) -> p h m", m=M),
            rb[:].unsqueeze(2).broadcast_to([C, H, M]), op=ALU.mult),
            reads=[("wtch", par), ("rbb", par)], writes=[("bwb", par)])
        kb.op("dve", lambda e, ub=ub, bw=bw, einv=einv: e.tensor_mul(
            ub[:], bw[:], einv[:]),
            reads=[("bwb", par), ("einv", par)], writes=[("ubuf", par)])
        kb.dma("qs", lambda e, ch=ch, ub=ub: e.dma_start(
            out=sp["u"][ch * C:(ch + 1) * C, :], in_=ub[:]),
            reads=[("ubuf", par)], writes=[("spu", ch)])
        for hh in range(H):
            bank = 4 + hh // 8
            kb.op("pe", lambda e, hh=hh, bank=bank, kf=kf, qf=qf: e.matmul(
                pb(bank)[0:C, (hh % 8) * C:(hh % 8 + 1) * C],
                kf[:, hh * C:(hh + 1) * C], qf[:, hh * C:(hh + 1) * C],
                start=True, stop=True),
                reads=[("qfch", par), ("kfch", par)], writes=[("pb", bank)])
        for b_ in (4, 5):
            kb.op("dve", lambda e, b_=b_, pm=pm: e.tensor_tensor(
                pm[:, (b_ - 4) * 512:(b_ - 3) * 512].rearrange("p (h s) -> p h s", s=C),
                pb(b_)[0:C, :].rearrange("p (h s) -> p h s", s=C),
                sb["tri"][:].unsqueeze(1).broadcast_to([C, 8, C]), op=ALU.mult),
                reads=[("pb", b_), ("sb", "tri")], writes=[("pmb", par, b_)])
        for hh in range(H):
            kb.op("pe", lambda e, hh=hh, pm=pm, ub=ub: e.matmul(
                PSA.ap()[0:C, hh * M:(hh + 1) * M],
                pm[:, hh * C:(hh + 1) * C], ub[:, hh * M:(hh + 1) * M],
                start=True, stop=True),
                reads=[("pmb", par, 4 + hh // 8), ("ubuf", par)],
                writes=[("pb", hh // 4)])
        kb.op("act", lambda e, sib=sib: e.activation(
            out=sib[:], in_=PSA.ap()[0:C, :], func=AF.Copy, scale=SCALE),
            reads=[("pb", 0), ("pb", 1), ("pb", 2), ("pb", 3)], writes=[("sibuf", par)])
        kb.dma("qs", lambda e, ch=ch, sib=sib: e.dma_start(
            out=sp["si"][ch * C:(ch + 1) * C, :], in_=sib[:]),
            reads=[("sibuf", par)], writes=[("spsi", ch)])


def phase_aagg(self):
    kb, sb, sp = self.kb, self.sb, self.spills
    tl, nch = self.cfg.tloc, self.cfg.nch
    pb = self.pb
    PSA = self.PSA
    BC = 4
    nblk = ceil_div(nch, BC)
    ktm3 = sp["ktm"].rearrange("h t f -> t h f")
    vtm3 = sp["vtm"].rearrange("h t f -> t h f")
    for blk in range(nblk):
        chs = [ch for ch in range(blk * BC, min((blk + 1) * BC, nch))]
        par = blk % 2
        kt, vt = sb["ktch%d" % par], sb["vtch%d" % par]
        uc, ws = sb["ucb%d" % par], sb["wsb%d" % par]
        nbc = len(chs)
        for i, ch in enumerate(chs):
            kb.dma("qs", lambda e, ch=ch, i=i, kt=kt: e.dma_start(
                out=kt[:, i * D:(i + 1) * D], in_=ktm3[ch * C:(ch + 1) * C]),
                reads=[("spktm", h) for h in range(H)], writes=[("ktch", par, i)])
            kb.dma("qs", lambda e, ch=ch, i=i, vt=vt: e.dma_start(
                out=vt[:, i * D:(i + 1) * D], in_=vtm3[ch * C:(ch + 1) * C]),
                reads=[("spvtm", h) for h in range(H)], writes=[("vtch", par, i)])
            kb.dma("qa", lambda e, ch=ch, i=i, ws=ws: e.dma_start(
                out=ws[:, (i % 2) * 0: D] if False else ws[:],
                in_=sp["wsuf"][ch:ch + 1, :].broadcast_to([C, D])),
                reads=[("spwsuf",)], writes=[("wsbb", par)])
            kb.dma("qa", lambda e, ch=ch, i=i, uc=uc: e.dma_start(
                out=uc[:, i * D:(i + 1) * D], in_=sp["u"][ch * C:(ch + 1) * C, :]),
                reads=[("spu", ch)], writes=[("ucb", par, i)])
            kb.op("dve", lambda e, i=i, uc=uc, ws=ws: e.tensor_mul(
                uc[:, i * D:(i + 1) * D], uc[:, i * D:(i + 1) * D], ws[:]),
                reads=[("ucb", par, i), ("wsbb", par)], writes=[("ucb", par, i)])
        for hh in range(H):
            bank = hh % 2
            for i in range(nbc):
                kb.op("pe", lambda e, hh=hh, i=i, kt=kt, uc=uc, bank=bank, nbc=nbc: e.matmul(
                    pb(bank)[:, 0:M],
                    kt[:, i * D + hh * DK: i * D + (hh + 1) * DK],
                    uc[:, i * D + hh * M: i * D + (hh + 1) * M],
                    start=(i == 0), stop=(i == nbc - 1)),
                    reads=[("ktch", par, i), ("ucb", par, i)], writes=[("pb", bank)])
            if blk == 0:
                kb.op("dve", lambda e, hh=hh, bank=bank: e.tensor_copy(
                    sb["aggK"][:, hh * M:(hh + 1) * M], pb(bank)[:, 0:M]),
                    reads=[("pb", bank)], writes=[("aggK", hh)])
            else:
                kb.op("dve", lambda e, hh=hh, bank=bank: e.tensor_add(
                    sb["aggK"][:, hh * M:(hh + 1) * M],
                    sb["aggK"][:, hh * M:(hh + 1) * M], pb(bank)[:, 0:M]),
                    reads=[("pb", bank), ("aggK", hh)], writes=[("aggK", hh)])
            bank2 = 2 + hh % 2
            for i in range(nbc):
                kb.op("pe", lambda e, hh=hh, i=i, uc=uc, vt=vt, bank2=bank2, nbc=nbc: e.matmul(
                    pb(bank2)[:, 0:DK],
                    uc[:, i * D + hh * M: i * D + (hh + 1) * M],
                    vt[:, i * D + hh * DK: i * D + (hh + 1) * DK],
                    start=(i == 0), stop=(i == nbc - 1)),
                    reads=[("ucb", par, i), ("vtch", par, i)], writes=[("pb", bank2)])
            if blk == 0:
                kb.op("dve", lambda e, hh=hh, bank2=bank2: e.tensor_copy(
                    sb["aggV"][:, hh * DK:(hh + 1) * DK], pb(bank2)[:, 0:DK]),
                    reads=[("pb", bank2)], writes=[("aggV", hh)])
            else:
                kb.op("dve", lambda e, hh=hh, bank2=bank2: e.tensor_add(
                    sb["aggV"][:, hh * DK:(hh + 1) * DK],
                    sb["aggV"][:, hh * DK:(hh + 1) * DK], pb(bank2)[:, 0:DK]),
                    reads=[("pb", bank2), ("aggV", hh)], writes=[("aggV", hh)])
    kb.op("dve", lambda e: e.tensor_copy(sb["ccstg"][:], sb["aggK"][:]),
          reads=[("aggK", hh) for hh in range(H)], writes=[("ccstg",)])
    kb.dma("qs", lambda e: e.dma_start(out=sp["ccin"][0:128, :], in_=sb["ccstg"][:]),
           reads=[("ccstg",)], writes=[("spccin", 0)])
    kb.op("dve", lambda e: e.tensor_copy(sb["ccstg"][:], sb["aggV"][:]),
          reads=[("aggV", hh) for hh in range(H)] + [("spccin", 0)], writes=[("ccstg",)])
    kb.dma("qs", lambda e: e.dma_start(out=sp["ccin"][128:256, :], in_=sb["ccstg"][:]),
           reads=[("ccstg",)], writes=[("spccin", 1)])
    for jt in range(4):
        kb.op("pe", lambda e, jt=jt: e.matmul(
            PSA.ap()[0:1, jt * 512:(jt + 1) * 512], sb["ones32"][:],
            sb["cCrows"][:, jt * 512:(jt + 1) * 512], start=True, stop=True),
            reads=[("cCrows", jt), ("sb", "ones32")], writes=[("pb", jt)])
        kb.op("act", lambda e, jt=jt: e.activation(
            out=sb["erow"][:, jt * 512:(jt + 1) * 512],
            in_=PSA.ap()[0:1, jt * 512:(jt + 1) * 512], func=AF.Exp),
            reads=[("pb", jt)], writes=[("erow", jt)])
    kb.dma("qs", lambda e: e.dma_start(out=sp["ccin"][256:257, :], in_=sb["erow"][:]),
           reads=[("erow", jt) for jt in range(4)], writes=[("spccin", 2)])


def phase_gather(self):
    kb, sp = self.kb, self.spills
    if self.cfg.host_gather:
        return
    n = self.cfg.ncores
    sem = self.nc.alloc_semaphore("sem_cc")
    kb.op("pool", lambda e: e.collective_compute(
        "AllGather", ALU.bypass, replica_groups=[list(range(n))],
        ins=[sp["ccin"][:].opt()], outs=[sp["ccout"][:].opt()]),
        reads=[("spccin", 0), ("spccin", 1), ("spccin", 2)],
        writes=[("spccout",)])


Builder.phase_fence = phase_fence
Builder.alloc_ascan_arena = alloc_ascan_arena
Builder.alloc_aagg_arena = alloc_aagg_arena
Builder.phase_ascan = phase_ascan
Builder.phase_aagg = phase_aagg
Builder.phase_gather = phase_gather


def alloc_b_arena(self):
    tl = self.cfg.tloc
    self.arena_reset()
    self.arena("wo", [128, 16 * 128 * 16], BF16)
    for par in (0, 1):
        self.arena("bgch%d" % par, [64, D], BF16)
        self.arena("bqf%d" % par, [128, H * C], BF16)
        self.arena("bkt%d" % par, [64, D], BF16)
        self.arena("bvt%d" % par, [64, D], BF16)
    self.arena("bub0", [64, D], BF16)
    self.arena("bsi0", [64, D], BF16)
    self.arena("bgs0", [128, H * C], BF16)
    self.arena("becb0", [128, D], BF16)
    self.arena("bh0", [128, D], F32)
    self.arena("bofm0", [128, D], BF16)
    self.arena("befull", [64, D], BF16)
    self.arena("bes", [64, D], BF16)
    self.arena("bs0", [64, D], BF16)
    self.arena("bamt", [128, H * C], BF16)
    self.arena("bufm", [128, H * C], BF16)

    self.arena("bosq", [128, H * C], BF16)
    self.arena("brstdb", [128, H * C], BF16)
    self.arena("bo1", [128, H * C], BF16)
    self.arena("br", [128, D], F32)
    self.arena("bx1", [128, D], BF16)
    self.arena("bmx", [64, H], F32)
    self.arena("bsm", [64, H], F32)
    self.arena("brs", [64, H], F32)
    self.arena("brr", [1, H * C], BF16)
    self.arena("bln", [128, 4], F32)
    self.arena("be16", [16, 128], BF16)
    self.sb["bfold"] = self.sb["bx1"]
    self.sb["bytm"] = self.sb["bsi0"]


def phase_bscan(self):
    kb, sb, sp, din = self.kb, self.sb, self.spills, self.din
    tl, nch, cfg = self.cfg.tloc, self.cfg.nch, self.cfg
    pb = self.pb
    PSA = self.PSA
    wo4 = din["Wo"].rearrange("(dc p) f -> p dc f", p=128)
    kb.dma("qp", lambda e: e.dma_start(out=sb["wo"][:], in_=wo4), writes=[("wo",)])
    kb.op("dve", lambda e: e.memset(sb["Sk"][:], 0.0), writes=[("Sk",)])
    kb.op("dve", lambda e: e.memset(sb["Sv"][:], 0.0), writes=[("Sv",)])
    nfold = cfg.ncores - 1
    for j in range(nfold):
        rb_ = 257 * j
        kb.dma("qp", lambda e, rb_=rb_: e.dma_start(
            out=sb["row2"][0:1, :], in_=sp["ccout"][rb_ + 256:rb_ + 257, :]),
            reads=[("spccout",)], writes=[("row2",)])
        kb.dma("qp", lambda e, rb_=rb_: e.dma_start(
            out=sb["be16"][:], in_=sp["ccout"][rb_ + 256:rb_ + 257, :].rearrange(
                "o (h m) -> (o h) m", m=128)),
            reads=[("spccout",)], writes=[("be16",)])
        for jt in range(4):
            kb.op("pe", lambda e, j=j, jt=jt: e.matmul(
                PSA.ap()[:, jt * 512:(jt + 1) * 512],
                sb["mselM"][:, j * 128:(j + 1) * 128],
                sb["row2"][:, jt * 512:(jt + 1) * 512], start=True, stop=True),
                reads=[("row2",), ("sb", "mselM")], writes=[("pb", jt)])
        kb.op("dve", lambda e: e.tensor_mul(sb["Sk"][:], sb["Sk"][:], PSA.ap()[:]),
              reads=[("pb", 0), ("pb", 1), ("pb", 2), ("pb", 3), ("Sk",)],
              writes=[("Sk",)])
        kb.dma("qp", lambda e, rb_=rb_: e.dma_start(
            out=sb["bfold"][:].bitcast(BF16)[:, 0:D], in_=sp["ccout"][rb_:rb_ + 128, :]),
            reads=[("spccout",)], writes=[("bx1",)])
        kb.op("dve", lambda e, j=j: e.scalar_tensor_tensor(
            sb["Sk"][:], sb["bfold"][:].bitcast(BF16)[:, 0:D], sb["mselcol"][:, j:j + 1],
            sb["Sk"][:], op0=ALU.mult, op1=ALU.add),
            reads=[("bx1",), ("Sk",), ("sb", "mselcol")], writes=[("Sk",)])
        kb.op("dve", lambda e, j=j: e.tensor_scalar(
            sb["be16"][:], sb["be16"][:], sb["mselcol"][0:16, j:j + 1],
            sb["m1col"][0:16, j:j + 1], op0=ALU.mult, op1=ALU.add),
            reads=[("be16",), ("sb", "mselcol"), ("sb", "m1col")], writes=[("be16",)])
        kb.op("pe", lambda e: e.transpose(
            pb(6).bitcast(BF16)[:, 0:16], sb["be16"][:], sb["ident"][0:16, 0:16]),
            reads=[("be16",), ("sb", "ident")], writes=[("pb", 6)])
        kb.op("dve", lambda e: e.tensor_copy(sb["ecct"][:], pb(6).bitcast(BF16)[:, 0:16]),
              reads=[("pb", 6)], writes=[("ecct",)])
        for hh in range(H):
            kb.op("act", lambda e, hh=hh: e.activation(
                out=sb["Sv"][:, hh * DK:(hh + 1) * DK],
                in_=sb["Sv"][:, hh * DK:(hh + 1) * DK], func=AF.Copy,
                scale=sb["ecct"][:, hh:hh + 1]),
                reads=[("ecct",), ("Sv",)], writes=[("Sv",)])
        kb.dma("qp", lambda e, rb_=rb_: e.dma_start(
            out=sb["bfold"][:], in_=sp["ccout"][rb_ + 128:rb_ + 256, :]),
            reads=[("spccout",), ("Sk",)], writes=[("bx1",)])
        kb.op("dve", lambda e, j=j: e.scalar_tensor_tensor(
            sb["Sv"][:], sb["bfold"][:].bitcast(BF16)[:, 0:D], sb["mselcol"][:, j:j + 1],
            sb["Sv"][:], op0=ALU.mult, op1=ALU.add),
            reads=[("bx1",), ("Sv",), ("sb", "mselcol")], writes=[("Sv",)])
    q3 = sp["q"].rearrange("h p t -> p h t")
    gs3 = sp["gs"].rearrange("h p t -> p h t")
    ktm3 = sp["ktm"].rearrange("h t f -> t h f")
    vtm3 = sp["vtm"].rearrange("h t f -> t h f")
    for ch in range(nch):
        par = ch % 2
        gch, qf = sb["bgch%d" % par], sb["bqf%d" % par]
        kt, vt = sb["bkt%d" % par], sb["bvt%d" % par]
        ub, sib = sb["bub0"], sb["bsi0"]
        gsf, ecb = sb["bgs0"], sb["becb0"]
        ppar = 0
        hb, ofm = sb["bh0"], sb["bofm0"]
        kb.dma("qs", lambda e, ch=ch, gch=gch: e.dma_start(
            out=gch[:], in_=sp["g"][ch * C:(ch + 1) * C, :]),
            reads=[("spg", ch * C // 128)], writes=[("bgch", par)])
        kb.dma("qs", lambda e, ch=ch, qf=qf: e.dma_start(
            out=qf[:], in_=q3[:, :, ch * C:(ch + 1) * C]),
            reads=[("spq", h) for h in range(H)], writes=[("bqf", par)])
        kb.dma("qs", lambda e, ch=ch, kt=kt: e.dma_start(
            out=kt[:], in_=ktm3[ch * C:(ch + 1) * C]),
            reads=[("spktm", h) for h in range(H)], writes=[("bkt", par)])
        kb.dma("qs", lambda e, ch=ch, vt=vt: e.dma_start(
            out=vt[:], in_=vtm3[ch * C:(ch + 1) * C]),
            reads=[("spvtm", h) for h in range(H)], writes=[("bvt", par)])
        kb.dma("qs", lambda e, ch=ch, ub=ub: e.dma_start(
            out=ub[:], in_=sp["u"][ch * C:(ch + 1) * C, :]),
            reads=[("spu", ch)], writes=[("bub",)])
        kb.dma("qs", lambda e, ch=ch, sib=sib: e.dma_start(
            out=sib[:], in_=sp["si"][ch * C:(ch + 1) * C, :]),
            reads=[("spsi", ch)], writes=[("bsiy",)])
        kb.dma("qa", lambda e, ch=ch, gsf=gsf: e.dma_start(
            out=gsf[:], in_=gs3[:, :, ch * C:(ch + 1) * C]),
            reads=[("spgs", h) for h in range(H)], writes=[("bgs",)])
        kb.dma("qa", lambda e, ch=ch, ecb=ecb: e.dma_start(
            out=ecb[:], in_=sp["ec"][ch:ch + 1, :].broadcast_to([128, D])),
            reads=[("spec",)], writes=[("becb",)])
        kb.dma("qa", lambda e, ch=ch, hb=hb: e.dma_start(
            out=hb[(ch % 2) * 64:(ch % 2) * 64 + 64, :],
            in_=din["h"][ch * C:(ch + 1) * C, :]),
            writes=[("bh", ch % 2)])
        for jt in range(4):
            kb.op("pe", lambda e, jt=jt, gch=gch: e.matmul(
                PSA.ap()[0:C, jt * 512:(jt + 1) * 512], sb["tri"][:],
                gch[:, jt * 512:(jt + 1) * 512], start=True, stop=True),
                reads=[("bgch", par), ("sb", "tri")], writes=[("pb", jt)])
        kb.op("act", lambda e: e.activation(
            out=sb["befull"][:], in_=PSA.ap()[0:C, :], func=AF.Exp),
            reads=[("pb", 0), ("pb", 1), ("pb", 2), ("pb", 3)], writes=[("befull",)])
        e16sl = sb["ec16all"][:, ch * 128:(ch + 1) * 128]
        kb.op("pe", lambda e, e16sl=e16sl: e.transpose(
            pb(6).bitcast(BF16)[:, 0:16], e16sl, sb["ident"][0:16, 0:16]),
            reads=[("ec16all",), ("sb", "ident")], writes=[("pb", 6)])
        kb.op("dve", lambda e: e.tensor_copy(sb["ecct"][:], pb(6).bitcast(BF16)[:, 0:16]),
              reads=[("pb", 6)], writes=[("ecct",)])
        kb.op("dve", lambda e: e.tensor_scalar_mul(sb["Skbf"][:], sb["Sk"][:], SCALE),
              reads=[("Sk",)], writes=[("Skbf",)])
        kb.op("pool", lambda e: e.tensor_copy(sb["Svbf"][:], sb["Sv"][:]),
              reads=[("Sv",)], writes=[("Svbf",)])
        for hh in range(H):
            kb.op("pe", lambda e, hh=hh, qf=qf: e.matmul(
                PSA.ap()[0:C, hh * M:(hh + 1) * M],
                qf[:, hh * C:(hh + 1) * C], sb["Skbf"][:, hh * M:(hh + 1) * M],
                start=True, stop=True),
                reads=[("bqf", par), ("Skbf",)], writes=[("pb", hh // 4)])
        kb.op("dve", lambda e, sib=sib: e.tensor_add(
            sb["bs0"][:], PSA.ap()[0:C, :], sib[:]),
            reads=[("pb", 0), ("pb", 1), ("pb", 2), ("pb", 3), ("bsiy",)],
            writes=[("bs0",)])
        kb.op("dve", lambda e: e.tensor_mul(sb["bs0"][:], sb["bs0"][:], sb["befull"][:]),
              reads=[("bs0",), ("befull",)], writes=[("bs0",)])
        s3 = sb["bs0"][:].rearrange("p (h m) -> p h m", m=M)
        kb.op("dve", lambda e, s3=s3: e.reduce_max(sb["bmx"][:], s3, axis=AX.X),
              reads=[("bs0",)], writes=[("bmx",)])
        kb.op("dve", lambda e, s3=s3: e.tensor_tensor(
            s3, s3, sb["bmx"][:].unsqueeze(2).broadcast_to([C, H, M]), op=ALU.subtract),
            reads=[("bs0",), ("bmx",)], writes=[("bs0",)])
        kb.op("act", lambda e: e.activation(
            out=sb["bes"][:], in_=sb["bs0"][:], func=AF.Exp),
            reads=[("bs0",)], writes=[("bes",)])
        es3 = sb["bes"][:].rearrange("p (h m) -> p h m", m=M)
        kb.op("dve", lambda e, es3=es3: e.reduce_sum(sb["bsm"][:], es3, axis=AX.X),
              reads=[("bes",)], writes=[("bsm",)])
        kb.op("dve", lambda e: e.reciprocal(sb["brs"][:], sb["bsm"][:]),
              reads=[("bsm",)], writes=[("brs",)])
        kb.op("dve", lambda e, es3=es3: e.tensor_tensor(
            es3, es3, sb["brs"][:].unsqueeze(2).broadcast_to([C, H, M]), op=ALU.mult),
            reads=[("bes",), ("brs",)], writes=[("bes",)])
        kb.op("dve", lambda e: e.tensor_mul(sb["bes"][:], sb["bes"][:], sb["befull"][:]),
              reads=[("bes",), ("befull",)], writes=[("bes",)])
        for nm, srcb, dstb in (("amt", "bes", "bamt"), ("ufm", None, "bufm")):
            srct = sb["bes"] if nm == "amt" else ub
            srcreg = ("bes",) if nm == "amt" else ("bub",)
            for hh in range(H):
                kb.op("pe", lambda e, hh=hh, srct=srct: e.transpose(
                    pb(6).bitcast(BF16)[:, hh * C:(hh + 1) * C],
                    srct[:, hh * M:(hh + 1) * M], sb["ident"][0:C, 0:C]),
                    reads=[srcreg, ("sb", "ident")], writes=[("pb", 6)])
            kb.op("act", lambda e, dstb=dstb: e.activation(
                out=sb[dstb][:], in_=pb(6).bitcast(BF16)[:, 0:H * C], func=AF.Copy),
                reads=[("pb", 6) for hh in range(H)], writes=[(dstb,)])
        for hh in range(H):
            bank = 4 + hh // 8
            kb.op("pe", lambda e, hh=hh, bank=bank: e.matmul(
                pb(bank)[0:C, (hh % 8) * C:(hh % 8 + 1) * C],
                sb["bufm"][:, hh * C:(hh + 1) * C], sb["bamt"][:, hh * C:(hh + 1) * C],
                start=True, stop=True),
                reads=[("bufm",), ("bamt",)], writes=[("pb", bank)])
        for b_ in (4, 5):
            kb.op("dve", lambda e, b_=b_: e.tensor_tensor(
                sb["bytm"][:, (b_ - 4) * 512:(b_ - 3) * 512].rearrange(
                    "p (h s) -> p h s", s=C),
                pb(b_)[0:C, :].rearrange("p (h s) -> p h s", s=C),
                sb["tri"][:].unsqueeze(1).broadcast_to([C, 8, C]), op=ALU.mult),
                reads=[("pb", b_), ("sb", "tri")], writes=[("bsiy",)])
        for hh in range(H):
            kb.op("pe", lambda e, hh=hh, vt=vt: e.matmul(
                PSA.ap()[:, hh * C:(hh + 1) * C],
                vt[:, hh * DK:(hh + 1) * DK], sb["bytm"][:, hh * C:(hh + 1) * C],
                start=True, stop=False),
                reads=[("bvt", par), ("bsiy",)], writes=[("pb", hh // 8)])
            kb.op("pe", lambda e, hh=hh: e.matmul(
                PSA.ap()[:, hh * C:(hh + 1) * C],
                sb["Svbf"][:, hh * DK:(hh + 1) * DK], sb["bamt"][:, hh * C:(hh + 1) * C],
                start=False, stop=True),
                reads=[("Svbf",), ("bamt",)], writes=[("pb", hh // 8)])
        kb.op("act", lambda e: e.activation(
            out=sb["bosq"][:], in_=PSA.ap()[:, 0:H * C], func=AF.Square),
            reads=[("pb", 0), ("pb", 1)], writes=[("bosq",)])
        for i2 in range(2):
            kb.op("pe", lambda e, i2=i2: e.matmul(
                PSA.ap()[0:1, 1024 + i2 * 512:1024 + (i2 + 1) * 512],
                sb["ones128"][:], sb["bosq"][:, i2 * 512:(i2 + 1) * 512],
                start=True, stop=True),
                reads=[("bosq",), ("sb", "ones128")], writes=[("pb", 2 + i2)])
        kb.op("act", lambda e: e.activation(
            out=sb["brr"][:], in_=PSA.ap()[0:1, 1024:2048], func=AF.Ln,
            scale=1.0 / 128, bias=EPS),
            reads=[("pb", 2), ("pb", 3)], writes=[("brr",)])
        kb.op("act", lambda e: e.activation(
            out=sb["brr"][:], in_=sb["brr"][:], func=AF.Exp, scale=-0.5),
            reads=[("brr",)], writes=[("brr",)])
        for i2 in range(2):
            kb.op("pe", lambda e, i2=i2: e.matmul(
                PSA.ap()[:, 1024 + i2 * 512:1024 + (i2 + 1) * 512],
                sb["ones1x128"][:], sb["brr"][:, i2 * 512:(i2 + 1) * 512],
                start=True, stop=True),
                reads=[("brr",), ("sb", "ones1x128")], writes=[("pb", 2 + i2)])
        kb.op("act", lambda e: e.activation(
            out=sb["brstdb"][:], in_=PSA.ap()[:, 1024:2048], func=AF.Copy),
            reads=[("pb", 2), ("pb", 3)], writes=[("brstdb",)])
        kb.op("dve", lambda e: e.scalar_tensor_tensor(
            sb["bo1"][:], PSA.ap()[:, 0:H * C], sb["normw"][:], sb["brstdb"][:],
            op0=ALU.mult, op1=ALU.mult),
            reads=[("pb", 0), ("pb", 1), ("brstdb",), ("sb", "normw")],
            writes=[("bo1",)])
        kb.op("pool", lambda e, gsf=gsf, ofm=ofm, ch=ch: e.tensor_tensor(
            ofm[:].rearrange("p (h c t) -> p h c t", c=2, t=C)[:, :, ch % 2, :],
            sb["bo1"][:].rearrange("p (h t) -> p h t", t=C), gsf[:].rearrange(
                "p (h t) -> p h t", t=C),
            op=ALU.mult),
            reads=[("bo1",), ("bgs",)], writes=[("bofm", ch % 2)])
        for hh in range(H):
            bank = 4 + hh // 4
            kb.op("pe", lambda e, hh=hh, kt=kt, ub=ub, bank=bank: e.matmul(
                pb(bank)[:, (hh % 4) * M:(hh % 4 + 1) * M],
                kt[:, hh * DK:(hh + 1) * DK], ub[:, hh * M:(hh + 1) * M],
                start=True, stop=True),
                reads=[("bkt", par), ("bub",)], writes=[("pb", bank)])
        for b_ in range(4, 8):
            kb.op("dve", lambda e, b_=b_: e.tensor_add(
                sb["Sk"][:, (b_ - 4) * 512:(b_ - 3) * 512],
                sb["Sk"][:, (b_ - 4) * 512:(b_ - 3) * 512], pb(b_)[:]),
                reads=[("pb", b_), ("Sk",), ("Skbf",)], writes=[("Sk",)])
        kb.op("dve", lambda e, ecb=ecb: e.tensor_mul(sb["Sk"][:], sb["Sk"][:], ecb[:]),
              reads=[("Sk",), ("becb",)], writes=[("Sk",)])
        for hh in range(H):
            kb.op("pe", lambda e, hh=hh, ub=ub, vt=vt: e.matmul(
                PSA.ap()[:, hh * DK:(hh + 1) * DK],
                ub[:, hh * M:(hh + 1) * M], vt[:, hh * DK:(hh + 1) * DK],
                start=True, stop=True),
                reads=[("bub",), ("bvt", par), ("bo1",)], writes=[("pb", hh // 4)])
        for b_ in range(4):
            kb.op("dve", lambda e, b_=b_: e.tensor_add(
                sb["Sv"][:, b_ * 512:(b_ + 1) * 512],
                sb["Sv"][:, b_ * 512:(b_ + 1) * 512], PSA.ap()[:, b_ * 512:(b_ + 1) * 512]),
                reads=[("pb", b_), ("Sv",), ("Svbf",)], writes=[("Sv",)])
        for hh in range(H):
            kb.op("act", lambda e, hh=hh: e.activation(
                out=sb["Sv"][:, hh * DK:(hh + 1) * DK],
                in_=sb["Sv"][:, hh * DK:(hh + 1) * DK], func=AF.Copy,
                scale=sb["ecct"][:, hh:hh + 1]),
                reads=[("ecct",), ("Sv",)], writes=[("Sv",)])
        if ch % 2 == 1:
            for jt in range(4):
                for dc in range(16):
                    lhs = sb["bofm%d" % ppar][:, dc * 128:(dc + 1) * 128]
                    kb.op("pe", lambda e, jt=jt, dc=dc, lhs=lhs: e.matmul(
                        PSA.ap()[:, jt * 512:(jt + 1) * 512],
                        lhs, sb["wo"][:, dc * D + jt * 512: dc * D + (jt + 1) * 512],
                        start=(dc == 0), stop=(dc == 15)),
                        reads=[("bofm", 0), ("bofm", 1), ("wo",)],
                        writes=[("pb", jt)])
            kb.op("dve", lambda e, ppar=ppar: e.tensor_add(
                sb["br"][:], PSA.ap()[:], sb["bh%d" % ppar][:]),
                reads=[("pb", 0), ("pb", 1), ("pb", 2), ("pb", 3),
                       ("bh", 0), ("bh", 1)],
                writes=[("br",)])
            kb.op("dve", lambda e: e.reduce_sum(
                sb["bln"][:, 0:1], sb["br"][:].unsqueeze(1), axis=AX.X),
                reads=[("br",)], writes=[("bln", 0)])
            kb.op("act", lambda e: e.activation(
                out=PSA.ap()[:], in_=sb["br"][:], func=AF.Square),
                reads=[("br",), ("pb", 0), ("pb", 1), ("pb", 2), ("pb", 3)],
                writes=[("pb", 0), ("pb", 1), ("pb", 2), ("pb", 3)])
            kb.op("dve", lambda e: e.reduce_sum(
                sb["bln"][:, 1:2], PSA.ap()[:].unsqueeze(1), axis=AX.X),
                reads=[("pb", 0), ("pb", 1), ("pb", 2), ("pb", 3)], writes=[("bln", 1)])
            kb.op("dve", lambda e: e.tensor_scalar(
                sb["bln"][:, 0:1], sb["bln"][:, 0:1], 1.0 / D, None, op0=ALU.mult),
                reads=[("bln", 0)], writes=[("bln", 0)])
            kb.op("dve", lambda e: e.tensor_scalar(
                sb["bln"][:, 1:2], sb["bln"][:, 1:2], 1.0 / D, None, op0=ALU.mult),
                reads=[("bln", 1)], writes=[("bln", 1)])
            kb.op("dve", lambda e: e.tensor_mul(
                sb["bln"][:, 2:3], sb["bln"][:, 0:1], sb["bln"][:, 0:1]),
                reads=[("bln", 0)], writes=[("bln", 2)])
            kb.op("dve", lambda e: e.tensor_sub(
                sb["bln"][:, 1:2], sb["bln"][:, 1:2], sb["bln"][:, 2:3]),
                reads=[("bln", 1), ("bln", 2)], writes=[("bln", 1)])
            kb.op("act", lambda e: e.activation(
                out=sb["bln"][:, 1:2], in_=sb["bln"][:, 1:2], func=AF.Ln, bias=LNEPS),
                reads=[("bln", 1)], writes=[("bln", 1)])
            kb.op("act", lambda e: e.activation(
                out=sb["bln"][:, 1:2], in_=sb["bln"][:, 1:2], func=AF.Exp, scale=-0.5),
                reads=[("bln", 1)], writes=[("bln", 1)])
            kb.op("dve", lambda e: e.tensor_scalar(
                sb["bln"][:, 2:3], sb["bln"][:, 0:1], -1.0, None, op0=ALU.mult),
                reads=[("bln", 0)], writes=[("bln", 2)])
            kb.op("dve", lambda e: e.tensor_scalar(
                sb["bx1"][:], sb["br"][:], sb["bln"][:, 2:3], sb["bln"][:, 1:2],
                op0=ALU.add, op1=ALU.mult),
                reads=[("br",), ("bln", 1), ("bln", 2)], writes=[("bx1",)])
            kb.dma("qa", lambda e, ch=ch: e.dma_start(
                out=sp["x1"][(ch - 1) * C:(ch + 1) * C, :], in_=sb["bx1"][:]),
                reads=[("bx1",)], writes=[("spx1", ch // 2)])


Builder.alloc_b_arena = alloc_b_arena
Builder.phase_bscan = phase_bscan


def alloc_f1_arena(self):
    tl = self.cfg.tloc
    self.arena_reset()
    self.arena("x1fm", [128, 16 * tl], BF16)
    for par in (0, 1):
        self.arena("wgb%d" % par, [128, 16 * 128], BF16)
        self.arena("wub%d" % par, [128, 16 * 128], BF16)
        self.arena("xt%d" % par, [128, D], BF16)
    self.arena("gact", [128, tl], BF16)
    self.arena("uact", [128, tl], BF16)
    self.arena("gm0", [128, tl], BF16)
    self.arena("gm1", [128, tl], BF16)


def phase_f1(self):
    kb, sb, sp, din = self.kb, self.sb, self.spills, self.din
    tl, ntb = self.cfg.tloc, self.cfg.ntb
    pb = self.pb
    PSA = self.PSA
    njt = ceil_div(tl, 512)
    for tb in range(ntb):
        par = tb % 2
        xt = sb["xt%d" % par]
        kb.dma("qs", lambda e, tb=tb, xt=xt: e.dma_start(
            out=xt[:], in_=sp["x1"][tb * 128:(tb + 1) * 128, :]),
            reads=[("spx1", tb // 1)], writes=[("xt", par)])
        for half in range(2):
            for dc8 in range(8):
                dc = half * 8 + dc8
                kb.op("pe", lambda e, dc=dc, dc8=dc8, xt=xt: e.transpose(
                    pb(6).bitcast(BF16)[:, dc8 * 128:(dc8 + 1) * 128],
                    xt[:, dc * 128:(dc + 1) * 128], sb["ident"][:]),
                    reads=[("xt", par), ("sb", "ident")], writes=[("pb", 6)])
            kb.op("act", lambda e, tb=tb, half=half: e.activation(
                out=sb["x1fm"][:].rearrange("p (dc t) -> p dc t", t=tl)[
                    :, half * 8:(half + 1) * 8, tb * 128:(tb + 1) * 128],
                in_=pb(6).bitcast(BF16)[:, 0:1024].rearrange(
                    "p (dc t) -> p dc t", t=128),
                func=AF.Copy),
                reads=[("pb", 6) for i in range(8)], writes=[("x1fm", tb, half)])
    x1fm_rd = [("x1fm", tb, half) for tb in range(ntb) for half in range(2)]
    wg4 = din["WgT8"].rearrange("(dc p) f -> p dc f", p=128)
    wu4 = din["WuT8"].rearrange("(dc p) f -> p dc f", p=128)
    for hc in range(NHH):
        par = hc % 2
        wgb, wub = sb["wgb%d" % par], sb["wub%d" % par]
        kb.dma("qp", lambda e, hc=hc, wgb=wgb: e.dma_start(
            out=wgb[:], in_=wg4[:, :, hc * 128:(hc + 1) * 128]),
            writes=[("wgb", par)])
        kb.op("pool", lambda e, wgb=wgb: e.tensor_tensor(
            wgb[:], wgb[:], sb["c128"][:].broadcast_to([128, 16 * 128]), op=ALU.subtract),
            reads=[("wgb", par), ("sb", "c128")], writes=[("wgb", par)])
        kb.dma("qp", lambda e, hc=hc, wub=wub: e.dma_start(
            out=wub[:], in_=wu4[:, :, hc * 128:(hc + 1) * 128]),
            writes=[("wub", par)])
        kb.op("pool", lambda e, wub=wub: e.tensor_tensor(
            wub[:], wub[:], sb["c128"][:].broadcast_to([128, 16 * 128]), op=ALU.subtract),
            reads=[("wub", par), ("sb", "c128")], writes=[("wub", par)])
        for jt in range(njt):
            t0, tn = jt * 512, min(512, tl - jt * 512)
            for dc in range(16):
                kb.op("pe", lambda e, dc=dc, jt=jt, t0=t0, tn=tn, wgb=wgb: e.matmul(
                    PSA.ap()[:, jt * 512:jt * 512 + tn],
                    wgb[:, dc * 128:(dc + 1) * 128],
                    sb["x1fm"][:, dc * tl + t0: dc * tl + t0 + tn],
                    start=(dc == 0), stop=(dc == 15)),
                    reads=[("wgb", par)] + x1fm_rd, writes=[("pb", jt)])
            for dc in range(16):
                kb.op("pe", lambda e, dc=dc, jt=jt, t0=t0, tn=tn, wub=wub: e.matmul(
                    pb(4 + jt)[:, 0:tn],
                    wub[:, dc * 128:(dc + 1) * 128],
                    sb["x1fm"][:, dc * tl + t0: dc * tl + t0 + tn],
                    start=(dc == 0), stop=(dc == 15)),
                    reads=[("wub", par)] + x1fm_rd, writes=[("pb", 4 + jt)])
            kb.op("act", lambda e, hc=hc, jt=jt, t0=t0, tn=tn: e.activation(
                out=sb["gact"][:, t0:t0 + tn], in_=PSA.ap()[:, jt * 512:jt * 512 + tn],
                func=AF.Silu, scale=sb["sgm"][:, hc:hc + 1]),
                reads=[("pb", jt), ("sb", "sgm")], writes=[("gact", jt)])
            kb.op("act", lambda e, hc=hc, jt=jt, t0=t0, tn=tn: e.activation(
                out=sb["uact"][:, t0:t0 + tn], in_=pb(4 + jt)[:, 0:tn],
                func=AF.Copy, scale=sb["sum_"][:, hc:hc + 1]),
                reads=[("pb", 4 + jt), ("sb", "sum_")], writes=[("uact", jt)])
        gm = sb["gm%d" % par]
        kb.op("dve", lambda e, gm=gm: e.tensor_mul(gm[:], sb["gact"][:], sb["uact"][:]),
              reads=[("gact", jt) for jt in range(njt)] +
                    [("uact", jt) for jt in range(njt)],
              writes=[("gm", par)])
        kb.dma("qs", lambda e, hc=hc, gm=gm: e.dma_start(out=sp["G"][hc], in_=gm[:]),
               reads=[("gm", par)], writes=[("spG", hc)])


def alloc_f2_arena(self):
    tl, ntb = self.cfg.tloc, self.cfg.ntb
    self.arena_reset()
    self.arena("wdh", [128, NHH * 512], BF16)
    self.arena("gtb", [128, NHH * 128], BF16)
    self.arena("x1tb0", [128, 512], BF16)
    self.arena("x1tb1", [128, 512], BF16)
    self.arena("r2q0", [128, ntb * 512], BF16)
    self.arena("r2q1", [128, ntb * 512], BF16)
    self.arena("r2q2", [128, ntb * 512], BF16)
    self.arena("r2c", [128, 512], BF16)
    self.arena("outstg0", [128, D], F32)
    self.arena("fln", [128, 12], F32)
    self.arena("sdb", [128, 512], BF16)
    self.arena("sdrow", [1, D], BF16)


def phase_f2(self):
    kb, sb, sp, din = self.kb, self.sb, self.spills, self.din
    tl, ntb = self.cfg.tloc, self.cfg.ntb
    pb = self.pb
    PSA = self.PSA
    wd4 = din["WdT8"].rearrange("(hc p) f -> p hc f", p=128)
    g3 = sp["G"]
    kb.dma("qp", lambda e: e.dma_start(out=sb["sdrow"][:], in_=din["sdr"][:]),
           writes=[("sdrow",)])
    for q in range(4):
        kb.op("pe", lambda e, q=q: e.matmul(
            PSA.ap()[:, 1536:2048], sb["ones1x128"][:],
            sb["sdrow"][:, q * 512:(q + 1) * 512], start=True, stop=True),
            reads=[("sdrow",), ("sb", "ones1x128")], writes=[("pb", 3)])
        kb.op("act", lambda e: e.activation(
            out=sb["sdb"][:], in_=PSA.ap()[:, 1536:2048], func=AF.Copy),
            reads=[("pb", 3)], writes=[("sdb",)])
        kb.dma("qp", lambda e, q=q: e.dma_start(
            out=sb["wdh"][:], in_=wd4[:, :, q * 512:(q + 1) * 512]),
            writes=[("wdh",)])
        for hc in range(NHH):
            kb.op("pool", lambda e, hc=hc: e.tensor_tensor(
                sb["wdh"][:, hc * 512:(hc + 1) * 512],
                sb["wdh"][:, hc * 512:(hc + 1) * 512],
                sb["c128"][:].broadcast_to([128, 512]), op=ALU.subtract),
                reads=[("wdh",), ("sb", "c128")], writes=[("wdh",)])
            kb.op("pool", lambda e, hc=hc: e.tensor_tensor(
                sb["wdh"][:, hc * 512:(hc + 1) * 512],
                sb["wdh"][:, hc * 512:(hc + 1) * 512], sb["sdb"][:], op=ALU.mult),
                reads=[("wdh",), ("sdb",)], writes=[("wdh",)])
        for tb in range(ntb):
            par = tb % 2
            xt = sb["x1tb%d" % par]
            kb.dma("qs", lambda e, tb=tb: e.dma_start(
                out=sb["gtb"][:], in_=g3[:, :, tb * 128:(tb + 1) * 128].rearrange(
                    "hc p t -> p hc t")),
                reads=[("spG", hc) for hc in range(NHH)], writes=[("gtb",)])
            kb.dma("qa", lambda e, tb=tb, xt=xt, q=q: e.dma_start(
                out=xt[:], in_=sp["x1"][tb * 128:(tb + 1) * 128, q * 512:(q + 1) * 512]),
                reads=[("spx1", tb)], writes=[("x1tb", par)])
            bk = tb % 2
            for hc in range(NHH):
                kb.op("pe", lambda e, hc=hc, bk=bk: e.matmul(
                    PSA.ap()[:, bk * 512:(bk + 1) * 512],
                    sb["gtb"][:, hc * 128:(hc + 1) * 128],
                    sb["wdh"][:, hc * 512:(hc + 1) * 512],
                    start=(hc == 0), stop=(hc == NHH - 1)),
                    reads=[("gtb",), ("wdh",)], writes=[("pb", bk)])
            if q < 3:
                kb.op("dve", lambda e, tb=tb, bk=bk, xt=xt, q=q: e.tensor_add(
                    sb["r2q%d" % q][:, tb * 512:(tb + 1) * 512],
                    PSA.ap()[:, bk * 512:(bk + 1) * 512], xt[:]),
                    reads=[("pb", bk), ("x1tb", par)], writes=[("r2q", q, tb)])
            else:
                kb.op("dve", lambda e, tb=tb, bk=bk, xt=xt: e.tensor_add(
                    sb["r2c"][:], PSA.ap()[:, bk * 512:(bk + 1) * 512], xt[:]),
                    reads=[("pb", bk), ("x1tb", par)], writes=[("r2c",)])
                parts = [sb["r2q0"][:, tb * 512:(tb + 1) * 512],
                         sb["r2q1"][:, tb * 512:(tb + 1) * 512],
                         sb["r2q2"][:, tb * 512:(tb + 1) * 512],
                         sb["r2c"][:]]
                rregs = [("r2q", 0, tb), ("r2q", 1, tb), ("r2q", 2, tb), ("r2c",)]
                for i4, (p4, rr) in enumerate(zip(parts, rregs)):
                    kb.op("dve", lambda e, p4=p4, i4=i4: e.reduce_sum(
                        sb["fln"][:, i4:i4 + 1], p4.unsqueeze(1), axis=AX.X),
                        reads=[rr], writes=[("fln", i4)])
                    kb.op("act", lambda e, p4=p4, i4=i4: e.activation(
                        out=pb(4 + i4)[:], in_=p4, func=AF.Square),
                        reads=[rr, ("pb", 4 + i4)], writes=[("pb", 4 + i4)])
                    kb.op("dve", lambda e, i4=i4: e.reduce_sum(
                        sb["fln"][:, 4 + i4:5 + i4], pb(4 + i4)[:].unsqueeze(1), axis=AX.X),
                        reads=[("pb", 4 + i4)], writes=[("fln", 4 + i4)])
                for i4 in (1, 2, 3):
                    kb.op("dve", lambda e, i4=i4: e.tensor_add(
                        sb["fln"][:, 0:1], sb["fln"][:, 0:1], sb["fln"][:, i4:i4 + 1]),
                        reads=[("fln", 0), ("fln", i4)], writes=[("fln", 0)])
                    kb.op("dve", lambda e, i4=i4: e.tensor_add(
                        sb["fln"][:, 4:5], sb["fln"][:, 4:5], sb["fln"][:, 4 + i4:5 + i4]),
                        reads=[("fln", 4), ("fln", 4 + i4)], writes=[("fln", 4)])
                kb.op("dve", lambda e: e.tensor_scalar(
                    sb["fln"][:, 0:1], sb["fln"][:, 0:1], 1.0 / D, None, op0=ALU.mult),
                    reads=[("fln", 0)], writes=[("fln", 0)])
                kb.op("dve", lambda e: e.tensor_scalar(
                    sb["fln"][:, 4:5], sb["fln"][:, 4:5], 1.0 / D, None, op0=ALU.mult),
                    reads=[("fln", 4)], writes=[("fln", 4)])
                kb.op("dve", lambda e: e.tensor_mul(
                    sb["fln"][:, 8:9], sb["fln"][:, 0:1], sb["fln"][:, 0:1]),
                    reads=[("fln", 0)], writes=[("fln", 8)])
                kb.op("dve", lambda e: e.tensor_sub(
                    sb["fln"][:, 4:5], sb["fln"][:, 4:5], sb["fln"][:, 8:9]),
                    reads=[("fln", 4), ("fln", 8)], writes=[("fln", 4)])
                kb.op("act", lambda e: e.activation(
                    out=sb["fln"][:, 4:5], in_=sb["fln"][:, 4:5], func=AF.Ln, bias=LNEPS),
                    reads=[("fln", 4)], writes=[("fln", 4)])
                kb.op("act", lambda e: e.activation(
                    out=sb["fln"][:, 4:5], in_=sb["fln"][:, 4:5], func=AF.Exp, scale=-0.5),
                    reads=[("fln", 4)], writes=[("fln", 4)])
                kb.op("dve", lambda e: e.tensor_scalar(
                    sb["fln"][:, 8:9], sb["fln"][:, 0:1], -1.0, None, op0=ALU.mult),
                    reads=[("fln", 0)], writes=[("fln", 8)])
                for i4, (p4, rr) in enumerate(zip(parts, rregs)):
                    kb.op("dve", lambda e, p4=p4, i4=i4: e.tensor_scalar(
                        sb["outstg0"][:, i4 * 512:(i4 + 1) * 512], p4,
                        sb["fln"][:, 8:9], sb["fln"][:, 4:5],
                        op0=ALU.add, op1=ALU.mult),
                        reads=[rr, ("fln", 4), ("fln", 8)], writes=[("ost", i4)])
                kb.dma("qa", lambda e, tb=tb: e.dma_start(
                    out=self.out_t[tb * 128:(tb + 1) * 128, :], in_=sb["outstg0"][:]),
                    reads=[("ost", 0), ("ost", 1), ("ost", 2), ("ost", 3)],
                    writes=[("outd", tb)])


Builder.alloc_f1_arena = alloc_f1_arena
Builder.phase_f1 = phase_f1
Builder.alloc_f2_arena = alloc_f2_arena
Builder.phase_f2 = phase_f2


def kernel(**inputs):
    cfg = Cfg(tloc=T // NCORES, ncores=NCORES)
    in_maps = host_prep(inputs, cfg)
    b = build_program(cfg)
    res = run_bass_kernel_spmd(b.nc, in_maps, core_ids=list(range(NCORES)))
    out = np.concatenate([np.asarray(res.results[ci]["out"]) for ci in range(NCORES)],
                         axis=0)
    return out.reshape(1, T, D).astype(np.float32)


# version 09
# speedup vs baseline: 1.1109x; 1.0036x over previous
import numpy as np
import ml_dtypes
import concourse.bass as bass
import concourse.mybir as mybir
from concourse.bass_utils import run_bass_kernel_spmd

F32 = mybir.dt.float32
BF16 = mybir.dt.bfloat16
I32 = mybir.dt.int32
U8 = mybir.dt.uint8
AF = mybir.ActivationFunctionType
ALU = mybir.AluOpType
AX = mybir.AxisListType
BF = ml_dtypes.bfloat16

NCORES = 8
T = 16384
D = 2048
H = 16
DK = 128
M = 128
KW = 4
C = 64
HB = 64
HID = 5632
NHH = HID // 128
SCALE = DK ** -0.5
EPS = 1e-5
L2EPS = 1e-6
LNEPS = 1e-5

ENGINES = ("pe", "act", "dve", "pool", "sp")
PSUM_COARSE = False
QUEUES = ("qs", "qa", "qp")


class KB:

    def __init__(self, nc):
        self.nc = nc
        self.q = {e: [] for e in ENGINES}
        self.cnt = {e: 0 for e in ENGINES}
        self.dcnt = {qn: 0 for qn in QUEUES}
        self.wm = {e: {} for e in ENGINES}
        self.lastw = {}
        self.readers = {}
        self.sems = {}
        for s in ("pe", "act", "dve", "pool", "qs", "qa", "qp"):
            self.sems[s] = nc.alloc_semaphore("sem_" + s)
        self.dma_engine = {"qs": "sp", "qa": "act", "qp": "pool"}
        self.fence = {qn: 0 for qn in QUEUES}

    def _sem_val(self, tok):
        s, v = tok
        if s in QUEUES:
            return (s, 16 * self._fence_for(s, v))
        return (s, v)

    def _fence_for(self, qn, k):
        if self.fence[qn] >= k:
            return self.fence[qn]
        n = self.dcnt[qn]
        eng = self.dma_engine[qn]
        sems = self.sems
        self.q[eng].append(lambda e, qn=qn, n=n: e.wait_ge(sems[qn], 16 * n))
        self.wm[eng][qn] = 16 * n
        self.fence[qn] = n
        return n

    def _edges(self, eng, reads, writes):
        need = {}
        for r in reads:
            t = self.lastw.get(r)
            if t is not None:
                s, v = self._sem_val(t)
                need[s] = max(need.get(s, 0), v)
        for w in writes:
            t = self.lastw.get(w)
            if t is not None:
                s, v = self._sem_val(t)
                need[s] = max(need.get(s, 0), v)
            for t in self.readers.get(w, ()):
                s, v = self._sem_val(t)
                need[s] = max(need.get(s, 0), v)
        wmap = self.wm[eng]
        out = []
        for s, v in need.items():
            if wmap.get(s, 0) < v:
                out.append((s, v))
                wmap[s] = v
        return out

    def op(self, eng, fn, reads=(), writes=(), sem_extra=None):
        reads = list(reads)
        writes = list(writes)
        for r in list(reads):
            if isinstance(r[0], str) and r[0].startswith("pb"):
                reads.remove(r)
                if r not in writes:
                    writes.append(r)
        if PSUM_COARSE:
            nw = []
            coarse = False
            for w in writes:
                if isinstance(w[0], str) and w[0].startswith("pb"):
                    coarse = True
                else:
                    nw.append(w)
            if coarse:
                nw.append(("PSUM",))
            writes = nw
        waits = self._edges(eng, reads, writes)
        self.cnt[eng] += 1
        tok = (eng, self.cnt[eng])
        sem = self.sems[eng]
        sems = self.sems

        def emit(e, waits=waits, fn=fn, sem=sem, sem_extra=sem_extra):
            for s, v in waits[1:]:
                e.wait_ge(sems[s], v)
            inst = fn(e)
            if waits:
                inst._wait_ge(sems[waits[0][0]], waits[0][1])
            inst.then_inc(sem, 1)
            if sem_extra is not None:
                inst.then_inc(sem_extra[0], sem_extra[1])
        self.q[eng].append(emit)
        for w in writes:
            self.lastw[w] = tok
            self.readers[w] = []
        for r in reads:
            self.readers.setdefault(r, []).append(tok)
        return tok

    def dma(self, qn, fn, reads=(), writes=()):
        eng = self.dma_engine[qn]
        waits = self._edges(eng, reads, writes)
        self.dcnt[qn] += 1
        tok = (qn, self.dcnt[qn])
        sem = self.sems[qn]
        sems = self.sems

        def emit(e, waits=waits, fn=fn, sem=sem):
            for s, v in waits[1:]:
                e.wait_ge(sems[s], v)
            inst = fn(e)
            if waits:
                inst._wait_ge(sems[waits[0][0]], waits[0][1])
            inst.then_inc(sem, 16)
        self.q[eng].append(emit)
        for w in writes:
            self.lastw[w] = tok
            self.readers[w] = []
        for r in reads:
            self.readers.setdefault(r, []).append(tok)
        return tok

    def barrier_wait(self, eng, toks):
        need = {}
        for t in toks:
            s, v = self._sem_val(t)
            need[s] = max(need.get(s, 0), v)
        wmap = self.wm[eng]
        sems = self.sems
        for s, v in need.items():
            if wmap.get(s, 0) < v:
                wmap[s] = v
                self.q[eng].append(lambda e, s=s, v=v: e.wait_ge(sems[s], v))

    def finish(self):
        toks = [(e, self.cnt[e]) for e in ENGINES if e != "sp" and self.cnt[e]]
        toks += [(qn, self.dcnt[qn]) for qn in QUEUES if self.dcnt[qn]]
        for e in ENGINES:
            self.barrier_wait(e, toks)

    def replay(self):
        nc = self.nc
        with nc.Block() as block:
            @block.tensor
            def _(e):
                for f in self.q["pe"]:
                    f(e)

            @block.scalar
            def _(e):
                for f in self.q["act"]:
                    f(e)

            @block.vector
            def _(e):
                for f in self.q["dve"]:
                    f(e)

            @block.gpsimd
            def _(e):
                for f in self.q["pool"]:
                    f(e)

            @block.sync
            def _(e):
                for f in self.q["sp"]:
                    f(e)


def ceil_div(a, b):
    return (a + b - 1) // b



class Cfg:

    def __init__(self, tloc=2048, phases=("aproj", "az", "amisc", "ascan", "aagg",
                                          "gather", "bscan", "f1", "f2"),
                 ncores=NCORES, debug_taps=(), host_gather=False):
        self.tloc = tloc
        self.nch = tloc // C
        self.ntb = tloc // 128
        self.phases = phases
        self.ncores = ncores
        self.debug_taps = debug_taps
        self.host_gather = host_gather


class Builder:
    def __init__(self, cfg):
        self.cfg = cfg
        self.nc = bass.Bass("TRN2", target_bir_lowering=False)
        self.kb = KB(self.nc)
        self.din = {}
        self.spills = {}
        self.sb = {}
        self.ps = {}
        self.ar_off = 0
        self._declare_dram()
        self._declare_sbuf()

    def inp(self, name, shape, dtype=F32):
        self.din[name] = self.nc.dram_tensor(name, list(shape), dtype,
                                             kind="ExternalInput").ap()
        return self.din[name]

    def _declare_dram(self):
        cfg, tl, nch = self.cfg, self.cfg.tloc, self.cfg.nch
        inp = self.inp
        inp("hT", [D, HB + tl])
        inp("h", [tl, D])
        for w in ("Wq", "Wk", "Wv", "Ww", "Wo"):
            inp(w, [D, D])
        inp("cqr", [128, 16 * KW]); inp("ckr", [128, 16 * KW]); inp("cvr", [128, 16 * KW])
        inp("Wf1", [D, 128]); inp("Wf2", [128, D]); inp("Wbr", [128, 16 * H])
        inp("Wg1", [D, 128]); inp("Wg2", [128, D])
        inp("bg2m", [128, 16]); inp("normw", [128, 1])
        inp("WgT8", [NHH * 128, D], U8); inp("WuT8", [NHH * 128, D], U8)
        inp("WdT8", [4 * 128, NHH * 512], U8)
        inp("sgm", [128, NHH]); inp("sum_", [128, NHH]); inp("sdr", [1, D])
        inp("ident", [128, 128], BF16)
        inp("identf", [128, 128])
        inp("tri", [C, C], BF16)
        inp("sumsels", [128, 16 * 32], BF16)
        inp("trige", [32, 32])
        inp("ones32", [32, 1])
        inp("ones1x128", [1, 128], BF16)
        inp("ones128", [128, 1], BF16)
        inp("mselM", [2, (NCORES - 1) * 128], BF16)
        inp("mselcol", [128, NCORES - 1])
        inp("m1col", [128, NCORES - 1])
        self.out_t = self.nc.dram_tensor("out", [tl, D], F32, kind="ExternalOutput").ap()
        sp = self.spills

        def spill(name, shape, dtype=BF16, kind="Internal"):
            sp[name] = self.nc.dram_tensor("sp_" + name, list(shape), dtype, kind=kind).ap()
        spill("q", [H, 128, tl])
        spill("kfm", [H, 128, tl])
        spill("ktm", [H, tl, 128])
        spill("vtm", [H, tl, 128])
        spill("wtm", [H, tl, 128])
        spill("g", [tl, D])
        spill("gs", [H, 128, tl])
        spill("u", [tl, D])
        spill("si", [tl, D])
        spill("x1", [tl, D])
        spill("G", [NHH, 128, tl])
        spill("ec", [nch, D])
        spill("wsuf", [nch, D])
        kci = "ExternalOutput" if cfg.host_gather else "Internal"
        kco = "ExternalInput" if cfg.host_gather else "Internal"
        spill("ccin", [257, D], BF16, kind=kci)
        spill("ccout", [cfg.ncores * 257, D], BF16, kind=kco)
        self.dbg = {}
        for name in cfg.debug_taps:
            s = sp[name]
            self.dbg[name] = self.nc.dram_tensor(
                "dbg_" + name, list(s.shape), s.dtype, kind="ExternalOutput").ap()

    def sbuf(self, name, shape, dtype=BF16):
        t = self.nc.alloc_sbuf_tensor("sb_" + name, list(shape), dtype)
        self.sb[name] = t
        return t

    def arena(self, name, shape, dtype):
        esz = {F32: 4, BF16: 2, U8: 1, I32: 4}[dtype]
        nbytes = shape[1] * esz
        off = self.ar_off
        assert off + nbytes <= self.AR_BYTES, f"arena overflow {name}: {off}+{nbytes}"
        ap = self.AR.ap().bitcast(dtype)[0:shape[0], off // esz: off // esz + shape[1]]
        self.ar_off = off + ((nbytes + 31) // 32) * 32
        self.sb[name] = ap
        return ap

    def arena_reset(self):
        self.ar_off = 0

    def _declare_sbuf(self):
        cfg, tl, nch = self.cfg, self.cfg.tloc, self.cfg.nch
        self.AR_BYTES = 156 * 1024
        self.AR = self.nc.alloc_sbuf_tensor("sb_AR", [128, self.AR_BYTES], U8)
        sbuf = self.sbuf
        sbuf("ident", [128, 128], BF16)
        sbuf("m1col", [128, NCORES - 1], F32)
        sbuf("tri", [C, C], BF16)
        sbuf("ones1x128", [1, 128], BF16)
        sbuf("ones128", [128, 1], BF16)
        sbuf("beta", [128, (tl // 128) * H], BF16)
        sbuf("beta2", [64, 2 * (tl // 128) * H], BF16)
        sbuf("cCrows", [32, D], F32)
        sbuf("ec16all", [16, nch * 128], BF16)
        sbuf("Sk", [128, H * M], F32)
        sbuf("Sv", [128, H * DK], F32)
        sbuf("Skbf", [128, H * M], BF16)
        sbuf("Svbf", [128, H * DK], BF16)
        sbuf("ecct", [128, 16], F32)
        sbuf("normw", [128, 1], F32)
        sbuf("bg2m", [128, 16], F32)
        sbuf("cw", [128, 3 * 16 * KW], F32)
        sbuf("sumsels", [128, 16 * 32], BF16)
        sbuf("trige", [32, 32], F32)
        sbuf("ones32", [32, 1], F32)
        sbuf("sgm", [128, NHH], F32)
        sbuf("sum_", [128, NHH], F32)
        sbuf("mselM", [2, (NCORES - 1) * 128], BF16)
        sbuf("mselcol", [128, NCORES - 1], F32)
        sbuf("row2", [2, D], BF16)
        self.CVALS = [L2EPS, LNEPS, EPS, SCALE, -0.5, -1.0, 0.5, 2.0, 1.0 / 128, -128.0,
                      1.0 / D, 1.0]
        sbuf("cvals", [128, len(self.CVALS)], F32)
        sbuf("c128", [128, 1], BF16)
        self.PSA = self.nc.alloc_psum_tensor("psA", [128, 2048], F32)
        self.PBH = {i: self.nc.alloc_psum_tensor("ps%d" % i, [128, 512], F32)
                    for i in range(4, 8)}

    def pb(self, i):
        if i < 4:
            return self.PSA.ap()[:, i * 512:(i + 1) * 512]
        return self.PBH[i].ap()

    def ld_consts(self):
        kb, din, sb = self.kb, self.din, self.sb
        pairs = [("ident", "ident"), ("tri", "tri"), ("ones1x128", "ones1x128"),
                 ("ones128", "ones128"), ("normw", "normw"), ("bg2m", "bg2m"),
                 ("sumsels", "sumsels"), ("trige", "trige"), ("ones32", "ones32"),
                 ("m1col", "m1col"),
                 ("sgm", "sgm"), ("sum_", "sum_"), ("mselcol", "mselcol")]
        for dn, sn in pairs:
            kb.dma("qs", lambda e, d=din[dn], s=sb[sn]: e.dma_start(out=s[:], in_=d[:]),
                   writes=[("sb", sn)])
        kb.dma("qs", lambda e: e.dma_start(out=sb["mselM"][:], in_=din["mselM"][:]),
               writes=[("sb", "mselM")])
        for i, dn in enumerate(("cqr", "ckr", "cvr")):
            kb.dma("qs", lambda e, d=din[dn], i=i: e.dma_start(
                out=sb["cw"][:, i * 64:(i + 1) * 64], in_=d[:]),
                writes=[("sb", "cw", i)])
        kb.op("dve", lambda e: e.memset(sb["row2"][:], 1.0), writes=[("sb", "row2one")])
        kb.op("dve", lambda e: e.memset(sb["c128"][:], 128.0), writes=[("sb", "c128")])
        for i, v in enumerate(self.CVALS):
            kb.op("dve", lambda e, i=i, v=v: e.memset(sb["cvals"][:, i:i + 1], v),
                  writes=[("cvals", i)])
            self.nc.const_aps.register(F32, v, sb["cvals"][:, i:i + 1])

    def alloc_a_arena(self):
        tl = self.cfg.tloc
        self.arena_reset()
        self.arena("hfm", [128, 16 * (HB + tl)], BF16)
        self.arena("wpan0", [128, 16 * 128], BF16)
        self.arena("wpan1", [128, 16 * 128], BF16)
        self.arena("pbf0", [128, HB + tl], BF16)
        self.arena("pbf1", [128, HB + tl], BF16)
        self.arena("convy0", [128, tl], BF16)
        self.arena("convy1", [128, tl], BF16)
        self.arena("tstg0", [128, tl], BF16)
        self.arena("tstg1", [128, tl], BF16)
        self.arena("z1", [128, tl], BF16)
        self.arena("zg1", [128, tl], BF16)
        self.arena("wsmall", [128, 2 * 16 * 128], BF16)
        self.arena("wf2", [128, D], BF16)
        self.arena("wg2", [128, D], BF16)
        self.arena("wbr", [128, 16 * H], BF16)
        self.arena("ztmp0", [128, 512], F32)
        self.arena("ztmp1", [128, 512], F32)
        self.arena("gstg", [128, D], BF16)
        self.arena("ecall", [32, D], BF16)
        self.arena("wsufb", [32, D], BF16)

    def jgroups(self):
        tl = self.cfg.tloc
        return [(0, HB)] + [(HB + i * 512, min(512, tl - i * 512))
                            for i in range(ceil_div(tl, 512))]

    def ld_hfm(self):
        kb, sb, din = self.kb, self.sb, self.din
        tl = self.cfg.tloc
        for dc in range(16):
            kb.dma("qp", lambda e, dc=dc: e.dma_start(
                out=sb["hfm"][:, dc * (HB + tl):(dc + 1) * (HB + tl)],
                in_=din["hT"][dc * 128:(dc + 1) * 128, :]),
                writes=[("hfm", dc)])

    def phase_aproj(self):
        kb, sb, din, sp = self.kb, self.sb, self.din, self.spills
        tl, ntb = self.cfg.tloc, self.cfg.ntb
        pb = self.pb
        HT = HB + tl
        tensors = [("q", "Wq", 0, ("q",), False),
                   ("k", "Wk", 1, ("kfm", "ktm"), True),
                   ("v", "Wv", 2, ("vtm",), True),
                   ("w", "Ww", 2, ("wtm",), True)]
        for xi, (xn, wn, cwi, spnames, want_tm) in enumerate(tensors):
            for fc in range(16):
                par = fc % 2
                wp = sb["wpan%d" % par]
                kb.dma("qp", lambda e, wn=wn, fc=fc, wp=wp: e.dma_start(
                    out=wp[:], in_=din[wn][fc * 128:(fc + 1) * 128, :]),
                    writes=[("wpan", par)])
                pbf = sb["pbf%d" % par]
                for jg, (t0, tn) in enumerate(self.jgroups()):
                    bank = jg % 2
                    for dc in range(16):
                        kb.op("pe", lambda e, wp=wp, dc=dc, t0=t0, tn=tn, bank=bank: e.matmul(
                            pb(bank)[:, 0:tn],
                            wp[:, dc * 128:(dc + 1) * 128],
                            sb["hfm"][:, dc * HT + t0: dc * HT + t0 + tn],
                            start=(dc == 0), stop=(dc == 15)),
                            reads=[("wpan", par), ("hfm", dc)], writes=[("pb", bank)])
                    kb.op("act", lambda e, t0=t0, tn=tn, bank=bank, pbf=pbf: e.activation(
                        out=pbf[:, t0:t0 + tn], in_=pb(bank)[:, 0:tn], func=AF.Copy),
                        reads=[("pb", bank)], writes=[("pbf", par, jg)])
                cy = sb["convy%d" % par]
                pr = [("pbf", par, jg) for jg in range(len(self.jgroups()))]
                cwbase = cwi * 64 + fc * KW
                kb.op("dve", lambda e, cy=cy, pbf=pbf,
                      cc=sb["cw"][:, cwbase + 3:cwbase + 4]: e.tensor_scalar(
                    cy[:], pbf[:, HB:HB + tl], cc, None, op0=ALU.mult),
                    reads=pr + [("sb", "cw", cwi)], writes=[("convy", par)])
                for i, eng in ((2, "dve"), (1, "dve"), (0, "dve")):
                    kb.op(eng, lambda e, i=i, cy=cy, pbf=pbf,
                          cc=sb["cw"][:, cwbase + i:cwbase + i + 1]: e.scalar_tensor_tensor(
                        cy[:], pbf[:, HB - 3 + i:HB - 3 + i + tl], cc, cy[:],
                        op0=ALU.mult, op1=ALU.add),
                        reads=pr + [("convy", par)], writes=[("convy", par)])
                kb.op("act", lambda e, cy=cy: e.activation(out=cy[:], in_=cy[:], func=AF.Silu),
                      reads=[("convy", par)], writes=[("convy", par)])
                for nm in spnames:
                    if nm in ("q", "kfm"):
                        kb.dma("qs", lambda e, nm=nm, fc=fc, cy=cy: e.dma_start(
                            out=sp[nm][fc], in_=cy[:]),
                            reads=[("convy", par)], writes=[("sp" + nm, fc)])
                if want_tm:
                    nmtm = spnames[-1]
                    tstg = sb["tstg%d" % par]
                    for tb in range(ntb):
                        tbank = 2 + (tb // 8) % 2
                        kb.op("pe", lambda e, tb=tb, cy=cy, tbank=tbank: e.transpose(
                            pb(tbank).bitcast(BF16)[:, (tb % 8) * 128:(tb % 8) * 128 + 128],
                            cy[:, tb * 128:(tb + 1) * 128], sb["ident"][:]),
                            reads=[("convy", par), ("sb", "ident")],
                            writes=[("pbT", tbank, tb % 8)])
                        if tb % 8 == 7 or tb == ntb - 1:
                            lo = (tb // 8) * 8
                            kb.op("act", lambda e, lo=lo, tb=tb, tstg=tstg, tbank=tbank:
                                  e.activation(
                                      out=tstg[:, lo * 128:(tb + 1) * 128],
                                      in_=pb(tbank).bitcast(BF16)[:, 0:(tb + 1 - lo) * 128],
                                      func=AF.Copy),
                                  reads=[("pbT", tbank, i) for i in range(tb + 1 - lo)],
                                  writes=[("tstg", par, tb // 8)])
                    kb.dma("qs", lambda e, nmtm=nmtm, fc=fc, tstg=tstg: e.dma_start(
                        out=sp[nmtm][fc].rearrange("(tb p) f -> p tb f", p=128),
                        in_=tstg[:, 0:tl]),
                        reads=[("tstg", par, i) for i in range(ceil_div(ntb, 8))],
                        writes=[("sp" + nmtm, fc)])


def build_program(cfg):
    b = Builder(cfg)
    b.ld_consts()
    ph = cfg.phases
    if "aproj" in ph or "az" in ph or "amisc" in ph:
        b.alloc_a_arena()
        b.ld_hfm()
    for name in ("aproj", "az", "amisc", "ascan", "aagg", "gather", "bscan", "f1", "f2"):
        if name not in ph:
            continue
        if name == "ascan":
            b.alloc_ascan_arena()
        if name == "aagg":
            b.alloc_aagg_arena()
        if name == "bscan":
            b.alloc_b_arena()
        if name == "f1":
            b.alloc_f1_arena()
        if name == "f2":
            b.alloc_f2_arena()
        b.phase_fence()
        getattr(b, "phase_" + name)()
    b.kb.finish()
    for name, dst in b.dbg.items():
        srcap = b.spills[name]
        b.kb.dma("qa", lambda e, srcap=srcap, dst=dst: e.dma_start(
            out=dst.rearrange(_flat_str(dst.shape)), in_=srcap.rearrange(_flat_str(srcap.shape))))
    if b.dbg:
        b.kb.finish()
    b.kb.replay()
    return b


def _flat_str(shape):
    letters = "abcdef"[:len(shape)]
    return " ".join(letters) + " -> (" + " ".join(letters) + ")"


def host_prep(inputs, cfg):
    tl, n = cfg.tloc, cfg.ncores
    f32 = np.float32
    h_full = np.asarray(inputs["L0_attn_hidden_states"], f32)[0]
    hT_full = np.ascontiguousarray(h_full.T)
    zeros_halo = np.zeros((D, HB), f32)

    def u8_low_bytes(a_int32):
        a = np.ascontiguousarray(a_int32.astype(np.int32, copy=False))
        return np.ascontiguousarray(a.view(np.uint8).reshape(*a.shape, 4)[..., 0])

    g = {}
    def panelize(w, pf=128):
        d1, d2 = w.shape
        return np.ascontiguousarray(
            w.reshape(d1 // 128, 128, d2 // pf, pf).transpose(2, 1, 0, 3).reshape(d2 // pf * 128, d1 // 128 * pf))
    g["Wq"] = panelize(np.asarray(inputs["L0_attn_Wq"], f32))
    g["Wk"] = panelize(np.asarray(inputs["L0_attn_Wk"], f32))
    g["Wv"] = panelize(np.asarray(inputs["L0_attn_Wv"], f32))
    g["Ww"] = panelize(np.asarray(inputs["L0_attn_Ww"], f32))
    g["Wo"] = np.asarray(inputs["L0_attn_Wo"], f32)
    for cn, dn in (("cqr", "L0_attn_cq"), ("ckr", "L0_attn_ck"), ("cvr", "L0_attn_cv")):
        c = np.asarray(inputs[dn], f32)
        g[cn] = np.ascontiguousarray(c.reshape(16, 128, KW).transpose(1, 0, 2).reshape(128, 16 * KW))
    g["Wf1"] = np.asarray(inputs["L0_attn_Wf1"], f32)
    g["Wf2"] = np.asarray(inputs["L0_attn_Wf2"], f32)
    wb = np.asarray(inputs["L0_attn_Wb"], f32)
    g["Wbr"] = np.ascontiguousarray(wb.reshape(16, 128, H).transpose(1, 0, 2).reshape(128, 16 * H))
    g["Wg1"] = np.asarray(inputs["L0_attn_Wg1"], f32)
    g["Wg2"] = np.asarray(inputs["L0_attn_Wg2"], f32)
    bg2 = np.asarray(inputs["L0_attn_bg2"], f32)
    g["bg2m"] = np.ascontiguousarray(bg2.reshape(16, 128).T)
    g["normw"] = np.ascontiguousarray(np.asarray(inputs["L0_attn_norm_w"], f32).reshape(128, 1))
    g["WgT8"] = panelize(u8_low_bytes(np.ascontiguousarray(np.asarray(inputs["L0_ffn_wq_gate"]).T)))
    g["WuT8"] = panelize(u8_low_bytes(np.ascontiguousarray(np.asarray(inputs["L0_ffn_wq_up"]).T)))
    g["WdT8"] = panelize(u8_low_bytes(np.ascontiguousarray(np.asarray(inputs["L0_ffn_wq_down"]).T)), pf=512)
    g["sgm"] = np.ascontiguousarray(np.asarray(inputs["L0_ffn_s_gate"], f32).reshape(NHH, 128).T)
    g["sum_"] = np.ascontiguousarray(np.asarray(inputs["L0_ffn_s_up"], f32).reshape(NHH, 128).T)
    g["sdr"] = np.ascontiguousarray(np.asarray(inputs["L0_ffn_s_down"], f32).reshape(1, D))
    g["ident"] = np.eye(128, dtype=BF)
    g["identf"] = np.eye(128, dtype=f32)
    sl = np.arange(C)
    g["tri"] = (sl[:, None] <= sl[None, :]).astype(BF)
    ss = np.zeros((128, 16 * 32), BF)
    for tb in range(16):
        for t in range(128):
            ss[t, tb * 32 + 2 * tb + (t // 64)] = 1
    g["sumsels"] = ss
    jj = np.arange(32)
    g["trige"] = (jj[:, None] >= jj[None, :]).astype(f32)
    g["ones32"] = np.ones((32, 1), f32)
    g["ones1x128"] = np.ones((1, 128), BF)
    g["ones128"] = np.ones((128, 1), BF)

    in_maps = []
    for ci in range(n):
        m = dict(g)
        s = ci * tl
        m["h"] = h_full[s:s + tl]
        if s >= HB:
            m["hT"] = np.ascontiguousarray(hT_full[:, s - HB:s + tl])
        else:
            m["hT"] = np.ascontiguousarray(
                np.concatenate([zeros_halo[:, :HB - s], hT_full[:, 0:s + tl]], axis=1))
        msel = np.zeros((2, (NCORES - 1) * 128), BF)
        mcol = np.zeros((128, NCORES - 1), f32)
        for j in range(NCORES - 1):
            sel = 1.0 if j < ci else 0.0
            msel[0, j * 128:(j + 1) * 128] = sel
            msel[1, j * 128:(j + 1) * 128] = 1.0 - sel
            mcol[:, j] = sel
        m["mselM"] = msel
        m["mselcol"] = mcol
        m["m1col"] = np.ascontiguousarray(1.0 - mcol)
        if cfg.host_gather:
            m["sp_ccout"] = np.zeros((cfg.ncores * 257, D), BF)
        in_maps.append(m)
    return in_maps


AZ_CUT = 0


def phase_az(self):
    kb, sb, din, sp = self.kb, self.sb, self.din, self.spills
    tl, ntb, nch = self.cfg.tloc, self.cfg.ntb, self.cfg.nch
    pb = self.pb
    HT = HB + tl
    wf1_4 = din["Wf1"].rearrange("(dc p) f -> p dc f", p=128)
    kb.dma("qp", lambda e: e.dma_start(out=sb["wsmall"][:, 0:16 * 128], in_=wf1_4),
           writes=[("wsmall", 0)])
    kb.dma("qp", lambda e: e.dma_start(out=sb["wf2"][:], in_=din["Wf2"][:]),
           writes=[("wf2",)])
    for jg in range(ceil_div(tl, 512)):
        t0, tn = jg * 512, min(512, tl - jg * 512)
        bank = jg % 2
        for dc in range(16):
            kb.op("pe", lambda e, dc=dc, t0=t0, tn=tn, bank=bank: e.matmul(
                pb(bank)[:, 0:tn],
                sb["wsmall"][:, dc * 128:(dc + 1) * 128],
                sb["hfm"][:, dc * HT + HB + t0: dc * HT + HB + t0 + tn],
                start=(dc == 0), stop=(dc == 15)),
                reads=[("wsmall", 0), ("hfm", dc)], writes=[("pb", bank)])
        kb.op("act", lambda e, t0=t0, tn=tn, bank=bank: e.activation(
            out=sb["z1"][:, t0:t0 + tn], in_=pb(bank)[:, 0:tn], func=AF.Copy),
            reads=[("pb", bank)], writes=[("z1", jg)])
    if AZ_CUT >= 4:
        return
    njt = ceil_div(D, 512)
    for tb in range(ntb):
        for jt in range(njt):
            bank = 2 + (jt % 2)
            kb.op("pe", lambda e, tb=tb, jt=jt, bank=bank: e.matmul(
                pb(bank)[:],
                sb["z1"][:, tb * 128:(tb + 1) * 128],
                sb["wf2"][:, jt * 512:(jt + 1) * 512],
                start=True, stop=True),
                reads=[("z1", tb // 4), ("wf2",)], writes=[("pb", bank)])
            kb.op("act", lambda e, bank=bank: e.activation(
                out=sb["ztmp0"][:], in_=pb(bank)[:], func=AF.Abs),
                reads=[("pb", bank)], writes=[("zt0",)])
            kb.op("act", lambda e: e.activation(
                out=sb["ztmp0"][:], in_=sb["ztmp0"][:], func=AF.Exp, scale=-1.0),
                reads=[("zt0",)], writes=[("zt0",)])
            kb.op("act", lambda e: e.activation(
                out=sb["ztmp0"][:], in_=sb["ztmp0"][:], func=AF.Ln, bias=1.0),
                reads=[("zt0",)], writes=[("zt0",)])
            kb.op("dve", lambda e, bank=bank: e.tensor_single_scalar(
                sb["ztmp1"][:], pb(bank)[:], 0.0, op=ALU.min),
                reads=[("pb", bank)], writes=[("zt1",)])
            kb.op("dve", lambda e, jt=jt: e.tensor_sub(
                sb["gstg"][:, jt * 512:(jt + 1) * 512], sb["ztmp1"][:], sb["ztmp0"][:]),
                reads=[("zt0",), ("zt1",)], writes=[("gstg", jt)])

        kb.dma("qs", lambda e, tb=tb: e.dma_start(
            out=sp["g"][tb * 128:(tb + 1) * 128, :], in_=sb["gstg"][:]),
            reads=[("gstg", jt) for jt in range(njt)], writes=[("spg", tb)])
    if AZ_CUT >= 3:
        return
    for jt in range(njt):
        bank = 4 + jt % 2
        for tb in range(ntb):
            par = tb % 2
            kb.dma("qs", lambda e, jt=jt, tb=tb, par=par: e.dma_start(
                out=sb["ztmp%d" % par][:].bitcast(BF16)[:, 0:512],
                in_=sp["g"][tb * 128:(tb + 1) * 128, jt * 512:(jt + 1) * 512]),
                reads=[("spg", tb)], writes=[("zt", par)])
            kb.op("pe", lambda e, tb=tb, jt=jt, par=par, bank=bank: e.matmul(
                pb(bank)[0:32, :],
                sb["sumsels"][:, tb * 32:(tb + 1) * 32],
                sb["ztmp%d" % par][:].bitcast(BF16)[:, 0:512],
                start=(tb == 0), stop=(tb == ntb - 1)),
                reads=[("zt", par), ("sb", "sumsels")], writes=[("pbcc", bank)])
        kb.op("dve", lambda e, jt=jt, bank=bank: e.tensor_copy(
            sb["cCrows"][:, jt * 512:(jt + 1) * 512], pb(bank)[0:32, :]),
            reads=[("pbcc", bank)], writes=[("cCrows", jt)])
        kb.op("act", lambda e, jt=jt, bank=bank: e.activation(
            out=sb["ecall"][:, jt * 512:(jt + 1) * 512], in_=pb(bank)[0:32, :],
            func=AF.Exp),
            reads=[("pbcc", bank)], writes=[("ecall", jt)])
    for jt in range(njt):
        kb.op("pe", lambda e, jt=jt: e.matmul(
            pb(jt % 2)[0:32, :], sb["trige"][:],
            sb["cCrows"][:, jt * 512:(jt + 1) * 512], start=True, stop=True),
            reads=[("cCrows", jt), ("sb", "trige")], writes=[("pb", jt % 2)])
        kb.op("act", lambda e, jt=jt: e.activation(
            out=sb["wsufb"][:, jt * 512:(jt + 1) * 512], in_=pb(jt % 2)[0:32, :],
            func=AF.Exp),
            reads=[("pb", jt % 2)], writes=[("wsufb", jt)])
    if AZ_CUT >= 2:
        return
    kb.dma("qs", lambda e: e.dma_start(out=sp["ec"][:], in_=sb["ecall"][0:nch, :]),
           reads=[("ecall", jt) for jt in range(njt)], writes=[("spec",)])
    kb.dma("qs", lambda e: e.dma_start(out=sp["wsuf"][:], in_=sb["wsufb"][0:nch, :]),
           reads=[("wsufb", jt) for jt in range(njt)], writes=[("spwsuf",)])
    if AZ_CUT >= 1:
        return
    kb.dma("qp", lambda e: e.dma_start(
        out=sb["ec16all"][:, 0:nch * 128],
        in_=sp["ec"].rearrange("j (h m) -> h j m", m=128)),
        reads=[("spec",)], writes=[("ec16all",)])

AM_CUT = 0


def phase_amisc(self):
    kb, sb, din, sp = self.kb, self.sb, self.din, self.spills
    tl, ntb = self.cfg.tloc, self.cfg.ntb
    pb = self.pb
    HT = HB + tl
    kb.dma("qp", lambda e: e.dma_start(out=sb["wbr"][:], in_=din["Wbr"][:]),
           writes=[("wbr",)])
    wg1_4 = din["Wg1"].rearrange("(dc p) f -> p dc f", p=128)
    kb.dma("qp", lambda e: e.dma_start(out=sb["wsmall"][:, 16 * 128:32 * 128], in_=wg1_4),
           writes=[("wsmall", 1)])
    kb.dma("qp", lambda e: e.dma_start(out=sb["wg2"][:], in_=din["Wg2"][:]),
           writes=[("wg2",)])
    for tb in range(ntb):
        for dc in range(16):
            kb.op("pe", lambda e, tb=tb, dc=dc: e.matmul(
                pb(0)[:, 0:H],
                sb["hfm"][:, dc * HT + HB + tb * 128: dc * HT + HB + (tb + 1) * 128],
                sb["wbr"][:, dc * H:(dc + 1) * H],
                start=(dc == 0), stop=(dc == 15)),
                reads=[("wbr",), ("hfm", dc)], writes=[("pb", 0)])
        kb.op("act", lambda e, tb=tb: e.activation(
            out=sb["beta"][:, tb * H:(tb + 1) * H], in_=pb(0)[:, 0:H], func=AF.Sigmoid),
            reads=[("pb", 0)], writes=[("beta", tb)])
    b2 = sb["beta2"].ap().rearrange("p (c j) -> p c j", j=H)
    kb.dma("qs", lambda e, b2=b2: e.dma_start(
        out=b2[:, 0::2, :], in_=sb["beta"][0:64, :].rearrange("p (c j) -> p c j", j=H)),
        reads=[("beta", tb) for tb in range(ntb)], writes=[("beta2", 0)])
    kb.dma("qs", lambda e, b2=b2: e.dma_start(
        out=b2[:, 1::2, :], in_=sb["beta"][64:128, :].rearrange("p (c j) -> p c j", j=H)),
        reads=[("beta", tb) for tb in range(ntb)], writes=[("beta2", 1)])
    if AM_CUT >= 2:
        return
    for jg in range(ceil_div(tl, 512)):
        t0, tn = jg * 512, min(512, tl - jg * 512)
        bank = 1
        for dc in range(16):
            kb.op("pe", lambda e, dc=dc, t0=t0, tn=tn, bank=bank: e.matmul(
                pb(bank)[:, 0:tn],
                sb["wsmall"][:, (16 + dc) * 128:(17 + dc) * 128],
                sb["hfm"][:, dc * HT + HB + t0: dc * HT + HB + t0 + tn],
                start=(dc == 0), stop=(dc == 15)),
                reads=[("wsmall", 1), ("hfm", dc)], writes=[("pb", bank)])
        kb.op("act", lambda e, t0=t0, tn=tn, bank=bank: e.activation(
            out=sb["zg1"][:, t0:t0 + tn], in_=pb(bank)[:, 0:tn], func=AF.Copy),
            reads=[("pb", bank)], writes=[("zg1", jg)])
    if AM_CUT >= 1:
        return
    for fc2 in range(16):
        for jg in range(ceil_div(tl, 512)):
            t0, tn = jg * 512, min(512, tl - jg * 512)
            bank = 2 + (jg % 2)
            kb.op("pe", lambda e, fc2=fc2, t0=t0, tn=tn, bank=bank: e.matmul(
                pb(bank)[:, 0:tn],
                sb["wg2"][:, fc2 * 128:(fc2 + 1) * 128],
                sb["zg1"][:, t0:t0 + tn], start=True, stop=True),
                reads=[("wg2",), ("zg1", jg)], writes=[("pb", bank)])
            kb.op("dve", lambda e, fc2=fc2, t0=t0, tn=tn, bank=bank: e.tensor_scalar(
                sb["z1"][:, t0:t0 + tn], pb(bank)[:, 0:tn],
                sb["bg2m"][:, fc2:fc2 + 1], None, op0=ALU.add),
                reads=[("pb", bank), ("sb", "bg2m")], writes=[("z1", jg)])
            kb.op("act", lambda e, t0=t0, tn=tn: e.activation(
                out=sb["z1"][:, t0:t0 + tn], in_=sb["z1"][:, t0:t0 + tn],
                func=AF.Sigmoid),
                reads=[("z1", jg)], writes=[("z1", jg)])
        kb.dma("qs", lambda e, fc2=fc2: e.dma_start(
            out=sp["gs"][fc2], in_=sb["z1"][:, 0:tl]),
            reads=[("z1", jg) for jg in range(ceil_div(tl, 512))],
            writes=[("spgs", fc2)])


Builder.phase_az = phase_az
Builder.phase_amisc = phase_amisc


def phase_fence(self):
    kb = self.kb
    toks = [(e, kb.cnt[e]) for e in ENGINES if kb.cnt[e]]
    toks += [(qn, kb.dcnt[qn]) for qn in QUEUES if kb.dcnt[qn]]
    for e in ENGINES:
        kb.barrier_wait(e, toks)


def alloc_ascan_arena(self):
    tl = self.cfg.tloc
    self.arena_reset()
    for par in (0, 1):
        self.arena("gch%d" % par, [64, D], BF16)
        self.arena("wtch%d" % par, [64, D], BF16)
        self.arena("qfch%d" % par, [128, H * C], BF16)
        self.arena("kfch%d" % par, [128, H * C], BF16)
        self.arena("einv%d" % par, [64, D], BF16)
        self.arena("bw%d" % par, [64, D], BF16)
        self.arena("ubuf%d" % par, [64, D], BF16)
        self.arena("sibuf%d" % par, [64, D], BF16)
        self.arena("pm%d" % par, [64, 1024], BF16)
        self.arena("ssq%d" % par, [64, 16], F32)
        self.arena("rb%d" % par, [64, 16], BF16)

    self.arena("sqtmp", [64, D], F32)


def alloc_aagg_arena(self):
    self.arena_reset()
    for par in (0, 1):
        self.arena("ktch%d" % par, [64, 4 * D], BF16)
        self.arena("vtch%d" % par, [64, 4 * D], BF16)
        self.arena("ucb%d" % par, [64, 4 * D], BF16)
        self.arena("wsb%d" % par, [64, D], BF16)
    self.arena("ccstg", [128, D], BF16)
    self.arena("aggK", [128, H * M], F32)
    self.arena("aggV", [128, H * DK], F32)
    self.arena("erow", [1, D], BF16)


def phase_ascan(self):
    kb, sb, sp = self.kb, self.sb, self.spills
    tl, nch = self.cfg.tloc, self.cfg.nch
    pb = self.pb
    PSA = self.PSA
    wtm3 = sp["wtm"].rearrange("h t f -> t h f")
    q3 = sp["q"].rearrange("h p t -> p h t")
    k3 = sp["kfm"].rearrange("h p t -> p h t")
    for ch in range(nch):
        par = ch % 2
        gch, wt = sb["gch%d" % par], sb["wtch%d" % par]
        qf, kf = sb["qfch%d" % par], sb["kfch%d" % par]
        einv, bw = sb["einv%d" % par], sb["bw%d" % par]
        ub, sib, pm = sb["ubuf%d" % par], sb["sibuf%d" % par], sb["pm%d" % par]
        ssq, rb = sb["ssq%d" % par], sb["rb%d" % par]
        kb.dma("qs", lambda e, ch=ch, gch=gch: e.dma_start(
            out=gch[:], in_=sp["g"][ch * C:(ch + 1) * C, :]),
            reads=[("spg", ch * C // 128)], writes=[("gch", par)])
        kb.dma("qs", lambda e, ch=ch, wt=wt: e.dma_start(
            out=wt[:], in_=wtm3[ch * C:(ch + 1) * C]),
            reads=[("spwtm", h) for h in range(H)], writes=[("wtch", par)])
        kb.dma("qs", lambda e, ch=ch, qf=qf: e.dma_start(
            out=qf[:], in_=q3[:, :, ch * C:(ch + 1) * C]),
            reads=[("spq", h) for h in range(H)], writes=[("qfch", par)])
        kb.dma("qs", lambda e, ch=ch, kf=kf: e.dma_start(
            out=kf[:], in_=k3[:, :, ch * C:(ch + 1) * C]),
            reads=[("spkfm", h) for h in range(H)], writes=[("kfch", par)])
        for jt in range(4):
            kb.op("pe", lambda e, jt=jt, gch=gch: e.matmul(
                PSA.ap()[0:C, jt * 512:(jt + 1) * 512], sb["tri"][:],
                gch[:, jt * 512:(jt + 1) * 512], start=True, stop=True),
                reads=[("gch", par), ("sb", "tri")], writes=[("pb", jt)])
        kb.op("act", lambda e, einv=einv: e.activation(
            out=einv[:], in_=PSA.ap()[0:C, :], func=AF.Exp, scale=-1.0),
            reads=[("pb", 0), ("pb", 1), ("pb", 2), ("pb", 3)], writes=[("einv", par)])
        kb.op("act", lambda e, wt=wt: e.activation(
            out=sb["sqtmp"][:], in_=wt[:], func=AF.Square),
            reads=[("wtch", par)], writes=[("sqtmp",)])
        kb.op("dve", lambda e, ssq=ssq: e.reduce_sum(
            ssq[:], sb["sqtmp"][:].rearrange("p (h m) -> p h m", m=M), axis=AX.X),
            reads=[("sqtmp",)], writes=[("ssq", par)])
        kb.op("act", lambda e, ssq=ssq: e.activation(
            out=ssq[:], in_=ssq[:], func=AF.Ln, bias=L2EPS),
            reads=[("ssq", par)], writes=[("ssq", par)])
        kb.op("act", lambda e, ssq=ssq: e.activation(
            out=ssq[:], in_=ssq[:], func=AF.Exp, scale=-0.5),
            reads=[("ssq", par)], writes=[("ssq", par)])
        bslice = sb["beta2"][:, ch * H:(ch + 1) * H]
        kb.op("dve", lambda e, rb=rb, ssq=ssq, bslice=bslice: e.tensor_mul(
            rb[:], ssq[:], bslice),
            reads=[("ssq", par), ("beta2", 0), ("beta2", 1)], writes=[("rbb", par)])
        kb.op("dve", lambda e, bw=bw, wt=wt, rb=rb: e.tensor_tensor(
            bw[:].rearrange("p (h m) -> p h m", m=M),
            wt[:].rearrange("p (h m) -> p h m", m=M),
            rb[:].unsqueeze(2).broadcast_to([C, H, M]), op=ALU.mult),
            reads=[("wtch", par), ("rbb", par)], writes=[("bwb", par)])
        kb.op("dve", lambda e, ub=ub, bw=bw, einv=einv: e.tensor_mul(
            ub[:], bw[:], einv[:]),
            reads=[("bwb", par), ("einv", par)], writes=[("ubuf", par)])
        kb.dma("qs", lambda e, ch=ch, ub=ub: e.dma_start(
            out=sp["u"][ch * C:(ch + 1) * C, :], in_=ub[:]),
            reads=[("ubuf", par)], writes=[("spu", ch)])
        for hh in range(H):
            bank = 4 + hh // 8
            kb.op("pe", lambda e, hh=hh, bank=bank, kf=kf, qf=qf: e.matmul(
                pb(bank)[0:C, (hh % 8) * C:(hh % 8 + 1) * C],
                kf[:, hh * C:(hh + 1) * C], qf[:, hh * C:(hh + 1) * C],
                start=True, stop=True),
                reads=[("qfch", par), ("kfch", par)], writes=[("pb", bank)])
        for b_ in (4, 5):
            kb.op("dve", lambda e, b_=b_, pm=pm: e.tensor_tensor(
                pm[:, (b_ - 4) * 512:(b_ - 3) * 512].rearrange("p (h s) -> p h s", s=C),
                pb(b_)[0:C, :].rearrange("p (h s) -> p h s", s=C),
                sb["tri"][:].unsqueeze(1).broadcast_to([C, 8, C]), op=ALU.mult),
                reads=[("pb", b_), ("sb", "tri")], writes=[("pmb", par, b_)])
        for hh in range(H):
            kb.op("pe", lambda e, hh=hh, pm=pm, ub=ub: e.matmul(
                PSA.ap()[0:C, hh * M:(hh + 1) * M],
                pm[:, hh * C:(hh + 1) * C], ub[:, hh * M:(hh + 1) * M],
                start=True, stop=True),
                reads=[("pmb", par, 4 + hh // 8), ("ubuf", par)],
                writes=[("pb", hh // 4)])
        kb.op("act", lambda e, sib=sib: e.activation(
            out=sib[:], in_=PSA.ap()[0:C, :], func=AF.Copy, scale=SCALE),
            reads=[("pb", 0), ("pb", 1), ("pb", 2), ("pb", 3)], writes=[("sibuf", par)])
        kb.dma("qs", lambda e, ch=ch, sib=sib: e.dma_start(
            out=sp["si"][ch * C:(ch + 1) * C, :], in_=sib[:]),
            reads=[("sibuf", par)], writes=[("spsi", ch)])


def phase_aagg(self):
    kb, sb, sp = self.kb, self.sb, self.spills
    tl, nch = self.cfg.tloc, self.cfg.nch
    pb = self.pb
    PSA = self.PSA
    BC = 4
    nblk = ceil_div(nch, BC)
    ktm3 = sp["ktm"].rearrange("h t f -> t h f")
    vtm3 = sp["vtm"].rearrange("h t f -> t h f")
    for blk in range(nblk):
        chs = [ch for ch in range(blk * BC, min((blk + 1) * BC, nch))]
        par = blk % 2
        kt, vt = sb["ktch%d" % par], sb["vtch%d" % par]
        uc, ws = sb["ucb%d" % par], sb["wsb%d" % par]
        nbc = len(chs)
        for i, ch in enumerate(chs):
            kb.dma("qs", lambda e, ch=ch, i=i, kt=kt: e.dma_start(
                out=kt[:, i * D:(i + 1) * D], in_=ktm3[ch * C:(ch + 1) * C]),
                reads=[("spktm", h) for h in range(H)], writes=[("ktch", par, i)])
            kb.dma("qs", lambda e, ch=ch, i=i, vt=vt: e.dma_start(
                out=vt[:, i * D:(i + 1) * D], in_=vtm3[ch * C:(ch + 1) * C]),
                reads=[("spvtm", h) for h in range(H)], writes=[("vtch", par, i)])
            kb.dma("qa", lambda e, ch=ch, i=i, ws=ws: e.dma_start(
                out=ws[:, (i % 2) * 0: D] if False else ws[:],
                in_=sp["wsuf"][ch:ch + 1, :].broadcast_to([C, D])),
                reads=[("spwsuf",)], writes=[("wsbb", par)])
            kb.dma("qa", lambda e, ch=ch, i=i, uc=uc: e.dma_start(
                out=uc[:, i * D:(i + 1) * D], in_=sp["u"][ch * C:(ch + 1) * C, :]),
                reads=[("spu", ch)], writes=[("ucb", par, i)])
            kb.op("dve", lambda e, i=i, uc=uc, ws=ws: e.tensor_mul(
                uc[:, i * D:(i + 1) * D], uc[:, i * D:(i + 1) * D], ws[:]),
                reads=[("ucb", par, i), ("wsbb", par)], writes=[("ucb", par, i)])
        for hh in range(H):
            bank = hh % 2
            for i in range(nbc):
                kb.op("pe", lambda e, hh=hh, i=i, kt=kt, uc=uc, bank=bank, nbc=nbc: e.matmul(
                    pb(bank)[:, 0:M],
                    kt[:, i * D + hh * DK: i * D + (hh + 1) * DK],
                    uc[:, i * D + hh * M: i * D + (hh + 1) * M],
                    start=(i == 0), stop=(i == nbc - 1)),
                    reads=[("ktch", par, i), ("ucb", par, i)], writes=[("pb", bank)])
            if blk == 0:
                kb.op("dve", lambda e, hh=hh, bank=bank: e.tensor_copy(
                    sb["aggK"][:, hh * M:(hh + 1) * M], pb(bank)[:, 0:M]),
                    reads=[("pb", bank)], writes=[("aggK", hh)])
            else:
                kb.op("dve", lambda e, hh=hh, bank=bank: e.tensor_add(
                    sb["aggK"][:, hh * M:(hh + 1) * M],
                    sb["aggK"][:, hh * M:(hh + 1) * M], pb(bank)[:, 0:M]),
                    reads=[("pb", bank), ("aggK", hh)], writes=[("aggK", hh)])
            bank2 = 2 + hh % 2
            for i in range(nbc):
                kb.op("pe", lambda e, hh=hh, i=i, uc=uc, vt=vt, bank2=bank2, nbc=nbc: e.matmul(
                    pb(bank2)[:, 0:DK],
                    uc[:, i * D + hh * M: i * D + (hh + 1) * M],
                    vt[:, i * D + hh * DK: i * D + (hh + 1) * DK],
                    start=(i == 0), stop=(i == nbc - 1)),
                    reads=[("ucb", par, i), ("vtch", par, i)], writes=[("pb", bank2)])
            if blk == 0:
                kb.op("dve", lambda e, hh=hh, bank2=bank2: e.tensor_copy(
                    sb["aggV"][:, hh * DK:(hh + 1) * DK], pb(bank2)[:, 0:DK]),
                    reads=[("pb", bank2)], writes=[("aggV", hh)])
            else:
                kb.op("dve", lambda e, hh=hh, bank2=bank2: e.tensor_add(
                    sb["aggV"][:, hh * DK:(hh + 1) * DK],
                    sb["aggV"][:, hh * DK:(hh + 1) * DK], pb(bank2)[:, 0:DK]),
                    reads=[("pb", bank2), ("aggV", hh)], writes=[("aggV", hh)])
    kb.op("dve", lambda e: e.tensor_copy(sb["ccstg"][:], sb["aggK"][:]),
          reads=[("aggK", hh) for hh in range(H)], writes=[("ccstg",)])
    kb.dma("qs", lambda e: e.dma_start(out=sp["ccin"][0:128, :], in_=sb["ccstg"][:]),
           reads=[("ccstg",)], writes=[("spccin", 0)])
    kb.op("dve", lambda e: e.tensor_copy(sb["ccstg"][:], sb["aggV"][:]),
          reads=[("aggV", hh) for hh in range(H)] + [("spccin", 0)], writes=[("ccstg",)])
    kb.dma("qs", lambda e: e.dma_start(out=sp["ccin"][128:256, :], in_=sb["ccstg"][:]),
           reads=[("ccstg",)], writes=[("spccin", 1)])
    for jt in range(4):
        kb.op("pe", lambda e, jt=jt: e.matmul(
            PSA.ap()[0:1, jt * 512:(jt + 1) * 512], sb["ones32"][:],
            sb["cCrows"][:, jt * 512:(jt + 1) * 512], start=True, stop=True),
            reads=[("cCrows", jt), ("sb", "ones32")], writes=[("pb", jt)])
        kb.op("act", lambda e, jt=jt: e.activation(
            out=sb["erow"][:, jt * 512:(jt + 1) * 512],
            in_=PSA.ap()[0:1, jt * 512:(jt + 1) * 512], func=AF.Exp),
            reads=[("pb", jt)], writes=[("erow", jt)])
    kb.dma("qs", lambda e: e.dma_start(out=sp["ccin"][256:257, :], in_=sb["erow"][:]),
           reads=[("erow", jt) for jt in range(4)], writes=[("spccin", 2)])


def phase_gather(self):
    kb, sp = self.kb, self.spills
    if self.cfg.host_gather:
        return
    n = self.cfg.ncores
    sem = self.nc.alloc_semaphore("sem_cc")
    kb.op("pool", lambda e: e.collective_compute(
        "AllGather", ALU.bypass, replica_groups=[list(range(n))],
        ins=[sp["ccin"][:].opt()], outs=[sp["ccout"][:].opt()]),
        reads=[("spccin", 0), ("spccin", 1), ("spccin", 2)],
        writes=[("spccout",)])


Builder.phase_fence = phase_fence
Builder.alloc_ascan_arena = alloc_ascan_arena
Builder.alloc_aagg_arena = alloc_aagg_arena
Builder.phase_ascan = phase_ascan
Builder.phase_aagg = phase_aagg
Builder.phase_gather = phase_gather


def alloc_b_arena(self):
    tl = self.cfg.tloc
    self.arena_reset()
    self.arena("wo", [128, 16 * 128 * 16], BF16)
    for par in (0, 1):
        self.arena("bgch%d" % par, [64, D], BF16)
        self.arena("bqf%d" % par, [128, H * C], BF16)
        self.arena("bkt%d" % par, [64, D], BF16)
        self.arena("bvt%d" % par, [64, D], BF16)
    self.arena("bub0", [64, D], BF16)
    self.arena("bsi0", [64, D], BF16)
    self.arena("bgs0", [128, H * C], BF16)
    self.arena("becb0", [128, D], BF16)
    self.arena("bh0", [128, D], F32)
    self.arena("bofm0", [128, D], BF16)
    self.arena("befull", [64, D], BF16)
    self.arena("bes", [64, D], BF16)
    self.arena("bs0", [64, D], BF16)
    self.arena("bamt", [128, H * C], BF16)
    self.arena("bufm", [128, H * C], BF16)

    self.arena("bosq", [128, H * C], BF16)
    self.arena("brstdb", [128, H * C], BF16)
    self.arena("bo1", [128, H * C], BF16)
    self.arena("br", [128, D], F32)
    self.arena("bx1", [128, D], BF16)
    self.arena("bmx", [64, H], F32)
    self.arena("bsm", [64, H], F32)
    self.arena("brs", [64, H], F32)
    self.arena("brr", [1, H * C], BF16)
    self.arena("bln", [128, 4], F32)
    self.arena("be16", [16, 128], BF16)
    self.sb["bfold"] = self.sb["bx1"]
    self.sb["bytm"] = self.sb["bsi0"]


def phase_bscan(self):
    kb, sb, sp, din = self.kb, self.sb, self.spills, self.din
    tl, nch, cfg = self.cfg.tloc, self.cfg.nch, self.cfg
    pb = self.pb
    PSA = self.PSA
    wo4 = din["Wo"].rearrange("(dc p) f -> p dc f", p=128)
    kb.dma("qp", lambda e: e.dma_start(out=sb["wo"][:], in_=wo4), writes=[("wo",)])
    kb.op("dve", lambda e: e.memset(sb["Sk"][:], 0.0), writes=[("Sk",)])
    kb.op("dve", lambda e: e.memset(sb["Sv"][:], 0.0), writes=[("Sv",)])
    nfold = cfg.ncores - 1
    for j in range(nfold):
        rb_ = 257 * j
        kb.dma("qp", lambda e, rb_=rb_: e.dma_start(
            out=sb["row2"][0:1, :], in_=sp["ccout"][rb_ + 256:rb_ + 257, :]),
            reads=[("spccout",)], writes=[("row2",)])
        kb.dma("qp", lambda e, rb_=rb_: e.dma_start(
            out=sb["be16"][:], in_=sp["ccout"][rb_ + 256:rb_ + 257, :].rearrange(
                "o (h m) -> (o h) m", m=128)),
            reads=[("spccout",)], writes=[("be16",)])
        for jt in range(4):
            kb.op("pe", lambda e, j=j, jt=jt: e.matmul(
                PSA.ap()[:, jt * 512:(jt + 1) * 512],
                sb["mselM"][:, j * 128:(j + 1) * 128],
                sb["row2"][:, jt * 512:(jt + 1) * 512], start=True, stop=True),
                reads=[("row2",), ("sb", "mselM")], writes=[("pb", jt)])
        kb.op("dve", lambda e: e.tensor_mul(sb["Sk"][:], sb["Sk"][:], PSA.ap()[:]),
              reads=[("pb", 0), ("pb", 1), ("pb", 2), ("pb", 3), ("Sk",)],
              writes=[("Sk",)])
        kb.dma("qp", lambda e, rb_=rb_: e.dma_start(
            out=sb["bfold"][:].bitcast(BF16)[:, 0:D], in_=sp["ccout"][rb_:rb_ + 128, :]),
            reads=[("spccout",)], writes=[("bx1",)])
        kb.op("dve", lambda e, j=j: e.scalar_tensor_tensor(
            sb["Sk"][:], sb["bfold"][:].bitcast(BF16)[:, 0:D], sb["mselcol"][:, j:j + 1],
            sb["Sk"][:], op0=ALU.mult, op1=ALU.add),
            reads=[("bx1",), ("Sk",), ("sb", "mselcol")], writes=[("Sk",)])
        kb.op("dve", lambda e, j=j: e.tensor_scalar(
            sb["be16"][:], sb["be16"][:], sb["mselcol"][0:16, j:j + 1],
            sb["m1col"][0:16, j:j + 1], op0=ALU.mult, op1=ALU.add),
            reads=[("be16",), ("sb", "mselcol"), ("sb", "m1col")], writes=[("be16",)])
        kb.op("pe", lambda e: e.transpose(
            pb(6).bitcast(BF16)[:, 0:16], sb["be16"][:], sb["ident"][0:16, 0:16]),
            reads=[("be16",), ("sb", "ident")], writes=[("pb", 6)])
        kb.op("dve", lambda e: e.tensor_copy(sb["ecct"][:], pb(6).bitcast(BF16)[:, 0:16]),
              reads=[("pb", 6)], writes=[("ecct",)])
        for hh in range(H):
            kb.op("act", lambda e, hh=hh: e.activation(
                out=sb["Sv"][:, hh * DK:(hh + 1) * DK],
                in_=sb["Sv"][:, hh * DK:(hh + 1) * DK], func=AF.Copy,
                scale=sb["ecct"][:, hh:hh + 1]),
                reads=[("ecct",), ("Sv",)], writes=[("Sv",)])
        kb.dma("qp", lambda e, rb_=rb_: e.dma_start(
            out=sb["bfold"][:], in_=sp["ccout"][rb_ + 128:rb_ + 256, :]),
            reads=[("spccout",), ("Sk",)], writes=[("bx1",)])
        kb.op("dve", lambda e, j=j: e.scalar_tensor_tensor(
            sb["Sv"][:], sb["bfold"][:].bitcast(BF16)[:, 0:D], sb["mselcol"][:, j:j + 1],
            sb["Sv"][:], op0=ALU.mult, op1=ALU.add),
            reads=[("bx1",), ("Sv",), ("sb", "mselcol")], writes=[("Sv",)])
    q3 = sp["q"].rearrange("h p t -> p h t")
    gs3 = sp["gs"].rearrange("h p t -> p h t")
    ktm3 = sp["ktm"].rearrange("h t f -> t h f")
    vtm3 = sp["vtm"].rearrange("h t f -> t h f")
    for ch in range(nch):
        par = ch % 2
        gch, qf = sb["bgch%d" % par], sb["bqf%d" % par]
        kt, vt = sb["bkt%d" % par], sb["bvt%d" % par]
        ub, sib = sb["bub0"], sb["bsi0"]
        gsf, ecb = sb["bgs0"], sb["becb0"]
        ppar = 0
        hb, ofm = sb["bh0"], sb["bofm0"]
        kb.dma("qs", lambda e, ch=ch, gch=gch: e.dma_start(
            out=gch[:], in_=sp["g"][ch * C:(ch + 1) * C, :]),
            reads=[("spg", ch * C // 128)], writes=[("bgch", par)])
        kb.dma("qp", lambda e, ch=ch, qf=qf: e.dma_start(
            out=qf[:], in_=q3[:, :, ch * C:(ch + 1) * C]),
            reads=[("spq", h) for h in range(H)], writes=[("bqf", par)])
        kb.dma("qp", lambda e, ch=ch, kt=kt: e.dma_start(
            out=kt[:], in_=ktm3[ch * C:(ch + 1) * C]),
            reads=[("spktm", h) for h in range(H)], writes=[("bkt", par)])
        kb.dma("qa", lambda e, ch=ch, vt=vt: e.dma_start(
            out=vt[:], in_=vtm3[ch * C:(ch + 1) * C]),
            reads=[("spvtm", h) for h in range(H)], writes=[("bvt", par)])
        kb.dma("qs", lambda e, ch=ch, ub=ub: e.dma_start(
            out=ub[:], in_=sp["u"][ch * C:(ch + 1) * C, :]),
            reads=[("spu", ch)], writes=[("bub",)])
        kb.dma("qs", lambda e, ch=ch, sib=sib: e.dma_start(
            out=sib[:], in_=sp["si"][ch * C:(ch + 1) * C, :]),
            reads=[("spsi", ch)], writes=[("bsiy",)])
        kb.dma("qa", lambda e, ch=ch, gsf=gsf: e.dma_start(
            out=gsf[:], in_=gs3[:, :, ch * C:(ch + 1) * C]),
            reads=[("spgs", h) for h in range(H)], writes=[("bgs",)])
        kb.dma("qa", lambda e, ch=ch, ecb=ecb: e.dma_start(
            out=ecb[:], in_=sp["ec"][ch:ch + 1, :].broadcast_to([128, D])),
            reads=[("spec",)], writes=[("becb",)])
        kb.dma("qa", lambda e, ch=ch, hb=hb: e.dma_start(
            out=hb[(ch % 2) * 64:(ch % 2) * 64 + 64, :],
            in_=din["h"][ch * C:(ch + 1) * C, :]),
            writes=[("bh", ch % 2)])
        for jt in range(4):
            kb.op("pe", lambda e, jt=jt, gch=gch: e.matmul(
                PSA.ap()[0:C, jt * 512:(jt + 1) * 512], sb["tri"][:],
                gch[:, jt * 512:(jt + 1) * 512], start=True, stop=True),
                reads=[("bgch", par), ("sb", "tri")], writes=[("pb", jt)])
        kb.op("act", lambda e: e.activation(
            out=sb["befull"][:], in_=PSA.ap()[0:C, :], func=AF.Exp),
            reads=[("pb", 0), ("pb", 1), ("pb", 2), ("pb", 3)], writes=[("befull",)])
        e16sl = sb["ec16all"][:, ch * 128:(ch + 1) * 128]
        kb.op("pe", lambda e, e16sl=e16sl: e.transpose(
            pb(6).bitcast(BF16)[:, 0:16], e16sl, sb["ident"][0:16, 0:16]),
            reads=[("ec16all",), ("sb", "ident")], writes=[("pb", 6)])
        kb.op("dve", lambda e: e.tensor_copy(sb["ecct"][:], pb(6).bitcast(BF16)[:, 0:16]),
              reads=[("pb", 6)], writes=[("ecct",)])
        kb.op("dve", lambda e: e.tensor_scalar_mul(sb["Skbf"][:], sb["Sk"][:], SCALE),
              reads=[("Sk",)], writes=[("Skbf",)])
        kb.op("pool", lambda e: e.tensor_copy(sb["Svbf"][:], sb["Sv"][:]),
              reads=[("Sv",)], writes=[("Svbf",)])
        for hh in range(H):
            kb.op("pe", lambda e, hh=hh, qf=qf: e.matmul(
                PSA.ap()[0:C, hh * M:(hh + 1) * M],
                qf[:, hh * C:(hh + 1) * C], sb["Skbf"][:, hh * M:(hh + 1) * M],
                start=True, stop=True),
                reads=[("bqf", par), ("Skbf",)], writes=[("pb", hh // 4)])
        kb.op("dve", lambda e, sib=sib: e.tensor_add(
            sb["bs0"][:], PSA.ap()[0:C, :], sib[:]),
            reads=[("pb", 0), ("pb", 1), ("pb", 2), ("pb", 3), ("bsiy",)],
            writes=[("bs0",)])
        kb.op("dve", lambda e: e.tensor_mul(sb["bs0"][:], sb["bs0"][:], sb["befull"][:]),
              reads=[("bs0",), ("befull",)], writes=[("bs0",)])
        s3 = sb["bs0"][:].rearrange("p (h m) -> p h m", m=M)
        kb.op("dve", lambda e, s3=s3: e.reduce_max(sb["bmx"][:], s3, axis=AX.X),
              reads=[("bs0",)], writes=[("bmx",)])
        kb.op("dve", lambda e, s3=s3: e.tensor_tensor(
            s3, s3, sb["bmx"][:].unsqueeze(2).broadcast_to([C, H, M]), op=ALU.subtract),
            reads=[("bs0",), ("bmx",)], writes=[("bs0",)])
        kb.op("act", lambda e: e.activation(
            out=sb["bes"][:], in_=sb["bs0"][:], func=AF.Exp),
            reads=[("bs0",)], writes=[("bes",)])
        es3 = sb["bes"][:].rearrange("p (h m) -> p h m", m=M)
        kb.op("dve", lambda e, es3=es3: e.reduce_sum(sb["bsm"][:], es3, axis=AX.X),
              reads=[("bes",)], writes=[("bsm",)])
        kb.op("dve", lambda e: e.reciprocal(sb["brs"][:], sb["bsm"][:]),
              reads=[("bsm",)], writes=[("brs",)])
        kb.op("dve", lambda e, es3=es3: e.tensor_tensor(
            es3, es3, sb["brs"][:].unsqueeze(2).broadcast_to([C, H, M]), op=ALU.mult),
            reads=[("bes",), ("brs",)], writes=[("bes",)])
        kb.op("dve", lambda e: e.tensor_mul(sb["bes"][:], sb["bes"][:], sb["befull"][:]),
              reads=[("bes",), ("befull",)], writes=[("bes",)])
        for nm, srcb, dstb in (("amt", "bes", "bamt"), ("ufm", None, "bufm")):
            srct = sb["bes"] if nm == "amt" else ub
            srcreg = ("bes",) if nm == "amt" else ("bub",)
            for hh in range(H):
                kb.op("pe", lambda e, hh=hh, srct=srct: e.transpose(
                    pb(6).bitcast(BF16)[:, hh * C:(hh + 1) * C],
                    srct[:, hh * M:(hh + 1) * M], sb["ident"][0:C, 0:C]),
                    reads=[srcreg, ("sb", "ident")], writes=[("pb", 6)])
            kb.op("act", lambda e, dstb=dstb: e.activation(
                out=sb[dstb][:], in_=pb(6).bitcast(BF16)[:, 0:H * C], func=AF.Copy),
                reads=[("pb", 6) for hh in range(H)], writes=[(dstb,)])
        for hh in range(H):
            bank = 4 + hh // 8
            kb.op("pe", lambda e, hh=hh, bank=bank: e.matmul(
                pb(bank)[0:C, (hh % 8) * C:(hh % 8 + 1) * C],
                sb["bufm"][:, hh * C:(hh + 1) * C], sb["bamt"][:, hh * C:(hh + 1) * C],
                start=True, stop=True),
                reads=[("bufm",), ("bamt",)], writes=[("pb", bank)])
        for b_ in (4, 5):
            kb.op("dve", lambda e, b_=b_: e.tensor_tensor(
                sb["bytm"][:, (b_ - 4) * 512:(b_ - 3) * 512].rearrange(
                    "p (h s) -> p h s", s=C),
                pb(b_)[0:C, :].rearrange("p (h s) -> p h s", s=C),
                sb["tri"][:].unsqueeze(1).broadcast_to([C, 8, C]), op=ALU.mult),
                reads=[("pb", b_), ("sb", "tri")], writes=[("bsiy",)])
        for hh in range(H):
            kb.op("pe", lambda e, hh=hh, vt=vt: e.matmul(
                PSA.ap()[:, hh * C:(hh + 1) * C],
                vt[:, hh * DK:(hh + 1) * DK], sb["bytm"][:, hh * C:(hh + 1) * C],
                start=True, stop=False),
                reads=[("bvt", par), ("bsiy",)], writes=[("pb", hh // 8)])
            kb.op("pe", lambda e, hh=hh: e.matmul(
                PSA.ap()[:, hh * C:(hh + 1) * C],
                sb["Svbf"][:, hh * DK:(hh + 1) * DK], sb["bamt"][:, hh * C:(hh + 1) * C],
                start=False, stop=True),
                reads=[("Svbf",), ("bamt",)], writes=[("pb", hh // 8)])
        kb.op("act", lambda e: e.activation(
            out=sb["bosq"][:], in_=PSA.ap()[:, 0:H * C], func=AF.Square),
            reads=[("pb", 0), ("pb", 1)], writes=[("bosq",)])
        for i2 in range(2):
            kb.op("pe", lambda e, i2=i2: e.matmul(
                PSA.ap()[0:1, 1024 + i2 * 512:1024 + (i2 + 1) * 512],
                sb["ones128"][:], sb["bosq"][:, i2 * 512:(i2 + 1) * 512],
                start=True, stop=True),
                reads=[("bosq",), ("sb", "ones128")], writes=[("pb", 2 + i2)])
        kb.op("act", lambda e: e.activation(
            out=sb["brr"][:], in_=PSA.ap()[0:1, 1024:2048], func=AF.Ln,
            scale=1.0 / 128, bias=EPS),
            reads=[("pb", 2), ("pb", 3)], writes=[("brr",)])
        kb.op("act", lambda e: e.activation(
            out=sb["brr"][:], in_=sb["brr"][:], func=AF.Exp, scale=-0.5),
            reads=[("brr",)], writes=[("brr",)])
        for i2 in range(2):
            kb.op("pe", lambda e, i2=i2: e.matmul(
                PSA.ap()[:, 1024 + i2 * 512:1024 + (i2 + 1) * 512],
                sb["ones1x128"][:], sb["brr"][:, i2 * 512:(i2 + 1) * 512],
                start=True, stop=True),
                reads=[("brr",), ("sb", "ones1x128")], writes=[("pb", 2 + i2)])
        kb.op("act", lambda e: e.activation(
            out=sb["brstdb"][:], in_=PSA.ap()[:, 1024:2048], func=AF.Copy),
            reads=[("pb", 2), ("pb", 3)], writes=[("brstdb",)])
        kb.op("dve", lambda e: e.scalar_tensor_tensor(
            sb["bo1"][:], PSA.ap()[:, 0:H * C], sb["normw"][:], sb["brstdb"][:],
            op0=ALU.mult, op1=ALU.mult),
            reads=[("pb", 0), ("pb", 1), ("brstdb",), ("sb", "normw")],
            writes=[("bo1",)])
        kb.op("pool", lambda e, gsf=gsf, ofm=ofm, ch=ch: e.tensor_tensor(
            ofm[:].rearrange("p (h c t) -> p h c t", c=2, t=C)[:, :, ch % 2, :],
            sb["bo1"][:].rearrange("p (h t) -> p h t", t=C), gsf[:].rearrange(
                "p (h t) -> p h t", t=C),
            op=ALU.mult),
            reads=[("bo1",), ("bgs",)], writes=[("bofm", ch % 2)])
        for hh in range(H):
            bank = 4 + hh // 4
            kb.op("pe", lambda e, hh=hh, kt=kt, ub=ub, bank=bank: e.matmul(
                pb(bank)[:, (hh % 4) * M:(hh % 4 + 1) * M],
                kt[:, hh * DK:(hh + 1) * DK], ub[:, hh * M:(hh + 1) * M],
                start=True, stop=True),
                reads=[("bkt", par), ("bub",)], writes=[("pb", bank)])
        for b_ in range(4, 8):
            kb.op("dve", lambda e, b_=b_: e.tensor_add(
                sb["Sk"][:, (b_ - 4) * 512:(b_ - 3) * 512],
                sb["Sk"][:, (b_ - 4) * 512:(b_ - 3) * 512], pb(b_)[:]),
                reads=[("pb", b_), ("Sk",), ("Skbf",)], writes=[("Sk",)])
        kb.op("dve", lambda e, ecb=ecb: e.tensor_mul(sb["Sk"][:], sb["Sk"][:], ecb[:]),
              reads=[("Sk",), ("becb",)], writes=[("Sk",)])
        for hh in range(H):
            kb.op("pe", lambda e, hh=hh, ub=ub, vt=vt: e.matmul(
                PSA.ap()[:, hh * DK:(hh + 1) * DK],
                ub[:, hh * M:(hh + 1) * M], vt[:, hh * DK:(hh + 1) * DK],
                start=True, stop=True),
                reads=[("bub",), ("bvt", par), ("bo1",)], writes=[("pb", hh // 4)])
        for b_ in range(4):
            kb.op("dve", lambda e, b_=b_: e.tensor_add(
                sb["Sv"][:, b_ * 512:(b_ + 1) * 512],
                sb["Sv"][:, b_ * 512:(b_ + 1) * 512], PSA.ap()[:, b_ * 512:(b_ + 1) * 512]),
                reads=[("pb", b_), ("Sv",), ("Svbf",)], writes=[("Sv",)])
        for hh in range(H):
            kb.op("act", lambda e, hh=hh: e.activation(
                out=sb["Sv"][:, hh * DK:(hh + 1) * DK],
                in_=sb["Sv"][:, hh * DK:(hh + 1) * DK], func=AF.Copy,
                scale=sb["ecct"][:, hh:hh + 1]),
                reads=[("ecct",), ("Sv",)], writes=[("Sv",)])
        if ch % 2 == 1:
            for jt in range(4):
                for dc in range(16):
                    lhs = sb["bofm%d" % ppar][:, dc * 128:(dc + 1) * 128]
                    kb.op("pe", lambda e, jt=jt, dc=dc, lhs=lhs: e.matmul(
                        PSA.ap()[:, jt * 512:(jt + 1) * 512],
                        lhs, sb["wo"][:, dc * D + jt * 512: dc * D + (jt + 1) * 512],
                        start=(dc == 0), stop=(dc == 15)),
                        reads=[("bofm", 0), ("bofm", 1), ("wo",)],
                        writes=[("pb", jt)])
            kb.op("dve", lambda e, ppar=ppar: e.tensor_add(
                sb["br"][:], PSA.ap()[:], sb["bh%d" % ppar][:]),
                reads=[("pb", 0), ("pb", 1), ("pb", 2), ("pb", 3),
                       ("bh", 0), ("bh", 1)],
                writes=[("br",)])
            kb.op("dve", lambda e: e.reduce_sum(
                sb["bln"][:, 0:1], sb["br"][:].unsqueeze(1), axis=AX.X),
                reads=[("br",)], writes=[("bln", 0)])
            kb.op("act", lambda e: e.activation(
                out=PSA.ap()[:], in_=sb["br"][:], func=AF.Square),
                reads=[("br",), ("pb", 0), ("pb", 1), ("pb", 2), ("pb", 3)],
                writes=[("pb", 0), ("pb", 1), ("pb", 2), ("pb", 3)])
            kb.op("dve", lambda e: e.reduce_sum(
                sb["bln"][:, 1:2], PSA.ap()[:].unsqueeze(1), axis=AX.X),
                reads=[("pb", 0), ("pb", 1), ("pb", 2), ("pb", 3)], writes=[("bln", 1)])
            kb.op("dve", lambda e: e.tensor_scalar(
                sb["bln"][:, 0:1], sb["bln"][:, 0:1], 1.0 / D, None, op0=ALU.mult),
                reads=[("bln", 0)], writes=[("bln", 0)])
            kb.op("dve", lambda e: e.tensor_scalar(
                sb["bln"][:, 1:2], sb["bln"][:, 1:2], 1.0 / D, None, op0=ALU.mult),
                reads=[("bln", 1)], writes=[("bln", 1)])
            kb.op("dve", lambda e: e.tensor_mul(
                sb["bln"][:, 2:3], sb["bln"][:, 0:1], sb["bln"][:, 0:1]),
                reads=[("bln", 0)], writes=[("bln", 2)])
            kb.op("dve", lambda e: e.tensor_sub(
                sb["bln"][:, 1:2], sb["bln"][:, 1:2], sb["bln"][:, 2:3]),
                reads=[("bln", 1), ("bln", 2)], writes=[("bln", 1)])
            kb.op("act", lambda e: e.activation(
                out=sb["bln"][:, 1:2], in_=sb["bln"][:, 1:2], func=AF.Ln, bias=LNEPS),
                reads=[("bln", 1)], writes=[("bln", 1)])
            kb.op("act", lambda e: e.activation(
                out=sb["bln"][:, 1:2], in_=sb["bln"][:, 1:2], func=AF.Exp, scale=-0.5),
                reads=[("bln", 1)], writes=[("bln", 1)])
            kb.op("dve", lambda e: e.tensor_scalar(
                sb["bln"][:, 2:3], sb["bln"][:, 0:1], -1.0, None, op0=ALU.mult),
                reads=[("bln", 0)], writes=[("bln", 2)])
            kb.op("dve", lambda e: e.tensor_scalar(
                sb["bx1"][:], sb["br"][:], sb["bln"][:, 2:3], sb["bln"][:, 1:2],
                op0=ALU.add, op1=ALU.mult),
                reads=[("br",), ("bln", 1), ("bln", 2)], writes=[("bx1",)])
            kb.dma("qa", lambda e, ch=ch: e.dma_start(
                out=sp["x1"][(ch - 1) * C:(ch + 1) * C, :], in_=sb["bx1"][:]),
                reads=[("bx1",)], writes=[("spx1", ch // 2)])


Builder.alloc_b_arena = alloc_b_arena
Builder.phase_bscan = phase_bscan


def alloc_f1_arena(self):
    tl = self.cfg.tloc
    self.arena_reset()
    self.arena("x1fm", [128, 16 * tl], BF16)
    for par in (0, 1):
        self.arena("wgb%d" % par, [128, 16 * 128], BF16)
        self.arena("wub%d" % par, [128, 16 * 128], BF16)
        self.arena("xt%d" % par, [128, D], BF16)
    self.arena("gact", [128, tl], BF16)
    self.arena("uact", [128, tl], BF16)
    self.arena("gm0", [128, tl], BF16)
    self.arena("gm1", [128, tl], BF16)


def phase_f1(self):
    kb, sb, sp, din = self.kb, self.sb, self.spills, self.din
    tl, ntb = self.cfg.tloc, self.cfg.ntb
    pb = self.pb
    PSA = self.PSA
    njt = ceil_div(tl, 512)
    for tb in range(ntb):
        par = tb % 2
        xt = sb["xt%d" % par]
        kb.dma("qs", lambda e, tb=tb, xt=xt: e.dma_start(
            out=xt[:], in_=sp["x1"][tb * 128:(tb + 1) * 128, :]),
            reads=[("spx1", tb // 1)], writes=[("xt", par)])
        for half in range(2):
            for dc8 in range(8):
                dc = half * 8 + dc8
                kb.op("pe", lambda e, dc=dc, dc8=dc8, xt=xt: e.transpose(
                    pb(6).bitcast(BF16)[:, dc8 * 128:(dc8 + 1) * 128],
                    xt[:, dc * 128:(dc + 1) * 128], sb["ident"][:]),
                    reads=[("xt", par), ("sb", "ident")], writes=[("pb", 6)])
            kb.op("act", lambda e, tb=tb, half=half: e.activation(
                out=sb["x1fm"][:].rearrange("p (dc t) -> p dc t", t=tl)[
                    :, half * 8:(half + 1) * 8, tb * 128:(tb + 1) * 128],
                in_=pb(6).bitcast(BF16)[:, 0:1024].rearrange(
                    "p (dc t) -> p dc t", t=128),
                func=AF.Copy),
                reads=[("pb", 6) for i in range(8)], writes=[("x1fm", tb, half)])
    x1fm_rd = [("x1fm", tb, half) for tb in range(ntb) for half in range(2)]

    for hc in range(NHH):
        par = hc % 2
        wgb, wub = sb["wgb%d" % par], sb["wub%d" % par]
        kb.dma("qp", lambda e, hc=hc, wgb=wgb: e.dma_start(
            out=wgb[:], in_=din["WgT8"][hc * 128:(hc + 1) * 128, :]),
            writes=[("wgb", par)])
        kb.op("pool", lambda e, wgb=wgb: e.tensor_tensor(
            wgb[:], wgb[:], sb["c128"][:].broadcast_to([128, 16 * 128]), op=ALU.subtract),
            reads=[("wgb", par), ("sb", "c128")], writes=[("wgb", par)])
        kb.dma("qp", lambda e, hc=hc, wub=wub: e.dma_start(
            out=wub[:], in_=din["WuT8"][hc * 128:(hc + 1) * 128, :]),
            writes=[("wub", par)])
        kb.op("pool", lambda e, wub=wub: e.tensor_tensor(
            wub[:], wub[:], sb["c128"][:].broadcast_to([128, 16 * 128]), op=ALU.subtract),
            reads=[("wub", par), ("sb", "c128")], writes=[("wub", par)])
        for jt in range(njt):
            t0, tn = jt * 512, min(512, tl - jt * 512)
            for dc in range(16):
                kb.op("pe", lambda e, dc=dc, jt=jt, t0=t0, tn=tn, wgb=wgb: e.matmul(
                    PSA.ap()[:, jt * 512:jt * 512 + tn],
                    wgb[:, dc * 128:(dc + 1) * 128],
                    sb["x1fm"][:, dc * tl + t0: dc * tl + t0 + tn],
                    start=(dc == 0), stop=(dc == 15)),
                    reads=[("wgb", par)] + x1fm_rd, writes=[("pb", jt)])
            for dc in range(16):
                kb.op("pe", lambda e, dc=dc, jt=jt, t0=t0, tn=tn, wub=wub: e.matmul(
                    pb(4 + jt)[:, 0:tn],
                    wub[:, dc * 128:(dc + 1) * 128],
                    sb["x1fm"][:, dc * tl + t0: dc * tl + t0 + tn],
                    start=(dc == 0), stop=(dc == 15)),
                    reads=[("wub", par)] + x1fm_rd, writes=[("pb", 4 + jt)])
            kb.op("act", lambda e, hc=hc, jt=jt, t0=t0, tn=tn: e.activation(
                out=sb["gact"][:, t0:t0 + tn], in_=PSA.ap()[:, jt * 512:jt * 512 + tn],
                func=AF.Silu, scale=sb["sgm"][:, hc:hc + 1]),
                reads=[("pb", jt), ("sb", "sgm")], writes=[("gact", jt)])
            kb.op("act", lambda e, hc=hc, jt=jt, t0=t0, tn=tn: e.activation(
                out=sb["uact"][:, t0:t0 + tn], in_=pb(4 + jt)[:, 0:tn],
                func=AF.Copy, scale=sb["sum_"][:, hc:hc + 1]),
                reads=[("pb", 4 + jt), ("sb", "sum_")], writes=[("uact", jt)])
        gm = sb["gm%d" % par]
        kb.op("dve", lambda e, gm=gm: e.tensor_mul(gm[:], sb["gact"][:], sb["uact"][:]),
              reads=[("gact", jt) for jt in range(njt)] +
                    [("uact", jt) for jt in range(njt)],
              writes=[("gm", par)])
        kb.dma("qs", lambda e, hc=hc, gm=gm: e.dma_start(out=sp["G"][hc], in_=gm[:]),
               reads=[("gm", par)], writes=[("spG", hc)])


def alloc_f2_arena(self):
    tl, ntb = self.cfg.tloc, self.cfg.ntb
    self.arena_reset()
    self.arena("wdh", [128, NHH * 512], BF16)
    self.arena("gtb", [128, NHH * 128], BF16)
    self.arena("x1tb0", [128, 512], BF16)
    self.arena("x1tb1", [128, 512], BF16)
    self.arena("r2q0", [128, ntb * 512], BF16)
    self.arena("r2q1", [128, ntb * 512], BF16)
    self.arena("r2q2", [128, ntb * 512], BF16)
    self.arena("r2c", [128, 512], BF16)
    self.arena("outstg0", [128, D], F32)
    self.arena("fln", [128, 12], F32)
    self.arena("sdb", [128, 512], BF16)
    self.arena("sdrow", [1, D], BF16)


def phase_f2(self):
    kb, sb, sp, din = self.kb, self.sb, self.spills, self.din
    tl, ntb = self.cfg.tloc, self.cfg.ntb
    pb = self.pb
    PSA = self.PSA
    g3 = sp["G"]
    kb.dma("qp", lambda e: e.dma_start(out=sb["sdrow"][:], in_=din["sdr"][:]),
           writes=[("sdrow",)])
    for q in range(4):
        kb.op("pe", lambda e, q=q: e.matmul(
            PSA.ap()[:, 1536:2048], sb["ones1x128"][:],
            sb["sdrow"][:, q * 512:(q + 1) * 512], start=True, stop=True),
            reads=[("sdrow",), ("sb", "ones1x128")], writes=[("pb", 3)])
        kb.op("act", lambda e: e.activation(
            out=sb["sdb"][:], in_=PSA.ap()[:, 1536:2048], func=AF.Copy),
            reads=[("pb", 3)], writes=[("sdb",)])
        kb.dma("qp", lambda e, q=q: e.dma_start(
            out=sb["wdh"][:], in_=din["WdT8"][q * 128:(q + 1) * 128, :]),
            writes=[("wdh",)])
        for hc in range(NHH):
            kb.op("pool", lambda e, hc=hc: e.tensor_tensor(
                sb["wdh"][:, hc * 512:(hc + 1) * 512],
                sb["wdh"][:, hc * 512:(hc + 1) * 512],
                sb["c128"][:].broadcast_to([128, 512]), op=ALU.subtract),
                reads=[("wdh",), ("sb", "c128")], writes=[("wdh",)])
            kb.op("pool", lambda e, hc=hc: e.tensor_tensor(
                sb["wdh"][:, hc * 512:(hc + 1) * 512],
                sb["wdh"][:, hc * 512:(hc + 1) * 512], sb["sdb"][:], op=ALU.mult),
                reads=[("wdh",), ("sdb",)], writes=[("wdh",)])
        for tb in range(ntb):
            par = tb % 2
            xt = sb["x1tb%d" % par]
            kb.dma("qs", lambda e, tb=tb: e.dma_start(
                out=sb["gtb"][:], in_=g3[:, :, tb * 128:(tb + 1) * 128].rearrange(
                    "hc p t -> p hc t")),
                reads=[("spG", hc) for hc in range(NHH)], writes=[("gtb",)])
            kb.dma("qa", lambda e, tb=tb, xt=xt, q=q: e.dma_start(
                out=xt[:], in_=sp["x1"][tb * 128:(tb + 1) * 128, q * 512:(q + 1) * 512]),
                reads=[("spx1", tb)], writes=[("x1tb", par)])
            bk = tb % 2
            for hc in range(NHH):
                kb.op("pe", lambda e, hc=hc, bk=bk: e.matmul(
                    PSA.ap()[:, bk * 512:(bk + 1) * 512],
                    sb["gtb"][:, hc * 128:(hc + 1) * 128],
                    sb["wdh"][:, hc * 512:(hc + 1) * 512],
                    start=(hc == 0), stop=(hc == NHH - 1)),
                    reads=[("gtb",), ("wdh",)], writes=[("pb", bk)])
            if q < 3:
                kb.op("dve", lambda e, tb=tb, bk=bk, xt=xt, q=q: e.tensor_add(
                    sb["r2q%d" % q][:, tb * 512:(tb + 1) * 512],
                    PSA.ap()[:, bk * 512:(bk + 1) * 512], xt[:]),
                    reads=[("pb", bk), ("x1tb", par)], writes=[("r2q", q, tb)])
            else:
                kb.op("dve", lambda e, tb=tb, bk=bk, xt=xt: e.tensor_add(
                    sb["r2c"][:], PSA.ap()[:, bk * 512:(bk + 1) * 512], xt[:]),
                    reads=[("pb", bk), ("x1tb", par)], writes=[("r2c",)])
                parts = [sb["r2q0"][:, tb * 512:(tb + 1) * 512],
                         sb["r2q1"][:, tb * 512:(tb + 1) * 512],
                         sb["r2q2"][:, tb * 512:(tb + 1) * 512],
                         sb["r2c"][:]]
                rregs = [("r2q", 0, tb), ("r2q", 1, tb), ("r2q", 2, tb), ("r2c",)]
                for i4, (p4, rr) in enumerate(zip(parts, rregs)):
                    kb.op("dve", lambda e, p4=p4, i4=i4: e.reduce_sum(
                        sb["fln"][:, i4:i4 + 1], p4.unsqueeze(1), axis=AX.X),
                        reads=[rr], writes=[("fln", i4)])
                    kb.op("act", lambda e, p4=p4, i4=i4: e.activation(
                        out=pb(4 + i4)[:], in_=p4, func=AF.Square),
                        reads=[rr, ("pb", 4 + i4)], writes=[("pb", 4 + i4)])
                    kb.op("dve", lambda e, i4=i4: e.reduce_sum(
                        sb["fln"][:, 4 + i4:5 + i4], pb(4 + i4)[:].unsqueeze(1), axis=AX.X),
                        reads=[("pb", 4 + i4)], writes=[("fln", 4 + i4)])
                for i4 in (1, 2, 3):
                    kb.op("dve", lambda e, i4=i4: e.tensor_add(
                        sb["fln"][:, 0:1], sb["fln"][:, 0:1], sb["fln"][:, i4:i4 + 1]),
                        reads=[("fln", 0), ("fln", i4)], writes=[("fln", 0)])
                    kb.op("dve", lambda e, i4=i4: e.tensor_add(
                        sb["fln"][:, 4:5], sb["fln"][:, 4:5], sb["fln"][:, 4 + i4:5 + i4]),
                        reads=[("fln", 4), ("fln", 4 + i4)], writes=[("fln", 4)])
                kb.op("dve", lambda e: e.tensor_scalar(
                    sb["fln"][:, 0:1], sb["fln"][:, 0:1], 1.0 / D, None, op0=ALU.mult),
                    reads=[("fln", 0)], writes=[("fln", 0)])
                kb.op("dve", lambda e: e.tensor_scalar(
                    sb["fln"][:, 4:5], sb["fln"][:, 4:5], 1.0 / D, None, op0=ALU.mult),
                    reads=[("fln", 4)], writes=[("fln", 4)])
                kb.op("dve", lambda e: e.tensor_mul(
                    sb["fln"][:, 8:9], sb["fln"][:, 0:1], sb["fln"][:, 0:1]),
                    reads=[("fln", 0)], writes=[("fln", 8)])
                kb.op("dve", lambda e: e.tensor_sub(
                    sb["fln"][:, 4:5], sb["fln"][:, 4:5], sb["fln"][:, 8:9]),
                    reads=[("fln", 4), ("fln", 8)], writes=[("fln", 4)])
                kb.op("act", lambda e: e.activation(
                    out=sb["fln"][:, 4:5], in_=sb["fln"][:, 4:5], func=AF.Ln, bias=LNEPS),
                    reads=[("fln", 4)], writes=[("fln", 4)])
                kb.op("act", lambda e: e.activation(
                    out=sb["fln"][:, 4:5], in_=sb["fln"][:, 4:5], func=AF.Exp, scale=-0.5),
                    reads=[("fln", 4)], writes=[("fln", 4)])
                kb.op("dve", lambda e: e.tensor_scalar(
                    sb["fln"][:, 8:9], sb["fln"][:, 0:1], -1.0, None, op0=ALU.mult),
                    reads=[("fln", 0)], writes=[("fln", 8)])
                for i4, (p4, rr) in enumerate(zip(parts, rregs)):
                    kb.op("dve", lambda e, p4=p4, i4=i4: e.tensor_scalar(
                        sb["outstg0"][:, i4 * 512:(i4 + 1) * 512], p4,
                        sb["fln"][:, 8:9], sb["fln"][:, 4:5],
                        op0=ALU.add, op1=ALU.mult),
                        reads=[rr, ("fln", 4), ("fln", 8)], writes=[("ost", i4)])
                kb.dma("qa", lambda e, tb=tb: e.dma_start(
                    out=self.out_t[tb * 128:(tb + 1) * 128, :], in_=sb["outstg0"][:]),
                    reads=[("ost", 0), ("ost", 1), ("ost", 2), ("ost", 3)],
                    writes=[("outd", tb)])


Builder.alloc_f1_arena = alloc_f1_arena
Builder.phase_f1 = phase_f1
Builder.alloc_f2_arena = alloc_f2_arena
Builder.phase_f2 = phase_f2


def kernel(**inputs):
    cfg = Cfg(tloc=T // NCORES, ncores=NCORES)
    in_maps = host_prep(inputs, cfg)
    b = build_program(cfg)
    res = run_bass_kernel_spmd(b.nc, in_maps, core_ids=list(range(NCORES)))
    out = np.concatenate([np.asarray(res.results[ci]["out"]) for ci in range(NCORES)],
                         axis=0)
    return out.reshape(1, T, D).astype(np.float32)


# version 10
# speedup vs baseline: 1.8893x; 1.7006x over previous
import numpy as np
import ml_dtypes
import concourse.bass as bass
import concourse.mybir as mybir
from concourse.bass_utils import run_bass_kernel_spmd

F32 = mybir.dt.float32
BF16 = mybir.dt.bfloat16
I32 = mybir.dt.int32
U8 = mybir.dt.uint8
AF = mybir.ActivationFunctionType
ALU = mybir.AluOpType
AX = mybir.AxisListType
BF = ml_dtypes.bfloat16

NCORES = 8
T = 16384
D = 2048
H = 16
DK = 128
M = 128
KW = 4
C = 64
HB = 64
HID = 5632
NHH = HID // 128
SCALE = DK ** -0.5
EPS = 1e-5
L2EPS = 1e-6
LNEPS = 1e-5

ENGINES = ("pe", "act", "dve", "pool", "sp")
PSUM_COARSE = False
QUEUES = ("qs", "qa", "qp")


class KB:

    def __init__(self, nc):
        self.nc = nc
        self.q = {e: [] for e in ENGINES}
        self.cnt = {e: 0 for e in ENGINES}
        self.dcnt = {qn: 0 for qn in QUEUES}
        self.wm = {e: {} for e in ENGINES}
        self.lastw = {}
        self.readers = {}
        self.sems = {}
        for s in ("pe", "act", "dve", "pool", "qs", "qa", "qp"):
            self.sems[s] = nc.alloc_semaphore("sem_" + s)
        self.dma_engine = {"qs": "sp", "qa": "act", "qp": "pool"}
        self.fence = {qn: 0 for qn in QUEUES}

    def _sem_val(self, tok):
        s, v = tok
        if s in QUEUES:
            return (s, 16 * self._fence_for(s, v))
        return (s, v)

    def _fence_for(self, qn, k):
        if self.fence[qn] >= k:
            return self.fence[qn]
        n = self.dcnt[qn]
        eng = self.dma_engine[qn]
        sems = self.sems
        self.q[eng].append(lambda e, qn=qn, n=n: e.wait_ge(sems[qn], 16 * n))
        self.wm[eng][qn] = 16 * n
        self.fence[qn] = n
        return n

    def _edges(self, eng, reads, writes):
        need = {}
        for r in reads:
            t = self.lastw.get(r)
            if t is not None:
                s, v = self._sem_val(t)
                need[s] = max(need.get(s, 0), v)
        for w in writes:
            t = self.lastw.get(w)
            if t is not None:
                s, v = self._sem_val(t)
                need[s] = max(need.get(s, 0), v)
            for t in self.readers.get(w, ()):
                s, v = self._sem_val(t)
                need[s] = max(need.get(s, 0), v)
        wmap = self.wm[eng]
        out = []
        for s, v in need.items():
            if eng == "pe" and s == "pe":
                continue
            if wmap.get(s, 0) < v:
                out.append((s, v))
                wmap[s] = v
        return out

    def op(self, eng, fn, reads=(), writes=(), sem_extra=None):
        reads = list(reads)
        writes = list(writes)
        for r in list(reads):
            if isinstance(r[0], str) and r[0].startswith("pb"):
                reads.remove(r)
                if r not in writes:
                    writes.append(r)
        if PSUM_COARSE:
            nw = []
            coarse = False
            for w in writes:
                if isinstance(w[0], str) and w[0].startswith("pb"):
                    coarse = True
                else:
                    nw.append(w)
            if coarse:
                nw.append(("PSUM",))
            writes = nw
        waits = self._edges(eng, reads, writes)
        self.cnt[eng] += 1
        tok = (eng, self.cnt[eng])
        sem = self.sems[eng]
        sems = self.sems

        def emit(e, waits=waits, fn=fn, sem=sem, sem_extra=sem_extra):
            for s, v in waits[1:]:
                e.wait_ge(sems[s], v)
            inst = fn(e)
            if waits:
                inst._wait_ge(sems[waits[0][0]], waits[0][1])
            inst.then_inc(sem, 1)
            if sem_extra is not None:
                inst.then_inc(sem_extra[0], sem_extra[1])
        self.q[eng].append(emit)
        for w in writes:
            self.lastw[w] = tok
            self.readers[w] = []
        for r in reads:
            self.readers.setdefault(r, []).append(tok)
        return tok

    def dma(self, qn, fn, reads=(), writes=()):
        eng = self.dma_engine[qn]
        waits = self._edges(eng, reads, writes)
        self.dcnt[qn] += 1
        tok = (qn, self.dcnt[qn])
        sem = self.sems[qn]
        sems = self.sems

        def emit(e, waits=waits, fn=fn, sem=sem):
            for s, v in waits[1:]:
                e.wait_ge(sems[s], v)
            inst = fn(e)
            if waits:
                inst._wait_ge(sems[waits[0][0]], waits[0][1])
            inst.then_inc(sem, 16)
        self.q[eng].append(emit)
        for w in writes:
            self.lastw[w] = tok
            self.readers[w] = []
        for r in reads:
            self.readers.setdefault(r, []).append(tok)
        return tok

    def barrier_wait(self, eng, toks):
        need = {}
        for t in toks:
            s, v = self._sem_val(t)
            need[s] = max(need.get(s, 0), v)
        wmap = self.wm[eng]
        sems = self.sems
        for s, v in need.items():
            if wmap.get(s, 0) < v:
                wmap[s] = v
                self.q[eng].append(lambda e, s=s, v=v: e.wait_ge(sems[s], v))

    def finish(self):
        toks = [(e, self.cnt[e]) for e in ENGINES if e != "sp" and self.cnt[e]]
        toks += [(qn, self.dcnt[qn]) for qn in QUEUES if self.dcnt[qn]]
        for e in ENGINES:
            self.barrier_wait(e, toks)

    def replay(self):
        nc = self.nc
        with nc.Block() as block:
            @block.tensor
            def _(e):
                for f in self.q["pe"]:
                    f(e)

            @block.scalar
            def _(e):
                for f in self.q["act"]:
                    f(e)

            @block.vector
            def _(e):
                for f in self.q["dve"]:
                    f(e)

            @block.gpsimd
            def _(e):
                for f in self.q["pool"]:
                    f(e)

            @block.sync
            def _(e):
                for f in self.q["sp"]:
                    f(e)


def ceil_div(a, b):
    return (a + b - 1) // b



class Cfg:

    def __init__(self, tloc=2048, phases=("aproj", "az", "amisc", "ascan", "aagg",
                                          "gather", "bscan", "f1", "f2"),
                 ncores=NCORES, debug_taps=(), host_gather=False):
        self.tloc = tloc
        self.nch = tloc // C
        self.ntb = tloc // 128
        self.phases = phases
        self.ncores = ncores
        self.debug_taps = debug_taps
        self.host_gather = host_gather


class Builder:
    def __init__(self, cfg):
        self.cfg = cfg
        self.nc = bass.Bass("TRN2", target_bir_lowering=False)
        self.kb = KB(self.nc)
        self.din = {}
        self.spills = {}
        self.sb = {}
        self.ps = {}
        self.ar_off = 0
        self._declare_dram()
        self._declare_sbuf()

    def inp(self, name, shape, dtype=F32):
        self.din[name] = self.nc.dram_tensor(name, list(shape), dtype,
                                             kind="ExternalInput").ap()
        return self.din[name]

    def _declare_dram(self):
        cfg, tl, nch = self.cfg, self.cfg.tloc, self.cfg.nch
        inp = self.inp
        inp("hT", [D, HB + tl])
        inp("h", [tl, D])
        for w in ("Wq", "Wk", "Wv", "Ww", "Wo"):
            inp(w, [D, D])
        inp("cqr", [128, 16 * KW]); inp("ckr", [128, 16 * KW]); inp("cvr", [128, 16 * KW])
        inp("Wf1", [D, 128]); inp("Wf2", [128, D]); inp("Wbr", [128, 16 * H])
        inp("Wg1", [D, 128]); inp("Wg2", [128, D])
        inp("bg2m", [128, 16]); inp("normw", [128, 1])
        inp("WgT8", [NHH * 128, D], U8); inp("WuT8", [NHH * 128, D], U8)
        inp("WdT8", [4 * 128, NHH * 512], U8)
        inp("sgm", [128, NHH]); inp("sum_", [128, NHH]); inp("sdr", [1, D])
        inp("ident", [128, 128], BF16)
        inp("identf", [128, 128])
        inp("tri", [C, C], BF16)
        inp("sumsels", [128, 16 * 32], BF16)
        inp("trige", [32, 32])
        inp("ones32", [32, 1])
        inp("ones1x128", [1, 128], BF16)
        inp("ones128", [128, 1], BF16)
        inp("mselM", [2, (NCORES - 1) * 128], BF16)
        inp("mselcol", [128, NCORES - 1])
        inp("m1col", [128, NCORES - 1])
        self.out_t = self.nc.dram_tensor("out", [tl, D], F32, kind="ExternalOutput").ap()
        sp = self.spills

        def spill(name, shape, dtype=BF16, kind="Internal"):
            sp[name] = self.nc.dram_tensor("sp_" + name, list(shape), dtype, kind=kind).ap()
        spill("q", [H, 128, tl])
        spill("kfm", [H, 128, tl])
        spill("ktm", [H, tl, 128])
        spill("vtm", [H, tl, 128])
        spill("wtm", [H, tl, 128])
        spill("g", [tl, D])
        spill("gs", [H, 128, tl])
        spill("u", [tl, D])
        spill("si", [tl, D])
        spill("x1", [tl, D])
        spill("G", [NHH, 128, tl])
        spill("ec", [nch, D])
        spill("wsuf", [nch, D])
        kci = "ExternalOutput" if cfg.host_gather else "Internal"
        kco = "ExternalInput" if cfg.host_gather else "Internal"
        spill("ccin", [257, D], BF16, kind=kci)
        spill("ccout", [cfg.ncores * 257, D], BF16, kind=kco)
        self.dbg = {}
        for name in cfg.debug_taps:
            s = sp[name]
            self.dbg[name] = self.nc.dram_tensor(
                "dbg_" + name, list(s.shape), s.dtype, kind="ExternalOutput").ap()

    def sbuf(self, name, shape, dtype=BF16):
        t = self.nc.alloc_sbuf_tensor("sb_" + name, list(shape), dtype)
        self.sb[name] = t
        return t

    def arena(self, name, shape, dtype):
        esz = {F32: 4, BF16: 2, U8: 1, I32: 4}[dtype]
        nbytes = shape[1] * esz
        off = self.ar_off
        assert off + nbytes <= self.AR_BYTES, f"arena overflow {name}: {off}+{nbytes}"
        ap = self.AR.ap().bitcast(dtype)[0:shape[0], off // esz: off // esz + shape[1]]
        self.ar_off = off + ((nbytes + 31) // 32) * 32
        self.sb[name] = ap
        return ap

    def arena_reset(self):
        self.ar_off = 0

    def _declare_sbuf(self):
        cfg, tl, nch = self.cfg, self.cfg.tloc, self.cfg.nch
        self.AR_BYTES = 156 * 1024
        self.AR = self.nc.alloc_sbuf_tensor("sb_AR", [128, self.AR_BYTES], U8)
        sbuf = self.sbuf
        sbuf("ident", [128, 128], BF16)
        sbuf("m1col", [128, NCORES - 1], F32)
        sbuf("tri", [C, C], BF16)
        sbuf("ones1x128", [1, 128], BF16)
        sbuf("ones128", [128, 1], BF16)
        sbuf("beta", [128, (tl // 128) * H], BF16)
        sbuf("beta2", [64, 2 * (tl // 128) * H], BF16)
        sbuf("cCrows", [32, D], F32)
        sbuf("ec16all", [16, nch * 128], BF16)
        sbuf("Sk", [128, H * M], F32)
        sbuf("Sv", [128, H * DK], F32)
        sbuf("Skbf", [128, H * M], BF16)
        sbuf("Svbf", [128, H * DK], BF16)
        sbuf("ecct", [128, 16], F32)
        sbuf("normw", [128, 1], F32)
        sbuf("bg2m", [128, 16], F32)
        sbuf("cw", [128, 3 * 16 * KW], F32)
        sbuf("sumsels", [128, 16 * 32], BF16)
        sbuf("trige", [32, 32], F32)
        sbuf("ones32", [32, 1], F32)
        sbuf("sgm", [128, NHH], F32)
        sbuf("sum_", [128, NHH], F32)
        sbuf("mselM", [2, (NCORES - 1) * 128], BF16)
        sbuf("mselcol", [128, NCORES - 1], F32)
        sbuf("row2", [2, D], BF16)
        self.CVALS = [L2EPS, LNEPS, EPS, SCALE, -0.5, -1.0, 0.5, 2.0, 1.0 / 128, -128.0,
                      1.0 / D, 1.0]
        sbuf("cvals", [128, len(self.CVALS)], F32)
        sbuf("c128", [128, 1], BF16)
        self.PSA = self.nc.alloc_psum_tensor("psA", [128, 2048], F32)
        self.PBH = {i: self.nc.alloc_psum_tensor("ps%d" % i, [128, 512], F32)
                    for i in range(4, 8)}

    def pb(self, i):
        if i < 4:
            return self.PSA.ap()[:, i * 512:(i + 1) * 512]
        return self.PBH[i].ap()

    def ld_consts(self):
        kb, din, sb = self.kb, self.din, self.sb
        pairs = [("ident", "ident"), ("tri", "tri"), ("ones1x128", "ones1x128"),
                 ("ones128", "ones128"), ("normw", "normw"), ("bg2m", "bg2m"),
                 ("sumsels", "sumsels"), ("trige", "trige"), ("ones32", "ones32"),
                 ("m1col", "m1col"),
                 ("sgm", "sgm"), ("sum_", "sum_"), ("mselcol", "mselcol")]
        for dn, sn in pairs:
            kb.dma("qs", lambda e, d=din[dn], s=sb[sn]: e.dma_start(out=s[:], in_=d[:]),
                   writes=[("sb", sn)])
        kb.dma("qs", lambda e: e.dma_start(out=sb["mselM"][:], in_=din["mselM"][:]),
               writes=[("sb", "mselM")])
        for i, dn in enumerate(("cqr", "ckr", "cvr")):
            kb.dma("qs", lambda e, d=din[dn], i=i: e.dma_start(
                out=sb["cw"][:, i * 64:(i + 1) * 64], in_=d[:]),
                writes=[("sb", "cw", i)])
        kb.op("dve", lambda e: e.memset(sb["row2"][:], 1.0), writes=[("sb", "row2one")])
        kb.op("dve", lambda e: e.memset(sb["c128"][:], 128.0), writes=[("sb", "c128")])
        for i, v in enumerate(self.CVALS):
            kb.op("dve", lambda e, i=i, v=v: e.memset(sb["cvals"][:, i:i + 1], v),
                  writes=[("cvals", i)])
            self.nc.const_aps.register(F32, v, sb["cvals"][:, i:i + 1])

    def alloc_a_arena(self):
        tl = self.cfg.tloc
        self.arena_reset()
        self.arena("hfm", [128, 16 * (HB + tl)], BF16)
        self.arena("wpan0", [128, 16 * 128], BF16)
        self.arena("wpan1", [128, 16 * 128], BF16)
        self.arena("pbf0", [128, HB + tl], BF16)
        self.arena("pbf1", [128, HB + tl], BF16)
        self.arena("convy0", [128, tl], BF16)
        self.arena("convy1", [128, tl], BF16)
        self.arena("tstg0", [128, tl], BF16)
        self.arena("tstg1", [128, tl], BF16)
        self.arena("z1", [128, tl], BF16)
        self.arena("zg1", [128, tl], BF16)
        self.arena("wsmall", [128, 2 * 16 * 128], BF16)
        self.arena("wf2", [128, D], BF16)
        self.arena("wg2", [128, D], BF16)
        self.arena("wbr", [128, 16 * H], BF16)
        self.arena("ztmp0", [128, 512], F32)
        self.arena("ztmp1", [128, 512], F32)
        self.arena("gstg", [128, D], BF16)
        self.arena("ecall", [32, D], BF16)
        self.arena("wsufb", [32, D], BF16)

    def jgroups(self):
        tl = self.cfg.tloc
        return [(0, HB)] + [(HB + i * 512, min(512, tl - i * 512))
                            for i in range(ceil_div(tl, 512))]

    def ld_hfm(self):
        kb, sb, din = self.kb, self.sb, self.din
        tl = self.cfg.tloc
        for dc in range(16):
            kb.dma("qp", lambda e, dc=dc: e.dma_start(
                out=sb["hfm"][:, dc * (HB + tl):(dc + 1) * (HB + tl)],
                in_=din["hT"][dc * 128:(dc + 1) * 128, :]),
                writes=[("hfm", dc)])

    def phase_aproj(self):
        kb, sb, din, sp = self.kb, self.sb, self.din, self.spills
        tl, ntb = self.cfg.tloc, self.cfg.ntb
        pb = self.pb
        HT = HB + tl
        tensors = [("q", "Wq", 0, ("q",), False),
                   ("k", "Wk", 1, ("kfm", "ktm"), True),
                   ("v", "Wv", 2, ("vtm",), True),
                   ("w", "Ww", 2, ("wtm",), True)]
        for xi, (xn, wn, cwi, spnames, want_tm) in enumerate(tensors):
            for fc in range(16):
                par = fc % 2
                wp = sb["wpan%d" % par]
                kb.dma("qp", lambda e, wn=wn, fc=fc, wp=wp: e.dma_start(
                    out=wp[:], in_=din[wn][fc * 128:(fc + 1) * 128, :]),
                    writes=[("wpan", par)])
                pbf = sb["pbf%d" % par]
                for jg, (t0, tn) in enumerate(self.jgroups()):
                    bank = jg % 2
                    for dc in range(16):
                        kb.op("pe", lambda e, wp=wp, dc=dc, t0=t0, tn=tn, bank=bank: e.matmul(
                            pb(bank)[:, 0:tn],
                            wp[:, dc * 128:(dc + 1) * 128],
                            sb["hfm"][:, dc * HT + t0: dc * HT + t0 + tn],
                            start=(dc == 0), stop=(dc == 15)),
                            reads=[("wpan", par), ("hfm", dc)], writes=[("pb", bank)])
                    kb.op("act", lambda e, t0=t0, tn=tn, bank=bank, pbf=pbf: e.activation(
                        out=pbf[:, t0:t0 + tn], in_=pb(bank)[:, 0:tn], func=AF.Copy),
                        reads=[("pb", bank)], writes=[("pbf", par, jg)])
                cy = sb["convy%d" % par]
                pr = [("pbf", par, jg) for jg in range(len(self.jgroups()))]
                cwbase = cwi * 64 + fc * KW
                kb.op("dve", lambda e, cy=cy, pbf=pbf,
                      cc=sb["cw"][:, cwbase + 3:cwbase + 4]: e.tensor_scalar(
                    cy[:], pbf[:, HB:HB + tl], cc, None, op0=ALU.mult),
                    reads=pr + [("sb", "cw", cwi)], writes=[("convy", par)])
                for i, eng in ((2, "dve"), (1, "dve"), (0, "dve")):
                    kb.op(eng, lambda e, i=i, cy=cy, pbf=pbf,
                          cc=sb["cw"][:, cwbase + i:cwbase + i + 1]: e.scalar_tensor_tensor(
                        cy[:], pbf[:, HB - 3 + i:HB - 3 + i + tl], cc, cy[:],
                        op0=ALU.mult, op1=ALU.add),
                        reads=pr + [("convy", par)], writes=[("convy", par)])
                kb.op("act", lambda e, cy=cy: e.activation(out=cy[:], in_=cy[:], func=AF.Silu),
                      reads=[("convy", par)], writes=[("convy", par)])
                for nm in spnames:
                    if nm in ("q", "kfm"):
                        kb.dma("qs", lambda e, nm=nm, fc=fc, cy=cy: e.dma_start(
                            out=sp[nm][fc], in_=cy[:]),
                            reads=[("convy", par)], writes=[("sp" + nm, fc)])
                if want_tm:
                    nmtm = spnames[-1]
                    tstg = sb["tstg%d" % par]
                    for tb in range(ntb):
                        tbank = 2 + (tb // 8) % 2
                        kb.op("pe", lambda e, tb=tb, cy=cy, tbank=tbank: e.transpose(
                            pb(tbank).bitcast(BF16)[:, (tb % 8) * 128:(tb % 8) * 128 + 128],
                            cy[:, tb * 128:(tb + 1) * 128], sb["ident"][:]),
                            reads=[("convy", par), ("sb", "ident")],
                            writes=[("pbT", tbank, tb % 8)])
                        if tb % 8 == 7 or tb == ntb - 1:
                            lo = (tb // 8) * 8
                            kb.op("act", lambda e, lo=lo, tb=tb, tstg=tstg, tbank=tbank:
                                  e.activation(
                                      out=tstg[:, lo * 128:(tb + 1) * 128],
                                      in_=pb(tbank).bitcast(BF16)[:, 0:(tb + 1 - lo) * 128],
                                      func=AF.Copy),
                                  reads=[("pbT", tbank, i) for i in range(tb + 1 - lo)],
                                  writes=[("tstg", par, tb // 8)])
                    kb.dma("qs", lambda e, nmtm=nmtm, fc=fc, tstg=tstg: e.dma_start(
                        out=sp[nmtm][fc].rearrange("(tb p) f -> p tb f", p=128),
                        in_=tstg[:, 0:tl]),
                        reads=[("tstg", par, i) for i in range(ceil_div(ntb, 8))],
                        writes=[("sp" + nmtm, fc)])


def build_program(cfg):
    b = Builder(cfg)
    b.ld_consts()
    ph = cfg.phases
    if "aproj" in ph or "az" in ph or "amisc" in ph:
        b.alloc_a_arena()
        b.ld_hfm()
    for name in ("aproj", "az", "amisc", "ascan", "aagg", "gather", "bscan", "f1", "f2"):
        if name not in ph:
            continue
        if name == "ascan":
            b.alloc_ascan_arena()
        if name == "aagg":
            b.alloc_aagg_arena()
        if name == "bscan":
            b.alloc_b_arena()
        if name == "f1":
            b.alloc_f1_arena()
        if name == "f2":
            b.alloc_f2_arena()
        b.phase_fence()
        getattr(b, "phase_" + name)()
    b.kb.finish()
    for name, dst in b.dbg.items():
        srcap = b.spills[name]
        b.kb.dma("qa", lambda e, srcap=srcap, dst=dst: e.dma_start(
            out=dst.rearrange(_flat_str(dst.shape)), in_=srcap.rearrange(_flat_str(srcap.shape))))
    if b.dbg:
        b.kb.finish()
    b.kb.replay()
    return b


def _flat_str(shape):
    letters = "abcdef"[:len(shape)]
    return " ".join(letters) + " -> (" + " ".join(letters) + ")"


def host_prep(inputs, cfg):
    tl, n = cfg.tloc, cfg.ncores
    f32 = np.float32
    h_full = np.asarray(inputs["L0_attn_hidden_states"], f32)[0]
    hT_full = np.ascontiguousarray(h_full.T)
    zeros_halo = np.zeros((D, HB), f32)

    def u8_low_bytes(a_int32):
        a = np.ascontiguousarray(a_int32.astype(np.int32, copy=False))
        return np.ascontiguousarray(a.view(np.uint8).reshape(*a.shape, 4)[..., 0])

    g = {}
    def panelize(w, pf=128):
        d1, d2 = w.shape
        return np.ascontiguousarray(
            w.reshape(d1 // 128, 128, d2 // pf, pf).transpose(2, 1, 0, 3).reshape(d2 // pf * 128, d1 // 128 * pf))
    g["Wq"] = panelize(np.asarray(inputs["L0_attn_Wq"], f32))
    g["Wk"] = panelize(np.asarray(inputs["L0_attn_Wk"], f32))
    g["Wv"] = panelize(np.asarray(inputs["L0_attn_Wv"], f32))
    g["Ww"] = panelize(np.asarray(inputs["L0_attn_Ww"], f32))
    g["Wo"] = np.asarray(inputs["L0_attn_Wo"], f32)
    for cn, dn in (("cqr", "L0_attn_cq"), ("ckr", "L0_attn_ck"), ("cvr", "L0_attn_cv")):
        c = np.asarray(inputs[dn], f32)
        g[cn] = np.ascontiguousarray(c.reshape(16, 128, KW).transpose(1, 0, 2).reshape(128, 16 * KW))
    g["Wf1"] = np.asarray(inputs["L0_attn_Wf1"], f32)
    g["Wf2"] = np.asarray(inputs["L0_attn_Wf2"], f32)
    wb = np.asarray(inputs["L0_attn_Wb"], f32)
    g["Wbr"] = np.ascontiguousarray(wb.reshape(16, 128, H).transpose(1, 0, 2).reshape(128, 16 * H))
    g["Wg1"] = np.asarray(inputs["L0_attn_Wg1"], f32)
    g["Wg2"] = np.asarray(inputs["L0_attn_Wg2"], f32)
    bg2 = np.asarray(inputs["L0_attn_bg2"], f32)
    g["bg2m"] = np.ascontiguousarray(bg2.reshape(16, 128).T)
    g["normw"] = np.ascontiguousarray(np.asarray(inputs["L0_attn_norm_w"], f32).reshape(128, 1))
    g["WgT8"] = panelize(u8_low_bytes(np.ascontiguousarray(np.asarray(inputs["L0_ffn_wq_gate"]).T)))
    g["WuT8"] = panelize(u8_low_bytes(np.ascontiguousarray(np.asarray(inputs["L0_ffn_wq_up"]).T)))
    g["WdT8"] = panelize(u8_low_bytes(np.ascontiguousarray(np.asarray(inputs["L0_ffn_wq_down"]).T)), pf=512)
    g["sgm"] = np.ascontiguousarray(np.asarray(inputs["L0_ffn_s_gate"], f32).reshape(NHH, 128).T)
    g["sum_"] = np.ascontiguousarray(np.asarray(inputs["L0_ffn_s_up"], f32).reshape(NHH, 128).T)
    g["sdr"] = np.ascontiguousarray(np.asarray(inputs["L0_ffn_s_down"], f32).reshape(1, D))
    g["ident"] = np.eye(128, dtype=BF)
    g["identf"] = np.eye(128, dtype=f32)
    sl = np.arange(C)
    g["tri"] = (sl[:, None] <= sl[None, :]).astype(BF)
    ss = np.zeros((128, 16 * 32), BF)
    for tb in range(16):
        for t in range(128):
            ss[t, tb * 32 + 2 * tb + (t // 64)] = 1
    g["sumsels"] = ss
    jj = np.arange(32)
    g["trige"] = (jj[:, None] >= jj[None, :]).astype(f32)
    g["ones32"] = np.ones((32, 1), f32)
    g["ones1x128"] = np.ones((1, 128), BF)
    g["ones128"] = np.ones((128, 1), BF)

    in_maps = []
    for ci in range(n):
        m = dict(g)
        s = ci * tl
        m["h"] = h_full[s:s + tl]
        if s >= HB:
            m["hT"] = np.ascontiguousarray(hT_full[:, s - HB:s + tl])
        else:
            m["hT"] = np.ascontiguousarray(
                np.concatenate([zeros_halo[:, :HB - s], hT_full[:, 0:s + tl]], axis=1))
        msel = np.zeros((2, (NCORES - 1) * 128), BF)
        mcol = np.zeros((128, NCORES - 1), f32)
        for j in range(NCORES - 1):
            sel = 1.0 if j < ci else 0.0
            msel[0, j * 128:(j + 1) * 128] = sel
            msel[1, j * 128:(j + 1) * 128] = 1.0 - sel
            mcol[:, j] = sel
        m["mselM"] = msel
        m["mselcol"] = mcol
        m["m1col"] = np.ascontiguousarray(1.0 - mcol)
        if cfg.host_gather:
            m["sp_ccout"] = np.zeros((cfg.ncores * 257, D), BF)
        in_maps.append(m)
    return in_maps


AZ_CUT = 0


def phase_az(self):
    kb, sb, din, sp = self.kb, self.sb, self.din, self.spills
    tl, ntb, nch = self.cfg.tloc, self.cfg.ntb, self.cfg.nch
    pb = self.pb
    HT = HB + tl
    wf1_4 = din["Wf1"].rearrange("(dc p) f -> p dc f", p=128)
    kb.dma("qp", lambda e: e.dma_start(out=sb["wsmall"][:, 0:16 * 128], in_=wf1_4),
           writes=[("wsmall", 0)])
    kb.dma("qp", lambda e: e.dma_start(out=sb["wf2"][:], in_=din["Wf2"][:]),
           writes=[("wf2",)])
    for jg in range(ceil_div(tl, 512)):
        t0, tn = jg * 512, min(512, tl - jg * 512)
        bank = jg % 2
        for dc in range(16):
            kb.op("pe", lambda e, dc=dc, t0=t0, tn=tn, bank=bank: e.matmul(
                pb(bank)[:, 0:tn],
                sb["wsmall"][:, dc * 128:(dc + 1) * 128],
                sb["hfm"][:, dc * HT + HB + t0: dc * HT + HB + t0 + tn],
                start=(dc == 0), stop=(dc == 15)),
                reads=[("wsmall", 0), ("hfm", dc)], writes=[("pb", bank)])
        kb.op("act", lambda e, t0=t0, tn=tn, bank=bank: e.activation(
            out=sb["z1"][:, t0:t0 + tn], in_=pb(bank)[:, 0:tn], func=AF.Copy),
            reads=[("pb", bank)], writes=[("z1", jg)])
    if AZ_CUT >= 4:
        return
    njt = ceil_div(D, 512)
    for tb in range(ntb):
        for jt in range(njt):
            bank = 2 + (jt % 2)
            kb.op("pe", lambda e, tb=tb, jt=jt, bank=bank: e.matmul(
                pb(bank)[:],
                sb["z1"][:, tb * 128:(tb + 1) * 128],
                sb["wf2"][:, jt * 512:(jt + 1) * 512],
                start=True, stop=True),
                reads=[("z1", tb // 4), ("wf2",)], writes=[("pb", bank)])
            kb.op("act", lambda e, bank=bank: e.activation(
                out=sb["ztmp0"][:], in_=pb(bank)[:], func=AF.Abs),
                reads=[("pb", bank)], writes=[("zt0",)])
            kb.op("act", lambda e: e.activation(
                out=sb["ztmp0"][:], in_=sb["ztmp0"][:], func=AF.Exp, scale=-1.0),
                reads=[("zt0",)], writes=[("zt0",)])
            kb.op("act", lambda e: e.activation(
                out=sb["ztmp0"][:], in_=sb["ztmp0"][:], func=AF.Ln, bias=1.0),
                reads=[("zt0",)], writes=[("zt0",)])
            kb.op("dve", lambda e, bank=bank: e.tensor_single_scalar(
                sb["ztmp1"][:], pb(bank)[:], 0.0, op=ALU.min),
                reads=[("pb", bank)], writes=[("zt1",)])
            kb.op("dve", lambda e, jt=jt: e.tensor_sub(
                sb["gstg"][:, jt * 512:(jt + 1) * 512], sb["ztmp1"][:], sb["ztmp0"][:]),
                reads=[("zt0",), ("zt1",)], writes=[("gstg", jt)])

        kb.dma("qs", lambda e, tb=tb: e.dma_start(
            out=sp["g"][tb * 128:(tb + 1) * 128, :], in_=sb["gstg"][:]),
            reads=[("gstg", jt) for jt in range(njt)], writes=[("spg", tb)])
    if AZ_CUT >= 3:
        return
    for jt in range(njt):
        bank = 4 + jt % 2
        for tb in range(ntb):
            par = tb % 2
            kb.dma("qs", lambda e, jt=jt, tb=tb, par=par: e.dma_start(
                out=sb["ztmp%d" % par][:].bitcast(BF16)[:, 0:512],
                in_=sp["g"][tb * 128:(tb + 1) * 128, jt * 512:(jt + 1) * 512]),
                reads=[("spg", tb)], writes=[("zt", par)])
            kb.op("pe", lambda e, tb=tb, jt=jt, par=par, bank=bank: e.matmul(
                pb(bank)[0:32, :],
                sb["sumsels"][:, tb * 32:(tb + 1) * 32],
                sb["ztmp%d" % par][:].bitcast(BF16)[:, 0:512],
                start=(tb == 0), stop=(tb == ntb - 1)),
                reads=[("zt", par), ("sb", "sumsels")], writes=[("pbcc", bank)])
        kb.op("dve", lambda e, jt=jt, bank=bank: e.tensor_copy(
            sb["cCrows"][:, jt * 512:(jt + 1) * 512], pb(bank)[0:32, :]),
            reads=[("pbcc", bank)], writes=[("cCrows", jt)])
        kb.op("act", lambda e, jt=jt, bank=bank: e.activation(
            out=sb["ecall"][:, jt * 512:(jt + 1) * 512], in_=pb(bank)[0:32, :],
            func=AF.Exp),
            reads=[("pbcc", bank)], writes=[("ecall", jt)])
    for jt in range(njt):
        kb.op("pe", lambda e, jt=jt: e.matmul(
            pb(jt % 2)[0:32, :], sb["trige"][:],
            sb["cCrows"][:, jt * 512:(jt + 1) * 512], start=True, stop=True),
            reads=[("cCrows", jt), ("sb", "trige")], writes=[("pb", jt % 2)])
        kb.op("act", lambda e, jt=jt: e.activation(
            out=sb["wsufb"][:, jt * 512:(jt + 1) * 512], in_=pb(jt % 2)[0:32, :],
            func=AF.Exp),
            reads=[("pb", jt % 2)], writes=[("wsufb", jt)])
    if AZ_CUT >= 2:
        return
    kb.dma("qs", lambda e: e.dma_start(out=sp["ec"][:], in_=sb["ecall"][0:nch, :]),
           reads=[("ecall", jt) for jt in range(njt)], writes=[("spec",)])
    kb.dma("qs", lambda e: e.dma_start(out=sp["wsuf"][:], in_=sb["wsufb"][0:nch, :]),
           reads=[("wsufb", jt) for jt in range(njt)], writes=[("spwsuf",)])
    if AZ_CUT >= 1:
        return
    kb.dma("qp", lambda e: e.dma_start(
        out=sb["ec16all"][:, 0:nch * 128],
        in_=sp["ec"].rearrange("j (h m) -> h j m", m=128)),
        reads=[("spec",)], writes=[("ec16all",)])

AM_CUT = 0


def phase_amisc(self):
    kb, sb, din, sp = self.kb, self.sb, self.din, self.spills
    tl, ntb = self.cfg.tloc, self.cfg.ntb
    pb = self.pb
    HT = HB + tl
    kb.dma("qp", lambda e: e.dma_start(out=sb["wbr"][:], in_=din["Wbr"][:]),
           writes=[("wbr",)])
    wg1_4 = din["Wg1"].rearrange("(dc p) f -> p dc f", p=128)
    kb.dma("qp", lambda e: e.dma_start(out=sb["wsmall"][:, 16 * 128:32 * 128], in_=wg1_4),
           writes=[("wsmall", 1)])
    kb.dma("qp", lambda e: e.dma_start(out=sb["wg2"][:], in_=din["Wg2"][:]),
           writes=[("wg2",)])
    for tb in range(ntb):
        for dc in range(16):
            kb.op("pe", lambda e, tb=tb, dc=dc: e.matmul(
                pb(0)[:, 0:H],
                sb["hfm"][:, dc * HT + HB + tb * 128: dc * HT + HB + (tb + 1) * 128],
                sb["wbr"][:, dc * H:(dc + 1) * H],
                start=(dc == 0), stop=(dc == 15)),
                reads=[("wbr",), ("hfm", dc)], writes=[("pb", 0)])
        kb.op("act", lambda e, tb=tb: e.activation(
            out=sb["beta"][:, tb * H:(tb + 1) * H], in_=pb(0)[:, 0:H], func=AF.Sigmoid),
            reads=[("pb", 0)], writes=[("beta", tb)])
    b2 = sb["beta2"].ap().rearrange("p (c j) -> p c j", j=H)
    kb.dma("qs", lambda e, b2=b2: e.dma_start(
        out=b2[:, 0::2, :], in_=sb["beta"][0:64, :].rearrange("p (c j) -> p c j", j=H)),
        reads=[("beta", tb) for tb in range(ntb)], writes=[("beta2", 0)])
    kb.dma("qs", lambda e, b2=b2: e.dma_start(
        out=b2[:, 1::2, :], in_=sb["beta"][64:128, :].rearrange("p (c j) -> p c j", j=H)),
        reads=[("beta", tb) for tb in range(ntb)], writes=[("beta2", 1)])
    if AM_CUT >= 2:
        return
    for jg in range(ceil_div(tl, 512)):
        t0, tn = jg * 512, min(512, tl - jg * 512)
        bank = 1
        for dc in range(16):
            kb.op("pe", lambda e, dc=dc, t0=t0, tn=tn, bank=bank: e.matmul(
                pb(bank)[:, 0:tn],
                sb["wsmall"][:, (16 + dc) * 128:(17 + dc) * 128],
                sb["hfm"][:, dc * HT + HB + t0: dc * HT + HB + t0 + tn],
                start=(dc == 0), stop=(dc == 15)),
                reads=[("wsmall", 1), ("hfm", dc)], writes=[("pb", bank)])
        kb.op("act", lambda e, t0=t0, tn=tn, bank=bank: e.activation(
            out=sb["zg1"][:, t0:t0 + tn], in_=pb(bank)[:, 0:tn], func=AF.Copy),
            reads=[("pb", bank)], writes=[("zg1", jg)])
    if AM_CUT >= 1:
        return
    for fc2 in range(16):
        for jg in range(ceil_div(tl, 512)):
            t0, tn = jg * 512, min(512, tl - jg * 512)
            bank = 2 + (jg % 2)
            kb.op("pe", lambda e, fc2=fc2, t0=t0, tn=tn, bank=bank: e.matmul(
                pb(bank)[:, 0:tn],
                sb["wg2"][:, fc2 * 128:(fc2 + 1) * 128],
                sb["zg1"][:, t0:t0 + tn], start=True, stop=True),
                reads=[("wg2",), ("zg1", jg)], writes=[("pb", bank)])
            kb.op("dve", lambda e, fc2=fc2, t0=t0, tn=tn, bank=bank: e.tensor_scalar(
                sb["z1"][:, t0:t0 + tn], pb(bank)[:, 0:tn],
                sb["bg2m"][:, fc2:fc2 + 1], None, op0=ALU.add),
                reads=[("pb", bank), ("sb", "bg2m")], writes=[("z1", jg)])
            kb.op("act", lambda e, t0=t0, tn=tn: e.activation(
                out=sb["z1"][:, t0:t0 + tn], in_=sb["z1"][:, t0:t0 + tn],
                func=AF.Sigmoid),
                reads=[("z1", jg)], writes=[("z1", jg)])
        kb.dma("qs", lambda e, fc2=fc2: e.dma_start(
            out=sp["gs"][fc2], in_=sb["z1"][:, 0:tl]),
            reads=[("z1", jg) for jg in range(ceil_div(tl, 512))],
            writes=[("spgs", fc2)])


Builder.phase_az = phase_az
Builder.phase_amisc = phase_amisc


def phase_fence(self):
    kb = self.kb
    toks = [(e, kb.cnt[e]) for e in ENGINES if kb.cnt[e]]
    toks += [(qn, kb.dcnt[qn]) for qn in QUEUES if kb.dcnt[qn]]
    for e in ENGINES:
        kb.barrier_wait(e, toks)


def alloc_ascan_arena(self):
    tl = self.cfg.tloc
    self.arena_reset()
    for par in (0, 1):
        self.arena("gch%d" % par, [64, D], BF16)
        self.arena("wtch%d" % par, [64, D], BF16)
        self.arena("qfch%d" % par, [128, H * C], BF16)
        self.arena("kfch%d" % par, [128, H * C], BF16)
        self.arena("einv%d" % par, [64, D], BF16)
        self.arena("bw%d" % par, [64, D], BF16)
        self.arena("ubuf%d" % par, [64, D], BF16)
        self.arena("sibuf%d" % par, [64, D], BF16)
        self.arena("pm%d" % par, [64, 1024], BF16)
        self.arena("ssq%d" % par, [64, 16], F32)
        self.arena("rb%d" % par, [64, 16], BF16)

    self.arena("sqtmp", [64, D], F32)


def alloc_aagg_arena(self):
    self.arena_reset()
    for par in (0, 1):
        self.arena("ktch%d" % par, [64, 4 * D], BF16)
        self.arena("vtch%d" % par, [64, 4 * D], BF16)
        self.arena("ucb%d" % par, [64, 4 * D], BF16)
        self.arena("wsb%d" % par, [64, D], BF16)
    self.arena("ccstg", [128, D], BF16)
    self.arena("aggK", [128, H * M], F32)
    self.arena("aggV", [128, H * DK], F32)
    self.arena("erow", [1, D], BF16)


def phase_ascan(self):
    kb, sb, sp = self.kb, self.sb, self.spills
    tl, nch = self.cfg.tloc, self.cfg.nch
    pb = self.pb
    PSA = self.PSA
    wtm3 = sp["wtm"].rearrange("h t f -> t h f")
    q3 = sp["q"].rearrange("h p t -> p h t")
    k3 = sp["kfm"].rearrange("h p t -> p h t")
    for ch in range(nch):
        par = ch % 2
        gch, wt = sb["gch%d" % par], sb["wtch%d" % par]
        qf, kf = sb["qfch%d" % par], sb["kfch%d" % par]
        einv, bw = sb["einv%d" % par], sb["bw%d" % par]
        ub, sib, pm = sb["ubuf%d" % par], sb["sibuf%d" % par], sb["pm%d" % par]
        ssq, rb = sb["ssq%d" % par], sb["rb%d" % par]
        kb.dma("qs", lambda e, ch=ch, gch=gch: e.dma_start(
            out=gch[:], in_=sp["g"][ch * C:(ch + 1) * C, :]),
            reads=[("spg", ch * C // 128)], writes=[("gch", par)])
        kb.dma("qs", lambda e, ch=ch, wt=wt: e.dma_start(
            out=wt[:], in_=wtm3[ch * C:(ch + 1) * C]),
            reads=[("spwtm", h) for h in range(H)], writes=[("wtch", par)])
        kb.dma("qs", lambda e, ch=ch, qf=qf: e.dma_start(
            out=qf[:], in_=q3[:, :, ch * C:(ch + 1) * C]),
            reads=[("spq", h) for h in range(H)], writes=[("qfch", par)])
        kb.dma("qs", lambda e, ch=ch, kf=kf: e.dma_start(
            out=kf[:], in_=k3[:, :, ch * C:(ch + 1) * C]),
            reads=[("spkfm", h) for h in range(H)], writes=[("kfch", par)])
        for jt in range(4):
            kb.op("pe", lambda e, jt=jt, gch=gch: e.matmul(
                PSA.ap()[0:C, jt * 512:(jt + 1) * 512], sb["tri"][:],
                gch[:, jt * 512:(jt + 1) * 512], start=True, stop=True),
                reads=[("gch", par), ("sb", "tri")], writes=[("pb", jt)])
        kb.op("act", lambda e, einv=einv: e.activation(
            out=einv[:], in_=PSA.ap()[0:C, :], func=AF.Exp, scale=-1.0),
            reads=[("pb", 0), ("pb", 1), ("pb", 2), ("pb", 3)], writes=[("einv", par)])
        kb.op("act", lambda e, wt=wt: e.activation(
            out=sb["sqtmp"][:], in_=wt[:], func=AF.Square),
            reads=[("wtch", par)], writes=[("sqtmp",)])
        kb.op("dve", lambda e, ssq=ssq: e.reduce_sum(
            ssq[:], sb["sqtmp"][:].rearrange("p (h m) -> p h m", m=M), axis=AX.X),
            reads=[("sqtmp",)], writes=[("ssq", par)])
        kb.op("act", lambda e, ssq=ssq: e.activation(
            out=ssq[:], in_=ssq[:], func=AF.Ln, bias=L2EPS),
            reads=[("ssq", par)], writes=[("ssq", par)])
        kb.op("act", lambda e, ssq=ssq: e.activation(
            out=ssq[:], in_=ssq[:], func=AF.Exp, scale=-0.5),
            reads=[("ssq", par)], writes=[("ssq", par)])
        bslice = sb["beta2"][:, ch * H:(ch + 1) * H]
        kb.op("dve", lambda e, rb=rb, ssq=ssq, bslice=bslice: e.tensor_mul(
            rb[:], ssq[:], bslice),
            reads=[("ssq", par), ("beta2", 0), ("beta2", 1)], writes=[("rbb", par)])
        kb.op("dve", lambda e, bw=bw, wt=wt, rb=rb: e.tensor_tensor(
            bw[:].rearrange("p (h m) -> p h m", m=M),
            wt[:].rearrange("p (h m) -> p h m", m=M),
            rb[:].unsqueeze(2).broadcast_to([C, H, M]), op=ALU.mult),
            reads=[("wtch", par), ("rbb", par)], writes=[("bwb", par)])
        kb.op("dve", lambda e, ub=ub, bw=bw, einv=einv: e.tensor_mul(
            ub[:], bw[:], einv[:]),
            reads=[("bwb", par), ("einv", par)], writes=[("ubuf", par)])
        kb.dma("qs", lambda e, ch=ch, ub=ub: e.dma_start(
            out=sp["u"][ch * C:(ch + 1) * C, :], in_=ub[:]),
            reads=[("ubuf", par)], writes=[("spu", ch)])
        for hh in range(H):
            bank = 4 + hh // 8
            kb.op("pe", lambda e, hh=hh, bank=bank, kf=kf, qf=qf: e.matmul(
                pb(bank)[0:C, (hh % 8) * C:(hh % 8 + 1) * C],
                kf[:, hh * C:(hh + 1) * C], qf[:, hh * C:(hh + 1) * C],
                start=True, stop=True),
                reads=[("qfch", par), ("kfch", par)], writes=[("pb", bank)])
        for b_ in (4, 5):
            kb.op("dve", lambda e, b_=b_, pm=pm: e.tensor_tensor(
                pm[:, (b_ - 4) * 512:(b_ - 3) * 512].rearrange("p (h s) -> p h s", s=C),
                pb(b_)[0:C, :].rearrange("p (h s) -> p h s", s=C),
                sb["tri"][:].unsqueeze(1).broadcast_to([C, 8, C]), op=ALU.mult),
                reads=[("pb", b_), ("sb", "tri")], writes=[("pmb", par, b_)])
        for hh in range(H):
            kb.op("pe", lambda e, hh=hh, pm=pm, ub=ub: e.matmul(
                PSA.ap()[0:C, hh * M:(hh + 1) * M],
                pm[:, hh * C:(hh + 1) * C], ub[:, hh * M:(hh + 1) * M],
                start=True, stop=True),
                reads=[("pmb", par, 4 + hh // 8), ("ubuf", par)],
                writes=[("pb", hh // 4)])
        kb.op("act", lambda e, sib=sib: e.activation(
            out=sib[:], in_=PSA.ap()[0:C, :], func=AF.Copy, scale=SCALE),
            reads=[("pb", 0), ("pb", 1), ("pb", 2), ("pb", 3)], writes=[("sibuf", par)])
        kb.dma("qs", lambda e, ch=ch, sib=sib: e.dma_start(
            out=sp["si"][ch * C:(ch + 1) * C, :], in_=sib[:]),
            reads=[("sibuf", par)], writes=[("spsi", ch)])


def phase_aagg(self):
    kb, sb, sp = self.kb, self.sb, self.spills
    tl, nch = self.cfg.tloc, self.cfg.nch
    pb = self.pb
    PSA = self.PSA
    BC = 4
    nblk = ceil_div(nch, BC)
    ktm3 = sp["ktm"].rearrange("h t f -> t h f")
    vtm3 = sp["vtm"].rearrange("h t f -> t h f")
    for blk in range(nblk):
        chs = [ch for ch in range(blk * BC, min((blk + 1) * BC, nch))]
        par = blk % 2
        kt, vt = sb["ktch%d" % par], sb["vtch%d" % par]
        uc, ws = sb["ucb%d" % par], sb["wsb%d" % par]
        nbc = len(chs)
        for i, ch in enumerate(chs):
            kb.dma("qs", lambda e, ch=ch, i=i, kt=kt: e.dma_start(
                out=kt[:, i * D:(i + 1) * D], in_=ktm3[ch * C:(ch + 1) * C]),
                reads=[("spktm", h) for h in range(H)], writes=[("ktch", par, i)])
            kb.dma("qs", lambda e, ch=ch, i=i, vt=vt: e.dma_start(
                out=vt[:, i * D:(i + 1) * D], in_=vtm3[ch * C:(ch + 1) * C]),
                reads=[("spvtm", h) for h in range(H)], writes=[("vtch", par, i)])
            kb.dma("qa", lambda e, ch=ch, i=i, ws=ws: e.dma_start(
                out=ws[:, (i % 2) * 0: D] if False else ws[:],
                in_=sp["wsuf"][ch:ch + 1, :].broadcast_to([C, D])),
                reads=[("spwsuf",)], writes=[("wsbb", par)])
            kb.dma("qa", lambda e, ch=ch, i=i, uc=uc: e.dma_start(
                out=uc[:, i * D:(i + 1) * D], in_=sp["u"][ch * C:(ch + 1) * C, :]),
                reads=[("spu", ch)], writes=[("ucb", par, i)])
            kb.op("dve", lambda e, i=i, uc=uc, ws=ws: e.tensor_mul(
                uc[:, i * D:(i + 1) * D], uc[:, i * D:(i + 1) * D], ws[:]),
                reads=[("ucb", par, i), ("wsbb", par)], writes=[("ucb", par, i)])
        for hh in range(H):
            bank = hh % 2
            for i in range(nbc):
                kb.op("pe", lambda e, hh=hh, i=i, kt=kt, uc=uc, bank=bank, nbc=nbc: e.matmul(
                    pb(bank)[:, 0:M],
                    kt[:, i * D + hh * DK: i * D + (hh + 1) * DK],
                    uc[:, i * D + hh * M: i * D + (hh + 1) * M],
                    start=(i == 0), stop=(i == nbc - 1)),
                    reads=[("ktch", par, i), ("ucb", par, i)], writes=[("pb", bank)])
            if blk == 0:
                kb.op("dve", lambda e, hh=hh, bank=bank: e.tensor_copy(
                    sb["aggK"][:, hh * M:(hh + 1) * M], pb(bank)[:, 0:M]),
                    reads=[("pb", bank)], writes=[("aggK", hh)])
            else:
                kb.op("dve", lambda e, hh=hh, bank=bank: e.tensor_add(
                    sb["aggK"][:, hh * M:(hh + 1) * M],
                    sb["aggK"][:, hh * M:(hh + 1) * M], pb(bank)[:, 0:M]),
                    reads=[("pb", bank), ("aggK", hh)], writes=[("aggK", hh)])
            bank2 = 2 + hh % 2
            for i in range(nbc):
                kb.op("pe", lambda e, hh=hh, i=i, uc=uc, vt=vt, bank2=bank2, nbc=nbc: e.matmul(
                    pb(bank2)[:, 0:DK],
                    uc[:, i * D + hh * M: i * D + (hh + 1) * M],
                    vt[:, i * D + hh * DK: i * D + (hh + 1) * DK],
                    start=(i == 0), stop=(i == nbc - 1)),
                    reads=[("ucb", par, i), ("vtch", par, i)], writes=[("pb", bank2)])
            if blk == 0:
                kb.op("dve", lambda e, hh=hh, bank2=bank2: e.tensor_copy(
                    sb["aggV"][:, hh * DK:(hh + 1) * DK], pb(bank2)[:, 0:DK]),
                    reads=[("pb", bank2)], writes=[("aggV", hh)])
            else:
                kb.op("dve", lambda e, hh=hh, bank2=bank2: e.tensor_add(
                    sb["aggV"][:, hh * DK:(hh + 1) * DK],
                    sb["aggV"][:, hh * DK:(hh + 1) * DK], pb(bank2)[:, 0:DK]),
                    reads=[("pb", bank2), ("aggV", hh)], writes=[("aggV", hh)])
    kb.op("dve", lambda e: e.tensor_copy(sb["ccstg"][:], sb["aggK"][:]),
          reads=[("aggK", hh) for hh in range(H)], writes=[("ccstg",)])
    kb.dma("qs", lambda e: e.dma_start(out=sp["ccin"][0:128, :], in_=sb["ccstg"][:]),
           reads=[("ccstg",)], writes=[("spccin", 0)])
    kb.op("dve", lambda e: e.tensor_copy(sb["ccstg"][:], sb["aggV"][:]),
          reads=[("aggV", hh) for hh in range(H)] + [("spccin", 0)], writes=[("ccstg",)])
    kb.dma("qs", lambda e: e.dma_start(out=sp["ccin"][128:256, :], in_=sb["ccstg"][:]),
           reads=[("ccstg",)], writes=[("spccin", 1)])
    for jt in range(4):
        kb.op("pe", lambda e, jt=jt: e.matmul(
            PSA.ap()[0:1, jt * 512:(jt + 1) * 512], sb["ones32"][:],
            sb["cCrows"][:, jt * 512:(jt + 1) * 512], start=True, stop=True),
            reads=[("cCrows", jt), ("sb", "ones32")], writes=[("pb", jt)])
        kb.op("act", lambda e, jt=jt: e.activation(
            out=sb["erow"][:, jt * 512:(jt + 1) * 512],
            in_=PSA.ap()[0:1, jt * 512:(jt + 1) * 512], func=AF.Exp),
            reads=[("pb", jt)], writes=[("erow", jt)])
    kb.dma("qs", lambda e: e.dma_start(out=sp["ccin"][256:257, :], in_=sb["erow"][:]),
           reads=[("erow", jt) for jt in range(4)], writes=[("spccin", 2)])


def phase_gather(self):
    kb, sp = self.kb, self.spills
    if self.cfg.host_gather:
        return
    n = self.cfg.ncores
    sem = self.nc.alloc_semaphore("sem_cc")
    kb.op("pool", lambda e: e.collective_compute(
        "AllGather", ALU.bypass, replica_groups=[list(range(n))],
        ins=[sp["ccin"][:].opt()], outs=[sp["ccout"][:].opt()]),
        reads=[("spccin", 0), ("spccin", 1), ("spccin", 2)],
        writes=[("spccout",)])


Builder.phase_fence = phase_fence
Builder.alloc_ascan_arena = alloc_ascan_arena
Builder.alloc_aagg_arena = alloc_aagg_arena
Builder.phase_ascan = phase_ascan
Builder.phase_aagg = phase_aagg
Builder.phase_gather = phase_gather


def alloc_b_arena(self):
    tl = self.cfg.tloc
    self.arena_reset()
    self.arena("wo", [128, 16 * 128 * 16], BF16)
    for par in (0, 1):
        self.arena("bgch%d" % par, [64, D], BF16)
        self.arena("bqf%d" % par, [128, H * C], BF16)
        self.arena("bkt%d" % par, [64, D], BF16)
        self.arena("bvt%d" % par, [64, D], BF16)
    self.arena("bub0", [64, D], BF16)
    self.arena("bsi0", [64, D], BF16)
    self.arena("bgs0", [128, H * C], BF16)
    self.arena("becb0", [128, D], BF16)
    self.arena("bh0", [128, D], F32)
    self.arena("bofm0", [128, D], BF16)
    self.arena("befull", [64, D], BF16)
    self.arena("bes", [64, D], BF16)
    self.arena("bs0", [64, D], BF16)
    self.arena("bamt", [128, H * C], BF16)
    self.arena("bufm", [128, H * C], BF16)

    self.arena("bosq", [128, H * C], BF16)
    self.arena("brstdb", [128, H * C], BF16)
    self.arena("bo1", [128, H * C], BF16)
    self.arena("br", [128, D], F32)
    self.arena("bx1", [128, D], BF16)
    self.arena("bmx", [64, H], F32)
    self.arena("bsm", [64, H], F32)
    self.arena("brs", [64, H], F32)
    self.arena("brr", [1, H * C], BF16)
    self.arena("bln", [128, 4], F32)
    self.arena("be16", [16, 128], BF16)
    self.sb["bfold"] = self.sb["bx1"]
    self.sb["bytm"] = self.sb["bsi0"]


def phase_bscan(self):
    kb, sb, sp, din = self.kb, self.sb, self.spills, self.din
    tl, nch, cfg = self.cfg.tloc, self.cfg.nch, self.cfg
    pb = self.pb
    PSA = self.PSA
    wo4 = din["Wo"].rearrange("(dc p) f -> p dc f", p=128)
    kb.dma("qp", lambda e: e.dma_start(out=sb["wo"][:], in_=wo4), writes=[("wo",)])
    kb.op("dve", lambda e: e.memset(sb["Sk"][:], 0.0), writes=[("Sk",)])
    kb.op("dve", lambda e: e.memset(sb["Sv"][:], 0.0), writes=[("Sv",)])
    nfold = cfg.ncores - 1
    for j in range(nfold):
        rb_ = 257 * j
        kb.dma("qp", lambda e, rb_=rb_: e.dma_start(
            out=sb["row2"][0:1, :], in_=sp["ccout"][rb_ + 256:rb_ + 257, :]),
            reads=[("spccout",)], writes=[("row2",)])
        kb.dma("qp", lambda e, rb_=rb_: e.dma_start(
            out=sb["be16"][:], in_=sp["ccout"][rb_ + 256:rb_ + 257, :].rearrange(
                "o (h m) -> (o h) m", m=128)),
            reads=[("spccout",)], writes=[("be16",)])
        for jt in range(4):
            kb.op("pe", lambda e, j=j, jt=jt: e.matmul(
                PSA.ap()[:, jt * 512:(jt + 1) * 512],
                sb["mselM"][:, j * 128:(j + 1) * 128],
                sb["row2"][:, jt * 512:(jt + 1) * 512], start=True, stop=True),
                reads=[("row2",), ("sb", "mselM")], writes=[("pb", jt)])
        kb.op("dve", lambda e: e.tensor_mul(sb["Sk"][:], sb["Sk"][:], PSA.ap()[:]),
              reads=[("pb", 0), ("pb", 1), ("pb", 2), ("pb", 3), ("Sk",)],
              writes=[("Sk",)])
        kb.dma("qp", lambda e, rb_=rb_: e.dma_start(
            out=sb["bfold"][:].bitcast(BF16)[:, 0:D], in_=sp["ccout"][rb_:rb_ + 128, :]),
            reads=[("spccout",)], writes=[("bx1",)])
        kb.op("dve", lambda e, j=j: e.scalar_tensor_tensor(
            sb["Sk"][:], sb["bfold"][:].bitcast(BF16)[:, 0:D], sb["mselcol"][:, j:j + 1],
            sb["Sk"][:], op0=ALU.mult, op1=ALU.add),
            reads=[("bx1",), ("Sk",), ("sb", "mselcol")], writes=[("Sk",)])
        kb.op("dve", lambda e, j=j: e.tensor_scalar(
            sb["be16"][:], sb["be16"][:], sb["mselcol"][0:16, j:j + 1],
            sb["m1col"][0:16, j:j + 1], op0=ALU.mult, op1=ALU.add),
            reads=[("be16",), ("sb", "mselcol"), ("sb", "m1col")], writes=[("be16",)])
        kb.op("pe", lambda e: e.transpose(
            pb(6).bitcast(BF16)[:, 0:16], sb["be16"][:], sb["ident"][0:16, 0:16]),
            reads=[("be16",), ("sb", "ident")], writes=[("pb", 6)])
        kb.op("dve", lambda e: e.tensor_copy(sb["ecct"][:], pb(6).bitcast(BF16)[:, 0:16]),
              reads=[("pb", 6)], writes=[("ecct",)])
        for hh in range(H):
            kb.op("act", lambda e, hh=hh: e.activation(
                out=sb["Sv"][:, hh * DK:(hh + 1) * DK],
                in_=sb["Sv"][:, hh * DK:(hh + 1) * DK], func=AF.Copy,
                scale=sb["ecct"][:, hh:hh + 1]),
                reads=[("ecct",), ("Sv",)], writes=[("Sv",)])
        kb.dma("qp", lambda e, rb_=rb_: e.dma_start(
            out=sb["bfold"][:], in_=sp["ccout"][rb_ + 128:rb_ + 256, :]),
            reads=[("spccout",), ("Sk",)], writes=[("bx1",)])
        kb.op("dve", lambda e, j=j: e.scalar_tensor_tensor(
            sb["Sv"][:], sb["bfold"][:].bitcast(BF16)[:, 0:D], sb["mselcol"][:, j:j + 1],
            sb["Sv"][:], op0=ALU.mult, op1=ALU.add),
            reads=[("bx1",), ("Sv",), ("sb", "mselcol")], writes=[("Sv",)])
    q3 = sp["q"].rearrange("h p t -> p h t")
    gs3 = sp["gs"].rearrange("h p t -> p h t")
    ktm3 = sp["ktm"].rearrange("h t f -> t h f")
    vtm3 = sp["vtm"].rearrange("h t f -> t h f")
    def emit_loads_par(ch):
        par = ch % 2
        gch, qf = sb["bgch%d" % par], sb["bqf%d" % par]
        kt, vt = sb["bkt%d" % par], sb["bvt%d" % par]
        kb.dma("qs", lambda e, ch=ch, gch=gch: e.dma_start(
            out=gch[:], in_=sp["g"][ch * C:(ch + 1) * C, :]),
            reads=[("spg", ch * C // 128)], writes=[("bgch", par)])
        kb.dma("qp", lambda e, ch=ch, qf=qf: e.dma_start(
            out=qf[:], in_=q3[:, :, ch * C:(ch + 1) * C]),
            reads=[("spq", h) for h in range(H)], writes=[("bqf", par)])
        kb.dma("qp", lambda e, ch=ch, kt=kt: e.dma_start(
            out=kt[:], in_=ktm3[ch * C:(ch + 1) * C]),
            reads=[("spktm", h) for h in range(H)], writes=[("bkt", par)])
        kb.dma("qa", lambda e, ch=ch, vt=vt: e.dma_start(
            out=vt[:], in_=vtm3[ch * C:(ch + 1) * C]),
            reads=[("spvtm", h) for h in range(H)], writes=[("bvt", par)])

    def emit_loads_single(ch):
        ub, sib = sb["bub0"], sb["bsi0"]
        gsf, ecb = sb["bgs0"], sb["becb0"]
        hb = sb["bh0"]
        kb.dma("qs", lambda e, ch=ch, ub=ub: e.dma_start(
            out=ub[:], in_=sp["u"][ch * C:(ch + 1) * C, :]),
            reads=[("spu", ch)], writes=[("bub",)])
        kb.dma("qs", lambda e, ch=ch, sib=sib: e.dma_start(
            out=sib[:], in_=sp["si"][ch * C:(ch + 1) * C, :]),
            reads=[("spsi", ch)], writes=[("bsiy",)])
        kb.dma("qa", lambda e, ch=ch, gsf=gsf: e.dma_start(
            out=gsf[:], in_=gs3[:, :, ch * C:(ch + 1) * C]),
            reads=[("spgs", h) for h in range(H)], writes=[("bgs",)])
        kb.dma("qa", lambda e, ch=ch, ecb=ecb: e.dma_start(
            out=ecb[:], in_=sp["ec"][ch:ch + 1, :].broadcast_to([128, D])),
            reads=[("spec",)], writes=[("becb",)])
        kb.dma("qa", lambda e, ch=ch, hb=hb: e.dma_start(
            out=hb[(ch % 2) * 64:(ch % 2) * 64 + 64, :],
            in_=din["h"][ch * C:(ch + 1) * C, :]),
            writes=[("bh", ch % 2)])

    emit_loads_par(0)
    for ch in range(nch):
        par = ch % 2
        gch, qf = sb["bgch%d" % par], sb["bqf%d" % par]
        kt, vt = sb["bkt%d" % par], sb["bvt%d" % par]
        ub, sib = sb["bub0"], sb["bsi0"]
        gsf, ecb = sb["bgs0"], sb["becb0"]
        ppar = 0
        hb, ofm = sb["bh0"], sb["bofm0"]
        if ch + 1 < nch:
            emit_loads_par(ch + 1)
        emit_loads_single(ch)
        for jt in range(4):
            kb.op("pe", lambda e, jt=jt, gch=gch: e.matmul(
                PSA.ap()[0:C, jt * 512:(jt + 1) * 512], sb["tri"][:],
                gch[:, jt * 512:(jt + 1) * 512], start=True, stop=True),
                reads=[("bgch", par), ("sb", "tri")], writes=[("pb", jt)])
        kb.op("act", lambda e: e.activation(
            out=sb["befull"][:], in_=PSA.ap()[0:C, :], func=AF.Exp),
            reads=[("pb", 0), ("pb", 1), ("pb", 2), ("pb", 3)], writes=[("befull",)])
        e16sl = sb["ec16all"][:, ch * 128:(ch + 1) * 128]
        kb.op("pe", lambda e, e16sl=e16sl: e.transpose(
            pb(6).bitcast(BF16)[:, 0:16], e16sl, sb["ident"][0:16, 0:16]),
            reads=[("ec16all",), ("sb", "ident")], writes=[("pb", 6)])
        kb.op("dve", lambda e: e.tensor_copy(sb["ecct"][:], pb(6).bitcast(BF16)[:, 0:16]),
              reads=[("pb", 6)], writes=[("ecct",)])
        kb.op("dve", lambda e: e.tensor_scalar_mul(sb["Skbf"][:], sb["Sk"][:], SCALE),
              reads=[("Sk",)], writes=[("Skbf",)])
        kb.op("pool", lambda e: e.tensor_copy(sb["Svbf"][:], sb["Sv"][:]),
              reads=[("Sv",)], writes=[("Svbf",)])
        for hh in range(H):
            kb.op("pe", lambda e, hh=hh, qf=qf: e.matmul(
                PSA.ap()[0:C, hh * M:(hh + 1) * M],
                qf[:, hh * C:(hh + 1) * C], sb["Skbf"][:, hh * M:(hh + 1) * M],
                start=True, stop=True),
                reads=[("bqf", par), ("Skbf",)], writes=[("pb", hh // 4)])
        kb.op("dve", lambda e, sib=sib: e.tensor_add(
            sb["bs0"][:], PSA.ap()[0:C, :], sib[:]),
            reads=[("pb", 0), ("pb", 1), ("pb", 2), ("pb", 3), ("bsiy",)],
            writes=[("bs0",)])
        kb.op("dve", lambda e: e.tensor_mul(sb["bs0"][:], sb["bs0"][:], sb["befull"][:]),
              reads=[("bs0",), ("befull",)], writes=[("bs0",)])
        s3 = sb["bs0"][:].rearrange("p (h m) -> p h m", m=M)
        kb.op("dve", lambda e, s3=s3: e.reduce_max(sb["bmx"][:], s3, axis=AX.X),
              reads=[("bs0",)], writes=[("bmx",)])
        kb.op("dve", lambda e, s3=s3: e.tensor_tensor(
            s3, s3, sb["bmx"][:].unsqueeze(2).broadcast_to([C, H, M]), op=ALU.subtract),
            reads=[("bs0",), ("bmx",)], writes=[("bs0",)])
        kb.op("act", lambda e: e.activation(
            out=sb["bes"][:], in_=sb["bs0"][:], func=AF.Exp),
            reads=[("bs0",)], writes=[("bes",)])
        es3 = sb["bes"][:].rearrange("p (h m) -> p h m", m=M)
        kb.op("dve", lambda e, es3=es3: e.reduce_sum(sb["bsm"][:], es3, axis=AX.X),
              reads=[("bes",)], writes=[("bsm",)])
        kb.op("dve", lambda e: e.reciprocal(sb["brs"][:], sb["bsm"][:]),
              reads=[("bsm",)], writes=[("brs",)])
        kb.op("dve", lambda e, es3=es3: e.tensor_tensor(
            es3, es3, sb["brs"][:].unsqueeze(2).broadcast_to([C, H, M]), op=ALU.mult),
            reads=[("bes",), ("brs",)], writes=[("bes",)])
        kb.op("dve", lambda e: e.tensor_mul(sb["bes"][:], sb["bes"][:], sb["befull"][:]),
              reads=[("bes",), ("befull",)], writes=[("bes",)])
        for nm, srcb, dstb in (("amt", "bes", "bamt"), ("ufm", None, "bufm")):
            srct = sb["bes"] if nm == "amt" else ub
            srcreg = ("bes",) if nm == "amt" else ("bub",)
            for hh in range(H):
                kb.op("pe", lambda e, hh=hh, srct=srct: e.transpose(
                    pb(6).bitcast(BF16)[:, hh * C:(hh + 1) * C],
                    srct[:, hh * M:(hh + 1) * M], sb["ident"][0:C, 0:C]),
                    reads=[srcreg, ("sb", "ident")], writes=[("pb", 6)])
            kb.op("act", lambda e, dstb=dstb: e.activation(
                out=sb[dstb][:], in_=pb(6).bitcast(BF16)[:, 0:H * C], func=AF.Copy),
                reads=[("pb", 6) for hh in range(H)], writes=[(dstb,)])
        for hh in range(H):
            bank = 4 + hh // 8
            kb.op("pe", lambda e, hh=hh, bank=bank: e.matmul(
                pb(bank)[0:C, (hh % 8) * C:(hh % 8 + 1) * C],
                sb["bufm"][:, hh * C:(hh + 1) * C], sb["bamt"][:, hh * C:(hh + 1) * C],
                start=True, stop=True),
                reads=[("bufm",), ("bamt",)], writes=[("pb", bank)])
        for b_ in (4, 5):
            kb.op("dve", lambda e, b_=b_: e.tensor_tensor(
                sb["bytm"][:, (b_ - 4) * 512:(b_ - 3) * 512].rearrange(
                    "p (h s) -> p h s", s=C),
                pb(b_)[0:C, :].rearrange("p (h s) -> p h s", s=C),
                sb["tri"][:].unsqueeze(1).broadcast_to([C, 8, C]), op=ALU.mult),
                reads=[("pb", b_), ("sb", "tri")], writes=[("bsiy",)])
        for hh in range(H):
            kb.op("pe", lambda e, hh=hh, vt=vt: e.matmul(
                PSA.ap()[:, hh * C:(hh + 1) * C],
                vt[:, hh * DK:(hh + 1) * DK], sb["bytm"][:, hh * C:(hh + 1) * C],
                start=True, stop=False),
                reads=[("bvt", par), ("bsiy",)], writes=[("pb", hh // 8)])
            kb.op("pe", lambda e, hh=hh: e.matmul(
                PSA.ap()[:, hh * C:(hh + 1) * C],
                sb["Svbf"][:, hh * DK:(hh + 1) * DK], sb["bamt"][:, hh * C:(hh + 1) * C],
                start=False, stop=True),
                reads=[("Svbf",), ("bamt",)], writes=[("pb", hh // 8)])
        kb.op("act", lambda e: e.activation(
            out=sb["bosq"][:], in_=PSA.ap()[:, 0:H * C], func=AF.Square),
            reads=[("pb", 0), ("pb", 1)], writes=[("bosq",)])
        for i2 in range(2):
            kb.op("pe", lambda e, i2=i2: e.matmul(
                PSA.ap()[0:1, 1024 + i2 * 512:1024 + (i2 + 1) * 512],
                sb["ones128"][:], sb["bosq"][:, i2 * 512:(i2 + 1) * 512],
                start=True, stop=True),
                reads=[("bosq",), ("sb", "ones128")], writes=[("pb", 2 + i2)])
        kb.op("act", lambda e: e.activation(
            out=sb["brr"][:], in_=PSA.ap()[0:1, 1024:2048], func=AF.Ln,
            scale=1.0 / 128, bias=EPS),
            reads=[("pb", 2), ("pb", 3)], writes=[("brr",)])
        kb.op("act", lambda e: e.activation(
            out=sb["brr"][:], in_=sb["brr"][:], func=AF.Exp, scale=-0.5),
            reads=[("brr",)], writes=[("brr",)])
        for i2 in range(2):
            kb.op("pe", lambda e, i2=i2: e.matmul(
                PSA.ap()[:, 1024 + i2 * 512:1024 + (i2 + 1) * 512],
                sb["ones1x128"][:], sb["brr"][:, i2 * 512:(i2 + 1) * 512],
                start=True, stop=True),
                reads=[("brr",), ("sb", "ones1x128")], writes=[("pb", 2 + i2)])
        kb.op("act", lambda e: e.activation(
            out=sb["brstdb"][:], in_=PSA.ap()[:, 1024:2048], func=AF.Copy),
            reads=[("pb", 2), ("pb", 3)], writes=[("brstdb",)])
        kb.op("dve", lambda e: e.scalar_tensor_tensor(
            sb["bo1"][:], PSA.ap()[:, 0:H * C], sb["normw"][:], sb["brstdb"][:],
            op0=ALU.mult, op1=ALU.mult),
            reads=[("pb", 0), ("pb", 1), ("brstdb",), ("sb", "normw")],
            writes=[("bo1",)])
        kb.op("pool", lambda e, gsf=gsf, ofm=ofm, ch=ch: e.tensor_tensor(
            ofm[:].rearrange("p (h c t) -> p h c t", c=2, t=C)[:, :, ch % 2, :],
            sb["bo1"][:].rearrange("p (h t) -> p h t", t=C), gsf[:].rearrange(
                "p (h t) -> p h t", t=C),
            op=ALU.mult),
            reads=[("bo1",), ("bgs",)], writes=[("bofm", ch % 2)])
        for hh in range(H):
            bank = 4 + hh // 4
            kb.op("pe", lambda e, hh=hh, kt=kt, ub=ub, bank=bank: e.matmul(
                pb(bank)[:, (hh % 4) * M:(hh % 4 + 1) * M],
                kt[:, hh * DK:(hh + 1) * DK], ub[:, hh * M:(hh + 1) * M],
                start=True, stop=True),
                reads=[("bkt", par), ("bub",)], writes=[("pb", bank)])
        for b_ in range(4, 8):
            kb.op("dve", lambda e, b_=b_: e.tensor_add(
                sb["Sk"][:, (b_ - 4) * 512:(b_ - 3) * 512],
                sb["Sk"][:, (b_ - 4) * 512:(b_ - 3) * 512], pb(b_)[:]),
                reads=[("pb", b_), ("Sk",), ("Skbf",)], writes=[("Sk",)])
        kb.op("dve", lambda e, ecb=ecb: e.tensor_mul(sb["Sk"][:], sb["Sk"][:], ecb[:]),
              reads=[("Sk",), ("becb",)], writes=[("Sk",)])
        for hh in range(H):
            kb.op("pe", lambda e, hh=hh, ub=ub, vt=vt: e.matmul(
                PSA.ap()[:, hh * DK:(hh + 1) * DK],
                ub[:, hh * M:(hh + 1) * M], vt[:, hh * DK:(hh + 1) * DK],
                start=True, stop=True),
                reads=[("bub",), ("bvt", par), ("bo1",)], writes=[("pb", hh // 4)])
        for b_ in range(4):
            kb.op("dve", lambda e, b_=b_: e.tensor_add(
                sb["Sv"][:, b_ * 512:(b_ + 1) * 512],
                sb["Sv"][:, b_ * 512:(b_ + 1) * 512], PSA.ap()[:, b_ * 512:(b_ + 1) * 512]),
                reads=[("pb", b_), ("Sv",), ("Svbf",)], writes=[("Sv",)])
        for hh in range(H):
            kb.op("act", lambda e, hh=hh: e.activation(
                out=sb["Sv"][:, hh * DK:(hh + 1) * DK],
                in_=sb["Sv"][:, hh * DK:(hh + 1) * DK], func=AF.Copy,
                scale=sb["ecct"][:, hh:hh + 1]),
                reads=[("ecct",), ("Sv",)], writes=[("Sv",)])
        if ch % 2 == 1:
            for jt in range(4):
                for dc in range(16):
                    lhs = sb["bofm%d" % ppar][:, dc * 128:(dc + 1) * 128]
                    kb.op("pe", lambda e, jt=jt, dc=dc, lhs=lhs: e.matmul(
                        PSA.ap()[:, jt * 512:(jt + 1) * 512],
                        lhs, sb["wo"][:, dc * D + jt * 512: dc * D + (jt + 1) * 512],
                        start=(dc == 0), stop=(dc == 15)),
                        reads=[("bofm", 0), ("bofm", 1), ("wo",)],
                        writes=[("pb", jt)])
            kb.op("dve", lambda e, ppar=ppar: e.tensor_add(
                sb["br"][:], PSA.ap()[:], sb["bh%d" % ppar][:]),
                reads=[("pb", 0), ("pb", 1), ("pb", 2), ("pb", 3),
                       ("bh", 0), ("bh", 1)],
                writes=[("br",)])
            kb.op("dve", lambda e: e.reduce_sum(
                sb["bln"][:, 0:1], sb["br"][:].unsqueeze(1), axis=AX.X),
                reads=[("br",)], writes=[("bln", 0)])
            kb.op("act", lambda e: e.activation(
                out=PSA.ap()[:], in_=sb["br"][:], func=AF.Square),
                reads=[("br",), ("pb", 0), ("pb", 1), ("pb", 2), ("pb", 3)],
                writes=[("pb", 0), ("pb", 1), ("pb", 2), ("pb", 3)])
            kb.op("dve", lambda e: e.reduce_sum(
                sb["bln"][:, 1:2], PSA.ap()[:].unsqueeze(1), axis=AX.X),
                reads=[("pb", 0), ("pb", 1), ("pb", 2), ("pb", 3)], writes=[("bln", 1)])
            kb.op("dve", lambda e: e.tensor_scalar(
                sb["bln"][:, 0:1], sb["bln"][:, 0:1], 1.0 / D, None, op0=ALU.mult),
                reads=[("bln", 0)], writes=[("bln", 0)])
            kb.op("dve", lambda e: e.tensor_scalar(
                sb["bln"][:, 1:2], sb["bln"][:, 1:2], 1.0 / D, None, op0=ALU.mult),
                reads=[("bln", 1)], writes=[("bln", 1)])
            kb.op("dve", lambda e: e.tensor_mul(
                sb["bln"][:, 2:3], sb["bln"][:, 0:1], sb["bln"][:, 0:1]),
                reads=[("bln", 0)], writes=[("bln", 2)])
            kb.op("dve", lambda e: e.tensor_sub(
                sb["bln"][:, 1:2], sb["bln"][:, 1:2], sb["bln"][:, 2:3]),
                reads=[("bln", 1), ("bln", 2)], writes=[("bln", 1)])
            kb.op("act", lambda e: e.activation(
                out=sb["bln"][:, 1:2], in_=sb["bln"][:, 1:2], func=AF.Ln, bias=LNEPS),
                reads=[("bln", 1)], writes=[("bln", 1)])
            kb.op("act", lambda e: e.activation(
                out=sb["bln"][:, 1:2], in_=sb["bln"][:, 1:2], func=AF.Exp, scale=-0.5),
                reads=[("bln", 1)], writes=[("bln", 1)])
            kb.op("dve", lambda e: e.tensor_scalar(
                sb["bln"][:, 2:3], sb["bln"][:, 0:1], -1.0, None, op0=ALU.mult),
                reads=[("bln", 0)], writes=[("bln", 2)])
            kb.op("dve", lambda e: e.tensor_scalar(
                sb["bx1"][:], sb["br"][:], sb["bln"][:, 2:3], sb["bln"][:, 1:2],
                op0=ALU.add, op1=ALU.mult),
                reads=[("br",), ("bln", 1), ("bln", 2)], writes=[("bx1",)])
            kb.dma("qa", lambda e, ch=ch: e.dma_start(
                out=sp["x1"][(ch - 1) * C:(ch + 1) * C, :], in_=sb["bx1"][:]),
                reads=[("bx1",)], writes=[("spx1", ch // 2)])


Builder.alloc_b_arena = alloc_b_arena
Builder.phase_bscan = phase_bscan


def alloc_f1_arena(self):
    tl = self.cfg.tloc
    self.arena_reset()
    self.arena("x1fm", [128, 16 * tl], BF16)
    for par in (0, 1, 2, 3):
        self.arena("wgb%d" % par, [128, 16 * 128], BF16)
        self.arena("wub%d" % par, [128, 16 * 128], BF16)
    for par in (0, 1):
        self.arena("xt%d" % par, [128, D], BF16)
    self.arena("gact", [128, tl], BF16)
    self.arena("uact", [128, tl], BF16)
    self.arena("gm0", [128, tl], BF16)
    self.arena("gm1", [128, tl], BF16)


def phase_f1(self):
    kb, sb, sp, din = self.kb, self.sb, self.spills, self.din
    tl, ntb = self.cfg.tloc, self.cfg.ntb
    pb = self.pb
    PSA = self.PSA
    njt = ceil_div(tl, 512)
    for tb in range(ntb):
        par = tb % 2
        xt = sb["xt%d" % par]
        kb.dma("qs", lambda e, tb=tb, xt=xt: e.dma_start(
            out=xt[:], in_=sp["x1"][tb * 128:(tb + 1) * 128, :]),
            reads=[("spx1", tb // 1)], writes=[("xt", par)])
        for half in range(2):
            for dc8 in range(8):
                dc = half * 8 + dc8
                kb.op("pe", lambda e, dc=dc, dc8=dc8, xt=xt: e.transpose(
                    pb(6).bitcast(BF16)[:, dc8 * 128:(dc8 + 1) * 128],
                    xt[:, dc * 128:(dc + 1) * 128], sb["ident"][:]),
                    reads=[("xt", par), ("sb", "ident")], writes=[("pb", 6)])
            kb.op("act", lambda e, tb=tb, half=half: e.activation(
                out=sb["x1fm"][:].rearrange("p (dc t) -> p dc t", t=tl)[
                    :, half * 8:(half + 1) * 8, tb * 128:(tb + 1) * 128],
                in_=pb(6).bitcast(BF16)[:, 0:1024].rearrange(
                    "p (dc t) -> p dc t", t=128),
                func=AF.Copy),
                reads=[("pb", 6) for i in range(8)], writes=[("x1fm", tb, half)])
    x1fm_rd = [("x1fm", tb, half) for tb in range(ntb) for half in range(2)]

    def emit_wload(hc):
        par = hc % 4
        wgb, wub = sb["wgb%d" % par], sb["wub%d" % par]
        kb.dma("qp", lambda e, hc=hc, wgb=wgb: e.dma_start(
            out=wgb[:], in_=din["WgT8"][hc * 128:(hc + 1) * 128, :]),
            writes=[("wgb", par)])
        kb.op("dve", lambda e, wgb=wgb: e.tensor_scalar_add(wgb[:], wgb[:], -128.0),
              reads=[("wgb", par)], writes=[("wgb", par)])
        kb.dma("qp", lambda e, hc=hc, wub=wub: e.dma_start(
            out=wub[:], in_=din["WuT8"][hc * 128:(hc + 1) * 128, :]),
            writes=[("wub", par)])
        kb.op("dve", lambda e, wub=wub: e.tensor_scalar_add(wub[:], wub[:], -128.0),
              reads=[("wub", par)], writes=[("wub", par)])

    for hc in range(min(3, NHH)):
        emit_wload(hc)
    for hc in range(NHH):
        par = hc % 4
        wgb, wub = sb["wgb%d" % par], sb["wub%d" % par]
        if hc + 3 < NHH:
            emit_wload(hc + 3)
        for jt in range(njt):
            t0, tn = jt * 512, min(512, tl - jt * 512)
            for dc in range(16):
                kb.op("pe", lambda e, dc=dc, jt=jt, t0=t0, tn=tn, wgb=wgb: e.matmul(
                    PSA.ap()[:, jt * 512:jt * 512 + tn],
                    wgb[:, dc * 128:(dc + 1) * 128],
                    sb["x1fm"][:, dc * tl + t0: dc * tl + t0 + tn],
                    start=(dc == 0), stop=(dc == 15)),
                    reads=[("wgb", par)] + x1fm_rd, writes=[("pb", jt)])
            for dc in range(16):
                kb.op("pe", lambda e, dc=dc, jt=jt, t0=t0, tn=tn, wub=wub: e.matmul(
                    pb(4 + jt)[:, 0:tn],
                    wub[:, dc * 128:(dc + 1) * 128],
                    sb["x1fm"][:, dc * tl + t0: dc * tl + t0 + tn],
                    start=(dc == 0), stop=(dc == 15)),
                    reads=[("wub", par)] + x1fm_rd, writes=[("pb", 4 + jt)])
            kb.op("act", lambda e, hc=hc, jt=jt, t0=t0, tn=tn: e.activation(
                out=sb["gact"][:, t0:t0 + tn], in_=PSA.ap()[:, jt * 512:jt * 512 + tn],
                func=AF.Silu, scale=sb["sgm"][:, hc:hc + 1]),
                reads=[("pb", jt), ("sb", "sgm")], writes=[("gact", jt)])
            kb.op("act", lambda e, hc=hc, jt=jt, t0=t0, tn=tn: e.activation(
                out=sb["uact"][:, t0:t0 + tn], in_=pb(4 + jt)[:, 0:tn],
                func=AF.Copy, scale=sb["sum_"][:, hc:hc + 1]),
                reads=[("pb", 4 + jt), ("sb", "sum_")], writes=[("uact", jt)])
        gm = sb["gm%d" % (par % 2)]
        kb.op("dve", lambda e, gm=gm: e.tensor_mul(gm[:], sb["gact"][:], sb["uact"][:]),
              reads=[("gact", jt) for jt in range(njt)] +
                    [("uact", jt) for jt in range(njt)],
              writes=[("gm", par % 2)])
        kb.dma("qs", lambda e, hc=hc, gm=gm: e.dma_start(out=sp["G"][hc], in_=gm[:]),
               reads=[("gm", par % 2)], writes=[("spG", hc)])


def alloc_f2_arena(self):
    tl, ntb = self.cfg.tloc, self.cfg.ntb
    self.arena_reset()
    self.arena("wdh", [128, NHH * 512], BF16)
    self.arena("gtb", [128, NHH * 128], BF16)
    self.arena("x1tb0", [128, 512], BF16)
    self.arena("x1tb1", [128, 512], BF16)
    self.arena("r2q0", [128, ntb * 512], BF16)
    self.arena("r2q1", [128, ntb * 512], BF16)
    self.arena("r2q2", [128, ntb * 512], BF16)
    self.arena("r2c", [128, 512], BF16)
    self.arena("outstg0", [128, D], F32)
    self.arena("fln", [128, 12], F32)
    self.arena("sdb", [128, 512], BF16)
    self.arena("sdrow", [1, D], BF16)


def phase_f2(self):
    kb, sb, sp, din = self.kb, self.sb, self.spills, self.din
    tl, ntb = self.cfg.tloc, self.cfg.ntb
    pb = self.pb
    PSA = self.PSA
    g3 = sp["G"]
    kb.dma("qp", lambda e: e.dma_start(out=sb["sdrow"][:], in_=din["sdr"][:]),
           writes=[("sdrow",)])
    for q in range(4):
        kb.op("pe", lambda e, q=q: e.matmul(
            PSA.ap()[:, 1536:2048], sb["ones1x128"][:],
            sb["sdrow"][:, q * 512:(q + 1) * 512], start=True, stop=True),
            reads=[("sdrow",), ("sb", "ones1x128")], writes=[("pb", 3)])
        kb.op("act", lambda e: e.activation(
            out=sb["sdb"][:], in_=PSA.ap()[:, 1536:2048], func=AF.Copy),
            reads=[("pb", 3)], writes=[("sdb",)])
        kb.dma("qp", lambda e, q=q: e.dma_start(
            out=sb["wdh"][:], in_=din["WdT8"][q * 128:(q + 1) * 128, :]),
            writes=[("wdh",)])
        for hc in range(NHH):
            kb.op("dve", lambda e, hc=hc: e.scalar_tensor_tensor(
                sb["wdh"][:, hc * 512:(hc + 1) * 512],
                sb["wdh"][:, hc * 512:(hc + 1) * 512], -128.0, sb["sdb"][:],
                op0=ALU.add, op1=ALU.mult),
                reads=[("wdh",), ("sdb",)], writes=[("wdh",)])
        for tb in range(ntb):
            par = tb % 2
            xt = sb["x1tb%d" % par]
            kb.dma("qs", lambda e, tb=tb: e.dma_start(
                out=sb["gtb"][:], in_=g3[:, :, tb * 128:(tb + 1) * 128].rearrange(
                    "hc p t -> p hc t")),
                reads=[("spG", hc) for hc in range(NHH)], writes=[("gtb",)])
            kb.dma("qa", lambda e, tb=tb, xt=xt, q=q: e.dma_start(
                out=xt[:], in_=sp["x1"][tb * 128:(tb + 1) * 128, q * 512:(q + 1) * 512]),
                reads=[("spx1", tb)], writes=[("x1tb", par)])
            bk = tb % 2
            for hc in range(NHH):
                kb.op("pe", lambda e, hc=hc, bk=bk: e.matmul(
                    PSA.ap()[:, bk * 512:(bk + 1) * 512],
                    sb["gtb"][:, hc * 128:(hc + 1) * 128],
                    sb["wdh"][:, hc * 512:(hc + 1) * 512],
                    start=(hc == 0), stop=(hc == NHH - 1)),
                    reads=[("gtb",), ("wdh",)], writes=[("pb", bk)])
            if q < 3:
                kb.op("dve", lambda e, tb=tb, bk=bk, xt=xt, q=q: e.tensor_add(
                    sb["r2q%d" % q][:, tb * 512:(tb + 1) * 512],
                    PSA.ap()[:, bk * 512:(bk + 1) * 512], xt[:]),
                    reads=[("pb", bk), ("x1tb", par)], writes=[("r2q", q, tb)])
            else:
                kb.op("dve", lambda e, tb=tb, bk=bk, xt=xt: e.tensor_add(
                    sb["r2c"][:], PSA.ap()[:, bk * 512:(bk + 1) * 512], xt[:]),
                    reads=[("pb", bk), ("x1tb", par)], writes=[("r2c",)])
                parts = [sb["r2q0"][:, tb * 512:(tb + 1) * 512],
                         sb["r2q1"][:, tb * 512:(tb + 1) * 512],
                         sb["r2q2"][:, tb * 512:(tb + 1) * 512],
                         sb["r2c"][:]]
                rregs = [("r2q", 0, tb), ("r2q", 1, tb), ("r2q", 2, tb), ("r2c",)]
                for i4, (p4, rr) in enumerate(zip(parts, rregs)):
                    kb.op("dve", lambda e, p4=p4, i4=i4: e.reduce_sum(
                        sb["fln"][:, i4:i4 + 1], p4.unsqueeze(1), axis=AX.X),
                        reads=[rr], writes=[("fln", i4)])
                    kb.op("act", lambda e, p4=p4, i4=i4: e.activation(
                        out=pb(4 + i4)[:], in_=p4, func=AF.Square),
                        reads=[rr, ("pb", 4 + i4)], writes=[("pb", 4 + i4)])
                    kb.op("dve", lambda e, i4=i4: e.reduce_sum(
                        sb["fln"][:, 4 + i4:5 + i4], pb(4 + i4)[:].unsqueeze(1), axis=AX.X),
                        reads=[("pb", 4 + i4)], writes=[("fln", 4 + i4)])
                for i4 in (1, 2, 3):
                    kb.op("dve", lambda e, i4=i4: e.tensor_add(
                        sb["fln"][:, 0:1], sb["fln"][:, 0:1], sb["fln"][:, i4:i4 + 1]),
                        reads=[("fln", 0), ("fln", i4)], writes=[("fln", 0)])
                    kb.op("dve", lambda e, i4=i4: e.tensor_add(
                        sb["fln"][:, 4:5], sb["fln"][:, 4:5], sb["fln"][:, 4 + i4:5 + i4]),
                        reads=[("fln", 4), ("fln", 4 + i4)], writes=[("fln", 4)])
                kb.op("dve", lambda e: e.tensor_scalar(
                    sb["fln"][:, 0:1], sb["fln"][:, 0:1], 1.0 / D, None, op0=ALU.mult),
                    reads=[("fln", 0)], writes=[("fln", 0)])
                kb.op("dve", lambda e: e.tensor_scalar(
                    sb["fln"][:, 4:5], sb["fln"][:, 4:5], 1.0 / D, None, op0=ALU.mult),
                    reads=[("fln", 4)], writes=[("fln", 4)])
                kb.op("dve", lambda e: e.tensor_mul(
                    sb["fln"][:, 8:9], sb["fln"][:, 0:1], sb["fln"][:, 0:1]),
                    reads=[("fln", 0)], writes=[("fln", 8)])
                kb.op("dve", lambda e: e.tensor_sub(
                    sb["fln"][:, 4:5], sb["fln"][:, 4:5], sb["fln"][:, 8:9]),
                    reads=[("fln", 4), ("fln", 8)], writes=[("fln", 4)])
                kb.op("act", lambda e: e.activation(
                    out=sb["fln"][:, 4:5], in_=sb["fln"][:, 4:5], func=AF.Ln, bias=LNEPS),
                    reads=[("fln", 4)], writes=[("fln", 4)])
                kb.op("act", lambda e: e.activation(
                    out=sb["fln"][:, 4:5], in_=sb["fln"][:, 4:5], func=AF.Exp, scale=-0.5),
                    reads=[("fln", 4)], writes=[("fln", 4)])
                kb.op("dve", lambda e: e.tensor_scalar(
                    sb["fln"][:, 8:9], sb["fln"][:, 0:1], -1.0, None, op0=ALU.mult),
                    reads=[("fln", 0)], writes=[("fln", 8)])
                for i4, (p4, rr) in enumerate(zip(parts, rregs)):
                    kb.op("dve", lambda e, p4=p4, i4=i4: e.tensor_scalar(
                        sb["outstg0"][:, i4 * 512:(i4 + 1) * 512], p4,
                        sb["fln"][:, 8:9], sb["fln"][:, 4:5],
                        op0=ALU.add, op1=ALU.mult),
                        reads=[rr, ("fln", 4), ("fln", 8)], writes=[("ost", i4)])
                kb.dma("qa", lambda e, tb=tb: e.dma_start(
                    out=self.out_t[tb * 128:(tb + 1) * 128, :], in_=sb["outstg0"][:]),
                    reads=[("ost", 0), ("ost", 1), ("ost", 2), ("ost", 3)],
                    writes=[("outd", tb)])


Builder.alloc_f1_arena = alloc_f1_arena
Builder.phase_f1 = phase_f1
Builder.alloc_f2_arena = alloc_f2_arena
Builder.phase_f2 = phase_f2


def kernel(**inputs):
    cfg = Cfg(tloc=T // NCORES, ncores=NCORES)
    in_maps = host_prep(inputs, cfg)
    b = build_program(cfg)
    res = run_bass_kernel_spmd(b.nc, in_maps, core_ids=list(range(NCORES)))
    out = np.concatenate([np.asarray(res.results[ci]["out"]) for ci in range(NCORES)],
                         axis=0)
    return out.reshape(1, T, D).astype(np.float32)


# version 11 (final)
# speedup vs baseline: 2.0155x; 1.0668x over previous
import numpy as np
import ml_dtypes
import concourse.bass as bass
import concourse.mybir as mybir
from concourse.bass_utils import run_bass_kernel_spmd

F32 = mybir.dt.float32
BF16 = mybir.dt.bfloat16
I32 = mybir.dt.int32
U8 = mybir.dt.uint8
AF = mybir.ActivationFunctionType
ALU = mybir.AluOpType
AX = mybir.AxisListType
BF = ml_dtypes.bfloat16

NCORES = 8
T = 16384
D = 2048
H = 16
DK = 128
M = 128
KW = 4
C = 64
HB = 64
HID = 5632
NHH = HID // 128
SCALE = DK ** -0.5
EPS = 1e-5
L2EPS = 1e-6
LNEPS = 1e-5

ENGINES = ("pe", "act", "dve", "pool", "sp")
PSUM_COARSE = False
QUEUES = ("qs", "qa", "qp")


class KB:

    def __init__(self, nc):
        self.nc = nc
        self.q = {e: [] for e in ENGINES}
        self.cnt = {e: 0 for e in ENGINES}
        self.dcnt = {qn: 0 for qn in QUEUES}
        self.wm = {e: {} for e in ENGINES}
        self.lastw = {}
        self.readers = {}
        self.sems = {}
        for s in ("pe", "act", "dve", "pool", "qs", "qa", "qp"):
            self.sems[s] = nc.alloc_semaphore("sem_" + s)
        self.dma_engine = {"qs": "sp", "qa": "act", "qp": "pool"}
        self.fence = {qn: 0 for qn in QUEUES}

    def _sem_val(self, tok):
        s, v = tok
        if s in QUEUES:
            return (s, 16 * self._fence_for(s, v))
        return (s, v)

    def _fence_for(self, qn, k):
        if self.fence[qn] >= k:
            return self.fence[qn]
        n = self.dcnt[qn]
        eng = self.dma_engine[qn]
        sems = self.sems
        self.q[eng].append(lambda e, qn=qn, n=n: e.wait_ge(sems[qn], 16 * n))
        self.wm[eng][qn] = 16 * n
        self.fence[qn] = n
        return n

    def _edges(self, eng, reads, writes):
        need = {}
        for r in reads:
            t = self.lastw.get(r)
            if t is not None:
                s, v = self._sem_val(t)
                need[s] = max(need.get(s, 0), v)
        for w in writes:
            t = self.lastw.get(w)
            if t is not None:
                s, v = self._sem_val(t)
                need[s] = max(need.get(s, 0), v)
            for t in self.readers.get(w, ()):
                s, v = self._sem_val(t)
                need[s] = max(need.get(s, 0), v)
        wmap = self.wm[eng]
        out = []
        for s, v in need.items():
            if eng == "pe" and s == "pe":
                continue
            if wmap.get(s, 0) < v:
                out.append((s, v))
                wmap[s] = v
        return out

    def op(self, eng, fn, reads=(), writes=(), sem_extra=None):
        reads = list(reads)
        writes = list(writes)
        for r in list(reads):
            if isinstance(r[0], str) and r[0].startswith("pb"):
                reads.remove(r)
                if r not in writes:
                    writes.append(r)
        if PSUM_COARSE:
            nw = []
            coarse = False
            for w in writes:
                if isinstance(w[0], str) and w[0].startswith("pb"):
                    coarse = True
                else:
                    nw.append(w)
            if coarse:
                nw.append(("PSUM",))
            writes = nw
        waits = self._edges(eng, reads, writes)
        self.cnt[eng] += 1
        tok = (eng, self.cnt[eng])
        sem = self.sems[eng]
        sems = self.sems

        def emit(e, waits=waits, fn=fn, sem=sem, sem_extra=sem_extra):
            for s, v in waits[1:]:
                e.wait_ge(sems[s], v)
            inst = fn(e)
            if waits:
                inst._wait_ge(sems[waits[0][0]], waits[0][1])
            inst.then_inc(sem, 1)
            if sem_extra is not None:
                inst.then_inc(sem_extra[0], sem_extra[1])
        self.q[eng].append(emit)
        for w in writes:
            self.lastw[w] = tok
            self.readers[w] = []
        for r in reads:
            self.readers.setdefault(r, []).append(tok)
        return tok

    def dma(self, qn, fn, reads=(), writes=()):
        eng = self.dma_engine[qn]
        waits = self._edges(eng, reads, writes)
        self.dcnt[qn] += 1
        tok = (qn, self.dcnt[qn])
        sem = self.sems[qn]
        sems = self.sems

        def emit(e, waits=waits, fn=fn, sem=sem):
            for s, v in waits[1:]:
                e.wait_ge(sems[s], v)
            inst = fn(e)
            if waits:
                inst._wait_ge(sems[waits[0][0]], waits[0][1])
            inst.then_inc(sem, 16)
        self.q[eng].append(emit)
        for w in writes:
            self.lastw[w] = tok
            self.readers[w] = []
        for r in reads:
            self.readers.setdefault(r, []).append(tok)
        return tok

    def barrier_wait(self, eng, toks):
        need = {}
        for t in toks:
            s, v = self._sem_val(t)
            need[s] = max(need.get(s, 0), v)
        wmap = self.wm[eng]
        sems = self.sems
        for s, v in need.items():
            if wmap.get(s, 0) < v:
                wmap[s] = v
                self.q[eng].append(lambda e, s=s, v=v: e.wait_ge(sems[s], v))

    def finish(self):
        toks = [(e, self.cnt[e]) for e in ENGINES if e != "sp" and self.cnt[e]]
        toks += [(qn, self.dcnt[qn]) for qn in QUEUES if self.dcnt[qn]]
        for e in ENGINES:
            self.barrier_wait(e, toks)

    def replay(self):
        nc = self.nc
        with nc.Block() as block:
            @block.tensor
            def _(e):
                for f in self.q["pe"]:
                    f(e)

            @block.scalar
            def _(e):
                for f in self.q["act"]:
                    f(e)

            @block.vector
            def _(e):
                for f in self.q["dve"]:
                    f(e)

            @block.gpsimd
            def _(e):
                for f in self.q["pool"]:
                    f(e)

            @block.sync
            def _(e):
                for f in self.q["sp"]:
                    f(e)


def ceil_div(a, b):
    return (a + b - 1) // b



class Cfg:

    def __init__(self, tloc=2048, phases=("aproj", "az", "amisc", "ascan", "aagg",
                                          "gather", "bscan", "f1", "f2"),
                 ncores=NCORES, debug_taps=(), host_gather=False):
        self.tloc = tloc
        self.nch = tloc // C
        self.ntb = tloc // 128
        self.phases = phases
        self.ncores = ncores
        self.debug_taps = debug_taps
        self.host_gather = host_gather


class Builder:
    def __init__(self, cfg):
        self.cfg = cfg
        self.nc = bass.Bass("TRN2", target_bir_lowering=False)
        self.kb = KB(self.nc)
        self.din = {}
        self.spills = {}
        self.sb = {}
        self.ps = {}
        self.ar_off = 0
        self._declare_dram()
        self._declare_sbuf()

    def inp(self, name, shape, dtype=F32):
        self.din[name] = self.nc.dram_tensor(name, list(shape), dtype,
                                             kind="ExternalInput").ap()
        return self.din[name]

    def _declare_dram(self):
        cfg, tl, nch = self.cfg, self.cfg.tloc, self.cfg.nch
        inp = self.inp
        inp("hT", [D, HB + tl])
        inp("h", [tl, D])
        for w in ("Wq", "Wk", "Wv", "Ww", "Wo"):
            inp(w, [D, D])
        inp("cqr", [128, 16 * KW]); inp("ckr", [128, 16 * KW]); inp("cvr", [128, 16 * KW])
        inp("Wf1", [D, 128]); inp("Wf2", [128, D]); inp("Wbr", [128, 16 * H])
        inp("Wg1", [D, 128]); inp("Wg2", [128, D])
        inp("bg2m", [128, 16]); inp("normw", [128, 1])
        inp("WgT8", [NHH * 128, D], U8); inp("WuT8", [NHH * 128, D], U8)
        inp("WdT8", [4 * 128, NHH * 512], U8)
        inp("sgm", [128, NHH]); inp("sum_", [128, NHH]); inp("sdr", [1, D])
        inp("ident", [128, 128], BF16)
        inp("identf", [128, 128])
        inp("tri", [C, C], BF16)
        inp("sumsels", [128, 16 * 32], BF16)
        inp("trige", [32, 32])
        inp("ones32", [32, 1])
        inp("ones1x128", [1, 128], BF16)
        inp("ones128", [128, 1], BF16)
        inp("mselM", [2, (NCORES - 1) * 128], BF16)
        inp("mselcol", [128, NCORES - 1])
        inp("m1col", [128, NCORES - 1])
        self.out_t = self.nc.dram_tensor("out", [tl, D], F32, kind="ExternalOutput").ap()
        sp = self.spills

        def spill(name, shape, dtype=BF16, kind="Internal"):
            sp[name] = self.nc.dram_tensor("sp_" + name, list(shape), dtype, kind=kind).ap()
        spill("q", [H, 128, tl])
        spill("kfm", [H, 128, tl])
        spill("ktm", [H, tl, 128])
        spill("vtm", [H, tl, 128])
        spill("wtm", [H, tl, 128])
        spill("g", [tl, D])
        spill("gs", [H, 128, tl])
        spill("u", [tl, D])
        spill("si", [tl, D])
        spill("x1", [tl, D])
        spill("G", [NHH, 128, tl])
        spill("ec", [nch, D])
        spill("wsuf", [nch, D])
        kci = "ExternalOutput" if cfg.host_gather else "Internal"
        kco = "ExternalInput" if cfg.host_gather else "Internal"
        spill("ccin", [257, D], BF16, kind=kci)
        spill("ccout", [cfg.ncores * 257, D], BF16, kind=kco)
        self.dbg = {}
        for name in cfg.debug_taps:
            s = sp[name]
            self.dbg[name] = self.nc.dram_tensor(
                "dbg_" + name, list(s.shape), s.dtype, kind="ExternalOutput").ap()

    def sbuf(self, name, shape, dtype=BF16):
        t = self.nc.alloc_sbuf_tensor("sb_" + name, list(shape), dtype)
        self.sb[name] = t
        return t

    def arena(self, name, shape, dtype):
        esz = {F32: 4, BF16: 2, U8: 1, I32: 4}[dtype]
        nbytes = shape[1] * esz
        off = self.ar_off
        assert off + nbytes <= self.AR_BYTES, f"arena overflow {name}: {off}+{nbytes}"
        ap = self.AR.ap().bitcast(dtype)[0:shape[0], off // esz: off // esz + shape[1]]
        self.ar_off = off + ((nbytes + 31) // 32) * 32
        self.sb[name] = ap
        return ap

    def arena_reset(self):
        self.ar_off = 0

    def _declare_sbuf(self):
        cfg, tl, nch = self.cfg, self.cfg.tloc, self.cfg.nch
        self.AR_BYTES = 156 * 1024
        self.AR = self.nc.alloc_sbuf_tensor("sb_AR", [128, self.AR_BYTES], U8)
        sbuf = self.sbuf
        sbuf("ident", [128, 128], BF16)
        sbuf("m1col", [128, NCORES - 1], F32)
        sbuf("tri", [C, C], BF16)
        sbuf("ones1x128", [1, 128], BF16)
        sbuf("ones128", [128, 1], BF16)
        sbuf("beta", [128, (tl // 128) * H], BF16)
        sbuf("beta2", [64, 2 * (tl // 128) * H], BF16)
        sbuf("cCrows", [32, D], F32)
        sbuf("ec16all", [16, nch * 128], BF16)
        sbuf("Sk", [128, H * M], F32)
        sbuf("Sv", [128, H * DK], F32)
        sbuf("Skbf", [128, H * M], BF16)
        sbuf("Svbf", [128, H * DK], BF16)
        sbuf("ecct", [128, 16], F32)
        sbuf("normw", [128, 1], F32)
        sbuf("bg2m", [128, 16], F32)
        sbuf("cw", [128, 3 * 16 * KW], F32)
        sbuf("sumsels", [128, 16 * 32], BF16)
        sbuf("trige", [32, 32], F32)
        sbuf("ones32", [32, 1], F32)
        sbuf("sgm", [128, NHH], F32)
        sbuf("sum_", [128, NHH], F32)
        sbuf("mselM", [2, (NCORES - 1) * 128], BF16)
        sbuf("mselcol", [128, NCORES - 1], F32)
        sbuf("row2", [2, D], BF16)
        self.CVALS = [L2EPS, LNEPS, EPS, SCALE, -0.5, -1.0, 0.5, 2.0, 1.0 / 128, -128.0,
                      1.0 / D, 1.0]
        sbuf("cvals", [128, len(self.CVALS)], F32)
        sbuf("c128", [128, 1], BF16)
        self.PSA = self.nc.alloc_psum_tensor("psA", [128, 2048], F32)
        self.PBH = {i: self.nc.alloc_psum_tensor("ps%d" % i, [128, 512], F32)
                    for i in range(4, 8)}

    def pb(self, i):
        if i < 4:
            return self.PSA.ap()[:, i * 512:(i + 1) * 512]
        return self.PBH[i].ap()

    def ld_consts(self):
        kb, din, sb = self.kb, self.din, self.sb
        pairs = [("ident", "ident"), ("tri", "tri"), ("ones1x128", "ones1x128"),
                 ("ones128", "ones128"), ("normw", "normw"), ("bg2m", "bg2m"),
                 ("sumsels", "sumsels"), ("trige", "trige"), ("ones32", "ones32"),
                 ("m1col", "m1col"),
                 ("sgm", "sgm"), ("sum_", "sum_"), ("mselcol", "mselcol")]
        for dn, sn in pairs:
            kb.dma("qs", lambda e, d=din[dn], s=sb[sn]: e.dma_start(out=s[:], in_=d[:]),
                   writes=[("sb", sn)])
        kb.dma("qs", lambda e: e.dma_start(out=sb["mselM"][:], in_=din["mselM"][:]),
               writes=[("sb", "mselM")])
        for i, dn in enumerate(("cqr", "ckr", "cvr")):
            kb.dma("qs", lambda e, d=din[dn], i=i: e.dma_start(
                out=sb["cw"][:, i * 64:(i + 1) * 64], in_=d[:]),
                writes=[("sb", "cw", i)])
        kb.op("dve", lambda e: e.memset(sb["row2"][:], 1.0), writes=[("sb", "row2one")])
        kb.op("dve", lambda e: e.memset(sb["c128"][:], 128.0), writes=[("sb", "c128")])
        for i, v in enumerate(self.CVALS):
            kb.op("dve", lambda e, i=i, v=v: e.memset(sb["cvals"][:, i:i + 1], v),
                  writes=[("cvals", i)])
            self.nc.const_aps.register(F32, v, sb["cvals"][:, i:i + 1])

    def alloc_a_arena(self):
        tl = self.cfg.tloc
        self.arena_reset()
        self.arena("hfm", [128, 16 * (HB + tl)], BF16)
        self.arena("wpan0", [128, 16 * 128], BF16)
        self.arena("wpan1", [128, 16 * 128], BF16)
        self.arena("pbf0", [128, HB + tl], BF16)
        self.arena("pbf1", [128, HB + tl], BF16)
        self.arena("convy0", [128, tl], BF16)
        self.arena("convy1", [128, tl], BF16)
        self.arena("tstg0", [128, tl], BF16)
        self.arena("tstg1", [128, tl], BF16)
        self.arena("z1", [128, tl], BF16)
        self.arena("zg1", [128, tl], BF16)
        self.arena("wsmall", [128, 2 * 16 * 128], BF16)
        self.arena("wf2", [128, D], BF16)
        self.arena("wg2", [128, D], BF16)
        self.arena("wbr", [128, 16 * H], BF16)
        self.arena("ztmp0", [128, 512], F32)
        self.arena("ztmp1", [128, 512], F32)
        self.arena("ztmp2", [128, 512], F32)
        self.arena("ztmp3", [128, 512], F32)
        self.arena("gstg", [128, D], BF16)
        self.arena("ecall", [32, D], BF16)
        self.arena("wsufb", [32, D], BF16)

    def jgroups(self):
        tl = self.cfg.tloc
        return [(0, HB)] + [(HB + i * 512, min(512, tl - i * 512))
                            for i in range(ceil_div(tl, 512))]

    def ld_hfm(self):
        kb, sb, din = self.kb, self.sb, self.din
        tl = self.cfg.tloc
        for dc in range(16):
            kb.dma("qp", lambda e, dc=dc: e.dma_start(
                out=sb["hfm"][:, dc * (HB + tl):(dc + 1) * (HB + tl)],
                in_=din["hT"][dc * 128:(dc + 1) * 128, :]),
                writes=[("hfm", dc)])

    def phase_aproj(self):
        kb, sb, din, sp = self.kb, self.sb, self.din, self.spills
        tl, ntb = self.cfg.tloc, self.cfg.ntb
        pb = self.pb
        HT = HB + tl
        tensors = [("q", "Wq", 0, ("q",), False),
                   ("k", "Wk", 1, ("kfm", "ktm"), True),
                   ("v", "Wv", 2, ("vtm",), True),
                   ("w", "Ww", 2, ("wtm",), True)]
        for xi, (xn, wn, cwi, spnames, want_tm) in enumerate(tensors):
            for fc in range(16):
                par = fc % 2
                wp = sb["wpan%d" % par]
                kb.dma("qp", lambda e, wn=wn, fc=fc, wp=wp: e.dma_start(
                    out=wp[:], in_=din[wn][fc * 128:(fc + 1) * 128, :]),
                    writes=[("wpan", par)])
                pbf = sb["pbf%d" % par]
                for jg, (t0, tn) in enumerate(self.jgroups()):
                    bank = jg % 4
                    for dc in range(16):
                        kb.op("pe", lambda e, wp=wp, dc=dc, t0=t0, tn=tn, bank=bank: e.matmul(
                            pb(bank)[:, 0:tn],
                            wp[:, dc * 128:(dc + 1) * 128],
                            sb["hfm"][:, dc * HT + t0: dc * HT + t0 + tn],
                            start=(dc == 0), stop=(dc == 15)),
                            reads=[("wpan", par), ("hfm", dc)], writes=[("pb", bank)])
                    kb.op("act", lambda e, t0=t0, tn=tn, bank=bank, pbf=pbf: e.activation(
                        out=pbf[:, t0:t0 + tn], in_=pb(bank)[:, 0:tn], func=AF.Copy),
                        reads=[("pb", bank)], writes=[("pbf", par, jg)])
                cy = sb["convy%d" % par]
                pr = [("pbf", par, jg) for jg in range(len(self.jgroups()))]
                cwbase = cwi * 64 + fc * KW
                kb.op("dve", lambda e, cy=cy, pbf=pbf,
                      cc=sb["cw"][:, cwbase + 3:cwbase + 4]: e.tensor_scalar(
                    cy[:], pbf[:, HB:HB + tl], cc, None, op0=ALU.mult),
                    reads=pr + [("sb", "cw", cwi)], writes=[("convy", par)])
                for i, eng in ((2, "dve"), (1, "dve"), (0, "dve")):
                    kb.op(eng, lambda e, i=i, cy=cy, pbf=pbf,
                          cc=sb["cw"][:, cwbase + i:cwbase + i + 1]: e.scalar_tensor_tensor(
                        cy[:], pbf[:, HB - 3 + i:HB - 3 + i + tl], cc, cy[:],
                        op0=ALU.mult, op1=ALU.add),
                        reads=pr + [("convy", par)], writes=[("convy", par)])
                kb.op("act", lambda e, cy=cy: e.activation(out=cy[:], in_=cy[:], func=AF.Silu),
                      reads=[("convy", par)], writes=[("convy", par)])
                for nm in spnames:
                    if nm in ("q", "kfm"):
                        kb.dma("qs", lambda e, nm=nm, fc=fc, cy=cy: e.dma_start(
                            out=sp[nm][fc], in_=cy[:]),
                            reads=[("convy", par)], writes=[("sp" + nm, fc)])
                if want_tm:
                    nmtm = spnames[-1]
                    tstg = sb["tstg%d" % par]
                    for tb in range(ntb):
                        tbank = 6
                        kb.op("pe", lambda e, tb=tb, cy=cy, tbank=tbank: e.transpose(
                            pb(tbank).bitcast(BF16)[:, (tb % 8) * 128:(tb % 8) * 128 + 128],
                            cy[:, tb * 128:(tb + 1) * 128], sb["ident"][:]),
                            reads=[("convy", par), ("sb", "ident")],
                            writes=[("pbT", tbank, tb % 8)])
                        if tb % 8 == 7 or tb == ntb - 1:
                            lo = (tb // 8) * 8
                            kb.op("act", lambda e, lo=lo, tb=tb, tstg=tstg, tbank=tbank:
                                  e.activation(
                                      out=tstg[:, lo * 128:(tb + 1) * 128],
                                      in_=pb(tbank).bitcast(BF16)[:, 0:(tb + 1 - lo) * 128],
                                      func=AF.Copy),
                                  reads=[("pbT", tbank, i) for i in range(tb + 1 - lo)],
                                  writes=[("tstg", par, tb // 8)])
                    kb.dma("qs", lambda e, nmtm=nmtm, fc=fc, tstg=tstg: e.dma_start(
                        out=sp[nmtm][fc].rearrange("(tb p) f -> p tb f", p=128),
                        in_=tstg[:, 0:tl]),
                        reads=[("tstg", par, i) for i in range(ceil_div(ntb, 8))],
                        writes=[("sp" + nmtm, fc)])


def build_program(cfg):
    b = Builder(cfg)
    b.ld_consts()
    ph = cfg.phases
    if "aproj" in ph or "az" in ph or "amisc" in ph:
        b.alloc_a_arena()
        b.ld_hfm()
    for name in ("aproj", "az", "amisc", "ascan", "aagg", "gather", "bscan", "f1", "f2"):
        if name not in ph:
            continue
        if name == "ascan":
            b.alloc_ascan_arena()
        if name == "aagg":
            b.alloc_aagg_arena()
        if name == "bscan":
            b.alloc_b_arena()
        if name == "f1":
            b.alloc_f1_arena()
        if name == "f2":
            b.alloc_f2_arena()
        b.phase_fence()
        getattr(b, "phase_" + name)()
    b.kb.finish()
    for name, dst in b.dbg.items():
        srcap = b.spills[name]
        b.kb.dma("qa", lambda e, srcap=srcap, dst=dst: e.dma_start(
            out=dst.rearrange(_flat_str(dst.shape)), in_=srcap.rearrange(_flat_str(srcap.shape))))
    if b.dbg:
        b.kb.finish()
    b.kb.replay()
    return b


def _flat_str(shape):
    letters = "abcdef"[:len(shape)]
    return " ".join(letters) + " -> (" + " ".join(letters) + ")"


def host_prep(inputs, cfg):
    tl, n = cfg.tloc, cfg.ncores
    f32 = np.float32
    h_full = np.asarray(inputs["L0_attn_hidden_states"], f32)[0]
    hT_full = np.ascontiguousarray(h_full.T)
    zeros_halo = np.zeros((D, HB), f32)

    def u8_low_bytes(a_int32):
        a = np.ascontiguousarray(a_int32.astype(np.int32, copy=False))
        return np.ascontiguousarray(a.view(np.uint8).reshape(*a.shape, 4)[..., 0])

    g = {}
    def panelize(w, pf=128):
        d1, d2 = w.shape
        return np.ascontiguousarray(
            w.reshape(d1 // 128, 128, d2 // pf, pf).transpose(2, 1, 0, 3).reshape(d2 // pf * 128, d1 // 128 * pf))
    g["Wq"] = panelize(np.asarray(inputs["L0_attn_Wq"], f32))
    g["Wk"] = panelize(np.asarray(inputs["L0_attn_Wk"], f32))
    g["Wv"] = panelize(np.asarray(inputs["L0_attn_Wv"], f32))
    g["Ww"] = panelize(np.asarray(inputs["L0_attn_Ww"], f32))
    g["Wo"] = np.asarray(inputs["L0_attn_Wo"], f32)
    for cn, dn in (("cqr", "L0_attn_cq"), ("ckr", "L0_attn_ck"), ("cvr", "L0_attn_cv")):
        c = np.asarray(inputs[dn], f32)
        g[cn] = np.ascontiguousarray(c.reshape(16, 128, KW).transpose(1, 0, 2).reshape(128, 16 * KW))
    g["Wf1"] = np.asarray(inputs["L0_attn_Wf1"], f32)
    g["Wf2"] = np.asarray(inputs["L0_attn_Wf2"], f32)
    wb = np.asarray(inputs["L0_attn_Wb"], f32)
    g["Wbr"] = np.ascontiguousarray(wb.reshape(16, 128, H).transpose(1, 0, 2).reshape(128, 16 * H))
    g["Wg1"] = np.asarray(inputs["L0_attn_Wg1"], f32)
    g["Wg2"] = np.asarray(inputs["L0_attn_Wg2"], f32)
    bg2 = np.asarray(inputs["L0_attn_bg2"], f32)
    g["bg2m"] = np.ascontiguousarray(bg2.reshape(16, 128).T)
    g["normw"] = np.ascontiguousarray(np.asarray(inputs["L0_attn_norm_w"], f32).reshape(128, 1))
    g["WgT8"] = panelize(u8_low_bytes(np.ascontiguousarray(np.asarray(inputs["L0_ffn_wq_gate"]).T)))
    g["WuT8"] = panelize(u8_low_bytes(np.ascontiguousarray(np.asarray(inputs["L0_ffn_wq_up"]).T)))
    g["WdT8"] = panelize(u8_low_bytes(np.ascontiguousarray(np.asarray(inputs["L0_ffn_wq_down"]).T)), pf=512)
    g["sgm"] = np.ascontiguousarray(np.asarray(inputs["L0_ffn_s_gate"], f32).reshape(NHH, 128).T)
    g["sum_"] = np.ascontiguousarray(np.asarray(inputs["L0_ffn_s_up"], f32).reshape(NHH, 128).T)
    g["sdr"] = np.ascontiguousarray(np.asarray(inputs["L0_ffn_s_down"], f32).reshape(1, D))
    g["ident"] = np.eye(128, dtype=BF)
    g["identf"] = np.eye(128, dtype=f32)
    sl = np.arange(C)
    g["tri"] = (sl[:, None] <= sl[None, :]).astype(BF)
    ss = np.zeros((128, 16 * 32), BF)
    for tb in range(16):
        for t in range(128):
            ss[t, tb * 32 + 2 * tb + (t // 64)] = 1
    g["sumsels"] = ss
    jj = np.arange(32)
    g["trige"] = (jj[:, None] >= jj[None, :]).astype(f32)
    g["ones32"] = np.ones((32, 1), f32)
    g["ones1x128"] = np.ones((1, 128), BF)
    g["ones128"] = np.ones((128, 1), BF)

    in_maps = []
    for ci in range(n):
        m = dict(g)
        s = ci * tl
        m["h"] = h_full[s:s + tl]
        if s >= HB:
            m["hT"] = np.ascontiguousarray(hT_full[:, s - HB:s + tl])
        else:
            m["hT"] = np.ascontiguousarray(
                np.concatenate([zeros_halo[:, :HB - s], hT_full[:, 0:s + tl]], axis=1))
        msel = np.zeros((2, (NCORES - 1) * 128), BF)
        mcol = np.zeros((128, NCORES - 1), f32)
        for j in range(NCORES - 1):
            sel = 1.0 if j < ci else 0.0
            msel[0, j * 128:(j + 1) * 128] = sel
            msel[1, j * 128:(j + 1) * 128] = 1.0 - sel
            mcol[:, j] = sel
        m["mselM"] = msel
        m["mselcol"] = mcol
        m["m1col"] = np.ascontiguousarray(1.0 - mcol)
        if cfg.host_gather:
            m["sp_ccout"] = np.zeros((cfg.ncores * 257, D), BF)
        in_maps.append(m)
    return in_maps


AZ_CUT = 0


def phase_az(self):
    kb, sb, din, sp = self.kb, self.sb, self.din, self.spills
    tl, ntb, nch = self.cfg.tloc, self.cfg.ntb, self.cfg.nch
    pb = self.pb
    HT = HB + tl
    wf1_4 = din["Wf1"].rearrange("(dc p) f -> p dc f", p=128)
    kb.dma("qp", lambda e: e.dma_start(out=sb["wsmall"][:, 0:16 * 128], in_=wf1_4),
           writes=[("wsmall", 0)])
    kb.dma("qp", lambda e: e.dma_start(out=sb["wf2"][:], in_=din["Wf2"][:]),
           writes=[("wf2",)])
    for jg in range(ceil_div(tl, 512)):
        t0, tn = jg * 512, min(512, tl - jg * 512)
        bank = jg % 2
        for dc in range(16):
            kb.op("pe", lambda e, dc=dc, t0=t0, tn=tn, bank=bank: e.matmul(
                pb(bank)[:, 0:tn],
                sb["wsmall"][:, dc * 128:(dc + 1) * 128],
                sb["hfm"][:, dc * HT + HB + t0: dc * HT + HB + t0 + tn],
                start=(dc == 0), stop=(dc == 15)),
                reads=[("wsmall", 0), ("hfm", dc)], writes=[("pb", bank)])
        kb.op("act", lambda e, t0=t0, tn=tn, bank=bank: e.activation(
            out=sb["z1"][:, t0:t0 + tn], in_=pb(bank)[:, 0:tn], func=AF.Copy),
            reads=[("pb", bank)], writes=[("z1", jg)])
    if AZ_CUT >= 4:
        return
    njt = ceil_div(D, 512)
    for tb in range(ntb):
        for jt in range(njt):
            bank = (tb * njt + jt) % 4
            zpar = (tb * njt + jt) % 2
            kb.op("pe", lambda e, tb=tb, jt=jt, bank=bank: e.matmul(
                pb(bank)[:],
                sb["z1"][:, tb * 128:(tb + 1) * 128],
                sb["wf2"][:, jt * 512:(jt + 1) * 512],
                start=True, stop=True),
                reads=[("z1", tb // 4), ("wf2",)], writes=[("pb", bank)])
            kb.op("act", lambda e, bank=bank, zpar=zpar: e.activation(
                out=sb["ztmp%d" % zpar][:], in_=pb(bank)[:], func=AF.Abs),
                reads=[("pb", bank)], writes=[("zt0", zpar)])
            kb.op("act", lambda e, zpar=zpar: e.activation(
                out=sb["ztmp%d" % zpar][:], in_=sb["ztmp%d" % zpar][:], func=AF.Exp, scale=-1.0),
                reads=[("zt0", zpar)], writes=[("zt0", zpar)])
            kb.op("act", lambda e, zpar=zpar: e.activation(
                out=sb["ztmp%d" % zpar][:], in_=sb["ztmp%d" % zpar][:], func=AF.Ln, bias=1.0),
                reads=[("zt0", zpar)], writes=[("zt0", zpar)])
            kb.op("dve", lambda e, bank=bank, zpar=zpar: e.tensor_single_scalar(
                sb["ztmp%d" % (2 + zpar)][:], pb(bank)[:], 0.0, op=ALU.min),
                reads=[("pb", bank)], writes=[("zt1", zpar)])
            kb.op("dve", lambda e, jt=jt, zpar=zpar: e.tensor_sub(
                sb["gstg"][:, jt * 512:(jt + 1) * 512], sb["ztmp%d" % (2 + zpar)][:],
                sb["ztmp%d" % zpar][:]),
                reads=[("zt0", zpar), ("zt1", zpar)], writes=[("gstg", jt)])

        kb.dma("qs", lambda e, tb=tb: e.dma_start(
            out=sp["g"][tb * 128:(tb + 1) * 128, :], in_=sb["gstg"][:]),
            reads=[("gstg", jt) for jt in range(njt)], writes=[("spg", tb)])
    if AZ_CUT >= 3:
        return
    for jt in range(njt):
        bank = 4 + jt % 2
        for tb in range(ntb):
            par = tb % 2
            kb.dma("qs", lambda e, jt=jt, tb=tb, par=par: e.dma_start(
                out=sb["ztmp%d" % par][:].bitcast(BF16)[:, 0:512],
                in_=sp["g"][tb * 128:(tb + 1) * 128, jt * 512:(jt + 1) * 512]),
                reads=[("spg", tb)], writes=[("zt", par)])
            kb.op("pe", lambda e, tb=tb, jt=jt, par=par, bank=bank: e.matmul(
                pb(bank)[0:32, :],
                sb["sumsels"][:, tb * 32:(tb + 1) * 32],
                sb["ztmp%d" % par][:].bitcast(BF16)[:, 0:512],
                start=(tb == 0), stop=(tb == ntb - 1)),
                reads=[("zt", par), ("sb", "sumsels")], writes=[("pbcc", bank)])
        kb.op("dve", lambda e, jt=jt, bank=bank: e.tensor_copy(
            sb["cCrows"][:, jt * 512:(jt + 1) * 512], pb(bank)[0:32, :]),
            reads=[("pbcc", bank)], writes=[("cCrows", jt)])
        kb.op("act", lambda e, jt=jt, bank=bank: e.activation(
            out=sb["ecall"][:, jt * 512:(jt + 1) * 512], in_=pb(bank)[0:32, :],
            func=AF.Exp),
            reads=[("pbcc", bank)], writes=[("ecall", jt)])
    for jt in range(njt):
        kb.op("pe", lambda e, jt=jt: e.matmul(
            pb(jt % 2)[0:32, :], sb["trige"][:],
            sb["cCrows"][:, jt * 512:(jt + 1) * 512], start=True, stop=True),
            reads=[("cCrows", jt), ("sb", "trige")], writes=[("pb", jt % 2)])
        kb.op("act", lambda e, jt=jt: e.activation(
            out=sb["wsufb"][:, jt * 512:(jt + 1) * 512], in_=pb(jt % 2)[0:32, :],
            func=AF.Exp),
            reads=[("pb", jt % 2)], writes=[("wsufb", jt)])
    if AZ_CUT >= 2:
        return
    kb.dma("qs", lambda e: e.dma_start(out=sp["ec"][:], in_=sb["ecall"][0:nch, :]),
           reads=[("ecall", jt) for jt in range(njt)], writes=[("spec",)])
    kb.dma("qs", lambda e: e.dma_start(out=sp["wsuf"][:], in_=sb["wsufb"][0:nch, :]),
           reads=[("wsufb", jt) for jt in range(njt)], writes=[("spwsuf",)])
    if AZ_CUT >= 1:
        return
    kb.dma("qp", lambda e: e.dma_start(
        out=sb["ec16all"][:, 0:nch * 128],
        in_=sp["ec"].rearrange("j (h m) -> h j m", m=128)),
        reads=[("spec",)], writes=[("ec16all",)])

AM_CUT = 0


def phase_amisc(self):
    kb, sb, din, sp = self.kb, self.sb, self.din, self.spills
    tl, ntb = self.cfg.tloc, self.cfg.ntb
    pb = self.pb
    HT = HB + tl
    kb.dma("qp", lambda e: e.dma_start(out=sb["wbr"][:], in_=din["Wbr"][:]),
           writes=[("wbr",)])
    wg1_4 = din["Wg1"].rearrange("(dc p) f -> p dc f", p=128)
    kb.dma("qp", lambda e: e.dma_start(out=sb["wsmall"][:, 16 * 128:32 * 128], in_=wg1_4),
           writes=[("wsmall", 1)])
    kb.dma("qp", lambda e: e.dma_start(out=sb["wg2"][:], in_=din["Wg2"][:]),
           writes=[("wg2",)])
    for tb in range(ntb):
        for dc in range(16):
            kb.op("pe", lambda e, tb=tb, dc=dc: e.matmul(
                pb(0)[:, 0:H],
                sb["hfm"][:, dc * HT + HB + tb * 128: dc * HT + HB + (tb + 1) * 128],
                sb["wbr"][:, dc * H:(dc + 1) * H],
                start=(dc == 0), stop=(dc == 15)),
                reads=[("wbr",), ("hfm", dc)], writes=[("pb", 0)])
        kb.op("act", lambda e, tb=tb: e.activation(
            out=sb["beta"][:, tb * H:(tb + 1) * H], in_=pb(0)[:, 0:H], func=AF.Sigmoid),
            reads=[("pb", 0)], writes=[("beta", tb)])
    b2 = sb["beta2"].ap().rearrange("p (c j) -> p c j", j=H)
    kb.dma("qs", lambda e, b2=b2: e.dma_start(
        out=b2[:, 0::2, :], in_=sb["beta"][0:64, :].rearrange("p (c j) -> p c j", j=H)),
        reads=[("beta", tb) for tb in range(ntb)], writes=[("beta2", 0)])
    kb.dma("qs", lambda e, b2=b2: e.dma_start(
        out=b2[:, 1::2, :], in_=sb["beta"][64:128, :].rearrange("p (c j) -> p c j", j=H)),
        reads=[("beta", tb) for tb in range(ntb)], writes=[("beta2", 1)])
    if AM_CUT >= 2:
        return
    for jg in range(ceil_div(tl, 512)):
        t0, tn = jg * 512, min(512, tl - jg * 512)
        bank = 1
        for dc in range(16):
            kb.op("pe", lambda e, dc=dc, t0=t0, tn=tn, bank=bank: e.matmul(
                pb(bank)[:, 0:tn],
                sb["wsmall"][:, (16 + dc) * 128:(17 + dc) * 128],
                sb["hfm"][:, dc * HT + HB + t0: dc * HT + HB + t0 + tn],
                start=(dc == 0), stop=(dc == 15)),
                reads=[("wsmall", 1), ("hfm", dc)], writes=[("pb", bank)])
        kb.op("act", lambda e, t0=t0, tn=tn, bank=bank: e.activation(
            out=sb["zg1"][:, t0:t0 + tn], in_=pb(bank)[:, 0:tn], func=AF.Copy),
            reads=[("pb", bank)], writes=[("zg1", jg)])
    if AM_CUT >= 1:
        return
    for fc2 in range(16):
        for jg in range(ceil_div(tl, 512)):
            t0, tn = jg * 512, min(512, tl - jg * 512)
            bank = (fc2 * ceil_div(tl, 512) + jg) % 4
            kb.op("pe", lambda e, fc2=fc2, t0=t0, tn=tn, bank=bank: e.matmul(
                pb(bank)[:, 0:tn],
                sb["wg2"][:, fc2 * 128:(fc2 + 1) * 128],
                sb["zg1"][:, t0:t0 + tn], start=True, stop=True),
                reads=[("wg2",), ("zg1", jg)], writes=[("pb", bank)])
            kb.op("dve", lambda e, fc2=fc2, t0=t0, tn=tn, bank=bank: e.tensor_scalar(
                sb["z1"][:, t0:t0 + tn], pb(bank)[:, 0:tn],
                sb["bg2m"][:, fc2:fc2 + 1], None, op0=ALU.add),
                reads=[("pb", bank), ("sb", "bg2m")], writes=[("z1", jg)])
            kb.op("act", lambda e, t0=t0, tn=tn: e.activation(
                out=sb["z1"][:, t0:t0 + tn], in_=sb["z1"][:, t0:t0 + tn],
                func=AF.Sigmoid),
                reads=[("z1", jg)], writes=[("z1", jg)])
        kb.dma("qs", lambda e, fc2=fc2: e.dma_start(
            out=sp["gs"][fc2], in_=sb["z1"][:, 0:tl]),
            reads=[("z1", jg) for jg in range(ceil_div(tl, 512))],
            writes=[("spgs", fc2)])


Builder.phase_az = phase_az
Builder.phase_amisc = phase_amisc


def phase_fence(self):
    kb = self.kb
    toks = [(e, kb.cnt[e]) for e in ENGINES if kb.cnt[e]]
    toks += [(qn, kb.dcnt[qn]) for qn in QUEUES if kb.dcnt[qn]]
    for e in ENGINES:
        kb.barrier_wait(e, toks)


def alloc_ascan_arena(self):
    tl = self.cfg.tloc
    self.arena_reset()
    for par in (0, 1):
        self.arena("gch%d" % par, [64, D], BF16)
        self.arena("wtch%d" % par, [64, D], BF16)
        self.arena("qfch%d" % par, [128, H * C], BF16)
        self.arena("kfch%d" % par, [128, H * C], BF16)
        self.arena("einv%d" % par, [64, D], BF16)
        self.arena("bw%d" % par, [64, D], BF16)
        self.arena("ubuf%d" % par, [64, D], BF16)
        self.arena("sibuf%d" % par, [64, D], BF16)
        self.arena("pm%d" % par, [64, 1024], BF16)
        self.arena("ssq%d" % par, [64, 16], F32)
        self.arena("rb%d" % par, [64, 16], BF16)

    self.arena("sqtmp", [64, D], F32)


def alloc_aagg_arena(self):
    self.arena_reset()
    for par in (0, 1):
        self.arena("ktch%d" % par, [64, 4 * D], BF16)
        self.arena("vtch%d" % par, [64, 4 * D], BF16)
        self.arena("ucb%d" % par, [64, 4 * D], BF16)
        self.arena("wsb%d" % par, [64, D], BF16)
    self.arena("ccstg", [128, D], BF16)
    self.arena("aggK", [128, H * M], F32)
    self.arena("aggV", [128, H * DK], F32)
    self.arena("erow", [1, D], BF16)


def phase_ascan(self):
    kb, sb, sp = self.kb, self.sb, self.spills
    tl, nch = self.cfg.tloc, self.cfg.nch
    pb = self.pb
    PSA = self.PSA
    wtm3 = sp["wtm"].rearrange("h t f -> t h f")
    q3 = sp["q"].rearrange("h p t -> p h t")
    k3 = sp["kfm"].rearrange("h p t -> p h t")
    for ch in range(nch):
        par = ch % 2
        gch, wt = sb["gch%d" % par], sb["wtch%d" % par]
        qf, kf = sb["qfch%d" % par], sb["kfch%d" % par]
        einv, bw = sb["einv%d" % par], sb["bw%d" % par]
        ub, sib, pm = sb["ubuf%d" % par], sb["sibuf%d" % par], sb["pm%d" % par]
        ssq, rb = sb["ssq%d" % par], sb["rb%d" % par]
        kb.dma("qs", lambda e, ch=ch, gch=gch: e.dma_start(
            out=gch[:], in_=sp["g"][ch * C:(ch + 1) * C, :]),
            reads=[("spg", ch * C // 128)], writes=[("gch", par)])
        kb.dma("qs", lambda e, ch=ch, wt=wt: e.dma_start(
            out=wt[:], in_=wtm3[ch * C:(ch + 1) * C]),
            reads=[("spwtm", h) for h in range(H)], writes=[("wtch", par)])
        kb.dma("qs", lambda e, ch=ch, qf=qf: e.dma_start(
            out=qf[:], in_=q3[:, :, ch * C:(ch + 1) * C]),
            reads=[("spq", h) for h in range(H)], writes=[("qfch", par)])
        kb.dma("qs", lambda e, ch=ch, kf=kf: e.dma_start(
            out=kf[:], in_=k3[:, :, ch * C:(ch + 1) * C]),
            reads=[("spkfm", h) for h in range(H)], writes=[("kfch", par)])
        for jt in range(4):
            kb.op("pe", lambda e, jt=jt, gch=gch: e.matmul(
                PSA.ap()[0:C, jt * 512:(jt + 1) * 512], sb["tri"][:],
                gch[:, jt * 512:(jt + 1) * 512], start=True, stop=True),
                reads=[("gch", par), ("sb", "tri")], writes=[("pb", jt)])
        kb.op("act", lambda e, einv=einv: e.activation(
            out=einv[:], in_=PSA.ap()[0:C, :], func=AF.Exp, scale=-1.0),
            reads=[("pb", 0), ("pb", 1), ("pb", 2), ("pb", 3)], writes=[("einv", par)])
        kb.op("act", lambda e, wt=wt: e.activation(
            out=sb["sqtmp"][:], in_=wt[:], func=AF.Square),
            reads=[("wtch", par)], writes=[("sqtmp",)])
        kb.op("dve", lambda e, ssq=ssq: e.reduce_sum(
            ssq[:], sb["sqtmp"][:].rearrange("p (h m) -> p h m", m=M), axis=AX.X),
            reads=[("sqtmp",)], writes=[("ssq", par)])
        kb.op("act", lambda e, ssq=ssq: e.activation(
            out=ssq[:], in_=ssq[:], func=AF.Ln, bias=L2EPS),
            reads=[("ssq", par)], writes=[("ssq", par)])
        kb.op("act", lambda e, ssq=ssq: e.activation(
            out=ssq[:], in_=ssq[:], func=AF.Exp, scale=-0.5),
            reads=[("ssq", par)], writes=[("ssq", par)])
        bslice = sb["beta2"][:, ch * H:(ch + 1) * H]
        kb.op("dve", lambda e, rb=rb, ssq=ssq, bslice=bslice: e.tensor_mul(
            rb[:], ssq[:], bslice),
            reads=[("ssq", par), ("beta2", 0), ("beta2", 1)], writes=[("rbb", par)])
        kb.op("dve", lambda e, bw=bw, wt=wt, rb=rb: e.tensor_tensor(
            bw[:].rearrange("p (h m) -> p h m", m=M),
            wt[:].rearrange("p (h m) -> p h m", m=M),
            rb[:].unsqueeze(2).broadcast_to([C, H, M]), op=ALU.mult),
            reads=[("wtch", par), ("rbb", par)], writes=[("bwb", par)])
        kb.op("dve", lambda e, ub=ub, bw=bw, einv=einv: e.tensor_mul(
            ub[:], bw[:], einv[:]),
            reads=[("bwb", par), ("einv", par)], writes=[("ubuf", par)])
        kb.dma("qs", lambda e, ch=ch, ub=ub: e.dma_start(
            out=sp["u"][ch * C:(ch + 1) * C, :], in_=ub[:]),
            reads=[("ubuf", par)], writes=[("spu", ch)])
        for hh in range(H):
            bank = 4 + hh // 8
            kb.op("pe", lambda e, hh=hh, bank=bank, kf=kf, qf=qf: e.matmul(
                pb(bank)[0:C, (hh % 8) * C:(hh % 8 + 1) * C],
                kf[:, hh * C:(hh + 1) * C], qf[:, hh * C:(hh + 1) * C],
                start=True, stop=True),
                reads=[("qfch", par), ("kfch", par)], writes=[("pb", bank)])
        for b_ in (4, 5):
            kb.op("dve", lambda e, b_=b_, pm=pm: e.tensor_tensor(
                pm[:, (b_ - 4) * 512:(b_ - 3) * 512].rearrange("p (h s) -> p h s", s=C),
                pb(b_)[0:C, :].rearrange("p (h s) -> p h s", s=C),
                sb["tri"][:].unsqueeze(1).broadcast_to([C, 8, C]), op=ALU.mult),
                reads=[("pb", b_), ("sb", "tri")], writes=[("pmb", par, b_)])
        for hh in range(H):
            kb.op("pe", lambda e, hh=hh, pm=pm, ub=ub: e.matmul(
                PSA.ap()[0:C, hh * M:(hh + 1) * M],
                pm[:, hh * C:(hh + 1) * C], ub[:, hh * M:(hh + 1) * M],
                start=True, stop=True),
                reads=[("pmb", par, 4 + hh // 8), ("ubuf", par)],
                writes=[("pb", hh // 4)])
        kb.op("act", lambda e, sib=sib: e.activation(
            out=sib[:], in_=PSA.ap()[0:C, :], func=AF.Copy, scale=SCALE),
            reads=[("pb", 0), ("pb", 1), ("pb", 2), ("pb", 3)], writes=[("sibuf", par)])
        kb.dma("qs", lambda e, ch=ch, sib=sib: e.dma_start(
            out=sp["si"][ch * C:(ch + 1) * C, :], in_=sib[:]),
            reads=[("sibuf", par)], writes=[("spsi", ch)])


def phase_aagg(self):
    kb, sb, sp = self.kb, self.sb, self.spills
    tl, nch = self.cfg.tloc, self.cfg.nch
    pb = self.pb
    PSA = self.PSA
    BC = 4
    nblk = ceil_div(nch, BC)
    ktm3 = sp["ktm"].rearrange("h t f -> t h f")
    vtm3 = sp["vtm"].rearrange("h t f -> t h f")
    for blk in range(nblk):
        chs = [ch for ch in range(blk * BC, min((blk + 1) * BC, nch))]
        par = blk % 2
        kt, vt = sb["ktch%d" % par], sb["vtch%d" % par]
        uc, ws = sb["ucb%d" % par], sb["wsb%d" % par]
        nbc = len(chs)
        for i, ch in enumerate(chs):
            kb.dma("qs", lambda e, ch=ch, i=i, kt=kt: e.dma_start(
                out=kt[:, i * D:(i + 1) * D], in_=ktm3[ch * C:(ch + 1) * C]),
                reads=[("spktm", h) for h in range(H)], writes=[("ktch", par, i)])
            kb.dma("qs", lambda e, ch=ch, i=i, vt=vt: e.dma_start(
                out=vt[:, i * D:(i + 1) * D], in_=vtm3[ch * C:(ch + 1) * C]),
                reads=[("spvtm", h) for h in range(H)], writes=[("vtch", par, i)])
            kb.dma("qa", lambda e, ch=ch, i=i, ws=ws: e.dma_start(
                out=ws[:, (i % 2) * 0: D] if False else ws[:],
                in_=sp["wsuf"][ch:ch + 1, :].broadcast_to([C, D])),
                reads=[("spwsuf",)], writes=[("wsbb", par)])
            kb.dma("qa", lambda e, ch=ch, i=i, uc=uc: e.dma_start(
                out=uc[:, i * D:(i + 1) * D], in_=sp["u"][ch * C:(ch + 1) * C, :]),
                reads=[("spu", ch)], writes=[("ucb", par, i)])
            kb.op("dve", lambda e, i=i, uc=uc, ws=ws: e.tensor_mul(
                uc[:, i * D:(i + 1) * D], uc[:, i * D:(i + 1) * D], ws[:]),
                reads=[("ucb", par, i), ("wsbb", par)], writes=[("ucb", par, i)])
        for hh in range(H):
            bank = hh % 2
            for i in range(nbc):
                kb.op("pe", lambda e, hh=hh, i=i, kt=kt, uc=uc, bank=bank, nbc=nbc: e.matmul(
                    pb(bank)[:, 0:M],
                    kt[:, i * D + hh * DK: i * D + (hh + 1) * DK],
                    uc[:, i * D + hh * M: i * D + (hh + 1) * M],
                    start=(i == 0), stop=(i == nbc - 1)),
                    reads=[("ktch", par, i), ("ucb", par, i)], writes=[("pb", bank)])
            if blk == 0:
                kb.op("dve", lambda e, hh=hh, bank=bank: e.tensor_copy(
                    sb["aggK"][:, hh * M:(hh + 1) * M], pb(bank)[:, 0:M]),
                    reads=[("pb", bank)], writes=[("aggK", hh)])
            else:
                kb.op("dve", lambda e, hh=hh, bank=bank: e.tensor_add(
                    sb["aggK"][:, hh * M:(hh + 1) * M],
                    sb["aggK"][:, hh * M:(hh + 1) * M], pb(bank)[:, 0:M]),
                    reads=[("pb", bank), ("aggK", hh)], writes=[("aggK", hh)])
            bank2 = 2 + hh % 2
            for i in range(nbc):
                kb.op("pe", lambda e, hh=hh, i=i, uc=uc, vt=vt, bank2=bank2, nbc=nbc: e.matmul(
                    pb(bank2)[:, 0:DK],
                    uc[:, i * D + hh * M: i * D + (hh + 1) * M],
                    vt[:, i * D + hh * DK: i * D + (hh + 1) * DK],
                    start=(i == 0), stop=(i == nbc - 1)),
                    reads=[("ucb", par, i), ("vtch", par, i)], writes=[("pb", bank2)])
            if blk == 0:
                kb.op("dve", lambda e, hh=hh, bank2=bank2: e.tensor_copy(
                    sb["aggV"][:, hh * DK:(hh + 1) * DK], pb(bank2)[:, 0:DK]),
                    reads=[("pb", bank2)], writes=[("aggV", hh)])
            else:
                kb.op("dve", lambda e, hh=hh, bank2=bank2: e.tensor_add(
                    sb["aggV"][:, hh * DK:(hh + 1) * DK],
                    sb["aggV"][:, hh * DK:(hh + 1) * DK], pb(bank2)[:, 0:DK]),
                    reads=[("pb", bank2), ("aggV", hh)], writes=[("aggV", hh)])
    kb.op("dve", lambda e: e.tensor_copy(sb["ccstg"][:], sb["aggK"][:]),
          reads=[("aggK", hh) for hh in range(H)], writes=[("ccstg",)])
    kb.dma("qs", lambda e: e.dma_start(out=sp["ccin"][0:128, :], in_=sb["ccstg"][:]),
           reads=[("ccstg",)], writes=[("spccin", 0)])
    kb.op("dve", lambda e: e.tensor_copy(sb["ccstg"][:], sb["aggV"][:]),
          reads=[("aggV", hh) for hh in range(H)] + [("spccin", 0)], writes=[("ccstg",)])
    kb.dma("qs", lambda e: e.dma_start(out=sp["ccin"][128:256, :], in_=sb["ccstg"][:]),
           reads=[("ccstg",)], writes=[("spccin", 1)])
    for jt in range(4):
        kb.op("pe", lambda e, jt=jt: e.matmul(
            PSA.ap()[0:1, jt * 512:(jt + 1) * 512], sb["ones32"][:],
            sb["cCrows"][:, jt * 512:(jt + 1) * 512], start=True, stop=True),
            reads=[("cCrows", jt), ("sb", "ones32")], writes=[("pb", jt)])
        kb.op("act", lambda e, jt=jt: e.activation(
            out=sb["erow"][:, jt * 512:(jt + 1) * 512],
            in_=PSA.ap()[0:1, jt * 512:(jt + 1) * 512], func=AF.Exp),
            reads=[("pb", jt)], writes=[("erow", jt)])
    kb.dma("qs", lambda e: e.dma_start(out=sp["ccin"][256:257, :], in_=sb["erow"][:]),
           reads=[("erow", jt) for jt in range(4)], writes=[("spccin", 2)])


def phase_gather(self):
    kb, sp = self.kb, self.spills
    if self.cfg.host_gather:
        return
    n = self.cfg.ncores
    sem = self.nc.alloc_semaphore("sem_cc")
    kb.op("pool", lambda e: e.collective_compute(
        "AllGather", ALU.bypass, replica_groups=[list(range(n))],
        ins=[sp["ccin"][:].opt()], outs=[sp["ccout"][:].opt()]),
        reads=[("spccin", 0), ("spccin", 1), ("spccin", 2)],
        writes=[("spccout",)])


Builder.phase_fence = phase_fence
Builder.alloc_ascan_arena = alloc_ascan_arena
Builder.alloc_aagg_arena = alloc_aagg_arena
Builder.phase_ascan = phase_ascan
Builder.phase_aagg = phase_aagg
Builder.phase_gather = phase_gather


def alloc_b_arena(self):
    tl = self.cfg.tloc
    self.arena_reset()
    self.arena("wo", [128, 16 * 128 * 16], BF16)
    for par in (0, 1):
        self.arena("bgch%d" % par, [64, D], BF16)
        self.arena("bqf%d" % par, [128, H * C], BF16)
        self.arena("bkt%d" % par, [64, D], BF16)
        self.arena("bvt%d" % par, [64, D], BF16)
    self.arena("bub0", [64, D], BF16)
    self.arena("bsi0", [64, D], BF16)
    self.arena("bgs0", [128, H * C], BF16)
    self.arena("becb0", [128, D], BF16)
    self.arena("bh0", [128, D], F32)
    self.arena("bofm0", [128, D], BF16)
    self.arena("befull", [64, D], BF16)
    self.arena("bes", [64, D], BF16)
    self.arena("bs0", [64, D], BF16)
    self.arena("bamt", [128, H * C], BF16)
    self.arena("bufm", [128, H * C], BF16)

    self.arena("bosq", [128, H * C], BF16)
    self.arena("brstdb", [128, H * C], BF16)
    self.arena("bo1", [128, H * C], BF16)
    self.arena("br", [128, D], F32)
    self.arena("bx1", [128, D], BF16)
    self.arena("bmx", [64, H], F32)
    self.arena("bsm", [64, H], F32)
    self.arena("brs", [64, H], F32)
    self.arena("brr", [1, H * C], BF16)
    self.arena("bln", [128, 4], F32)
    self.arena("be16", [16, 128], BF16)
    self.sb["bfold"] = self.sb["bx1"]
    self.sb["bytm"] = self.sb["bsi0"]


def phase_bscan(self):
    kb, sb, sp, din = self.kb, self.sb, self.spills, self.din
    tl, nch, cfg = self.cfg.tloc, self.cfg.nch, self.cfg
    pb = self.pb
    PSA = self.PSA
    wo4 = din["Wo"].rearrange("(dc p) f -> p dc f", p=128)
    kb.dma("qp", lambda e: e.dma_start(out=sb["wo"][:], in_=wo4), writes=[("wo",)])
    kb.op("dve", lambda e: e.memset(sb["Sk"][:], 0.0), writes=[("Sk",)])
    kb.op("dve", lambda e: e.memset(sb["Sv"][:], 0.0), writes=[("Sv",)])
    nfold = cfg.ncores - 1
    for j in range(nfold):
        rb_ = 257 * j
        kb.dma("qp", lambda e, rb_=rb_: e.dma_start(
            out=sb["row2"][0:1, :], in_=sp["ccout"][rb_ + 256:rb_ + 257, :]),
            reads=[("spccout",)], writes=[("row2",)])
        kb.dma("qp", lambda e, rb_=rb_: e.dma_start(
            out=sb["be16"][:], in_=sp["ccout"][rb_ + 256:rb_ + 257, :].rearrange(
                "o (h m) -> (o h) m", m=128)),
            reads=[("spccout",)], writes=[("be16",)])
        for jt in range(4):
            kb.op("pe", lambda e, j=j, jt=jt: e.matmul(
                PSA.ap()[:, jt * 512:(jt + 1) * 512],
                sb["mselM"][:, j * 128:(j + 1) * 128],
                sb["row2"][:, jt * 512:(jt + 1) * 512], start=True, stop=True),
                reads=[("row2",), ("sb", "mselM")], writes=[("pb", jt)])
        kb.op("dve", lambda e: e.tensor_mul(sb["Sk"][:], sb["Sk"][:], PSA.ap()[:]),
              reads=[("pb", 0), ("pb", 1), ("pb", 2), ("pb", 3), ("Sk",)],
              writes=[("Sk",)])
        kb.dma("qp", lambda e, rb_=rb_: e.dma_start(
            out=sb["bfold"][:].bitcast(BF16)[:, 0:D], in_=sp["ccout"][rb_:rb_ + 128, :]),
            reads=[("spccout",)], writes=[("bx1",)])
        kb.op("dve", lambda e, j=j: e.scalar_tensor_tensor(
            sb["Sk"][:], sb["bfold"][:].bitcast(BF16)[:, 0:D], sb["mselcol"][:, j:j + 1],
            sb["Sk"][:], op0=ALU.mult, op1=ALU.add),
            reads=[("bx1",), ("Sk",), ("sb", "mselcol")], writes=[("Sk",)])
        kb.op("dve", lambda e, j=j: e.tensor_scalar(
            sb["be16"][:], sb["be16"][:], sb["mselcol"][0:16, j:j + 1],
            sb["m1col"][0:16, j:j + 1], op0=ALU.mult, op1=ALU.add),
            reads=[("be16",), ("sb", "mselcol"), ("sb", "m1col")], writes=[("be16",)])
        kb.op("pe", lambda e: e.transpose(
            pb(6).bitcast(BF16)[:, 0:16], sb["be16"][:], sb["ident"][0:16, 0:16]),
            reads=[("be16",), ("sb", "ident")], writes=[("pb", 6)])
        kb.op("dve", lambda e: e.tensor_copy(sb["ecct"][:], pb(6).bitcast(BF16)[:, 0:16]),
              reads=[("pb", 6)], writes=[("ecct",)])
        for hh in range(H):
            kb.op("act", lambda e, hh=hh: e.activation(
                out=sb["Sv"][:, hh * DK:(hh + 1) * DK],
                in_=sb["Sv"][:, hh * DK:(hh + 1) * DK], func=AF.Copy,
                scale=sb["ecct"][:, hh:hh + 1]),
                reads=[("ecct",), ("Sv",)], writes=[("Sv",)])
        kb.dma("qp", lambda e, rb_=rb_: e.dma_start(
            out=sb["bfold"][:], in_=sp["ccout"][rb_ + 128:rb_ + 256, :]),
            reads=[("spccout",), ("Sk",)], writes=[("bx1",)])
        kb.op("dve", lambda e, j=j: e.scalar_tensor_tensor(
            sb["Sv"][:], sb["bfold"][:].bitcast(BF16)[:, 0:D], sb["mselcol"][:, j:j + 1],
            sb["Sv"][:], op0=ALU.mult, op1=ALU.add),
            reads=[("bx1",), ("Sv",), ("sb", "mselcol")], writes=[("Sv",)])
    q3 = sp["q"].rearrange("h p t -> p h t")
    gs3 = sp["gs"].rearrange("h p t -> p h t")
    ktm3 = sp["ktm"].rearrange("h t f -> t h f")
    vtm3 = sp["vtm"].rearrange("h t f -> t h f")
    def emit_loads_par(ch):
        par = ch % 2
        gch, qf = sb["bgch%d" % par], sb["bqf%d" % par]
        kt, vt = sb["bkt%d" % par], sb["bvt%d" % par]
        kb.dma("qs", lambda e, ch=ch, gch=gch: e.dma_start(
            out=gch[:], in_=sp["g"][ch * C:(ch + 1) * C, :]),
            reads=[("spg", ch * C // 128)], writes=[("bgch", par)])
        kb.dma("qp", lambda e, ch=ch, qf=qf: e.dma_start(
            out=qf[:], in_=q3[:, :, ch * C:(ch + 1) * C]),
            reads=[("spq", h) for h in range(H)], writes=[("bqf", par)])
        kb.dma("qp", lambda e, ch=ch, kt=kt: e.dma_start(
            out=kt[:], in_=ktm3[ch * C:(ch + 1) * C]),
            reads=[("spktm", h) for h in range(H)], writes=[("bkt", par)])
        kb.dma("qa", lambda e, ch=ch, vt=vt: e.dma_start(
            out=vt[:], in_=vtm3[ch * C:(ch + 1) * C]),
            reads=[("spvtm", h) for h in range(H)], writes=[("bvt", par)])

    def emit_loads_single(ch):
        ub, sib = sb["bub0"], sb["bsi0"]
        gsf, ecb = sb["bgs0"], sb["becb0"]
        hb = sb["bh0"]
        kb.dma("qs", lambda e, ch=ch, ub=ub: e.dma_start(
            out=ub[:], in_=sp["u"][ch * C:(ch + 1) * C, :]),
            reads=[("spu", ch)], writes=[("bub",)])
        kb.dma("qs", lambda e, ch=ch, sib=sib: e.dma_start(
            out=sib[:], in_=sp["si"][ch * C:(ch + 1) * C, :]),
            reads=[("spsi", ch)], writes=[("bsiy",)])
        kb.dma("qa", lambda e, ch=ch, gsf=gsf: e.dma_start(
            out=gsf[:], in_=gs3[:, :, ch * C:(ch + 1) * C]),
            reads=[("spgs", h) for h in range(H)], writes=[("bgs",)])
        kb.dma("qa", lambda e, ch=ch, ecb=ecb: e.dma_start(
            out=ecb[:], in_=sp["ec"][ch:ch + 1, :].broadcast_to([128, D])),
            reads=[("spec",)], writes=[("becb",)])
        kb.dma("qa", lambda e, ch=ch, hb=hb: e.dma_start(
            out=hb[(ch % 2) * 64:(ch % 2) * 64 + 64, :],
            in_=din["h"][ch * C:(ch + 1) * C, :]),
            writes=[("bh", ch % 2)])

    emit_loads_par(0)
    for ch in range(nch):
        par = ch % 2
        gch, qf = sb["bgch%d" % par], sb["bqf%d" % par]
        kt, vt = sb["bkt%d" % par], sb["bvt%d" % par]
        ub, sib = sb["bub0"], sb["bsi0"]
        gsf, ecb = sb["bgs0"], sb["becb0"]
        ppar = 0
        hb, ofm = sb["bh0"], sb["bofm0"]
        if ch + 1 < nch:
            emit_loads_par(ch + 1)
        emit_loads_single(ch)
        for jt in range(4):
            kb.op("pe", lambda e, jt=jt, gch=gch: e.matmul(
                PSA.ap()[0:C, jt * 512:(jt + 1) * 512], sb["tri"][:],
                gch[:, jt * 512:(jt + 1) * 512], start=True, stop=True),
                reads=[("bgch", par), ("sb", "tri")], writes=[("pb", jt)])
        kb.op("act", lambda e: e.activation(
            out=sb["befull"][:], in_=PSA.ap()[0:C, :], func=AF.Exp),
            reads=[("pb", 0), ("pb", 1), ("pb", 2), ("pb", 3)], writes=[("befull",)])
        e16sl = sb["ec16all"][:, ch * 128:(ch + 1) * 128]
        kb.op("pe", lambda e, e16sl=e16sl: e.transpose(
            pb(6).bitcast(BF16)[:, 0:16], e16sl, sb["ident"][0:16, 0:16]),
            reads=[("ec16all",), ("sb", "ident")], writes=[("pb", 6)])
        kb.op("dve", lambda e: e.tensor_copy(sb["ecct"][:], pb(6).bitcast(BF16)[:, 0:16]),
              reads=[("pb", 6)], writes=[("ecct",)])
        kb.op("dve", lambda e: e.tensor_scalar_mul(sb["Skbf"][:], sb["Sk"][:], SCALE),
              reads=[("Sk",)], writes=[("Skbf",)])
        kb.op("pool", lambda e: e.tensor_copy(sb["Svbf"][:], sb["Sv"][:]),
              reads=[("Sv",)], writes=[("Svbf",)])
        for hh in range(H):
            kb.op("pe", lambda e, hh=hh, qf=qf: e.matmul(
                PSA.ap()[0:C, hh * M:(hh + 1) * M],
                qf[:, hh * C:(hh + 1) * C], sb["Skbf"][:, hh * M:(hh + 1) * M],
                start=True, stop=True),
                reads=[("bqf", par), ("Skbf",)], writes=[("pb", hh // 4)])
        kb.op("dve", lambda e, sib=sib: e.tensor_add(
            sb["bs0"][:], PSA.ap()[0:C, :], sib[:]),
            reads=[("pb", 0), ("pb", 1), ("pb", 2), ("pb", 3), ("bsiy",)],
            writes=[("bs0",)])
        kb.op("dve", lambda e: e.tensor_mul(sb["bs0"][:], sb["bs0"][:], sb["befull"][:]),
              reads=[("bs0",), ("befull",)], writes=[("bs0",)])
        s3 = sb["bs0"][:].rearrange("p (h m) -> p h m", m=M)
        kb.op("dve", lambda e, s3=s3: e.reduce_max(sb["bmx"][:], s3, axis=AX.X),
              reads=[("bs0",)], writes=[("bmx",)])
        kb.op("dve", lambda e, s3=s3: e.tensor_tensor(
            s3, s3, sb["bmx"][:].unsqueeze(2).broadcast_to([C, H, M]), op=ALU.subtract),
            reads=[("bs0",), ("bmx",)], writes=[("bs0",)])
        kb.op("act", lambda e: e.activation(
            out=sb["bes"][:], in_=sb["bs0"][:], func=AF.Exp),
            reads=[("bs0",)], writes=[("bes",)])
        es3 = sb["bes"][:].rearrange("p (h m) -> p h m", m=M)
        kb.op("dve", lambda e, es3=es3: e.reduce_sum(sb["bsm"][:], es3, axis=AX.X),
              reads=[("bes",)], writes=[("bsm",)])
        kb.op("dve", lambda e: e.reciprocal(sb["brs"][:], sb["bsm"][:]),
              reads=[("bsm",)], writes=[("brs",)])
        kb.op("dve", lambda e, es3=es3: e.tensor_tensor(
            es3, es3, sb["brs"][:].unsqueeze(2).broadcast_to([C, H, M]), op=ALU.mult),
            reads=[("bes",), ("brs",)], writes=[("bes",)])
        kb.op("dve", lambda e: e.tensor_mul(sb["bes"][:], sb["bes"][:], sb["befull"][:]),
              reads=[("bes",), ("befull",)], writes=[("bes",)])
        for nm, srcb, dstb in (("amt", "bes", "bamt"), ("ufm", None, "bufm")):
            srct = sb["bes"] if nm == "amt" else ub
            srcreg = ("bes",) if nm == "amt" else ("bub",)
            for hh in range(H):
                kb.op("pe", lambda e, hh=hh, srct=srct: e.transpose(
                    pb(6).bitcast(BF16)[:, hh * C:(hh + 1) * C],
                    srct[:, hh * M:(hh + 1) * M], sb["ident"][0:C, 0:C]),
                    reads=[srcreg, ("sb", "ident")], writes=[("pb", 6)])
            kb.op("act", lambda e, dstb=dstb: e.activation(
                out=sb[dstb][:], in_=pb(6).bitcast(BF16)[:, 0:H * C], func=AF.Copy),
                reads=[("pb", 6) for hh in range(H)], writes=[(dstb,)])
        for hh in range(H):
            bank = 4 + hh // 8
            kb.op("pe", lambda e, hh=hh, bank=bank: e.matmul(
                pb(bank)[0:C, (hh % 8) * C:(hh % 8 + 1) * C],
                sb["bufm"][:, hh * C:(hh + 1) * C], sb["bamt"][:, hh * C:(hh + 1) * C],
                start=True, stop=True),
                reads=[("bufm",), ("bamt",)], writes=[("pb", bank)])
        for b_ in (4, 5):
            kb.op("dve", lambda e, b_=b_: e.tensor_tensor(
                sb["bytm"][:, (b_ - 4) * 512:(b_ - 3) * 512].rearrange(
                    "p (h s) -> p h s", s=C),
                pb(b_)[0:C, :].rearrange("p (h s) -> p h s", s=C),
                sb["tri"][:].unsqueeze(1).broadcast_to([C, 8, C]), op=ALU.mult),
                reads=[("pb", b_), ("sb", "tri")], writes=[("bsiy",)])
        for hh in range(H):
            kb.op("pe", lambda e, hh=hh, vt=vt: e.matmul(
                PSA.ap()[:, hh * C:(hh + 1) * C],
                vt[:, hh * DK:(hh + 1) * DK], sb["bytm"][:, hh * C:(hh + 1) * C],
                start=True, stop=False),
                reads=[("bvt", par), ("bsiy",)], writes=[("pb", hh // 8)])
            kb.op("pe", lambda e, hh=hh: e.matmul(
                PSA.ap()[:, hh * C:(hh + 1) * C],
                sb["Svbf"][:, hh * DK:(hh + 1) * DK], sb["bamt"][:, hh * C:(hh + 1) * C],
                start=False, stop=True),
                reads=[("Svbf",), ("bamt",)], writes=[("pb", hh // 8)])
        kb.op("act", lambda e: e.activation(
            out=sb["bosq"][:], in_=PSA.ap()[:, 0:H * C], func=AF.Square),
            reads=[("pb", 0), ("pb", 1)], writes=[("bosq",)])
        for i2 in range(2):
            kb.op("pe", lambda e, i2=i2: e.matmul(
                PSA.ap()[0:1, 1024 + i2 * 512:1024 + (i2 + 1) * 512],
                sb["ones128"][:], sb["bosq"][:, i2 * 512:(i2 + 1) * 512],
                start=True, stop=True),
                reads=[("bosq",), ("sb", "ones128")], writes=[("pb", 2 + i2)])
        kb.op("act", lambda e: e.activation(
            out=sb["brr"][:], in_=PSA.ap()[0:1, 1024:2048], func=AF.Ln,
            scale=1.0 / 128, bias=EPS),
            reads=[("pb", 2), ("pb", 3)], writes=[("brr",)])
        kb.op("act", lambda e: e.activation(
            out=sb["brr"][:], in_=sb["brr"][:], func=AF.Exp, scale=-0.5),
            reads=[("brr",)], writes=[("brr",)])
        for i2 in range(2):
            kb.op("pe", lambda e, i2=i2: e.matmul(
                PSA.ap()[:, 1024 + i2 * 512:1024 + (i2 + 1) * 512],
                sb["ones1x128"][:], sb["brr"][:, i2 * 512:(i2 + 1) * 512],
                start=True, stop=True),
                reads=[("brr",), ("sb", "ones1x128")], writes=[("pb", 2 + i2)])
        kb.op("act", lambda e: e.activation(
            out=sb["brstdb"][:], in_=PSA.ap()[:, 1024:2048], func=AF.Copy),
            reads=[("pb", 2), ("pb", 3)], writes=[("brstdb",)])
        kb.op("dve", lambda e: e.scalar_tensor_tensor(
            sb["bo1"][:], PSA.ap()[:, 0:H * C], sb["normw"][:], sb["brstdb"][:],
            op0=ALU.mult, op1=ALU.mult),
            reads=[("pb", 0), ("pb", 1), ("brstdb",), ("sb", "normw")],
            writes=[("bo1",)])
        kb.op("pool", lambda e, gsf=gsf, ofm=ofm, ch=ch: e.tensor_tensor(
            ofm[:].rearrange("p (h c t) -> p h c t", c=2, t=C)[:, :, ch % 2, :],
            sb["bo1"][:].rearrange("p (h t) -> p h t", t=C), gsf[:].rearrange(
                "p (h t) -> p h t", t=C),
            op=ALU.mult),
            reads=[("bo1",), ("bgs",)], writes=[("bofm", ch % 2)])
        for hh in range(H):
            bank = 4 + hh // 4
            kb.op("pe", lambda e, hh=hh, kt=kt, ub=ub, bank=bank: e.matmul(
                pb(bank)[:, (hh % 4) * M:(hh % 4 + 1) * M],
                kt[:, hh * DK:(hh + 1) * DK], ub[:, hh * M:(hh + 1) * M],
                start=True, stop=True),
                reads=[("bkt", par), ("bub",)], writes=[("pb", bank)])
        for b_ in range(4, 8):
            kb.op("dve", lambda e, b_=b_: e.tensor_add(
                sb["Sk"][:, (b_ - 4) * 512:(b_ - 3) * 512],
                sb["Sk"][:, (b_ - 4) * 512:(b_ - 3) * 512], pb(b_)[:]),
                reads=[("pb", b_), ("Sk",), ("Skbf",)], writes=[("Sk",)])
        kb.op("pool", lambda e, ecb=ecb: e.tensor_mul(
            sb["Sk"][:, 0:1024], sb["Sk"][:, 0:1024], ecb[:, 0:1024]),
            reads=[("Sk",), ("becb",)], writes=[("Sk",)])
        kb.op("dve", lambda e, ecb=ecb: e.tensor_mul(
            sb["Sk"][:, 1024:2048], sb["Sk"][:, 1024:2048], ecb[:, 1024:2048]),
            reads=[("Sk",), ("becb",)], writes=[("Sk",)])
        for hh in range(H):
            kb.op("pe", lambda e, hh=hh, ub=ub, vt=vt: e.matmul(
                PSA.ap()[:, hh * DK:(hh + 1) * DK],
                ub[:, hh * M:(hh + 1) * M], vt[:, hh * DK:(hh + 1) * DK],
                start=True, stop=True),
                reads=[("bub",), ("bvt", par), ("bo1",)], writes=[("pb", hh // 4)])
        for b_ in range(4):
            kb.op("dve", lambda e, b_=b_: e.tensor_add(
                sb["Sv"][:, b_ * 512:(b_ + 1) * 512],
                sb["Sv"][:, b_ * 512:(b_ + 1) * 512], PSA.ap()[:, b_ * 512:(b_ + 1) * 512]),
                reads=[("pb", b_), ("Sv",), ("Svbf",)], writes=[("Sv",)])
        for hh in range(H):
            kb.op("act", lambda e, hh=hh: e.activation(
                out=sb["Sv"][:, hh * DK:(hh + 1) * DK],
                in_=sb["Sv"][:, hh * DK:(hh + 1) * DK], func=AF.Copy,
                scale=sb["ecct"][:, hh:hh + 1]),
                reads=[("ecct",), ("Sv",)], writes=[("Sv",)])
        if ch % 2 == 1:
            for jt in range(4):
                for dc in range(16):
                    lhs = sb["bofm%d" % ppar][:, dc * 128:(dc + 1) * 128]
                    kb.op("pe", lambda e, jt=jt, dc=dc, lhs=lhs: e.matmul(
                        PSA.ap()[:, jt * 512:(jt + 1) * 512],
                        lhs, sb["wo"][:, dc * D + jt * 512: dc * D + (jt + 1) * 512],
                        start=(dc == 0), stop=(dc == 15)),
                        reads=[("bofm", 0), ("bofm", 1), ("wo",)],
                        writes=[("pb", jt)])
            kb.op("dve", lambda e, ppar=ppar: e.tensor_add(
                sb["br"][:], PSA.ap()[:], sb["bh%d" % ppar][:]),
                reads=[("pb", 0), ("pb", 1), ("pb", 2), ("pb", 3),
                       ("bh", 0), ("bh", 1)],
                writes=[("br",)])
            kb.op("dve", lambda e: e.reduce_sum(
                sb["bln"][:, 0:1], sb["br"][:].unsqueeze(1), axis=AX.X),
                reads=[("br",)], writes=[("bln", 0)])
            kb.op("act", lambda e: e.activation(
                out=PSA.ap()[:], in_=sb["br"][:], func=AF.Square),
                reads=[("br",), ("pb", 0), ("pb", 1), ("pb", 2), ("pb", 3)],
                writes=[("pb", 0), ("pb", 1), ("pb", 2), ("pb", 3)])
            kb.op("dve", lambda e: e.reduce_sum(
                sb["bln"][:, 1:2], PSA.ap()[:].unsqueeze(1), axis=AX.X),
                reads=[("pb", 0), ("pb", 1), ("pb", 2), ("pb", 3)], writes=[("bln", 1)])
            kb.op("dve", lambda e: e.tensor_scalar(
                sb["bln"][:, 0:1], sb["bln"][:, 0:1], 1.0 / D, None, op0=ALU.mult),
                reads=[("bln", 0)], writes=[("bln", 0)])
            kb.op("dve", lambda e: e.tensor_scalar(
                sb["bln"][:, 1:2], sb["bln"][:, 1:2], 1.0 / D, None, op0=ALU.mult),
                reads=[("bln", 1)], writes=[("bln", 1)])
            kb.op("dve", lambda e: e.tensor_mul(
                sb["bln"][:, 2:3], sb["bln"][:, 0:1], sb["bln"][:, 0:1]),
                reads=[("bln", 0)], writes=[("bln", 2)])
            kb.op("dve", lambda e: e.tensor_sub(
                sb["bln"][:, 1:2], sb["bln"][:, 1:2], sb["bln"][:, 2:3]),
                reads=[("bln", 1), ("bln", 2)], writes=[("bln", 1)])
            kb.op("act", lambda e: e.activation(
                out=sb["bln"][:, 1:2], in_=sb["bln"][:, 1:2], func=AF.Ln, bias=LNEPS),
                reads=[("bln", 1)], writes=[("bln", 1)])
            kb.op("act", lambda e: e.activation(
                out=sb["bln"][:, 1:2], in_=sb["bln"][:, 1:2], func=AF.Exp, scale=-0.5),
                reads=[("bln", 1)], writes=[("bln", 1)])
            kb.op("dve", lambda e: e.tensor_scalar(
                sb["bln"][:, 2:3], sb["bln"][:, 0:1], -1.0, None, op0=ALU.mult),
                reads=[("bln", 0)], writes=[("bln", 2)])
            kb.op("dve", lambda e: e.tensor_scalar(
                sb["bx1"][:], sb["br"][:], sb["bln"][:, 2:3], sb["bln"][:, 1:2],
                op0=ALU.add, op1=ALU.mult),
                reads=[("br",), ("bln", 1), ("bln", 2)], writes=[("bx1",)])
            kb.dma("qa", lambda e, ch=ch: e.dma_start(
                out=sp["x1"][(ch - 1) * C:(ch + 1) * C, :], in_=sb["bx1"][:]),
                reads=[("bx1",)], writes=[("spx1", ch // 2)])


Builder.alloc_b_arena = alloc_b_arena
Builder.phase_bscan = phase_bscan


def alloc_f1_arena(self):
    tl = self.cfg.tloc
    self.arena_reset()
    self.arena("x1fm", [128, 16 * tl], BF16)
    for par in (0, 1, 2, 3):
        self.arena("wgb%d" % par, [128, 16 * 128], BF16)
        self.arena("wub%d" % par, [128, 16 * 128], BF16)
    for par in (0, 1):
        self.arena("xt%d" % par, [128, D], BF16)
    self.arena("gact", [128, tl], BF16)
    self.arena("uact", [128, tl], BF16)
    self.arena("gm0", [128, tl], BF16)
    self.arena("gm1", [128, tl], BF16)


def phase_f1(self):
    kb, sb, sp, din = self.kb, self.sb, self.spills, self.din
    tl, ntb = self.cfg.tloc, self.cfg.ntb
    pb = self.pb
    PSA = self.PSA
    njt = ceil_div(tl, 512)
    for tb in range(ntb):
        par = tb % 2
        xt = sb["xt%d" % par]
        kb.dma("qs", lambda e, tb=tb, xt=xt: e.dma_start(
            out=xt[:], in_=sp["x1"][tb * 128:(tb + 1) * 128, :]),
            reads=[("spx1", tb // 1)], writes=[("xt", par)])
        for half in range(2):
            for dc8 in range(8):
                dc = half * 8 + dc8
                kb.op("pe", lambda e, dc=dc, dc8=dc8, xt=xt: e.transpose(
                    pb(6).bitcast(BF16)[:, dc8 * 128:(dc8 + 1) * 128],
                    xt[:, dc * 128:(dc + 1) * 128], sb["ident"][:]),
                    reads=[("xt", par), ("sb", "ident")], writes=[("pb", 6)])
            kb.op("act", lambda e, tb=tb, half=half: e.activation(
                out=sb["x1fm"][:].rearrange("p (dc t) -> p dc t", t=tl)[
                    :, half * 8:(half + 1) * 8, tb * 128:(tb + 1) * 128],
                in_=pb(6).bitcast(BF16)[:, 0:1024].rearrange(
                    "p (dc t) -> p dc t", t=128),
                func=AF.Copy),
                reads=[("pb", 6) for i in range(8)], writes=[("x1fm", tb, half)])
    x1fm_rd = [("x1fm", tb, half) for tb in range(ntb) for half in range(2)]

    def emit_wload(hc):
        par = hc % 4
        wgb, wub = sb["wgb%d" % par], sb["wub%d" % par]
        kb.dma("qp", lambda e, hc=hc, wgb=wgb: e.dma_start(
            out=wgb[:], in_=din["WgT8"][hc * 128:(hc + 1) * 128, :]),
            writes=[("wgb", par)])
        kb.op("dve", lambda e, wgb=wgb: e.tensor_scalar_add(wgb[:], wgb[:], -128.0),
              reads=[("wgb", par)], writes=[("wgb", par)])
        kb.dma("qp", lambda e, hc=hc, wub=wub: e.dma_start(
            out=wub[:], in_=din["WuT8"][hc * 128:(hc + 1) * 128, :]),
            writes=[("wub", par)])
        kb.op("dve", lambda e, wub=wub: e.tensor_scalar_add(wub[:], wub[:], -128.0),
              reads=[("wub", par)], writes=[("wub", par)])

    for hc in range(min(3, NHH)):
        emit_wload(hc)
    for hc in range(NHH):
        par = hc % 4
        wgb, wub = sb["wgb%d" % par], sb["wub%d" % par]
        if hc + 3 < NHH:
            emit_wload(hc + 3)
        for jt in range(njt):
            t0, tn = jt * 512, min(512, tl - jt * 512)
            for dc in range(16):
                kb.op("pe", lambda e, dc=dc, jt=jt, t0=t0, tn=tn, wgb=wgb: e.matmul(
                    PSA.ap()[:, jt * 512:jt * 512 + tn],
                    wgb[:, dc * 128:(dc + 1) * 128],
                    sb["x1fm"][:, dc * tl + t0: dc * tl + t0 + tn],
                    start=(dc == 0), stop=(dc == 15)),
                    reads=[("wgb", par)] + x1fm_rd, writes=[("pb", jt)])
            for dc in range(16):
                kb.op("pe", lambda e, dc=dc, jt=jt, t0=t0, tn=tn, wub=wub: e.matmul(
                    pb(4 + jt)[:, 0:tn],
                    wub[:, dc * 128:(dc + 1) * 128],
                    sb["x1fm"][:, dc * tl + t0: dc * tl + t0 + tn],
                    start=(dc == 0), stop=(dc == 15)),
                    reads=[("wub", par)] + x1fm_rd, writes=[("pb", 4 + jt)])
            kb.op("act", lambda e, hc=hc, jt=jt, t0=t0, tn=tn: e.activation(
                out=sb["gact"][:, t0:t0 + tn], in_=PSA.ap()[:, jt * 512:jt * 512 + tn],
                func=AF.Silu, scale=sb["sgm"][:, hc:hc + 1]),
                reads=[("pb", jt), ("sb", "sgm")], writes=[("gact", jt)])
            kb.op("act", lambda e, hc=hc, jt=jt, t0=t0, tn=tn: e.activation(
                out=sb["uact"][:, t0:t0 + tn], in_=pb(4 + jt)[:, 0:tn],
                func=AF.Copy, scale=sb["sum_"][:, hc:hc + 1]),
                reads=[("pb", 4 + jt), ("sb", "sum_")], writes=[("uact", jt)])
        gm = sb["gm%d" % (par % 2)]
        kb.op("dve", lambda e, gm=gm: e.tensor_mul(gm[:], sb["gact"][:], sb["uact"][:]),
              reads=[("gact", jt) for jt in range(njt)] +
                    [("uact", jt) for jt in range(njt)],
              writes=[("gm", par % 2)])
        kb.dma("qs", lambda e, hc=hc, gm=gm: e.dma_start(out=sp["G"][hc], in_=gm[:]),
               reads=[("gm", par % 2)], writes=[("spG", hc)])


def alloc_f2_arena(self):
    tl, ntb = self.cfg.tloc, self.cfg.ntb
    self.arena_reset()
    self.arena("wdh", [128, NHH * 512], BF16)
    self.arena("gtb0", [128, NHH * 128], BF16)
    self.arena("gtb1", [128, NHH * 128], BF16)
    self.arena("x1tb0", [128, 512], BF16)
    self.arena("x1tb1", [128, 512], BF16)
    self.arena("r2q0", [128, ntb * 512], BF16)
    self.arena("r2q1", [128, ntb * 512], BF16)
    self.arena("r2q2", [128, ntb * 512], BF16)
    self.arena("r2c", [128, 512], BF16)
    self.arena("outstg0", [128, D], F32)
    self.arena("fln", [128, 12], F32)
    self.arena("sdb", [128, 512], BF16)
    self.arena("sdrow", [1, D], BF16)


def phase_f2(self):
    kb, sb, sp, din = self.kb, self.sb, self.spills, self.din
    tl, ntb = self.cfg.tloc, self.cfg.ntb
    pb = self.pb
    PSA = self.PSA
    g3 = sp["G"]
    kb.dma("qp", lambda e: e.dma_start(out=sb["sdrow"][:], in_=din["sdr"][:]),
           writes=[("sdrow",)])
    for q in range(4):
        kb.op("pe", lambda e, q=q: e.matmul(
            PSA.ap()[:, 1536:2048], sb["ones1x128"][:],
            sb["sdrow"][:, q * 512:(q + 1) * 512], start=True, stop=True),
            reads=[("sdrow",), ("sb", "ones1x128")], writes=[("pb", 3)])
        kb.op("act", lambda e: e.activation(
            out=sb["sdb"][:], in_=PSA.ap()[:, 1536:2048], func=AF.Copy),
            reads=[("pb", 3)], writes=[("sdb",)])
        kb.dma("qp", lambda e, q=q: e.dma_start(
            out=sb["wdh"][:], in_=din["WdT8"][q * 128:(q + 1) * 128, :]),
            writes=[("wdh",)])
        for hc in range(NHH):
            kb.op("dve", lambda e, hc=hc: e.scalar_tensor_tensor(
                sb["wdh"][:, hc * 512:(hc + 1) * 512],
                sb["wdh"][:, hc * 512:(hc + 1) * 512], -128.0, sb["sdb"][:],
                op0=ALU.add, op1=ALU.mult),
                reads=[("wdh",), ("sdb",)], writes=[("wdh",)])
        for tb in range(ntb):
            par = tb % 2
            xt = sb["x1tb%d" % par]
            gtb = sb["gtb%d" % (tb % 2)]
            kb.dma("qs", lambda e, tb=tb, gtb=gtb: e.dma_start(
                out=gtb[:], in_=g3[:, :, tb * 128:(tb + 1) * 128].rearrange(
                    "hc p t -> p hc t")),
                reads=[("spG", hc) for hc in range(NHH)], writes=[("gtb", tb % 2)])
            kb.dma("qa", lambda e, tb=tb, xt=xt, q=q: e.dma_start(
                out=xt[:], in_=sp["x1"][tb * 128:(tb + 1) * 128, q * 512:(q + 1) * 512]),
                reads=[("spx1", tb)], writes=[("x1tb", par)])
            bk = tb % 2
            for hc in range(NHH):
                kb.op("pe", lambda e, hc=hc, bk=bk, gtb=gtb: e.matmul(
                    PSA.ap()[:, bk * 512:(bk + 1) * 512],
                    gtb[:, hc * 128:(hc + 1) * 128],
                    sb["wdh"][:, hc * 512:(hc + 1) * 512],
                    start=(hc == 0), stop=(hc == NHH - 1)),
                    reads=[("gtb", tb % 2), ("wdh",)], writes=[("pb", bk)])
            if q < 3:
                kb.op("dve", lambda e, tb=tb, bk=bk, xt=xt, q=q: e.tensor_add(
                    sb["r2q%d" % q][:, tb * 512:(tb + 1) * 512],
                    PSA.ap()[:, bk * 512:(bk + 1) * 512], xt[:]),
                    reads=[("pb", bk), ("x1tb", par)], writes=[("r2q", q, tb)])
            else:
                kb.op("dve", lambda e, tb=tb, bk=bk, xt=xt: e.tensor_add(
                    sb["r2c"][:], PSA.ap()[:, bk * 512:(bk + 1) * 512], xt[:]),
                    reads=[("pb", bk), ("x1tb", par)], writes=[("r2c",)])
                parts = [sb["r2q0"][:, tb * 512:(tb + 1) * 512],
                         sb["r2q1"][:, tb * 512:(tb + 1) * 512],
                         sb["r2q2"][:, tb * 512:(tb + 1) * 512],
                         sb["r2c"][:]]
                rregs = [("r2q", 0, tb), ("r2q", 1, tb), ("r2q", 2, tb), ("r2c",)]
                for i4, (p4, rr) in enumerate(zip(parts, rregs)):
                    kb.op("dve", lambda e, p4=p4, i4=i4: e.reduce_sum(
                        sb["fln"][:, i4:i4 + 1], p4.unsqueeze(1), axis=AX.X),
                        reads=[rr], writes=[("fln", i4)])
                    kb.op("act", lambda e, p4=p4, i4=i4: e.activation(
                        out=pb(4 + i4)[:], in_=p4, func=AF.Square),
                        reads=[rr, ("pb", 4 + i4)], writes=[("pb", 4 + i4)])
                    kb.op("dve", lambda e, i4=i4: e.reduce_sum(
                        sb["fln"][:, 4 + i4:5 + i4], pb(4 + i4)[:].unsqueeze(1), axis=AX.X),
                        reads=[("pb", 4 + i4)], writes=[("fln", 4 + i4)])
                for i4 in (1, 2, 3):
                    kb.op("dve", lambda e, i4=i4: e.tensor_add(
                        sb["fln"][:, 0:1], sb["fln"][:, 0:1], sb["fln"][:, i4:i4 + 1]),
                        reads=[("fln", 0), ("fln", i4)], writes=[("fln", 0)])
                    kb.op("dve", lambda e, i4=i4: e.tensor_add(
                        sb["fln"][:, 4:5], sb["fln"][:, 4:5], sb["fln"][:, 4 + i4:5 + i4]),
                        reads=[("fln", 4), ("fln", 4 + i4)], writes=[("fln", 4)])
                kb.op("dve", lambda e: e.tensor_scalar(
                    sb["fln"][:, 0:1], sb["fln"][:, 0:1], 1.0 / D, None, op0=ALU.mult),
                    reads=[("fln", 0)], writes=[("fln", 0)])
                kb.op("dve", lambda e: e.tensor_scalar(
                    sb["fln"][:, 4:5], sb["fln"][:, 4:5], 1.0 / D, None, op0=ALU.mult),
                    reads=[("fln", 4)], writes=[("fln", 4)])
                kb.op("dve", lambda e: e.tensor_mul(
                    sb["fln"][:, 8:9], sb["fln"][:, 0:1], sb["fln"][:, 0:1]),
                    reads=[("fln", 0)], writes=[("fln", 8)])
                kb.op("dve", lambda e: e.tensor_sub(
                    sb["fln"][:, 4:5], sb["fln"][:, 4:5], sb["fln"][:, 8:9]),
                    reads=[("fln", 4), ("fln", 8)], writes=[("fln", 4)])
                kb.op("act", lambda e: e.activation(
                    out=sb["fln"][:, 4:5], in_=sb["fln"][:, 4:5], func=AF.Ln, bias=LNEPS),
                    reads=[("fln", 4)], writes=[("fln", 4)])
                kb.op("act", lambda e: e.activation(
                    out=sb["fln"][:, 4:5], in_=sb["fln"][:, 4:5], func=AF.Exp, scale=-0.5),
                    reads=[("fln", 4)], writes=[("fln", 4)])
                kb.op("dve", lambda e: e.tensor_scalar(
                    sb["fln"][:, 8:9], sb["fln"][:, 0:1], -1.0, None, op0=ALU.mult),
                    reads=[("fln", 0)], writes=[("fln", 8)])
                for i4, (p4, rr) in enumerate(zip(parts, rregs)):
                    kb.op("dve", lambda e, p4=p4, i4=i4: e.tensor_scalar(
                        sb["outstg0"][:, i4 * 512:(i4 + 1) * 512], p4,
                        sb["fln"][:, 8:9], sb["fln"][:, 4:5],
                        op0=ALU.add, op1=ALU.mult),
                        reads=[rr, ("fln", 4), ("fln", 8)], writes=[("ost", i4)])
                kb.dma("qa", lambda e, tb=tb: e.dma_start(
                    out=self.out_t[tb * 128:(tb + 1) * 128, :], in_=sb["outstg0"][:]),
                    reads=[("ost", 0), ("ost", 1), ("ost", 2), ("ost", 3)],
                    writes=[("outd", tb)])


Builder.alloc_f1_arena = alloc_f1_arena
Builder.phase_f1 = phase_f1
Builder.alloc_f2_arena = alloc_f2_arena
Builder.phase_f2 = phase_f2


def kernel(**inputs):
    cfg = Cfg(tloc=T // NCORES, ncores=NCORES)
    in_maps = host_prep(inputs, cfg)
    b = build_program(cfg)
    res = run_bass_kernel_spmd(b.nc, in_maps, core_ids=list(range(NCORES)))
    out = np.concatenate([np.asarray(res.results[ci]["out"]) for ci in range(NCORES)],
                         axis=0)
    return out.reshape(1, T, D).astype(np.float32)
